# version 16
# baseline (speedup 1.0000x reference)
import numpy as np
from contextlib import ExitStack
import concourse.bass as bass
import concourse.mybir as mybir
from concourse.alu_op_type import AluOpType

F32 = mybir.dt.float32
BF16 = mybir.dt.bfloat16
AF = mybir.ActivationFunctionType
AX = mybir.AxisListType

D = 1024
T = 4096
TOWN = 2048
NT = 32
NSLOT = 16
NDC = 8
FF = 4096
NE = 9
NEG = -1.0e9
J_ITERS = 7

SNAKE_A = [0, 3, 4, 7, 8, 11, 12, 15, 16, 19, 20, 23, 24, 27, 28, 31]
SNAKE_B = [1, 2, 5, 6, 9, 10, 13, 14, 17, 18, 21, 22, 25, 26, 29, 30]

ENG_OF_SEM = {"t": "tensor", "s": "scalar", "v": "vector", "g": "gpsimd"}


class Buf:
    __slots__ = ("deps",)

    def __init__(self):
        self.deps = []

    def read_by(self, sem, val):
        if val is not None:
            self.deps.append((sem, val))

    def take(self):
        d = self.deps
        self.deps = []
        return d


class Prog:
    ENGS = ("tensor", "scalar", "vector", "gpsimd", "sync")

    def __init__(self, nc, es):
        self.nc = nc
        self.es = es
        self.streams = {e: [] for e in self.ENGS}
        self.seq = 0
        self.sems = {}
        self.counts = {"t": 0, "s": 0, "v": 0, "g": 0, "hw": 0, "sw": 0}
        self.watermark = {e: {} for e in self.ENGS}

    def sem(self, name):
        if name not in self.sems:
            self.sems[name] = self.es.enter_context(self.nc.semaphore(name))
            self.counts[name] = 0
        return self.sems[name]

    SELF_SEM = {"scalar": "s", "vector": "v", "gpsimd": "g"}

    def op(self, eng, fn, waits=(), inc=None, inc_by=1):
        wm = self.watermark[eng]
        own = self.SELF_SEM.get(eng)
        waits = list(waits)
        if own is not None:
            self.sem(own)
            waits.append((own, self.counts[own]))
            if inc is None:
                inc = own
        need = {}
        for w, v in waits:
            if v is None or v <= 0:
                continue
            if ENG_OF_SEM.get(w) == eng and w != own:
                continue
            if v > need.get(w, -1):
                need[w] = v
        wl = []
        for w, v in need.items():
            if wm.get(w, 0) >= v:
                continue
            wm[w] = v
            self.sem(w)
            wl.append((w, v))
        val = None
        if inc is not None:
            self.sem(inc)
            self.counts[inc] += inc_by
            val = self.counts[inc]
        semh = self.sems[inc] if inc is not None else None

        def run(e, fn=fn, wl=tuple(wl), semh=semh, by=inc_by):
            for wname, wval in wl:
                e.wait_ge(self.sems[wname], wval)
            ins = fn(e)
            if semh is not None:
                ins.then_inc(semh, by)
        self.seq += 1
        self.streams[eng].append((self.seq, run))
        return val

    def dma(self, queue, out, in_, waits=(), sem=None):
        if sem is None:
            sem = "hw" if queue in ("sync", "scalar") else "sw"
        return self.op(queue, lambda e, o=out, i=in_: e.dma_start(out=o, in_=i),
                       waits=waits, inc=sem, inc_by=16)

    def barrier_wait(self, eng, waits):
        wm = self.watermark[eng]
        for w, v in waits:
            if v is None or v <= 0 or ENG_OF_SEM.get(w) == eng:
                continue
            if wm.get(w, 0) >= v:
                continue
            wm[w] = v
            self.sem(w)
            sh = self.sems[w]
            self.seq += 1
            self.streams[eng].append(
                (self.seq, lambda e, sh=sh, v=v: e.wait_ge(sh, v)))

    def phase_barrier(self, exclude=("pl", "pt")):
        snap = [(k, v) for k, v in self.counts.items() if k not in exclude]
        for eng in self.ENGS:
            self.barrier_wait(eng, snap)

    def finalize(self, truncate=None):
        import os
        if truncate is None:
            truncate = int(os.environ.get("MK_TRUNC", "0")) or None
        with self.nc.Block() as blk:
            for eng in self.ENGS:
                ops = self.streams[eng]
                if truncate is not None:
                    ops = [x for x in ops if x[0] <= truncate]
                if not ops:
                    continue

                def body(e, ops=ops):
                    for _, f in ops:
                        f(e)
                getattr(blk, eng)(body)


def host_prep_l0(inputs):
    I = inputs
    f32 = np.float32
    shared = {}
    shared["wqT"] = np.ascontiguousarray(np.asarray(I["L0_attn_Wq"], f32).T)
    shared["wkT"] = np.ascontiguousarray(np.asarray(I["L0_attn_Wk"], f32).T)
    shared["wvT"] = np.ascontiguousarray(np.asarray(I["L0_attn_Wv"], f32).T)
    shared["woT"] = np.ascontiguousarray(np.asarray(I["L0_attn_Wo"], f32).T)
    shared["rwT"] = np.ascontiguousarray(np.asarray(I["L0_ffn_router_w"], f32).T)
    shared["rb"] = np.asarray(I["L0_ffn_router_b"], f32).reshape(1, 8).copy()
    w1T = np.empty((NE, D, FF), f32)
    w2T = np.empty((NE, FF, D), f32)
    b1 = np.empty((NE, 1, FF), f32)
    b2 = np.empty((NE, 1, D), f32)
    w1T[0] = np.asarray(I["L0_ffn_shared_w1"], f32).T
    w2T[0] = np.asarray(I["L0_ffn_shared_w2"], f32).T
    b1[0, 0] = I["L0_ffn_shared_b1"]
    b2[0, 0] = I["L0_ffn_shared_b2"]
    for e in range(8):
        w1T[e + 1] = np.asarray(I["L0_ffn_spec_w1"][e], f32).T
        w2T[e + 1] = np.asarray(I["L0_ffn_spec_w2"][e], f32).T
        b1[e + 1, 0] = I["L0_ffn_spec_b1"][e]
        b2[e + 1, 0] = I["L0_ffn_spec_b2"][e]
    shared["w1T"] = np.ascontiguousarray(w1T)
    shared["w2T"] = np.ascontiguousarray(w2T)
    shared["b1r"] = np.ascontiguousarray(b1)
    shared["b2r"] = np.ascontiguousarray(b2)
    shared["eye"] = np.eye(128, dtype=f32)
    shared["ones1"] = np.ones((1, 512), f32)

    qr = np.arange(128)[:, None]
    cc = np.arange(512)[None, :]
    smask_low = np.where(cc > 256 + qr, NEG, 0.0).astype(f32)
    smask_up = np.where(cc > 384 + qr, NEG, 0.0).astype(f32)
    kr = np.arange(128)[:, None]
    qq = np.arange(128)[None, :]
    tri = np.where(kr > qq, NEG, 0.0).astype(f32)
    full = np.full((128, 128), NEG, f32)
    zero = np.zeros((128, 128), f32)
    tmaskT_low = np.concatenate([tri, full], axis=1)
    tmaskT_up = np.concatenate([zero, tri], axis=1)
    smask_roleA = np.stack([smask_low, smask_up])
    smask_roleB = np.stack([smask_up, smask_low])
    tmaskT_roleA = np.stack([tmaskT_low, tmaskT_up])
    tmaskT_roleB = np.stack([tmaskT_up, tmaskT_low])

    x = np.asarray(I["L0_attn_x"], f32)
    per_core = []
    for core in range(8):
        b, role = core // 2, core % 2
        snake = SNAKE_A if role == 0 else SNAKE_B
        xb = x[b]
        x_own = np.ascontiguousarray(
            np.stack([xb[128 * g:128 * g + 128] for g in snake]).reshape(TOWN, D))
        per_core.append({
            "x": np.ascontiguousarray(xb),
            "x_own": x_own,
            "smask": smask_roleA if role == 0 else smask_roleB,
            "tmaskT": tmaskT_roleA if role == 0 else tmaskT_roleB,
            **shared,
        })
    return per_core


def unscramble(part, role):
    snake = SNAKE_A if role == 0 else SNAKE_B
    return [(g, part[128 * j:128 * j + 128]) for j, g in enumerate(snake)]


def declare_l0(nc, out_kind="ExternalOutput", shared=None):
    dd = {}
    dd["x"] = nc.dram_tensor("x", [T, D], F32, kind="ExternalInput").ap()
    dd["x_own"] = nc.dram_tensor("x_own", [TOWN, D], F32, kind="ExternalInput").ap()
    for w in ("wqT", "wkT", "wvT", "woT"):
        dd[w] = nc.dram_tensor(w, [D, D], F32, kind="ExternalInput").ap()
    dd["rwT"] = nc.dram_tensor("rwT", [D, 8], F32, kind="ExternalInput").ap()
    dd["rb"] = nc.dram_tensor("rb", [1, 8], F32, kind="ExternalInput").ap()
    dd["w1T"] = nc.dram_tensor("w1T", [NE, D, FF], F32, kind="ExternalInput").ap()
    dd["w2T"] = nc.dram_tensor("w2T", [NE, FF, D], F32, kind="ExternalInput").ap()
    dd["b1r"] = nc.dram_tensor("b1r", [NE, 1, FF], F32, kind="ExternalInput").ap()
    dd["b2r"] = nc.dram_tensor("b2r", [NE, 1, D], F32, kind="ExternalInput").ap()
    if shared and "eye" in shared:
        dd["eye"] = shared["eye"]
        dd["ones1"] = shared["ones1"]
    else:
        dd["eye"] = nc.dram_tensor("eye", [128, 128], F32, kind="ExternalInput").ap()
        dd["ones1"] = nc.dram_tensor("ones1", [1, 512], F32, kind="ExternalInput").ap()
    dd["smask"] = nc.dram_tensor("smask", [2, 128, 512], F32, kind="ExternalInput").ap()
    dd["tmaskT"] = nc.dram_tensor("tmaskT", [2, 128, 256], F32, kind="ExternalInput").ap()
    dd["outT_sp"] = nc.dram_tensor("outT_sp", [2, 128, 4 * TOWN], BF16, kind="Internal").ap()
    dd["x1p"] = nc.dram_tensor("x1p", [TOWN, D], F32, kind=out_kind).ap()
    return dd


def emit_l0(nc, es, P, dd, upto="full", ntc=8, sub=9, scatter=None):
    d_x = dd["x"]
    d_xown = dd["x_own"]
    d_wT = {w: dd[w] for w in ("wqT", "wkT", "wvT", "woT")}
    d_rwT = dd["rwT"]
    d_rb = dd["rb"]
    d_w1T = dd["w1T"]
    d_w2T = dd["w2T"]
    d_b1 = dd["b1r"]
    d_b2 = dd["b2r"]
    d_eye = dd["eye"]
    d_ones = dd["ones1"]
    d_smask = dd["smask"]
    d_tmaskT = dd["tmaskT"]
    d_osp = dd["outT_sp"]
    d_x1p = dd["x1p"]
    les = ExitStack()

    def sb(name, shape, dt):
        return les.enter_context(nc.sbuf_tensor(name, shape, dt))

    def ps(name, shape, dt):
        return les.enter_context(nc.psum_tensor(name, shape, dt))

    s_eye32 = sb("eye32", [128, 128], F32)
    s_eyeb = sb("eyeb", [128, 128], BF16)
    s_ones = sb("ones_b", [1, 512], BF16)
    s_onesrow = sb("onesrow", [1, 128], BF16)
    s_ones32 = sb("ones32r", [1, 128], F32)
    s_smask = sb("smask_s", [128, 2 * 512], F32)
    s_rw = sb("rw_b", [128, NDC * 8], F32)
    s_rb = sb("rb_b", [1, 8], F32)
    s_xnT_own = sb("xnT_own", [128, NDC * TOWN], BF16)
    s_js = None
    s_xnt32 = None
    s_jvs = sb("junk_v", [128, D], BF16)
    s_gc = sb("gc", [128, 40], F32)
    s_stat = sb("stat", [128, 8], F32)
    s_xtile = sb("xtile", [128, D], F32)
    s_xn_b = sb("xn_b", [128, D], F32)
    s_comb = sb("comb", [128, NSLOT * 8], F32)
    s_rsm = sb("rsm", [128, 12], F32)
    s_b1 = sb("b1_b", [1, FF], BF16)
    s_b2 = sb("b2_b", [1, D], BF16)
    s_gtile = sb("gtile", [128, D], F32)
    s_x1t = sb("x1t", [128, D], F32)

    pb = [ps(f"pb{i}", [128, 512], F32) for i in range(8)]
    B_ = [Buf() for _ in range(8)]
    Bxtile, Bxn, Bgc, Btau, Bstat, Brsm = (Buf() for _ in range(6))
    Bxnt32 = Buf()
    Bjs, Bjv, Bgtile, Bx1t = (Buf() for _ in range(4))

    P.dma("sync", s_eye32[:], d_eye, sem="c0")
    P.dma("gpsimd", s_eyeb[:], d_eye, sem="sw")
    v_eyeb = P.counts["sw"]
    for p in range(2):
        P.dma("sync", s_smask[:, p * 512:(p + 1) * 512], d_smask[p], sem="c0")
    for dt in range(NDC):
        P.dma("sync", s_rw[:, dt * 8:(dt + 1) * 8],
              d_rwT[dt * 128:(dt + 1) * 128, :], sem="c0")
    P.dma("sync", s_rb[:], d_rb, sem="c0")
    v_hw0 = P.counts["c0"]
    v_eye32 = v_hw0
    P.dma("gpsimd", s_ones[:], d_ones)
    v_sw0 = P.counts["sw"]
    v_ones = P.op("vector", lambda e: e.memset(s_onesrow[:], 1.0), inc="v")
    v_ones = P.op("vector", lambda e: e.memset(s_ones32[:], 1.0), inc="v")
    s_eps = sb("epscol", [128, 1], F32)
    v_eps = P.op("vector", lambda e: e.memset(s_eps[:], 1e-5), inc="v")

    def ln_tile(src_row):
        vd = P.dma("sync", s_xtile[:], src_row, waits=Bxtile.take())
        wx = [("hw", vd)]
        vm = P.op("vector", lambda e: e.tensor_scalar(
            s_jvs[:, 0:D], s_xtile[:], 1.0 / D, None, AluOpType.mult,
            AluOpType.add, accum_out=s_stat[:, 0:1]),
            waits=wx + Bjv.take() + Bstat.take(), inc="v")
        vs = P.op("scalar", lambda e: e.activation(
            s_js[:, 0:D], s_xtile[:], AF.Square, accum_out=s_stat[:, 1:2]),
            waits=wx + Bjs.take(), inc="s")
        Bxtile.read_by("v", vm)
        Bxtile.read_by("s", vs)
        Bjv.read_by("v", vm)
        Bjs.read_by("s", vs)
        v1 = P.op("vector", lambda e: e.scalar_tensor_tensor(
            s_stat[:, 2:3], s_stat[:, 0:1], -1.0, s_stat[:, 0:1],
            AluOpType.mult, AluOpType.mult), waits=[("s", vs)], inc="v")
        v2 = P.op("vector", lambda e: e.scalar_tensor_tensor(
            s_stat[:, 3:4], s_stat[:, 1:2], 1.0 / D, s_stat[:, 2:3],
            AluOpType.mult, AluOpType.add), inc="v")
        s1 = P.op("scalar", lambda e: e.activation(
            s_stat[:, 4:5], s_stat[:, 3:4], AF.Sqrt, bias=s_eps[:]),
            waits=[("v", v2), ("v", v_eps)], inc="s")
        v3 = P.op("vector", lambda e: e.reciprocal(s_stat[:, 5:6], s_stat[:, 4:5]),
                  waits=[("s", s1)], inc="v")
        v4 = P.op("vector", lambda e: e.scalar_tensor_tensor(
            s_stat[:, 6:7], s_stat[:, 0:1], -1.0, s_stat[:, 5:6],
            AluOpType.mult, AluOpType.mult), inc="v")
        sx = P.op("scalar", lambda e: e.activation(
            s_xn_b[:], s_xtile[:], AF.Identity, bias=s_stat[:, 6:7],
            scale=s_stat[:, 5:6]), waits=[("v", v4)] + Bxn.take(), inc="s")
        Bxtile.read_by("s", sx)
        Bstat.read_by("s", sx)
        return sx

    def xpose_xn(dest, dest_stride, destcol, wait_s, also32=False):
        for g4 in range(2):
            bank = 4 + g4
            tvl = None
            for slot in range(4):
                dt = g4 * 4 + slot
                tvl = P.op("tensor", lambda e, dt=dt, bank=bank, slot=slot: e.transpose(
                    pb[bank][:, slot * 128:slot * 128 + 128],
                    s_xn_b[:, dt * 128:(dt + 1) * 128], s_eye32[:]),
                    waits=[("s", wait_s), ("c0", v_eye32)] +
                          (B_[bank].take() if slot == 0 else []), inc="t")
            for slot in range(4):
                dt = g4 * 4 + slot
                sv = P.op("scalar",
                          lambda e, dt=dt, bank=bank, slot=slot: e.copy(
                              dest[:, dt * dest_stride + destcol:dt * dest_stride + destcol + 128],
                              pb[bank][:, slot * 128:slot * 128 + 128]),
                          waits=[("t", tvl)], inc="s")
                B_[bank].read_by("s", sv)
                if also32:
                    sv32 = P.op("scalar",
                                lambda e, dt=dt, bank=bank, slot=slot: e.copy(
                                    s_xnt32[:, dt * 128:(dt + 1) * 128],
                                    pb[bank][:, slot * 128:slot * 128 + 128]),
                                waits=(Bxnt32.take() if dt == 0 else []), inc="s")
                    B_[bank].read_by("s", sv32)
        Bxn.read_by("t", P.counts["t"])
        return P.counts["s"]

    def mm_group(bank, cols, mms, waits=(), extra_first=(), rows=(0, 128)):
        n = len(mms)
        w0 = list(waits) + B_[bank].take() + list(extra_first)
        last = None
        for i, (fl, fr) in enumerate(mms):
            last = P.op("tensor",
                        lambda e, fl=fl, fr=fr, i=i: e.matmul(
                            pb[bank][rows[0]:rows[1], cols[0]:cols[1]], fl(), fr(),
                            start=(i == 0), stop=(i == n - 1)),
                        waits=(w0 if i == 0 else ()),
                        inc=("t" if i == n - 1 else None))
        return last

    def emit_router_tile(j, v32):
        mms = [(lambda dt=dt: s_xnt32[:, dt * 128:(dt + 1) * 128],
                lambda dt=dt: s_rw[:, dt * 8:(dt + 1) * 8]) for dt in range(NDC)]
        mms.append((lambda: s_ones32[0:1, 0:128], lambda: s_rb[0:1, 0:8]))
        tv = mm_group(6, (0, 8), mms, waits=[("c0", v_hw0), ("s", v32),
                                             ("v", v_ones)])
        Bxnt32.read_by("t", tv)
        vm = P.op("vector", lambda e: e.tensor_reduce(
            s_rsm[:, 0:1], pb[6][:, 0:8], AX.X, AluOpType.max, negate=True),
            waits=[("t", tv)] + Brsm.take(), inc="v")
        se = P.op("scalar", lambda e: e.activation(
            s_rsm[:, 1:9], pb[6][:, 0:8], AF.Exp, bias=s_rsm[:, 0:1],
            accum_out=s_rsm[:, 9:10]), waits=[("v", vm)], inc="s")
        B_[6].read_by("s", se)
        vv = P.op("vector", lambda e: e.reciprocal(s_rsm[:, 10:11], s_rsm[:, 9:10]),
                  waits=[("s", se)], inc="v")
        vv = P.op("vector", lambda e, j=j: e.tensor_scalar(
            s_comb[:, j * 8:j * 8 + 8], s_rsm[:, 1:9], s_rsm[:, 10:11], None,
            AluOpType.mult), inc="v")
        vv = P.op("vector", lambda e, j=j: e.tensor_reduce(
            s_rsm[:, 0:1], s_comb[:, j * 8:j * 8 + 8], AX.X, AluOpType.max), inc="v")
        vv = P.op("vector", lambda e, j=j: e.tensor_scalar(
            s_rsm[:, 1:9], s_comb[:, j * 8:j * 8 + 8], s_rsm[:, 0:1], None,
            AluOpType.is_equal), inc="v")
        vv = P.op("vector", lambda e, j=j: e.scalar_tensor_tensor(
            s_rsm[:, 1:9], s_rsm[:, 1:9], -4.0, s_comb[:, j * 8:j * 8 + 8],
            AluOpType.mult, AluOpType.add), inc="v")
        vv = P.op("vector", lambda e: e.tensor_reduce(
            s_rsm[:, 9:10], s_rsm[:, 1:9], AX.X, AluOpType.max), inc="v")
        vv = P.op("vector", lambda e: e.tensor_tensor(
            s_rsm[:, 11:12], s_rsm[:, 0:1], s_rsm[:, 9:10], AluOpType.add), inc="v")
        vv = P.op("vector", lambda e: e.reciprocal(s_rsm[:, 10:11], s_rsm[:, 11:12]),
                  inc="v")
        vv = P.op("vector", lambda e, j=j: e.tensor_scalar(
            s_rsm[:, 1:9], s_comb[:, j * 8:j * 8 + 8], s_rsm[:, 9:10], None,
            AluOpType.is_ge), inc="v")
        vv = P.op("vector", lambda e, j=j: e.tensor_tensor(
            s_rsm[:, 1:9], s_rsm[:, 1:9], s_comb[:, j * 8:j * 8 + 8],
            AluOpType.mult), inc="v")
        vv = P.op("vector", lambda e, j=j: e.tensor_scalar(
            s_comb[:, j * 8:j * 8 + 8], s_rsm[:, 1:9], s_rsm[:, 10:11], None,
            AluOpType.mult), inc="v")
        Brsm.read_by("v", vv)

    emitted_xnt_own = [False]
    v_own_holder = [0]
    LEVELS = {"kv0": 0, "own": 1, "attn0": 2, "kv1": 3, "attn1": 4, "router": 5,
              "full": 9}
    LVL = LEVELS[upto]

    def finish_debug():
        P.phase_barrier()
        for j in range(NSLOT):
            sd = P.dma("sync", d_x1p[j * 128:j * 128 + 128, :], s_xtile[:],
                       sem="st")
        P.barrier_wait("sync", [("st", P.counts["st"])])
        les.close()
        return

    def emit_half(half):
        nonlocal s_js, s_xnt32
        ph = ExitStack()
        s_js = ph.enter_context(nc.sbuf_tensor(f"junk_s{half}", [128, T], BF16))
        s_xnt32 = ph.enter_context(nc.sbuf_tensor(f"xnt32_{half}", [128, NDC * 128], F32))
        s_KT = ph.enter_context(nc.sbuf_tensor(f"KT{half}", [128, 4 * T], BF16))
        s_V = ph.enter_context(nc.sbuf_tensor(f"V{half}", [128, NT * 512], BF16))
        BKT, BV = Buf(), Buf()

        kv = ExitStack()
        s_wk = kv.enter_context(nc.sbuf_tensor(f"wk{half}", [128, NDC * 512], BF16))
        s_wv = kv.enter_context(nc.sbuf_tensor(f"wv{half}", [128, NDC * 512], BF16))
        s_xnTc = kv.enter_context(nc.sbuf_tensor(f"xnTc{half}", [128, NDC * 512], BF16))
        BxnTc = Buf()
        P.phase_barrier()
        for dt in range(NDC):
            P.dma("gpsimd", s_wk[:, dt * 512:(dt + 1) * 512],
                  d_wT["wkT"][dt * 128:(dt + 1) * 128, half * 512:half * 512 + 512])
            P.dma("gpsimd", s_wv[:, dt * 512:(dt + 1) * 512],
                  d_wT["wvT"][dt * 128:(dt + 1) * 128, half * 512:half * 512 + 512])
        v_swk = P.counts["sw"]
        for tc in range(ntc):
            for ti in range(4):
                tok0 = tc * 512 + ti * 128
                if sub >= 2:
                    sx = ln_tile(d_x[tok0:tok0 + 128, :])
                if sub >= 3:
                    xpose_xn(s_xnTc, 512, ti * 128, sx)
            vxn = P.counts["s"]
            wxn = [("s", vxn), ("sw", v_swk)]
            for ec in range(4 if sub >= 4 else 0):
                bank = ec % 2
                mms = [(lambda dt=dt, ec=ec: s_wk[:, dt * 512 + ec * 128:dt * 512 + ec * 128 + 128],
                        lambda dt=dt: s_xnTc[:, dt * 512:(dt + 1) * 512])
                       for dt in range(NDC)]
                tv = mm_group(bank, (0, 512), mms, waits=wxn)
                sv = P.op("scalar", lambda e, ec=ec, tc=tc, bank=bank: e.copy(
                    s_KT[:, ec * T + tc * 512:ec * T + tc * 512 + 512],
                    pb[bank][:, 0:512]), waits=[("t", tv)], inc="s")
                B_[bank].read_by("s", sv)
            for ti in range(4 if sub >= 5 else 0):
                bank = 2 + ti % 2
                tt = tc * 4 + ti
                mms = [(lambda dt=dt, ti=ti: s_xnTc[:, dt * 512 + ti * 128:dt * 512 + ti * 128 + 128],
                        lambda dt=dt: s_wv[:, dt * 512:(dt + 1) * 512])
                       for dt in range(NDC)]
                tv = mm_group(bank, (0, 512), mms, waits=wxn)
                sv = P.op("scalar", lambda e, tt=tt, bank=bank: e.copy(
                    s_V[:, tt * 512:tt * 512 + 512], pb[bank][:, 0:512]),
                    waits=[("t", tv)], inc="s")
                B_[bank].read_by("s", sv)
            BxnTc.read_by("t", P.counts["t"])
        P.phase_barrier()
        kv.close()

        if LVL < 1 and half == 0:
            ph.close()
            return
        if not emitted_xnt_own[0]:
            emitted_xnt_own[0] = True
            for j in range(NSLOT):
                sx = ln_tile(d_xown[j * 128:j * 128 + 128, :])
                v32 = xpose_xn(s_xnT_own, TOWN, j * 128, sx, also32=True)
                emit_router_tile(j, v32)
            v_own_holder[0] = P.counts["s"]

        if LVL < 2 + 2 * half:
            ph.close()
            return
        at = ExitStack()
        s_wq = at.enter_context(nc.sbuf_tensor(f"wq{half}", [128, NDC * 512], BF16))
        s_scores = at.enter_context(nc.sbuf_tensor(f"sc{half}", [128, T], F32))
        s_attnT = at.enter_context(nc.sbuf_tensor(f"aT{half}", [128, T], BF16))
        s_QT = at.enter_context(nc.sbuf_tensor(f"QT{half}", [128, 4 * 128], BF16))
        s_outTh = at.enter_context(nc.sbuf_tensor(f"oT{half}", [128, 4 * TOWN], BF16))
        Bscores, BattnT, BQT, Btaub = Buf(), Buf(), Buf(), Buf()
        P.phase_barrier()
        for dt in range(NDC):
            P.dma("gpsimd", s_wq[:, dt * 512:(dt + 1) * 512],
                  d_wT["wqT"][dt * 128:(dt + 1) * 128, half * 512:half * 512 + 512])
        v_swq = P.counts["sw"]

        for j in range(NSLOT):
            E = 256 * (j + 1)
            nkc = E // 256
            qdeps = BQT.take()
            tvq = None
            for ec in range(4):
                mms = [(lambda dt=dt, ec=ec: s_wq[:, dt * 512 + ec * 128:dt * 512 + ec * 128 + 128],
                        lambda dt=dt, j=j: s_xnT_own[:, dt * TOWN + j * 128:dt * TOWN + j * 128 + 128])
                       for dt in range(NDC)]
                tvq = mm_group(6, (ec * 128, ec * 128 + 128), mms,
                               waits=[("sw", v_swq), ("s", v_own_holder[0])],
                               extra_first=(qdeps if ec == 0 else ()))
            for ec in range(4):
                sv = P.op("scalar", lambda e, ec=ec: e.activation(
                    s_QT[:, ec * 128:ec * 128 + 128],
                    pb[6][:, ec * 128:ec * 128 + 128], AF.Copy, scale=0.125),
                    waits=[("t", tvq)], inc="s")
                B_[6].read_by("s", sv)
            vqt = P.counts["s"]
            for hl in range(8):
                ec, po = hl // 2, 64 * (hl % 2)
                sdeps = Bscores.take()
                for kc in range(nkc):
                    bank = kc % 2
                    co = 0
                    tv = mm_group(bank, (co, co + 256),
                                  [(lambda ec=ec, po=po: s_QT[po:po + 64, ec * 128:ec * 128 + 128],
                                    lambda ec=ec, po=po, kc=kc: s_KT[po:po + 64, ec * T + kc * 256:ec * T + kc * 256 + 256])],
                                  waits=[("s", vqt)])
                    if kc >= nkc - 2:
                        mo = 256 * (kc - (nkc - 2))
                        ev = P.op("vector", lambda e, kc=kc, bank=bank, co=co, mo=mo, j=j:
                                  e.tensor_tensor(
                                      s_scores[:, kc * 256:kc * 256 + 256],
                                      pb[bank][:, co:co + 256],
                                      s_smask[:, (j % 2) * 512 + mo:(j % 2) * 512 + mo + 256],
                                      AluOpType.add),
                                  waits=[("t", tv), ("c0", v_hw0)] +
                                        (sdeps if kc == 0 else []), inc="v")
                        B_[bank].read_by("v", ev)
                    else:
                        ev = P.op("scalar", lambda e, kc=kc, bank=bank, co=co:
                                  e.copy(s_scores[:, kc * 256:kc * 256 + 256],
                                         pb[bank][:, co:co + 256]),
                                  waits=[("t", tv)] + (sdeps if kc == 0 else []),
                                  inc="s")
                        B_[bank].read_by("s", ev)
                wsc = [("s", P.counts["s"]), ("v", P.counts["v"])]
                vr = P.op("vector", lambda e, hl=hl, E=E: e.tensor_reduce(
                    s_gc[:, 24 + hl:25 + hl], s_scores[:, 0:E], AX.X,
                    AluOpType.max, negate=True),
                    waits=wsc + (Bgc.take() if hl == 0 else []), inc="v")
                vr = P.op("vector", lambda e, hl=hl: e.tensor_scalar(
                    s_gc[:, 24 + hl:25 + hl], s_gc[:, 24 + hl:25 + hl], 1.0,
                    None, AluOpType.add), inc="v")
                vr = P.op("vector", lambda e, hl=hl: e.tensor_scalar(
                    s_gc[:, 16 + hl:17 + hl], s_gc[:, 24 + hl:25 + hl], -1.0,
                    None, AluOpType.mult), inc="v")
                adeps = BattnT.take()
                for it in range(J_ITERS):
                    sg = P.op("scalar", lambda e, hl=hl, E=E: e.activation(
                        s_js[:, 0:E], s_scores[:, 0:E], AF.Relu,
                        bias=s_gc[:, 24 + hl:25 + hl],
                        accum_out=s_gc[:, hl:hl + 1]),
                        waits=[("v", vr)] + Bjs.take(), inc="s")
                    Bjs.read_by("s", sg)
                    vc = P.op("vector", lambda e, hl=hl, E=E: e.tensor_scalar(
                        s_attnT[:, 0:E], s_scores[:, 0:E],
                        s_gc[:, 16 + hl:17 + hl], None, AluOpType.is_gt,
                        AluOpType.add, accum_out=s_gc[:, 8 + hl:9 + hl]),
                        waits=(adeps if it == 0 else []), inc="v")
                    vu = P.op("vector", lambda e, hl=hl: e.reciprocal(
                        s_gc[:, 32 + hl:33 + hl], s_gc[:, 8 + hl:9 + hl]), inc="v")
                    vu = P.op("vector", lambda e, hl=hl: e.scalar_tensor_tensor(
                        s_gc[:, 32 + hl:33 + hl], s_gc[:, hl:hl + 1], -1.0,
                        s_gc[:, 32 + hl:33 + hl], AluOpType.add, AluOpType.mult),
                        waits=[("s", sg)], inc="v")
                    vr = P.op("vector", lambda e, hl=hl: e.tensor_tensor(
                        s_gc[:, 24 + hl:25 + hl], s_gc[:, 24 + hl:25 + hl],
                        s_gc[:, 32 + hl:33 + hl], AluOpType.subtract), inc="v")
                    vu = P.op("vector", lambda e, hl=hl: e.tensor_tensor(
                        s_gc[:, 16 + hl:17 + hl], s_gc[:, 16 + hl:17 + hl],
                        s_gc[:, 32 + hl:33 + hl], AluOpType.add), inc="v")
                Bscores.read_by("v", vr)
                Bscores.read_by("s", P.counts["s"])
                jdeps = Bjs.take()
                sgf = P.op("scalar", lambda e, hl=hl, E=E: e.activation(
                    s_js[:, 0:E], s_scores[:, 0:E], AF.Relu,
                    bias=s_gc[:, 24 + hl:25 + hl]),
                    waits=[("v", vr)] + jdeps, inc="s")
                Bscores.read_by("s", sgf)
                ttl = None
                evl = None
                for grp in range(j + 1):
                    bank = 3 + (grp % 2) * 2
                    zdeps = B_[bank].take()
                    tvz = None
                    for kt in range(2):
                        ktg = grp * 2 + kt
                        tvz = P.op("tensor",
                                   lambda e, ktg=ktg, bank=bank, kt=kt:
                                   e.matmul(pb[bank][:, kt * 128:kt * 128 + 128],
                                            s_js[:, ktg * 128:ktg * 128 + 128],
                                            s_eyeb[:], start=True, stop=True),
                                   waits=(([("s", sgf), ("sw", v_eyeb)] + zdeps)
                                          if kt == 0 else ()), inc="t")
                    ttl = tvz
                    evl = P.op("scalar", lambda e, grp=grp, bank=bank: e.copy(
                        s_attnT[:, grp * 256:grp * 256 + 256],
                        pb[bank][:, 0:256]), waits=[("t", tvz)], inc="s")
                    B_[bank].read_by("s", evl)
                Bjs.read_by("t", ttl)
                mms = []
                for ktg in range(2 * (j + 1)):
                    mms.append((lambda ktg=ktg, hl=hl: s_V[:, ktg * 512 + hl * 64:ktg * 512 + hl * 64 + 64],
                                lambda ktg=ktg: s_attnT[:, ktg * 128:ktg * 128 + 128]))
                tv = mm_group(2, (0, 128), mms, waits=[("s", evl)], rows=(0, 64))
                BattnT.read_by("t", tv)
                BQT.read_by("t", tv)
                BKT.read_by("t", tv)
                BV.read_by("t", tv)
                dl, po_o = hl // 2, 64 * (hl % 2)
                sv = P.op("scalar", lambda e, dl=dl, po_o=po_o, j=j: e.copy(
                    s_outTh[po_o:po_o + 64, dl * TOWN + j * 128:dl * TOWN + j * 128 + 128],
                    pb[2][0:64, 0:128]), waits=[("t", tv)], inc="s")
                B_[2].read_by("s", sv)

        vsp = P.dma("sync", d_osp[half], s_outTh[:],
                    waits=[("s", P.counts["s"])], sem="osp")
        P.phase_barrier()
        at.close()
        ph.close()

    emit_half(0)
    if LVL < 3:
        return finish_debug()
    emit_half(1)
    if LVL < 5:
        return finish_debug()

    ff = ExitStack()
    s_outT = ff.enter_context(nc.sbuf_tensor("outT", [128, NDC * TOWN], BF16))
    s_wo = ff.enter_context(nc.sbuf_tensor("wo_b", [128, NDC * D], BF16))
    s_w1e = ff.enter_context(nc.sbuf_tensor("w1e", [128, 2 * NDC * 512], BF16))
    s_w2c = [ff.enter_context(nc.sbuf_tensor(f"w2c{i}", [128, D], BF16))
             for i in range(3)]
    s_h = ff.enter_context(nc.sbuf_tensor("h_sb", [128, 32 * 512], BF16))
    s_acc = ff.enter_context(nc.sbuf_tensor("acc", [128, 4 * D], F32))
    s_stile = ff.enter_context(nc.sbuf_tensor("stile", [128, 4 * D], F32))
    Bh, Bacc, Bstile = Buf(), Buf(), Buf()
    Bw1e = [Buf(), Buf()]
    Bw2 = [Buf(), Buf(), Buf()]
    Bb1, Bb2 = Buf(), Buf()
    P.phase_barrier()
    v_spills = P.counts["osp"] if "osp" in P.counts else 32
    for half in (0, 1):
        P.dma("sync", s_outT[:, half * 4 * TOWN:(half + 1) * 4 * TOWN], d_osp[half],
              waits=[("osp", v_spills)], sem="osp")
    v_hwo = P.counts["osp"]
    for dt in range(NDC):
        P.dma("gpsimd", s_wo[:, dt * D:(dt + 1) * D],
              d_wT["woT"][dt * 128:(dt + 1) * 128, :])
    v_swo = P.counts["sw"]

    for ch in range(4):
        sdeps = Bstile.take()
        for ti in range(4):
            j = ch * 4 + ti
            for ecc in range(2):
                bank = ecc
                mms = [(lambda dt=dt, j=j: s_outT[:, dt * TOWN + j * 128:dt * TOWN + j * 128 + 128],
                        lambda dt=dt, ecc=ecc: s_wo[:, dt * D + ecc * 512:dt * D + ecc * 512 + 512])
                       for dt in range(NDC)]
                tv = mm_group(bank, (0, 512), mms,
                              waits=[("sw", v_swo), ("osp", v_hwo)],
                              extra_first=(sdeps if ti == 0 and ecc == 0 else ()))
                sv = P.op("scalar", lambda e, ti=ti, ecc=ecc, bank=bank: e.copy(
                    s_stile[:, ti * D + ecc * 512:ti * D + ecc * 512 + 512],
                    pb[bank][:, 0:512]), waits=[("t", tv)], inc="s")
                B_[bank].read_by("s", sv)
        for e_i in range(NE):
            hdeps = Bh.take()
            vb1 = P.dma("gpsimd", s_b1[:], d_b1[e_i], waits=Bb1.take(), sem="b1s")
            vb2 = P.dma("gpsimd", s_b2[:], d_b2[e_i], waits=Bb2.take(), sem="b2s")
            for fc in range(8):
                lv = None
                par = fc % 2
                w1deps = Bw1e[par].take()
                wsem1 = "w1a" if par == 0 else "w1b"
                for dt in range(NDC):
                    lv = P.dma("gpsimd",
                               s_w1e[:, par * 4096 + dt * 512:par * 4096 + (dt + 1) * 512],
                               d_w1T[e_i][dt * 128:(dt + 1) * 128, fc * 512:(fc + 1) * 512],
                               waits=(w1deps if dt == 0 else []), sem=wsem1)
                wld = [(wsem1, lv)]
                for ft in range(4):
                    fglob = fc * 4 + ft
                    bank = 2 + fglob % 2
                    mms = [(lambda dt=dt, ft=ft, par=par: s_w1e[:, par * 4096 + dt * 512 + ft * 128:par * 4096 + dt * 512 + ft * 128 + 128],
                            lambda dt=dt, ch=ch: s_xnT_own[:, dt * TOWN + ch * 512:dt * TOWN + ch * 512 + 512])
                           for dt in range(NDC)]
                    mms.append((lambda fglob=fglob: s_b1[0:1, fglob * 128:fglob * 128 + 128],
                                lambda: s_ones[0:1, 0:512]))
                    tv = mm_group(bank, (0, 512), mms,
                                  waits=wld + [("b1s", vb1), ("s", v_own_holder[0])],
                                  extra_first=(hdeps if fglob == 0 else ()))
                    Bb1.read_by("t", tv)
                    sv = P.op("scalar", lambda e, fglob=fglob, bank=bank:
                              e.activation(s_h[:, fglob * 512:fglob * 512 + 512],
                                           pb[bank][:, 0:512], AF.Gelu_apprx_tanh),
                              waits=[("t", tv)], inc="s")
                    B_[bank].read_by("s", sv)
                    Bw1e[par].read_by("t", tv)
            vh = P.counts["s"]
            odeps = [B_[b].take() for b in range(8)]
            for ft in range(32):
                wbuf2 = s_w2c[ft % 3]
                wsem = ("w2a", "w2b", "w2c")[ft % 3]
                lv = P.dma("gpsimd", wbuf2[:],
                           d_w2T[e_i][ft * 128:(ft + 1) * 128, :],
                           waits=Bw2[ft % 3].take(), sem=wsem)
                for ti in range(4):
                    for dc in range(2):
                        bank = ti * 2 + dc
                        tvw = P.op("tensor",
                             lambda e, ft=ft, ti=ti, dc=dc, bank=bank, wbuf2=wbuf2:
                             e.matmul(pb[bank][:, 0:512],
                                      s_h[:, ft * 512 + ti * 128:ft * 512 + ti * 128 + 128],
                                      wbuf2[:, dc * 512:dc * 512 + 512],
                                      start=(ft == 0), stop=False),
                             waits=([(wsem, lv), ("s", vh)] +
                                    (odeps[bank] if ft == 0 else [])),
                             inc=("t" if (ti == 3 and dc == 1) else None))
                Bw2[ft % 3].read_by("t", tvw)
            vo = None
            for ti in range(4):
                for dc in range(2):
                    bank = ti * 2 + dc
                    vo = P.op("tensor", lambda e, dc=dc, bank=bank:
                              e.matmul(pb[bank][:, 0:512],
                                       s_onesrow[0:1, 0:128],
                                       s_b2[0:1, dc * 512:dc * 512 + 512],
                                       start=False, stop=True),
                              waits=[("v", v_ones), ("b2s", vb2)], inc="t")
                    Bb2.read_by("t", vo)
            Bh.read_by("t", vo)
            for ti in range(4):
                for dc in range(2):
                    bank = ti * 2 + dc
                    j = ch * 4 + ti
                    if e_i == 0:
                        sv = P.op("scalar", lambda e, ti=ti, dc=dc, bank=bank:
                                  e.copy(s_acc[:, ti * D + dc * 512:ti * D + dc * 512 + 512],
                                         pb[bank][:, 0:512]),
                                  waits=[("t", vo)] +
                                        (Bacc.take() if ti == 0 and dc == 0 else []),
                                  inc="s")
                        B_[bank].read_by("s", sv)
                    else:
                        vv = P.op("vector",
                                  lambda e, ti=ti, dc=dc, bank=bank, j=j, e_i=e_i:
                                  e.scalar_tensor_tensor(
                                      s_acc[:, ti * D + dc * 512:ti * D + dc * 512 + 512],
                                      pb[bank][:, 0:512],
                                      s_comb[:, j * 8 + (e_i - 1):j * 8 + e_i],
                                      s_acc[:, ti * D + dc * 512:ti * D + dc * 512 + 512],
                                      AluOpType.mult, AluOpType.add),
                                  waits=[("t", vo), ("s", P.counts["s"])], inc="v")
                        B_[bank].read_by("v", vv)
        for ti in range(4):
            j = ch * 4 + ti
            sg = P.op("scalar", lambda e, ti=ti: e.activation(
                s_gtile[:], s_acc[:, ti * D:(ti + 1) * D], AF.Silu),
                waits=[("v", P.counts["v"])] + Bgtile.take(), inc="s")
            Bacc.read_by("s", sg)
            vd = P.dma("sync", s_xtile[:], d_xown[j * 128:j * 128 + 128, :],
                       waits=Bxtile.take())
            vv = P.op("vector", lambda e, ti=ti: e.tensor_tensor(
                s_gtile[:], s_gtile[:], s_stile[:, ti * D:(ti + 1) * D],
                AluOpType.mult), waits=[("s", sg)], inc="v")
            vv = P.op("vector", lambda e: e.tensor_tensor(
                s_x1t[:], s_xtile[:], s_gtile[:], AluOpType.add),
                waits=[("hw", vd)] + Bx1t.take(), inc="v")
            Bxtile.read_by("v", vv)
            Bgtile.read_by("v", vv)
            Bstile.read_by("v", vv)
            sd = P.dma("sync", d_x1p[j * 128:j * 128 + 128, :], s_x1t[:],
                       waits=[("v", vv)], sem="st")
            Bx1t.read_by("st", sd)
            if scatter is not None:
                sc = scatter(j, s_x1t, [("v", vv)])
                Bx1t.read_by("sc", sc)

    P.phase_barrier()
    ff.close()
    les.close()
    return


def build_l0(upto="full", ntc=8, sub=9):
    nc = bass.Bass("TRN2", target_bir_lowering=False)
    es = ExitStack()
    P = Prog(nc, es)
    dd = declare_l0(nc)
    emit_l0(nc, es, P, dd, upto=upto, ntc=ntc, sub=sub)
    P.barrier_wait("sync", [("st", P.counts["st"])])
    P.finalize()
    es.close()
    return nc, es



def host_prep_l1(inputs, x1=None):
    I = inputs
    f32 = np.float32
    sh = {}
    for nm, key in (("wq", "L1_attn_wq"), ("wk", "L1_attn_wk"),
                    ("wv", "L1_attn_wv"), ("wout", "L1_attn_w_out")):
        sh[nm] = np.ascontiguousarray(np.asarray(I[key], f32))
    sh["wgq"] = np.ascontiguousarray(np.asarray(I["L1_attn_wgq"], f32))
    sh["wgk"] = np.ascontiguousarray(np.asarray(I["L1_attn_wgk"], f32))
    sh["wgv"] = np.ascontiguousarray(np.asarray(I["L1_attn_wgv"], f32))
    sh["rout"] = np.ascontiguousarray(np.asarray(I["L1_attn_r_out_w"], f32))
    gwbd = np.zeros((D, 32), f32)
    glw = np.asarray(I["L1_attn_gate_local_w"], f32)
    grw = np.asarray(I["L1_attn_gate_remote_w"], f32)
    for h in range(16):
        gwbd[64 * h:64 * h + 64, h] = glw[h]
        gwbd[64 * h:64 * h + 64, 16 + h] = grw[h]
    sh["gwbd"] = gwbd
    ehot = np.zeros((32, 8 * 128), f32)
    for ec in range(8):
        for m in range(128):
            ehot[16 + 2 * ec + m // 64, ec * 128 + m] = 1.0
    sh["ehot"] = ehot
    sh["mu2"] = np.ascontiguousarray(np.asarray(I["L1_ffn_mu"], f32).reshape(D, 256))
    sh["w1s"] = np.ascontiguousarray(np.asarray(I["L1_ffn_w1"], f32))
    sh["w2s"] = np.ascontiguousarray(np.asarray(I["L1_ffn_w2"], f32))
    sh["b1T"] = np.ascontiguousarray(np.asarray(I["L1_ffn_b1"], f32).T.reshape(2, 128, 16))
    sh["b2T"] = np.ascontiguousarray(np.asarray(I["L1_ffn_b2"], f32).T.reshape(8, 128, 16))
    sh["gammab"] = np.ascontiguousarray(
        np.broadcast_to(np.asarray(I["L1_ffn_ln_gamma"], f32), (128, D)))
    sh["betab"] = np.ascontiguousarray(
        np.broadcast_to(np.asarray(I["L1_ffn_ln_beta"], f32), (128, D)))
    sh["scaleb"] = np.ascontiguousarray(
        np.broadcast_to(np.asarray(I["L1_ffn_scale"], f32).reshape(1), (128, 1)))
    sh["eye"] = np.eye(128, dtype=f32)
    sh["ones1"] = np.ones((1, 512), f32)
    pmat = np.zeros((32, 128, 64), f32)
    for tt in range(32):
        for r in range(128):
            pmat[tt, r, 2 * tt + r // 64] = 1.0 / 64.0
    sh["pmat"] = pmat

    per_core = []
    for core in range(8):
        b, role = core // 2, core % 2
        snake = SNAKE_A if role == 0 else SNAKE_B
        wmask = np.zeros((16, 128, 512), f32)
        lmask = np.zeros((16, 128, 64), f32)
        for j, g in enumerate(snake):
            qg = 128 * g + np.arange(128)[:, None]
            kg = 128 * (2 * j - 2) + np.arange(512)[None, :]
            ok = (kg <= qg) & (kg > qg - 256) & (kg >= 0)
            wmask[j] = np.where(ok, 0.0, NEG)
            lat = np.arange(64)[None, :]
            lmask[j] = np.where(lat <= qg // 64, 0.0, NEG)
        m = {"wmask": wmask, "lmask": lmask, **sh}
        if x1 is not None:
            xb = np.ascontiguousarray(np.asarray(x1[b], f32))
            m["x1"] = xb
            m["x1_own"] = np.ascontiguousarray(
                np.stack([xb[128 * g:128 * g + 128] for g in snake]).reshape(TOWN, D))
        per_core.append(m)
    return per_core


def declare_l1(nc, shared=None, with_x1=True):
    dd = {}
    if with_x1:
        dd["x1"] = nc.dram_tensor("x1", [T, D], F32, kind="ExternalInput").ap()
        dd["x1_own"] = nc.dram_tensor("x1_own", [TOWN, D], F32, kind="ExternalInput").ap()
    for w in ("wq", "wk", "wv", "wout"):
        dd[w] = nc.dram_tensor(w, [D, D], F32, kind="ExternalInput").ap()
    for w in ("wgq", "wgk", "wgv"):
        dd[w] = nc.dram_tensor(w, [D, 512], F32, kind="ExternalInput").ap()
    dd["rout"] = nc.dram_tensor("rout", [512, D], F32, kind="ExternalInput").ap()
    dd["gwbd"] = nc.dram_tensor("gwbd", [D, 32], F32, kind="ExternalInput").ap()
    dd["ehot"] = nc.dram_tensor("ehot", [32, 8 * 128], F32, kind="ExternalInput").ap()
    dd["mu2"] = nc.dram_tensor("mu2", [D, 256], F32, kind="ExternalInput").ap()
    dd["w1s"] = nc.dram_tensor("w1s", [16, D, 256], F32, kind="ExternalInput").ap()
    dd["w2s"] = nc.dram_tensor("w2s", [16, 256, D], F32, kind="ExternalInput").ap()
    dd["b1T"] = nc.dram_tensor("b1T", [2, 128, 16], F32, kind="ExternalInput").ap()
    dd["b2T"] = nc.dram_tensor("b2T", [8, 128, 16], F32, kind="ExternalInput").ap()
    dd["gammab"] = nc.dram_tensor("gammab", [128, D], F32, kind="ExternalInput").ap()
    dd["betab"] = nc.dram_tensor("betab", [128, D], F32, kind="ExternalInput").ap()
    dd["scaleb"] = nc.dram_tensor("scaleb", [128, 1], F32, kind="ExternalInput").ap()
    if shared and "eye" in shared:
        dd["eye"] = shared["eye"]
        dd["ones1"] = shared["ones1"]
    else:
        dd["eye"] = nc.dram_tensor("eye", [128, 128], F32, kind="ExternalInput").ap()
        dd["ones1"] = nc.dram_tensor("ones1", [1, 512], F32, kind="ExternalInput").ap()
    dd["wmask"] = nc.dram_tensor("wmask", [16, 128, 512], F32, kind="ExternalInput").ap()
    dd["lmask"] = nc.dram_tensor("lmask", [16, 128, 64], F32, kind="ExternalInput").ap()
    dd["pmat"] = nc.dram_tensor("pmat", [32, 128, 64], F32, kind="ExternalInput").ap()
    dd["opre_sp"] = nc.dram_tensor("opre_sp", [NSLOT, 128, 16 * 128], BF16, kind="Internal").ap()
    dd["x2p"] = nc.dram_tensor("x2p", [TOWN, D], F32, kind="ExternalOutput").ap()
    return dd


def emit_l1(nc, es, P, dd, x1_row=None, x1own_row=None):
    if x1_row is None:
        x1_row = lambda tok0: dd["x1"][tok0:tok0 + 128, :]
    if x1own_row is None:
        x1own_row = lambda r0: dd["x1_own"][r0:r0 + 128, :]
    d_w = {w: dd[w] for w in ("wq", "wk", "wv", "wout")}
    d_wg = {w: dd[w] for w in ("wgq", "wgk", "wgv")}
    d_rout = dd["rout"]
    d_gwbd = dd["gwbd"]
    d_ehot = dd["ehot"]
    d_mu = dd["mu2"]
    d_w1s = dd["w1s"]
    d_w2s = dd["w2s"]
    d_b1T = dd["b1T"]
    d_b2T = dd["b2T"]
    d_gammab = dd["gammab"]
    d_betab = dd["betab"]
    d_scaleb = dd["scaleb"]
    d_eye = dd["eye"]
    d_ones = dd["ones1"]
    d_wmask = dd["wmask"]
    d_lmask = dd["lmask"]
    d_pmat = dd["pmat"]
    d_opsp = dd["opre_sp"]
    d_x2p = dd["x2p"]
    les = ExitStack()

    def sb(name, shape, dt):
        return les.enter_context(nc.sbuf_tensor(name, shape, dt))

    def ps(name, shape, dt):
        return les.enter_context(nc.psum_tensor(name, shape, dt))

    s_eye32 = sb("eye32L1", [128, 128], F32)
    s_ones = sb("ones_bL1", [1, 512], BF16)
    s_ones32 = sb("ones32rL1", [1, 128], F32)
    s_gammab = sb("gammab_s", [128, D], F32)
    s_betab = sb("betab_s", [128, D], F32)
    s_scaleb = sb("scaleb_s", [128, 1], F32)
    s_stat = sb("statL1", [128, 12], F32)
    s_xtile = sb("xtileL1", [128, D], F32)
    s_xn_b = sb("xn_bL1", [128, D], F32)
    s_xi = sb("xi", [128, D], F32)
    s_xu = sb("xu", [128, D], F32)
    s_js = sb("junk_sL1", [128, D], BF16)
    s_jv = sb("junk_vL1", [128, D], BF16)
    s_muu = sb("muu", [128, 8 * 256], BF16)
    s_pooled = sb("pooled", [64, D], F32)
    s_pooledT = sb("pooledT", [128, 8 * 64], BF16)
    s_GK = sb("GK", [64, 512], F32)
    s_GV = sb("GV", [64, 512], BF16)
    s_GKT = sb("GKT", [128, 4 * 64], BF16)
    s_gates = sb("gates", [128, NSLOT * 32], F32)
    s_so = sb("so", [128, 2 * D], BF16)
    s_eps = sb("epscolL1", [128, 1], F32)
    s_e12 = sb("e12col", [128, 1], F32)
    s_onescol = sb("onescolL1", [128, 1], F32)
    s_x2t = sb("x2t", [128, D], F32)

    pb = [ps(f"pbL1{i}", [128, 512], F32) for i in range(8)]
    B_ = [Buf() for _ in range(8)]
    Bxtile, Bxn, Bxi, Bxu, Bstat, Bjs, Bjv = (Buf() for _ in range(7))
    Bpooled, BpooledT, Bgates, Bx2t, Bmuu = (Buf() for _ in range(5))

    P.dma("sync", s_eye32[:], d_eye, sem="c0")
    P.dma("sync", s_gammab[:], d_gammab, sem="c0")
    P.dma("sync", s_betab[:], d_betab, sem="c0")
    P.dma("sync", s_scaleb[:], d_scaleb, sem="c0")
    v_hw0 = P.counts["c0"]
    P.dma("gpsimd", s_ones[:], d_ones)
    v_sw0 = P.counts["sw"]
    v_ones = P.op("vector", lambda e: e.memset(s_ones32[:], 1.0), inc="v")
    v_ones = P.op("vector", lambda e: e.memset(s_eps[:], 1e-5), inc="v")
    v_ones = P.op("vector", lambda e: e.memset(s_e12[:], 1e-12), inc="v")
    v_ones = P.op("vector", lambda e: e.memset(s_onescol[:], 1.0), inc="v")

    def ln_into(inbuf, outbuf, Bin, Bout, eps_ap, src_row=None, affine=False):
        wx = []
        if src_row is not None:
            vd = P.dma("sync", inbuf[:], src_row, waits=Bin.take())
            wx = [("hw", vd)]
        vm = P.op("vector", lambda e: e.tensor_scalar(
            s_jv[:, 0:D], inbuf[:], 1.0 / D, None, AluOpType.mult,
            AluOpType.add, accum_out=s_stat[:, 0:1]),
            waits=wx + Bjv.take() + Bstat.take(), inc="v")
        vs = P.op("scalar", lambda e: e.activation(
            s_js[:, 0:D], inbuf[:], AF.Square, accum_out=s_stat[:, 1:2]),
            waits=wx + Bjs.take(), inc="s")
        Bin.read_by("v", vm)
        Bin.read_by("s", vs)
        Bjv.read_by("v", vm)
        Bjs.read_by("s", vs)
        v1 = P.op("vector", lambda e: e.scalar_tensor_tensor(
            s_stat[:, 2:3], s_stat[:, 0:1], -1.0, s_stat[:, 0:1],
            AluOpType.mult, AluOpType.mult), waits=[("s", vs)], inc="v")
        v2 = P.op("vector", lambda e: e.scalar_tensor_tensor(
            s_stat[:, 3:4], s_stat[:, 1:2], 1.0 / D, s_stat[:, 2:3],
            AluOpType.mult, AluOpType.add), inc="v")
        s1 = P.op("scalar", lambda e: e.activation(
            s_stat[:, 4:5], s_stat[:, 3:4], AF.Sqrt, bias=eps_ap),
            waits=[("v", v2), ("v", v_ones)], inc="s")
        v3 = P.op("vector", lambda e: e.reciprocal(s_stat[:, 5:6], s_stat[:, 4:5]),
                  waits=[("s", s1)], inc="v")
        v4 = P.op("vector", lambda e: e.scalar_tensor_tensor(
            s_stat[:, 6:7], s_stat[:, 0:1], -1.0, s_stat[:, 5:6],
            AluOpType.mult, AluOpType.mult), inc="v")
        sx = P.op("scalar", lambda e: e.activation(
            outbuf[:], inbuf[:], AF.Identity, bias=s_stat[:, 6:7],
            scale=s_stat[:, 5:6]), waits=[("v", v4)] + Bout.take(), inc="s")
        Bin.read_by("s", sx)
        Bstat.read_by("s", sx)
        last = ("s", sx)
        if affine:
            vv = P.op("vector", lambda e: e.tensor_tensor(
                outbuf[:], outbuf[:], s_gammab[:], AluOpType.mult),
                waits=[("s", sx), ("c0", v_hw0)], inc="v")
            vv = P.op("vector", lambda e: e.tensor_tensor(
                outbuf[:], outbuf[:], s_betab[:], AluOpType.add), inc="v")
            last = ("v", vv)
        return last

    def l2unit(inbuf, outbuf, Bin, Bout, wait):
        ss = P.op("scalar", lambda e: e.activation(
            s_js[:, 0:D], inbuf[:], AF.Square, accum_out=s_stat[:, 7:8]),
            waits=[wait] + Bjs.take(), inc="s")
        Bjs.read_by("s", ss)
        Bin.read_by("s", ss)
        s2 = P.op("scalar", lambda e: e.activation(
            s_stat[:, 8:9], s_stat[:, 7:8], AF.Sqrt), waits=[("s", ss)], inc="s")
        vv = P.op("vector", lambda e: e.tensor_tensor(
            s_stat[:, 9:10], s_stat[:, 8:9], s_e12[:], AluOpType.max),
            waits=[("s", s2), ("v", v_ones)], inc="v")
        vv = P.op("vector", lambda e: e.reciprocal(s_stat[:, 10:11], s_stat[:, 9:10]),
                  inc="v")
        vv = P.op("vector", lambda e: e.tensor_tensor(
            s_stat[:, 10:11], s_stat[:, 10:11], s_scaleb[:], AluOpType.mult),
            waits=[("c0", v_hw0)], inc="v")
        sx = P.op("scalar", lambda e: e.activation(
            outbuf[:], inbuf[:], AF.Identity, scale=s_stat[:, 10:11]),
            waits=[("v", vv)] + Bout.take(), inc="s")
        Bin.read_by("s", sx)
        Bstat.read_by("s", sx)
        return sx

    def xpose(srcbuf, scol, destbuf, dstride, dcol, wait, nd=NDC, Bsrc=None):
        last = None
        for g4 in range((nd + 3) // 4):
            bank = 4 + g4 % 2
            tvs = None
            n_in_g = min(4, nd - g4 * 4)
            for slot in range(n_in_g):
                dt = g4 * 4 + slot
                tvs = P.op("tensor", lambda e, dt=dt, bank=bank, slot=slot: e.transpose(
                    pb[bank][:, slot * 128:slot * 128 + 128],
                    srcbuf[:, scol + dt * 128:scol + (dt + 1) * 128], s_eye32[:]),
                    waits=[wait, ("c0", v_hw0)] + (B_[bank].take() if slot == 0 else []),
                    inc="t")
            for slot in range(n_in_g):
                dt = g4 * 4 + slot
                sv = P.op("scalar", lambda e, dt=dt, bank=bank, slot=slot: e.copy(
                    destbuf[:, dstride * dt + dcol:dstride * dt + dcol + 128],
                    pb[bank][:, slot * 128:slot * 128 + 128]),
                    waits=[("t", tvs)], inc="s")
                B_[bank].read_by("s", sv)
                last = sv
        if Bsrc is not None:
            Bsrc.read_by("t", P.counts["t"])
        return last

    def mm_group(bank, cols, mms, waits=(), extra_first=(), rows=(0, 128)):
        n = len(mms)
        w0 = list(waits) + B_[bank].take() + list(extra_first)
        last = None
        for i, (fl, fr) in enumerate(mms):
            last = P.op("tensor",
                        lambda e, fl=fl, fr=fr, i=i: e.matmul(
                            pb[bank][rows[0]:rows[1], cols[0]:cols[1]], fl(), fr(),
                            start=(i == 0), stop=(i == n - 1)),
                        waits=(w0 if i == 0 else ()),
                        inc=("t" if i == n - 1 else None))
        return last

    pa = ExitStack()
    s_xni = pa.enter_context(nc.sbuf_tensor("xni", [128, NT * D], BF16))
    s_expT = pa.enter_context(nc.sbuf_tensor("expT", [128, 2 * T], F32))
    s_dsum = pa.enter_context(nc.sbuf_tensor("dsum", [128, 2 * 9], F32))
    pa2 = ExitStack()
    s_xuT = pa2.enter_context(nc.sbuf_tensor("xuT", [128, NDC * 512], BF16))
    s_mu = pa2.enter_context(nc.sbuf_tensor("mu_s", [128, 8 * 256], F32))
    s_msq = pa2.enter_context(nc.sbuf_tensor("msq", [128, 256], F32))
    s_pm = pa2.enter_context(nc.sbuf_tensor("pm", [128, 64], F32))
    s_nrow = pa2.enter_context(nc.sbuf_tensor("nrow", [1, 256], F32))
    s_recb = pa2.enter_context(nc.sbuf_tensor("recb", [128, 256], F32))
    Bxni, BexpT, BxuT, Bpm, Bmsq = (Buf() for _ in range(5))

    for dt in range(NDC):
        P.dma("sync", s_mu[:, dt * 256:(dt + 1) * 256], d_mu[dt * 128:(dt + 1) * 128, :],
              sem="c1")
    v_mu = P.counts["c1"]
    for dt in range(NDC):
        sq = P.op("scalar", lambda e, dt=dt: e.activation(
            s_msq[:], s_mu[:, dt * 256:(dt + 1) * 256], AF.Square),
            waits=[("c1", v_mu)] + Bmsq.take(), inc="s")
        tv = P.op("tensor", lambda e, dt=dt: e.matmul(
            pb[6][0:1, 0:256], s_onescol[:], s_msq[:],
            start=(dt == 0), stop=(dt == NDC - 1)),
            waits=[("s", sq), ("v", v_ones)] + (B_[6].take() if dt == 0 else []),
            inc="t")
        Bmsq.read_by("t", tv)
    sv = P.op("scalar", lambda e: e.activation(s_nrow[0:1, :], pb[6][0:1, 0:256],
                                               AF.Sqrt), waits=[("t", tv)], inc="s")
    B_[6].read_by("s", sv)
    vv = P.op("vector", lambda e: e.tensor_scalar(
        s_nrow[0:1, :], s_nrow[0:1, :], 1e-12, None, AluOpType.max),
        waits=[("s", sv)], inc="v")
    vv = P.op("vector", lambda e: e.reciprocal(s_nrow[0:1, :], s_nrow[0:1, :]), inc="v")
    tv = P.op("tensor", lambda e: e.matmul(
        pb[6][:, 256:512], s_ones32[0:1, 0:128], s_nrow[0:1, 0:256],
        start=True, stop=True), waits=[("v", vv)] + B_[6].take(), inc="t")
    sv = P.op("scalar", lambda e: e.copy(s_recb[:], pb[6][:, 256:512]),
              waits=[("t", tv)], inc="s")
    B_[6].read_by("s", sv)
    for dt in range(NDC):
        vv = P.op("vector", lambda e, dt=dt: e.tensor_tensor(
            s_muu[:, dt * 256:(dt + 1) * 256], s_mu[:, dt * 256:(dt + 1) * 256],
            s_recb[:], AluOpType.mult), waits=[("s", sv)] + (Bmuu.take() if dt == 0 else []),
            inc="v")
    v_muu = P.counts["v"]

    for tt in range(NT):
        vp = P.dma("sync", s_pm[:], d_pmat[tt], waits=Bpm.take(), sem="pm")
        wl = ln_into(s_xtile, s_xn_b, Bxtile, Bxn, s_eps[:],
                     src_row=x1_row(tt * 128))
        for dc in range(2):
            tvp = P.op("tensor", lambda e, tt=tt, dc=dc: e.matmul(
                pb[2 + dc][0:64, 0:512], s_pm[:],
                s_xn_b[:, dc * 512:(dc + 1) * 512],
                start=(tt == 0), stop=(tt == NT - 1)),
                waits=[("pm", vp), wl] + (B_[2 + dc].take() if tt == 0 else []),
                inc=("t" if tt == NT - 1 else None))
        Bpm.read_by("t", P.counts["t"] if tt == NT - 1 else P.counts["t"] + 1)
        Bxn.read_by("t", P.counts["t"])
        wi = ln_into(s_xn_b, s_xi, Bxn, Bxi, s_eps[:], affine=True)
        sc = P.op("scalar", lambda e, tt=tt: e.copy(
            s_xni[:, tt * D:(tt + 1) * D], s_xi[:]),
            waits=[wi] + (Bxni.take() if tt == 0 else []), inc="s")
        Bxi.read_by("s", sc)
        su = l2unit(s_xi, s_xu, Bxi, Bxu, ("s", sc))
        ti = tt % 4
        xpose(s_xu, 0, s_xuT, 512, ti * 128, ("s", su), Bsrc=Bxu)
        if ti == 3:
            nchunk = tt // 4
            vxt = P.counts["s"]
            for esc in range(2):
                bank = esc
                mms = [(lambda dt=dt, esc=esc: s_muu[:, dt * 256 + esc * 128:dt * 256 + esc * 128 + 128],
                        lambda dt=dt: s_xuT[:, dt * 512:(dt + 1) * 512])
                       for dt in range(NDC)]
                tvl = mm_group(bank, (0, 512), mms, waits=[("s", vxt), ("v", v_muu)])
                se = P.op("scalar", lambda e, esc=esc, nchunk=nchunk, bank=bank: e.activation(
                    s_expT[:, esc * T + nchunk * 512:esc * T + nchunk * 512 + 512],
                    pb[bank][:, 0:512], AF.Exp,
                    accum_out=s_dsum[:, esc * 9 + nchunk:esc * 9 + nchunk + 1]),
                    waits=[("t", tvl)] + (BexpT.take() if nchunk == 0 and esc == 0 else []),
                    inc="s")
                B_[bank].read_by("s", se)
            BxuT.read_by("t", P.counts["t"])
    v_exp = P.counts["s"]
    for esc in range(2):
        vv = P.op("vector", lambda e, esc=esc: e.tensor_reduce(
            s_dsum[:, esc * 9 + 8:esc * 9 + 9], s_dsum[:, esc * 9:esc * 9 + 8],
            AX.X, AluOpType.add), waits=[("s", v_exp)], inc="v")
        vv = P.op("vector", lambda e, esc=esc: e.reciprocal(
            s_dsum[:, esc * 9 + 8:esc * 9 + 9], s_dsum[:, esc * 9 + 8:esc * 9 + 9]),
            inc="v")
        vv = P.op("vector", lambda e, esc=esc: e.tensor_scalar(
            s_expT[:, esc * T:esc * T + T], s_expT[:, esc * T:esc * T + T],
            s_dsum[:, esc * 9 + 8:esc * 9 + 9], None, AluOpType.mult), inc="v")
    v_dispT = P.counts["v"]
    for dc in range(2):
        sv = P.op("scalar", lambda e, dc=dc: e.copy(
            s_pooled[0:64, dc * 512:(dc + 1) * 512], pb[2 + dc][0:64, 0:512]),
            waits=[("t", P.counts["t"])] + (Bpooled.take() if dc == 0 else []), inc="s")
        B_[2 + dc].read_by("s", sv)
    v_pool = P.counts["s"]

    P.phase_barrier()
    pa2.close()
    pbx = ExitStack()
    s_disp = pbx.enter_context(nc.sbuf_tensor("disp", [128, NT * 256], BF16))
    s_sin = pbx.enter_context(nc.sbuf_tensor("sin", [128, 2 * D], F32))
    s_sinT = pbx.enter_context(nc.sbuf_tensor("sinT", [128, NDC * 256], BF16))
    s_hT = pbx.enter_context(nc.sbuf_tensor("hT", [128, 2 * 16], BF16))
    s_soT = pbx.enter_context(nc.sbuf_tensor("soT", [128, NDC * 256], F32))
    s_w1l = pbx.enter_context(nc.sbuf_tensor("w1l", [128, NDC * 256], BF16))
    s_w2l = pbx.enter_context(nc.sbuf_tensor("w2l", [128, 2 * D], BF16))
    s_b1T = pbx.enter_context(nc.sbuf_tensor("b1T_s", [128, 2 * 16], F32))
    s_b2T = pbx.enter_context(nc.sbuf_tensor("b2T_s", [128, 8 * 16], F32))
    s_wgl = pbx.enter_context(nc.sbuf_tensor("wgl", [128, NDC * 512], BF16))
    Bdisp, Bsin, BsinT, BhT, BsoT, Bw1l, Bw2l = (Buf() for _ in range(7))
    P.phase_barrier()
    for g4 in range(2):
        bank = 4 + g4
        tvs = None
        for slot in range(4):
            dt = g4 * 4 + slot
            tvs = P.op("tensor", lambda e, dt=dt, bank=bank, slot=slot: e.transpose(
                pb[bank][:, slot * 128:slot * 128 + 64],
                s_pooled[0:64, dt * 128:(dt + 1) * 128], s_eye32[0:64, 0:64]),
                waits=[("s", v_pool), ("c0", v_hw0)] +
                      (B_[bank].take() if slot == 0 else []), inc="t")
        for slot in range(4):
            dt = g4 * 4 + slot
            sv = P.op("scalar", lambda e, dt=dt, bank=bank, slot=slot: e.copy(
                s_pooledT[:, dt * 64:(dt + 1) * 64],
                pb[bank][:, slot * 128:slot * 128 + 64]),
                waits=[("t", tvs)], inc="s")
            B_[bank].read_by("s", sv)
    Bpooled.read_by("t", P.counts["t"])
    v_pt = P.counts["s"]
    Bwgl = Buf()
    for wname, dest in (("wgk", s_GK), ("wgv", s_GV)):
        wgd = Bwgl.take()
        for dt in range(NDC):
            P.dma("gpsimd", s_wgl[:, dt * 512:(dt + 1) * 512],
                  d_wg[wname][dt * 128:(dt + 1) * 128, :],
                  waits=(wgd if dt == 0 else []))
        v_w = P.counts["sw"]
        mms = [(lambda dt=dt: s_pooledT[:, dt * 64:(dt + 1) * 64],
                lambda dt=dt: s_wgl[:, dt * 512:(dt + 1) * 512])
               for dt in range(NDC)]
        tv = mm_group(1, (0, 512), mms, waits=[("sw", v_w), ("s", v_pt)], rows=(0, 64))
        Bwgl.read_by("t", tv)
        sv = P.op("scalar", lambda e, dest=dest: e.copy(dest[:], pb[1][0:64, 0:512]),
                  waits=[("t", tv)], inc="s")
        B_[1].read_by("s", sv)
    v_gkv = P.counts["s"]
    tvs = None
    for lc in range(4):
        tvs = P.op("tensor", lambda e, lc=lc: e.transpose(
            pb[4][:, lc * 128:lc * 128 + 64],
            s_GK[0:64, lc * 128:lc * 128 + 128], s_eye32[0:64, 0:64]),
            waits=[("s", v_gkv)] + (B_[4].take() if lc == 0 else []), inc="t")
    for lc in range(4):
        sv = P.op("scalar", lambda e, lc=lc: e.copy(
            s_GKT[:, lc * 64:lc * 64 + 64], pb[4][:, lc * 128:lc * 128 + 64]),
            waits=[("t", tvs)], inc="s")
        B_[4].read_by("s", sv)
    v_gkt = P.counts["s"]

    for tt in range(NT):
        for esc in range(2):
            bank = 4 + esc
            tv = P.op("tensor", lambda e, tt=tt, esc=esc, bank=bank: e.transpose(
                pb[bank][:, 0:128], s_expT[:, esc * T + tt * 128:esc * T + tt * 128 + 128],
                s_eye32[:]), waits=[("v", v_dispT)] + B_[bank].take(), inc="t")
            sv = P.op("scalar", lambda e, tt=tt, esc=esc, bank=bank: e.copy(
                s_disp[:, tt * 256 + esc * 128:tt * 256 + esc * 128 + 128],
                pb[bank][:, 0:128]),
                waits=[("t", tv)] + (Bdisp.take() if tt == 0 and esc == 0 else []),
                inc="s")
            B_[bank].read_by("s", sv)
    BexpT.read_by("t", P.counts["t"])
    v_dispn = P.counts["s"]
    for esc in range(2):
        for dc in range(2):
            bank = esc * 2 + dc
            mms = [(lambda tt=tt, esc=esc: s_disp[:, tt * 256 + esc * 128:tt * 256 + esc * 128 + 128],
                    lambda tt=tt, dc=dc: s_xni[:, tt * D + dc * 512:tt * D + dc * 512 + 512])
                   for tt in range(NT)]
            tv = mm_group(bank, (0, 512), mms, waits=[("s", v_dispn)])
            sv = P.op("scalar", lambda e, esc=esc, dc=dc, bank=bank: e.copy(
                s_sin[:, esc * D + dc * 512:esc * D + dc * 512 + 512],
                pb[bank][:, 0:512]),
                waits=[("t", tv)] + (Bsin.take() if esc == 0 and dc == 0 else []), inc="s")
            B_[bank].read_by("s", sv)
    v_sin = P.counts["s"]
    for esc in range(2):
        xpose(s_sin, esc * D, s_sinT, 256, esc * 128, ("s", v_sin), Bsrc=Bsin)
    v_sinT = P.counts["s"]
    for hc in range(2):
        P.dma("sync", s_b1T[:, hc * 16:(hc + 1) * 16], d_b1T[hc], sem="c2")
    for dt in range(8):
        P.dma("sync", s_b2T[:, dt * 16:(dt + 1) * 16], d_b2T[dt], sem="c2")
    v_bT = P.counts["c2"]
    Bso = Buf()
    for ei in range(16):
        w1d = Bw1l.take()
        lv1 = None
        for dt in range(NDC):
            lv1 = P.dma("gpsimd", s_w1l[:, dt * 256:(dt + 1) * 256],
                        d_w1s[ei][dt * 128:(dt + 1) * 128, :],
                        waits=(w1d if dt == 0 else []), sem="w1")
        w2d = Bw2l.take()
        lv2 = None
        for hc in range(2):
            lv2 = P.dma("gpsimd", s_w2l[:, hc * D:(hc + 1) * D],
                        d_w2s[ei][hc * 128:(hc + 1) * 128, :],
                        waits=(w2d if hc == 0 else []), sem="w2a")
        hd = BhT.take()
        for hc in range(2):
            bank = hc
            mms = [(lambda dt=dt, hc=hc: s_w1l[:, dt * 256 + hc * 128:dt * 256 + hc * 128 + 128],
                    lambda dt=dt, ei=ei: s_sinT[:, dt * 256 + ei * 16:dt * 256 + ei * 16 + 16])
                   for dt in range(NDC)]
            tv = mm_group(bank, (0, 16), mms,
                          waits=[("w1", lv1), ("s", v_sinT), ("c2", v_bT)],
                          extra_first=(hd if hc == 0 else ()))
            Bw1l.read_by("t", tv)
            BsinT.read_by("t", tv)
            sv = P.op("scalar", lambda e, hc=hc, ei=ei, bank=bank: e.activation(
                s_hT[:, hc * 16:(hc + 1) * 16], pb[bank][:, 0:16], AF.Gelu,
                bias=s_b1T[:, hc * 16 + ei:hc * 16 + ei + 1]),
                waits=[("t", tv)], inc="s")
            B_[bank].read_by("s", sv)
        v_h = P.counts["s"]
        for dt in range(NDC):
            bank = 2 + dt % 2
            mms = [(lambda hc=hc, dt=dt: s_w2l[:, hc * D + dt * 128:hc * D + dt * 128 + 128],
                    lambda hc=hc: s_hT[:, hc * 16:(hc + 1) * 16])
                   for hc in range(2)]
            tv = mm_group(bank, (0, 16), mms, waits=[("w2a", lv2), ("s", v_h)])
            Bw2l.read_by("t", tv)
            BhT.read_by("t", tv)
            sv = P.op("scalar", lambda e, dt=dt, ei=ei, bank=bank: e.activation(
                s_soT[:, dt * 256 + ei * 16:dt * 256 + ei * 16 + 16],
                pb[bank][:, 0:16], AF.Identity,
                bias=s_b2T[:, dt * 16 + ei:dt * 16 + ei + 1]),
                waits=[("t", tv)] + (BsoT.take() if ei == 0 and dt == 0 else []),
                inc="s")
            B_[bank].read_by("s", sv)
    v_soT = P.counts["s"]
    for dt in range(NDC):
        bank = 4 + (dt % 2)
        tvs = None
        for esc in range(2):
            tvs = P.op("tensor", lambda e, dt=dt, esc=esc, bank=bank: e.transpose(
                pb[bank][:, esc * 128:esc * 128 + 128],
                s_soT[:, dt * 256 + esc * 128:dt * 256 + esc * 128 + 128],
                s_eye32[:]),
                waits=[("s", v_soT)] + (B_[bank].take() if esc == 0 else []), inc="t")
        for esc in range(2):
            sv = P.op("scalar", lambda e, dt=dt, esc=esc, bank=bank: e.copy(
                s_so[:, esc * D + dt * 128:esc * D + dt * 128 + 128],
                pb[bank][:, esc * 128:esc * 128 + 128]),
                waits=[("t", tvs)] + (Bso.take() if dt == 0 and esc == 0 else []),
                inc="s")
            B_[bank].read_by("s", sv)
    BsoT.read_by("t", P.counts["t"])
    P.phase_barrier()
    pbx.close()
    pa.close()

    PADC = 256
    KSTR = T + PADC
    cd = ExitStack()
    s_xnT_own = cd.enter_context(nc.sbuf_tensor("xnT_ownL1", [128, NDC * TOWN], BF16))
    s_K1 = cd.enter_context(nc.sbuf_tensor("K1", [128, 4 * KSTR], BF16))
    s_V1 = cd.enter_context(nc.sbuf_tensor("VL1", [128, 34 * 512], BF16))
    s_xnTc = cd.enter_context(nc.sbuf_tensor("xnTcL1", [128, NDC * 512], BF16))
    s_wkh = cd.enter_context(nc.sbuf_tensor("wkh", [128, NDC * 512], BF16))
    s_wvh = cd.enter_context(nc.sbuf_tensor("wvh", [128, NDC * 512], BF16))
    s_rout = cd.enter_context(nc.sbuf_tensor("rout_s", [128, 4 * D], BF16))
    s_wgq = cd.enter_context(nc.sbuf_tensor("wgq_s", [128, NDC * 512], BF16))
    s_gwb = cd.enter_context(nc.sbuf_tensor("gwb", [128, 4 * 32], BF16))
    s_QT = cd.enter_context(nc.sbuf_tensor("QTL1", [128, 4 * 128], BF16))
    s_GQT = cd.enter_context(nc.sbuf_tensor("GQT", [128, 4 * 128], BF16))
    s_sc = cd.enter_context(nc.sbuf_tensor("scL1", [128, 512], F32))
    s_pw = cd.enter_context(nc.sbuf_tensor("pw", [128, 512], F32))
    s_pT = cd.enter_context(nc.sbuf_tensor("pT", [128, 4 * 128], BF16))
    s_ls = cd.enter_context(nc.sbuf_tensor("ls", [128, 64], F32))
    s_lse = cd.enter_context(nc.sbuf_tensor("lse", [128, 64], F32))
    s_lpT = cd.enter_context(nc.sbuf_tensor("lpT", [64, 128], BF16))
    s_remT = cd.enter_context(nc.sbuf_tensor("remT", [128, 4 * 128], BF16))
    s_opst = cd.enter_context(nc.sbuf_tensor("opst", [128, 4 * 128], BF16))
    s_oprem = cd.enter_context(nc.sbuf_tensor("oprem", [128, 8 * 128], BF16))
    s_wmk = cd.enter_context(nc.sbuf_tensor("wmk", [128, 512], F32))
    s_lmk = cd.enter_context(nc.sbuf_tensor("lmk", [128, 64], F32))
    BK1, BV1, BxnTc1, BQT1, BGQT, Bsc, Bpw, BpT, Bwk = (Buf() for _ in range(9))
    Bls, Blse, BlpT, BremT, Bopst, Boprem, Bwmk, Blmk = (Buf() for _ in range(8))

    P.phase_barrier()
    for j in range(NSLOT):
        wl = ln_into(s_xtile, s_xn_b, Bxtile, Bxn, s_eps[:],
                     src_row=x1own_row(j * 128))
        xpose(s_xn_b, 0, s_xnT_own, TOWN, j * 128, wl, Bsrc=Bxn)
    v_own = P.counts["s"]

    for half in (0, 1):
        P.phase_barrier()
        wkd = Bwk.take()
        for dt in range(NDC):
            P.dma("gpsimd", s_wkh[:, dt * 512:(dt + 1) * 512],
                  d_w["wk"][dt * 128:(dt + 1) * 128, half * 512:half * 512 + 512],
                  waits=(wkd if dt == 0 else []))
            P.dma("gpsimd", s_wvh[:, dt * 512:(dt + 1) * 512],
                  d_w["wv"][dt * 128:(dt + 1) * 128, half * 512:half * 512 + 512])
        if half == 0:
            for dt in range(NDC):
                P.dma("gpsimd", s_wgq[:, dt * 512:(dt + 1) * 512],
                      d_wg["wgq"][dt * 128:(dt + 1) * 128, :])
            for lc in range(4):
                P.dma("gpsimd", s_rout[:, lc * D:(lc + 1) * D],
                      d_rout[lc * 128:(lc + 1) * 128, :])
        for ec in range(4):
            P.dma("gpsimd", s_gwb[:, ec * 32:(ec + 1) * 32],
                  d_gwbd[half * 512 + ec * 128:half * 512 + ec * 128 + 128, :])
        v_wh = P.counts["sw"]
        vg8 = P.op("vector", lambda e: e.tensor_scalar(
            s_gwb[:], s_gwb[:], 8.0, None, AluOpType.mult),
            waits=[("sw", v_wh)], inc="v")
        vz = P.op("vector", lambda e: e.memset(s_V1[:, 0:2 * 512], 0.0),
                  waits=BV1.take(), inc="v")
        for ec in range(4):
            vz = P.op("vector", lambda e, ec=ec: e.memset(
                s_K1[:, ec * KSTR:ec * KSTR + PADC], 0.0),
                waits=(BK1.take() if ec == 0 else []), inc="v")
        for tc in range(8):
            for ti in range(4):
                tok0 = tc * 512 + ti * 128
                wl = ln_into(s_xtile, s_xn_b, Bxtile, Bxn, s_eps[:],
                             src_row=x1_row(tok0))
                xpose(s_xn_b, 0, s_xnTc, 512, ti * 128, wl, Bsrc=Bxn)
            vxn = P.counts["s"]
            wxn = [("s", vxn), ("sw", v_wh), ("v", vz)]
            for ec in range(4):
                bank = ec % 2
                mms = [(lambda dt=dt, ec=ec: s_wkh[:, dt * 512 + ec * 128:dt * 512 + ec * 128 + 128],
                        lambda dt=dt: s_xnTc[:, dt * 512:(dt + 1) * 512])
                       for dt in range(NDC)]
                tv = mm_group(bank, (0, 512), mms, waits=wxn)
                Bwk.read_by("t", tv)
                sv = P.op("scalar", lambda e, ec=ec, tc=tc, bank=bank: e.copy(
                    s_K1[:, ec * KSTR + PADC + tc * 512:ec * KSTR + PADC + tc * 512 + 512],
                    pb[bank][:, 0:512]), waits=[("t", tv)], inc="s")
                B_[bank].read_by("s", sv)
            for ti in range(4):
                bank = 2 + ti % 2
                tt = tc * 4 + ti
                mms = [(lambda dt=dt, ti=ti: s_xnTc[:, dt * 512 + ti * 128:dt * 512 + ti * 128 + 128],
                        lambda dt=dt: s_wvh[:, dt * 512:(dt + 1) * 512])
                       for dt in range(NDC)]
                tv = mm_group(bank, (0, 512), mms, waits=wxn)
                sv = P.op("scalar", lambda e, tt=tt, bank=bank: e.copy(
                    s_V1[:, (2 + tt) * 512:(2 + tt) * 512 + 512], pb[bank][:, 0:512]),
                    waits=[("t", tv)], inc="s")
                B_[bank].read_by("s", sv)
            BxnTc1.read_by("t", P.counts["t"])
        v_kv = P.counts["s"]
        wkd = Bwk.take()
        for dt in range(NDC):
            P.dma("gpsimd", s_wkh[:, dt * 512:(dt + 1) * 512],
                  d_w["wq"][dt * 128:(dt + 1) * 128, half * 512:half * 512 + 512],
                  waits=(wkd if dt == 0 else []))
        v_wq = P.counts["sw"]

        for j in range(NSLOT):
            vmk = P.dma("sync", s_wmk[:], d_wmask[j], waits=Bwmk.take(), sem="wm")
            if half == 0:
                vlk = P.dma("sync", s_lmk[:], d_lmask[j], waits=Blmk.take(), sem="lm")
            qd = BQT1.take()
            tvq = None
            for ec in range(4):
                mms = [(lambda dt=dt, ec=ec: s_wkh[:, dt * 512 + ec * 128:dt * 512 + ec * 128 + 128],
                        lambda dt=dt, j=j: s_xnT_own[:, dt * TOWN + j * 128:dt * TOWN + j * 128 + 128])
                       for dt in range(NDC)]
                tvq = mm_group(6, (ec * 128, ec * 128 + 128), mms,
                               waits=[("sw", v_wq), ("s", v_own)],
                               extra_first=(qd if ec == 0 else ()))
                Bwk.read_by("t", tvq)
            for ec in range(4):
                sv = P.op("scalar", lambda e, ec=ec: e.activation(
                    s_QT[:, ec * 128:ec * 128 + 128],
                    pb[6][:, ec * 128:ec * 128 + 128], AF.Copy, scale=0.125),
                    waits=[("t", tvq)], inc="s")
                B_[6].read_by("s", sv)
            v_qt = P.counts["s"]
            mms = [(lambda ec=ec: s_QT[:, ec * 128:ec * 128 + 128],
                    lambda ec=ec: s_gwb[:, ec * 32:(ec + 1) * 32])
                   for ec in range(4)]
            tv = mm_group(6, (256, 288), mms, waits=[("s", v_qt), ("v", vg8)])
            BQT1.read_by("t", tv)
            for part in (0, 16):
                sv = P.op("scalar", lambda e, part=part, j=j, half=half: e.activation(
                    s_gates[:, j * 32 + part + half * 8:j * 32 + part + half * 8 + 8],
                    pb[6][:, 256 + part + half * 8:256 + part + half * 8 + 8],
                    AF.Sigmoid),
                    waits=[("t", tv)] + (Bgates.take() if part == 0 else []), inc="s")
                B_[6].read_by("s", sv)
            v_gate = P.counts["s"]
            od = Bopst.take()
            for hl in range(8):
                ec, po = hl // 2, 64 * (hl % 2)
                kcol0 = ec * KSTR + PADC + 128 * (2 * j - 2)
                bank = hl % 2
                tv = mm_group(bank, (0, 512),
                              [(lambda ec=ec, po=po: s_QT[po:po + 64, ec * 128:ec * 128 + 128],
                                lambda po=po, kcol0=kcol0: s_K1[po:po + 64, kcol0:kcol0 + 512])],
                              waits=[("s", v_kv), ("s", v_qt)])
                ev = P.op("vector", lambda e, bank=bank: e.tensor_tensor(
                    s_sc[:], pb[bank][:, 0:512], s_wmk[:], AluOpType.add),
                    waits=[("t", tv), ("wm", vmk)] + Bsc.take(), inc="v")
                B_[bank].read_by("v", ev)
                Bwmk.read_by("v", ev)
                vm = P.op("vector", lambda e: e.tensor_reduce(
                    s_stat[:, 11:12], s_sc[:], AX.X, AluOpType.max, negate=True),
                    waits=Bstat.take(), inc="v")
                se = P.op("scalar", lambda e: e.activation(
                    s_pw[:], s_sc[:], AF.Exp, bias=s_stat[:, 11:12],
                    accum_out=s_stat[:, 10:11]),
                    waits=[("v", vm)] + Bpw.take(), inc="s")
                Bsc.read_by("s", se)
                Bstat.read_by("s", se)
                vr = P.op("vector", lambda e: e.reciprocal(
                    s_stat[:, 9:10], s_stat[:, 10:11]), waits=[("s", se)], inc="v")
                vr = P.op("vector", lambda e, j=j, hl=hl, half=half: e.tensor_tensor(
                    s_stat[:, 9:10], s_stat[:, 9:10],
                    s_gates[:, j * 32 + half * 8 + hl:j * 32 + half * 8 + hl + 1],
                    AluOpType.mult), waits=[("s", v_gate)], inc="v")
                va = P.op("vector", lambda e: e.tensor_scalar(
                    s_pw[:], s_pw[:], s_stat[:, 9:10], None, AluOpType.mult),
                    inc="v")
                Bpw.read_by("v", va)
                Bstat.read_by("v", va)
                Bgates.read_by("v", va)
                vpt = xpose(s_pw, 0, s_pT, 128, 0, ("v", va), nd=4, Bsrc=Bpw)
                mms = []
                for kt in range(4):
                    vtile = 2 * j + kt
                    mms.append((lambda vtile=vtile, hl=hl: s_V1[:, vtile * 512 + hl * 64:vtile * 512 + hl * 64 + 64],
                                lambda kt=kt: s_pT[:, kt * 128:kt * 128 + 128]))
                tv = mm_group(2, (0, 128), mms,
                              waits=[("s", vpt), ("s", v_kv)], rows=(0, 64))
                BpT.read_by("t", tv)
                BV1.read_by("t", tv)
                BK1.read_by("t", tv)
                dl = hl // 2
                sv = P.op("scalar", lambda e, dl=dl, po=po: e.copy(
                    s_opst[po:po + 64, dl * 128:dl * 128 + 128], pb[2][0:64, 0:128]),
                    waits=[("t", tv)] + (od if hl == 0 else []), inc="s")
                B_[2].read_by("s", sv)
            v_loc = P.counts["s"]
            vsp = P.dma("sync", d_opsp[j][:, (12 * half) * 128:(12 * half) * 128 + 4 * 128],
                        s_opst[:], waits=[("s", v_loc)], sem="osp")
            if half == 1:
                Bopst.read_by("osp", P.counts["osp"])
            if half == 0:
                gqd = BGQT.take()
                tvg = None
                for lc in range(4):
                    mms = [(lambda dt=dt, lc=lc: s_wgq[:, dt * 512 + lc * 128:dt * 512 + lc * 128 + 128],
                            lambda dt=dt, j=j: s_xnT_own[:, dt * TOWN + j * 128:dt * TOWN + j * 128 + 128])
                           for dt in range(NDC)]
                    tvg = mm_group(5, (lc * 128, lc * 128 + 128), mms,
                                   waits=[("sw", v_wh), ("s", v_own)],
                                   extra_first=(gqd if lc == 0 else ()))
                for lc in range(4):
                    sv = P.op("scalar", lambda e, lc=lc: e.activation(
                        s_GQT[:, lc * 128:lc * 128 + 128],
                        pb[5][:, lc * 128:lc * 128 + 128], AF.Copy, scale=0.125),
                        waits=[("t", tvg)], inc="s")
                    B_[5].read_by("s", sv)
                v_gq = P.counts["s"]
                rd = BremT.take()
                for lh in range(8):
                    lc_l, po_l = lh // 2, 64 * (lh % 2)
                    tv = mm_group(3, (0, 64),
                                  [(lambda lc_l=lc_l, po_l=po_l: s_GQT[po_l:po_l + 64, lc_l * 128:lc_l * 128 + 128],
                                    lambda lc_l=lc_l, po_l=po_l: s_GKT[po_l:po_l + 64, lc_l * 64:lc_l * 64 + 64])],
                                  waits=[("s", v_gq), ("s", v_gkt)])
                    ev = P.op("vector", lambda e: e.tensor_tensor(
                        s_ls[:], pb[3][:, 0:64], s_lmk[:], AluOpType.add),
                        waits=[("t", tv), ("lm", vlk)] + Bls.take(), inc="v")
                    B_[3].read_by("v", ev)
                    Blmk.read_by("v", ev)
                    vm = P.op("vector", lambda e: e.tensor_reduce(
                        s_stat[:, 11:12], s_ls[:], AX.X, AluOpType.max, negate=True),
                        waits=Bstat.take(), inc="v")
                    se = P.op("scalar", lambda e: e.activation(
                        s_lse[:], s_ls[:], AF.Exp, bias=s_stat[:, 11:12],
                        accum_out=s_stat[:, 10:11]),
                        waits=[("v", vm)] + Blse.take(), inc="s")
                    Bls.read_by("s", se)
                    Bstat.read_by("s", se)
                    vr = P.op("vector", lambda e: e.reciprocal(
                        s_stat[:, 9:10], s_stat[:, 10:11]), waits=[("s", se)], inc="v")
                    va = P.op("vector", lambda e: e.tensor_scalar(
                        s_lse[:], s_lse[:], s_stat[:, 9:10], None, AluOpType.mult),
                        inc="v")
                    Bstat.read_by("v", va)
                    tv = P.op("tensor", lambda e: e.transpose(
                        pb[4][0:64, 0:128], s_lse[:], s_eye32[:]),
                        waits=[("v", va)] + B_[4].take(), inc="t")
                    Blse.read_by("t", tv)
                    sv = P.op("scalar", lambda e: e.copy(s_lpT[0:64, :],
                                                         pb[4][0:64, 0:128]),
                              waits=[("t", tv)] + BlpT.take(), inc="s")
                    B_[4].read_by("s", sv)
                    tv = mm_group(3, (128, 256),
                                  [(lambda lh=lh: s_GV[0:64, lh * 64:lh * 64 + 64],
                                    lambda: s_lpT[0:64, 0:128])],
                                  waits=[("s", P.counts["s"])], rows=(0, 64))
                    BlpT.read_by("t", tv)
                    sv = P.op("scalar", lambda e, lc_l=lc_l, po_l=po_l: e.copy(
                        s_remT[po_l:po_l + 64, lc_l * 128:lc_l * 128 + 128],
                        pb[3][0:64, 128:256]),
                        waits=[("t", tv)] + (rd if lh == 0 else []), inc="s")
                    B_[3].read_by("s", sv)
                v_rem = P.counts["s"]
                omd = Boprem.take()
                for ec in range(8):
                    bank = ec % 2
                    mms = [(lambda lc=lc, ec=ec: s_rout[:, lc * D + ec * 128:lc * D + ec * 128 + 128],
                            lambda lc=lc: s_remT[:, lc * 128:lc * 128 + 128])
                           for lc in range(4)]
                    tv = mm_group(bank, (0, 128), mms,
                                  waits=[("s", v_rem), ("sw", v_wh)],
                                  extra_first=(omd if ec == 0 else ()))
                    BremT.read_by("t", tv)
                    sv = P.op("scalar", lambda e, ec=ec, bank=bank: e.copy(
                        s_oprem[:, ec * 128:ec * 128 + 128], pb[bank][:, 0:128]),
                        waits=[("t", tv)], inc="s")
                    B_[bank].read_by("s", sv)
                v_orem = P.counts["s"]
                vsp = P.dma("sync", d_opsp[j][:, 4 * 128:12 * 128], s_oprem[:],
                            waits=[("s", v_orem)], sem="osp")
                Boprem.read_by("osp", P.counts["osp"])
                Bopst.read_by("osp", P.counts["osp"])
    v_gate_all = P.counts["s"]
    P.phase_barrier()
    cd.close()

    pe = ExitStack()
    s_wout = pe.enter_context(nc.sbuf_tensor("wout_s", [128, NDC * D], BF16))
    s_ehot = pe.enter_context(nc.sbuf_tensor("ehot_s", [32, 8 * 128], F32))
    s_xuTo = pe.enter_context(nc.sbuf_tensor("xuTo", [128, NDC * 128], BF16))
    s_ce = pe.enter_context(nc.sbuf_tensor("ce", [128, 256], F32))
    s_cT = pe.enter_context(nc.sbuf_tensor("cT", [128, 2 * 128], BF16))
    s_g1 = pe.enter_context(nc.sbuf_tensor("g1", [128, D], F32))
    s_opl = pe.enter_context(nc.sbuf_tensor("opl", [128, 16 * 128], BF16))
    s_gT = pe.enter_context(nc.sbuf_tensor("gT", [32, 128], F32))
    s_grb = pe.enter_context(nc.sbuf_tensor("grb", [128, 128], F32))
    s_otmp = pe.enter_context(nc.sbuf_tensor("otmp", [128, 128], F32))
    s_opreb = pe.enter_context(nc.sbuf_tensor("opreb", [128, 8 * 128], BF16))
    s_s1t = pe.enter_context(nc.sbuf_tensor("s1t", [128, D], F32))
    BxuTo, Bce, BcT, Bg1, Bopl, BgT, Bgrb, Botmp, Bopreb, Bs1t = (Buf() for _ in range(10))
    P.phase_barrier()
    for dt in range(NDC):
        P.dma("gpsimd", s_wout[:, dt * D:(dt + 1) * D],
              d_w["wout"][dt * 128:(dt + 1) * 128, :])
    v_wo = P.counts["sw"]
    v_eh = P.dma("sync", s_ehot[:], d_ehot, sem="lm")
    v_allsp = P.counts["osp"]

    for j in range(NSLOT):
        wl = ln_into(s_xtile, s_xn_b, Bxtile, Bxn, s_eps[:],
                     src_row=x1own_row(j * 128))
        wi = ln_into(s_xn_b, s_xi, Bxn, Bxi, s_eps[:], affine=True)
        su = l2unit(s_xi, s_xu, Bxi, Bxu, wi)
        vx = xpose(s_xu, 0, s_xuTo, 128, 0, ("s", su), Bsrc=Bxu)
        BxuTo.read_by("s", vx)
        mms = [(lambda dt=dt: s_xuTo[:, dt * 128:(dt + 1) * 128],
                lambda dt=dt: s_muu[:, dt * 256:(dt + 1) * 256])
               for dt in range(NDC)]
        tv = mm_group(6, (0, 256), mms, waits=[("s", vx), ("v", v_muu)])
        BxuTo.read_by("t", tv)
        vm = P.op("vector", lambda e: e.tensor_reduce(
            s_stat[:, 11:12], pb[6][:, 0:256], AX.X, AluOpType.max, negate=True),
            waits=[("t", tv)] + Bstat.take(), inc="v")
        se = P.op("scalar", lambda e: e.activation(
            s_ce[:], pb[6][:, 0:256], AF.Exp, bias=s_stat[:, 11:12],
            accum_out=s_stat[:, 10:11]), waits=[("v", vm)] + Bce.take(), inc="s")
        B_[6].read_by("s", se)
        Bstat.read_by("s", se)
        vr = P.op("vector", lambda e: e.reciprocal(s_stat[:, 9:10], s_stat[:, 10:11]),
                  waits=[("s", se)], inc="v")
        va = P.op("vector", lambda e: e.tensor_scalar(
            s_ce[:], s_ce[:], s_stat[:, 9:10], None, AluOpType.mult), inc="v")
        Bstat.read_by("v", va)
        for esc in range(2):
            bank = 4 + esc
            tv = P.op("tensor", lambda e, esc=esc, bank=bank: e.transpose(
                pb[bank][:, 0:128], s_ce[:, esc * 128:esc * 128 + 128], s_eye32[:]),
                waits=[("v", va)] + B_[bank].take(), inc="t")
            sv = P.op("scalar", lambda e, esc=esc, bank=bank: e.copy(
                s_cT[:, esc * 128:esc * 128 + 128], pb[bank][:, 0:128]),
                waits=[("t", tv)] + (BcT.take() if esc == 0 else []), inc="s")
            B_[bank].read_by("s", sv)
        Bce.read_by("t", P.counts["t"])
        v_ct = P.counts["s"]
        for dc in range(2):
            bank = dc
            mms = [(lambda esc=esc: s_cT[:, esc * 128:esc * 128 + 128],
                    lambda esc=esc, dc=dc: s_so[:, esc * D + dc * 512:esc * D + dc * 512 + 512])
                   for esc in range(2)]
            tv = mm_group(bank, (0, 512), mms, waits=[("s", v_ct)])
            BcT.read_by("t", tv)
            Bso.read_by("t", tv)
            sv = P.op("scalar", lambda e, dc=dc, bank=bank: e.activation(
                s_g1[:, dc * 512:(dc + 1) * 512], pb[bank][:, 0:512], AF.Silu),
                waits=[("t", tv)] + (Bg1.take() if dc == 0 else []), inc="s")
            B_[bank].read_by("s", sv)
        v_g1 = P.counts["s"]
        vop = P.dma("sync", s_opl[:], d_opsp[j], waits=[("osp", v_allsp)] + Bopl.take(),
                    sem="osp")
        tv = P.op("tensor", lambda e, j=j: e.transpose(
            pb[2][0:32, 0:128], s_gates[:, j * 32:(j + 1) * 32], s_eye32[:]),
            waits=B_[2].take() + [("s", v_gate_all)], inc="t")
        Bgates.read_by("t", tv)
        sv = P.op("scalar", lambda e: e.copy(s_gT[0:32, :], pb[2][0:32, 0:128]),
                  waits=[("t", tv)] + BgT.take(), inc="s")
        B_[2].read_by("s", sv)
        v_gT = P.counts["s"]
        for ec in range(NDC):
            bank = 2 + ec % 2
            tv = mm_group(bank, (0, 128),
                          [(lambda ec=ec: s_ehot[0:32, ec * 128:ec * 128 + 128],
                            lambda: s_gT[0:32, 0:128])],
                          waits=[("s", v_gT), ("lm", v_eh)])
            BgT.read_by("t", tv)
            vv = P.op("vector", lambda e, bank=bank: e.tensor_copy(
                s_grb[:], pb[bank][:, 0:128]),
                waits=[("t", tv)] + Bgrb.take(), inc="v")
            B_[bank].read_by("v", vv)
            vv = P.op("vector", lambda e, ec=ec: e.tensor_tensor(
                s_otmp[:], s_opl[:, (4 + ec) * 128:(4 + ec) * 128 + 128],
                s_grb[:], AluOpType.mult),
                waits=[("osp", vop)] + Botmp.take(), inc="v")
            Bgrb.read_by("v", vv)
            loccol = ec * 128 if ec < 4 else (8 + ec) * 128
            vv = P.op("vector", lambda e, ec=ec, loccol=loccol: e.tensor_tensor(
                s_opreb[:, ec * 128:ec * 128 + 128],
                s_opl[:, loccol:loccol + 128], s_otmp[:], AluOpType.add),
                waits=(Bopreb.take() if ec == 0 else []), inc="v")
            Botmp.read_by("v", vv)
            Bopl.read_by("v", vv)
        v_opb = P.counts["v"]
        for ecw in range(2):
            bank = ecw
            mms = [(lambda dt=dt: s_opreb[:, dt * 128:(dt + 1) * 128],
                    lambda dt=dt, ecw=ecw: s_wout[:, dt * D + ecw * 512:dt * D + ecw * 512 + 512])
                   for dt in range(NDC)]
            tv = mm_group(bank, (0, 512), mms, waits=[("v", v_opb), ("sw", v_wo)])
            Bopreb.read_by("t", tv)
            sv = P.op("scalar", lambda e, ecw=ecw, bank=bank: e.copy(
                s_s1t[:, ecw * 512:(ecw + 1) * 512], pb[bank][:, 0:512]),
                waits=[("t", tv)] + (Bs1t.take() if ecw == 0 else []), inc="s")
            B_[bank].read_by("s", sv)
        v_s1 = P.counts["s"]
        vv = P.op("vector", lambda e: e.tensor_tensor(
            s_g1[:], s_g1[:], s_s1t[:], AluOpType.mult),
            waits=[("s", v_s1), ("s", v_g1)], inc="v")
        Bs1t.read_by("v", vv)
        vv = P.op("vector", lambda e: e.tensor_tensor(
            s_x2t[:], s_xtile[:], s_g1[:], AluOpType.add),
            waits=Bx2t.take(), inc="v")
        Bg1.read_by("v", vv)
        Bxtile.read_by("v", vv)
        sd = P.dma("sync", d_x2p[j * 128:j * 128 + 128, :], s_x2t[:],
                   waits=[("v", vv)], sem="st")
        Bx2t.read_by("st", sd)

    P.phase_barrier()
    pe.close()
    les.close()
    return


def build_l1():
    nc = bass.Bass("TRN2", target_bir_lowering=False)
    es = ExitStack()
    P = Prog(nc, es)
    dd = declare_l1(nc)
    emit_l1(nc, es, P, dd)
    P.barrier_wait("sync", [("st", P.counts["st"])])
    P.finalize()
    es.close()
    return nc, es


U32 = mybir.dt.uint32

BLOCK_POS = {}
for _j, _g in enumerate(SNAKE_A):
    BLOCK_POS[_g] = (0, _j)
for _j, _g in enumerate(SNAKE_B):
    BLOCK_POS[_g] = (1, _j)

POLL_R = [4 * k for k in range(1, 151)]
while POLL_R[-1] < 4000000:
    POLL_R.append(min(int(POLL_R[-1] * 1.05) + 1, 4000000))
NREL = (POLL_R[-1] + 59999) // 60000


def build_fused():
    nc = bass.Bass("TRN2", target_bir_lowering=False)
    es = ExitStack()
    P = Prog(nc, es)
    dd0 = declare_l0(nc, out_kind="Internal")
    dd1 = declare_l1(nc, shared=dd0, with_x1=False)

    d_shx = nc.dram_tensor("x1shx", [T, D], F32, kind="Internal",
                           addr_space="Shared").ap()
    d_fl = nc.dram_tensor("exfl", [256, 4], U32, kind="Internal",
                          addr_space="Shared").ap()
    d_flf = d_fl.rearrange("a b -> (a b)")
    d_sidx = nc.dram_tensor("sidx", [128, NSLOT], U32, kind="ExternalInput").ap()
    d_fidx = nc.dram_tensor("fidx", [128, 1], U32, kind="ExternalInput").ap()
    d_run = nc.dram_tensor("runid", [128, 4], U32, kind="ExternalInput").ap()

    s_sidx = es.enter_context(nc.sbuf_tensor("s_sidx", [128, NSLOT], U32))
    s_fidx = es.enter_context(nc.sbuf_tensor("s_fidx", [128, 1], U32))
    s_run = es.enter_context(nc.sbuf_tensor("s_run", [128, 4], U32))
    s_flb = es.enter_context(nc.sbuf_tensor("s_flb", [1, 8], U32))
    s_tick = es.enter_context(nc.sbuf_tensor("s_tick", [128, 96], mybir.dt.bfloat16))

    P.dma("sync", s_sidx[:], d_sidx, sem="ex")
    P.dma("sync", s_fidx[:], d_fidx, sem="ex")
    P.dma("sync", s_run[:], d_run, sem="ex")

    def scatter(j, s_x1t, waits):
        return P.op("gpsimd", lambda e, j=j, s=s_x1t: e.indirect_dma_start(
            out=d_shx,
            out_offset=bass.IndirectOffsetOnAxis(ap=s_sidx[:, j:j + 1], axis=0),
            in_=s[:], in_offset=None),
            waits=waits + [("ex", 48)], inc="sc", inc_by=16)

    emit_l0(nc, es, P, dd0, scatter=scatter)

    P.op("gpsimd", lambda e: e.indirect_dma_start(
        out=d_fl,
        out_offset=bass.IndirectOffsetOnAxis(ap=s_fidx[:], axis=0),
        in_=s_run[:], in_offset=None),
        waits=[("sc", 16 * NSLOT)], inc="fl", inc_by=16)

    for rk in POLL_R:
        P.op("sync", lambda e: e.dma_start(out=s_flb[0:1, 0:4],
                                           in_=d_flf[0:4]),
             waits=[("pt", rk)], inc="pl", inc_by=16)
        P.op("sync", lambda e: e.dma_start(out=s_flb[0:1, 4:8],
                                           in_=d_flf[512:516]),
             waits=[("pt", rk)], inc="pl", inc_by=16)

    def spin(e):
        e.memset(s_flb[:], 0)
        with e.register("diff") as rdiff, e.register("rf") as rf, \
             e.register("rr") as rr, e.register("acc") as racc:
            e.reg_load(rr, s_run[0:1, 0:1])
            e.reg_mov(rdiff, 1)
            with e.While(rdiff):
                e.memset(s_tick[:], 0).then_inc(P.sems["pt"], 1)
                e.reg_mov(racc, 0)
                for cell in (0, 4):
                    e.reg_load(rf, s_flb[0:1, cell:cell + 1])
                    e.reg_alu(rf, rf, rr, AluOpType.bitwise_xor)
                    e.reg_alu(racc, racc, rf, AluOpType.bitwise_or)
                e.reg_alu(rdiff, racc, 0, AluOpType.bitwise_or)
        for _ in range(NREL):
            e.memset(s_tick[0:1, 0:4], 0).then_inc(P.sems["pt"], 60000)
        return e.memset(s_tick[0:1, 0:4], 0)

    P.sem("pt")
    P.counts["pt"] += NREL * 60000
    P.op("gpsimd", spin, waits=[("fl", 16)], inc="g")

    P.phase_barrier()

    def x1_row(tok0):
        return d_shx[tok0:tok0 + 128, :]

    def x1own_row(r0):
        return dd0["x1p"][r0:r0 + 128, :]

    emit_l1(nc, es, P, dd1, x1_row=x1_row, x1own_row=x1own_row)
    P.barrier_wait("sync", [("st", P.counts["st"])])
    P.finalize()
    es.close()
    return nc, es


def host_prep_fused(inputs):
    per0 = host_prep_l0(inputs)
    per1 = host_prep_l1(inputs, None)
    runid = np.uint32(np.random.randint(1, 2**31))
    out = []
    for c in range(8):
        role = c % 2
        snake = SNAKE_A if role == 0 else SNAKE_B
        sidx = np.empty((128, NSLOT), np.uint32)
        for j, g in enumerate(snake):
            sidx[:, j] = 128 * g + np.arange(128, dtype=np.uint32)
        ex = {
            "sidx": sidx,
            "fidx": (np.arange(128, dtype=np.uint32) + role * 128).reshape(128, 1),
            "runid": np.full((128, 4), runid, np.uint32),
        }
        out.append({**per0[c], **per1[c], **ex})
    return out


from concourse.bass_utils import run_bass_kernel_spmd

_NC_CACHE = {}


def kernel(**inputs):
    if "f" not in _NC_CACHE:
        _NC_CACHE["f"] = build_fused()
    nc, _es = _NC_CACHE["f"]
    per = host_prep_fused(inputs)
    res = run_bass_kernel_spmd(nc, per, core_ids=list(range(8)))
    out = np.zeros((4, T, D), np.float32)
    for c in range(8):
        part = np.asarray(res.results[c]["x2p"])
        snake = SNAKE_A if c % 2 == 0 else SNAKE_B
        for j, g in enumerate(snake):
            out[c // 2, 128 * g:128 * g + 128] = part[128 * j:128 * j + 128]
    return out


# version 17
# speedup vs baseline: 1.6026x; 1.6026x over previous
import numpy as np
from contextlib import ExitStack
import concourse.bass as bass
import concourse.mybir as mybir
from concourse.alu_op_type import AluOpType

F32 = mybir.dt.float32
BF16 = mybir.dt.bfloat16
AF = mybir.ActivationFunctionType
AX = mybir.AxisListType

D = 1024
T = 4096
TOWN = 2048
NT = 32
NSLOT = 16
NDC = 8
FF = 4096
NE = 9
NEG = -1.0e9
J_ITERS = 7

SNAKE_A = [0, 3, 4, 7, 8, 11, 12, 15, 16, 19, 20, 23, 24, 27, 28, 31]
SNAKE_B = [1, 2, 5, 6, 9, 10, 13, 14, 17, 18, 21, 22, 25, 26, 29, 30]

ENG_OF_SEM = {"t": "tensor", "s": "scalar", "v": "vector", "g": "gpsimd"}


class Buf:
    __slots__ = ("deps",)

    def __init__(self):
        self.deps = []

    def read_by(self, sem, val):
        if val is not None:
            self.deps.append((sem, val))

    def take(self):
        d = self.deps
        self.deps = []
        return d


class Prog:
    ENGS = ("tensor", "scalar", "vector", "gpsimd", "sync")

    def __init__(self, nc, es):
        self.nc = nc
        self.es = es
        self.streams = {e: [] for e in self.ENGS}
        self.seq = 0
        self.sems = {}
        self.counts = {"t": 0, "s": 0, "v": 0, "g": 0, "hw": 0, "sw": 0}
        self.watermark = {e: {} for e in self.ENGS}

    def sem(self, name):
        if name not in self.sems:
            self.sems[name] = self.es.enter_context(self.nc.semaphore(name))
            self.counts[name] = 0
        return self.sems[name]

    SELF_SEM = {"scalar": "s", "vector": "v", "gpsimd": "g"}

    def op(self, eng, fn, waits=(), inc=None, inc_by=1):
        wm = self.watermark[eng]
        own = self.SELF_SEM.get(eng)
        waits = list(waits)
        if own is not None:
            self.sem(own)
            waits.append((own, self.counts[own]))
            if inc is None:
                inc = own
        need = {}
        for w, v in waits:
            if v is None or v <= 0:
                continue
            if ENG_OF_SEM.get(w) == eng and w != own:
                continue
            if v > need.get(w, -1):
                need[w] = v
        wl = []
        for w, v in need.items():
            if wm.get(w, 0) >= v:
                continue
            wm[w] = v
            self.sem(w)
            wl.append((w, v))
        val = None
        if inc is not None:
            self.sem(inc)
            self.counts[inc] += inc_by
            val = self.counts[inc]
        semh = self.sems[inc] if inc is not None else None

        def run(e, fn=fn, wl=tuple(wl), semh=semh, by=inc_by):
            for wname, wval in wl:
                e.wait_ge(self.sems[wname], wval)
            ins = fn(e)
            if semh is not None:
                ins.then_inc(semh, by)
        self.seq += 1
        self.streams[eng].append((self.seq, run))
        return val

    def dma(self, queue, out, in_, waits=(), sem=None):
        if sem is None:
            sem = "hw" if queue in ("sync", "scalar") else "sw"
        return self.op(queue, lambda e, o=out, i=in_: e.dma_start(out=o, in_=i),
                       waits=waits, inc=sem, inc_by=16)

    def barrier_wait(self, eng, waits):
        wm = self.watermark[eng]
        for w, v in waits:
            if v is None or v <= 0 or ENG_OF_SEM.get(w) == eng:
                continue
            if wm.get(w, 0) >= v:
                continue
            wm[w] = v
            self.sem(w)
            sh = self.sems[w]
            self.seq += 1
            self.streams[eng].append(
                (self.seq, lambda e, sh=sh, v=v: e.wait_ge(sh, v)))

    def phase_barrier(self, exclude=("pl", "pt")):
        snap = [(k, v) for k, v in self.counts.items() if k not in exclude]
        for eng in self.ENGS:
            self.barrier_wait(eng, snap)

    def finalize(self, truncate=None):
        import os
        if truncate is None:
            truncate = int(os.environ.get("MK_TRUNC", "0")) or None
        with self.nc.Block() as blk:
            for eng in self.ENGS:
                ops = self.streams[eng]
                if truncate is not None:
                    ops = [x for x in ops if x[0] <= truncate]
                if not ops:
                    continue

                def body(e, ops=ops):
                    for _, f in ops:
                        f(e)
                getattr(blk, eng)(body)


def host_prep_l0(inputs):
    I = inputs
    f32 = np.float32
    shared = {}
    shared["wqT"] = np.ascontiguousarray(np.asarray(I["L0_attn_Wq"], f32).T)
    shared["wkT"] = np.ascontiguousarray(np.asarray(I["L0_attn_Wk"], f32).T)
    shared["wvT"] = np.ascontiguousarray(np.asarray(I["L0_attn_Wv"], f32).T)
    shared["woT"] = np.ascontiguousarray(np.asarray(I["L0_attn_Wo"], f32).T)
    shared["rwT"] = np.ascontiguousarray(np.asarray(I["L0_ffn_router_w"], f32).T)
    shared["rb"] = np.asarray(I["L0_ffn_router_b"], f32).reshape(1, 8).copy()
    w1T = np.empty((NE, D, FF), f32)
    w2T = np.empty((NE, FF, D), f32)
    b1 = np.empty((NE, 1, FF), f32)
    b2 = np.empty((NE, 1, D), f32)
    w1T[0] = np.asarray(I["L0_ffn_shared_w1"], f32).T
    w2T[0] = np.asarray(I["L0_ffn_shared_w2"], f32).T
    b1[0, 0] = I["L0_ffn_shared_b1"]
    b2[0, 0] = I["L0_ffn_shared_b2"]
    for e in range(8):
        w1T[e + 1] = np.asarray(I["L0_ffn_spec_w1"][e], f32).T
        w2T[e + 1] = np.asarray(I["L0_ffn_spec_w2"][e], f32).T
        b1[e + 1, 0] = I["L0_ffn_spec_b1"][e]
        b2[e + 1, 0] = I["L0_ffn_spec_b2"][e]
    shared["w1T"] = np.ascontiguousarray(w1T)
    shared["w2T"] = np.ascontiguousarray(w2T)
    shared["b1r"] = np.ascontiguousarray(b1)
    shared["b2r"] = np.ascontiguousarray(b2)
    shared["eye"] = np.eye(128, dtype=f32)
    shared["ones1"] = np.ones((1, 512), f32)

    qr = np.arange(128)[:, None]
    cc = np.arange(512)[None, :]
    smask_low = np.where(cc > 256 + qr, NEG, 0.0).astype(f32)
    smask_up = np.where(cc > 384 + qr, NEG, 0.0).astype(f32)
    kr = np.arange(128)[:, None]
    qq = np.arange(128)[None, :]
    tri = np.where(kr > qq, NEG, 0.0).astype(f32)
    full = np.full((128, 128), NEG, f32)
    zero = np.zeros((128, 128), f32)
    tmaskT_low = np.concatenate([tri, full], axis=1)
    tmaskT_up = np.concatenate([zero, tri], axis=1)
    smask_roleA = np.stack([smask_low, smask_up])
    smask_roleB = np.stack([smask_up, smask_low])
    tmaskT_roleA = np.stack([tmaskT_low, tmaskT_up])
    tmaskT_roleB = np.stack([tmaskT_up, tmaskT_low])

    x = np.asarray(I["L0_attn_x"], f32)
    per_core = []
    for core in range(8):
        b, role = core // 2, core % 2
        snake = SNAKE_A if role == 0 else SNAKE_B
        xb = x[b]
        x_own = np.ascontiguousarray(
            np.stack([xb[128 * g:128 * g + 128] for g in snake]).reshape(TOWN, D))
        per_core.append({
            "x": np.ascontiguousarray(xb),
            "x_own": x_own,
            "smask": smask_roleA if role == 0 else smask_roleB,
            "tmaskT": tmaskT_roleA if role == 0 else tmaskT_roleB,
            **shared,
        })
    return per_core


def unscramble(part, role):
    snake = SNAKE_A if role == 0 else SNAKE_B
    return [(g, part[128 * j:128 * j + 128]) for j, g in enumerate(snake)]


def declare_l0(nc, out_kind="ExternalOutput", shared=None):
    dd = {}
    dd["x"] = nc.dram_tensor("x", [T, D], F32, kind="ExternalInput").ap()
    dd["x_own"] = nc.dram_tensor("x_own", [TOWN, D], F32, kind="ExternalInput").ap()
    for w in ("wqT", "wkT", "wvT", "woT"):
        dd[w] = nc.dram_tensor(w, [D, D], F32, kind="ExternalInput").ap()
    dd["rwT"] = nc.dram_tensor("rwT", [D, 8], F32, kind="ExternalInput").ap()
    dd["rb"] = nc.dram_tensor("rb", [1, 8], F32, kind="ExternalInput").ap()
    dd["w1T"] = nc.dram_tensor("w1T", [NE, D, FF], F32, kind="ExternalInput").ap()
    dd["w2T"] = nc.dram_tensor("w2T", [NE, FF, D], F32, kind="ExternalInput").ap()
    dd["b1r"] = nc.dram_tensor("b1r", [NE, 1, FF], F32, kind="ExternalInput").ap()
    dd["b2r"] = nc.dram_tensor("b2r", [NE, 1, D], F32, kind="ExternalInput").ap()
    if shared and "eye" in shared:
        dd["eye"] = shared["eye"]
        dd["ones1"] = shared["ones1"]
    else:
        dd["eye"] = nc.dram_tensor("eye", [128, 128], F32, kind="ExternalInput").ap()
        dd["ones1"] = nc.dram_tensor("ones1", [1, 512], F32, kind="ExternalInput").ap()
    dd["smask"] = nc.dram_tensor("smask", [2, 128, 512], F32, kind="ExternalInput").ap()
    dd["tmaskT"] = nc.dram_tensor("tmaskT", [2, 128, 256], F32, kind="ExternalInput").ap()
    dd["outT_sp"] = nc.dram_tensor("outT_sp", [2, 128, 4 * TOWN], BF16, kind="Internal").ap()
    dd["x1p"] = nc.dram_tensor("x1p", [TOWN, D], F32, kind=out_kind).ap()
    return dd


def emit_l0(nc, es, P, dd, upto="full", ntc=8, sub=9, scatter=None):
    d_x = dd["x"]
    d_xown = dd["x_own"]
    d_wT = {w: dd[w] for w in ("wqT", "wkT", "wvT", "woT")}
    d_rwT = dd["rwT"]
    d_rb = dd["rb"]
    d_w1T = dd["w1T"]
    d_w2T = dd["w2T"]
    d_b1 = dd["b1r"]
    d_b2 = dd["b2r"]
    d_eye = dd["eye"]
    d_ones = dd["ones1"]
    d_smask = dd["smask"]
    d_tmaskT = dd["tmaskT"]
    d_osp = dd["outT_sp"]
    d_x1p = dd["x1p"]
    les = ExitStack()

    def sb(name, shape, dt):
        return les.enter_context(nc.sbuf_tensor(name, shape, dt))

    def ps(name, shape, dt):
        return les.enter_context(nc.psum_tensor(name, shape, dt))

    s_eye32 = sb("eye32", [128, 128], F32)
    s_eyeb = sb("eyeb", [128, 128], BF16)
    s_ones = sb("ones_b", [1, 512], BF16)
    s_onesrow = sb("onesrow", [1, 128], BF16)
    s_ones32 = sb("ones32r", [1, 128], F32)
    s_smask = sb("smask_s", [128, 2 * 512], F32)
    s_rw = sb("rw_b", [128, NDC * 8], F32)
    s_rb = sb("rb_b", [1, 8], F32)
    s_xnT_own = sb("xnT_own", [128, NDC * TOWN], BF16)
    s_js = None
    s_xnt32 = None
    s_jvs = sb("junk_v", [128, D], BF16)
    s_gc = sb("gc", [128, 40], F32)
    s_stat = sb("stat", [128, 8], F32)
    s_xtile = sb("xtile", [128, D], F32)
    s_xn_b = sb("xn_b", [128, D], F32)
    s_comb = sb("comb", [128, NSLOT * 8], F32)
    s_rsm = sb("rsm", [128, 12], F32)
    s_b1 = sb("b1_b", [1, FF], BF16)
    s_b2 = sb("b2_b", [1, D], BF16)
    s_gtile = sb("gtile", [128, D], F32)
    s_x1t = sb("x1t", [128, D], F32)

    pb = [ps(f"pb{i}", [128, 512], F32) for i in range(8)]
    B_ = [Buf() for _ in range(8)]
    Bxtile, Bxn, Bgc, Btau, Bstat, Brsm = (Buf() for _ in range(6))
    Bxnt32 = Buf()
    Bjs, Bjv, Bgtile, Bx1t = (Buf() for _ in range(4))

    P.dma("sync", s_eye32[:], d_eye, sem="c0")
    P.dma("gpsimd", s_eyeb[:], d_eye, sem="sw")
    v_eyeb = P.counts["sw"]
    for p in range(2):
        P.dma("sync", s_smask[:, p * 512:(p + 1) * 512], d_smask[p], sem="c0")
    for dt in range(NDC):
        P.dma("sync", s_rw[:, dt * 8:(dt + 1) * 8],
              d_rwT[dt * 128:(dt + 1) * 128, :], sem="c0")
    P.dma("sync", s_rb[:], d_rb, sem="c0")
    v_hw0 = P.counts["c0"]
    v_eye32 = v_hw0
    P.dma("gpsimd", s_ones[:], d_ones)
    v_sw0 = P.counts["sw"]
    v_ones = P.op("vector", lambda e: e.memset(s_onesrow[:], 1.0), inc="v")
    v_ones = P.op("vector", lambda e: e.memset(s_ones32[:], 1.0), inc="v")
    s_eps = sb("epscol", [128, 1], F32)
    v_eps = P.op("vector", lambda e: e.memset(s_eps[:], 1e-5), inc="v")

    def ln_tile(src_row):
        vd = P.dma("sync", s_xtile[:], src_row, waits=Bxtile.take())
        wx = [("hw", vd)]
        vm = P.op("vector", lambda e: e.tensor_scalar(
            s_jvs[:, 0:D], s_xtile[:], 1.0 / D, None, AluOpType.mult,
            AluOpType.add, accum_out=s_stat[:, 0:1]),
            waits=wx + Bjv.take() + Bstat.take(), inc="v")
        vs = P.op("scalar", lambda e: e.activation(
            s_js[:, 0:D], s_xtile[:], AF.Square, accum_out=s_stat[:, 1:2]),
            waits=wx + Bjs.take(), inc="s")
        Bxtile.read_by("v", vm)
        Bxtile.read_by("s", vs)
        Bjv.read_by("v", vm)
        Bjs.read_by("s", vs)
        v1 = P.op("vector", lambda e: e.scalar_tensor_tensor(
            s_stat[:, 2:3], s_stat[:, 0:1], -1.0, s_stat[:, 0:1],
            AluOpType.mult, AluOpType.mult), waits=[("s", vs)], inc="v")
        v2 = P.op("vector", lambda e: e.scalar_tensor_tensor(
            s_stat[:, 3:4], s_stat[:, 1:2], 1.0 / D, s_stat[:, 2:3],
            AluOpType.mult, AluOpType.add), inc="v")
        s1 = P.op("scalar", lambda e: e.activation(
            s_stat[:, 4:5], s_stat[:, 3:4], AF.Sqrt, bias=s_eps[:]),
            waits=[("v", v2), ("v", v_eps)], inc="s")
        v3 = P.op("vector", lambda e: e.reciprocal(s_stat[:, 5:6], s_stat[:, 4:5]),
                  waits=[("s", s1)], inc="v")
        v4 = P.op("vector", lambda e: e.scalar_tensor_tensor(
            s_stat[:, 6:7], s_stat[:, 0:1], -1.0, s_stat[:, 5:6],
            AluOpType.mult, AluOpType.mult), inc="v")
        sx = P.op("scalar", lambda e: e.activation(
            s_xn_b[:], s_xtile[:], AF.Identity, bias=s_stat[:, 6:7],
            scale=s_stat[:, 5:6]), waits=[("v", v4)] + Bxn.take(), inc="s")
        Bxtile.read_by("s", sx)
        Bstat.read_by("s", sx)
        return sx

    def xpose_xn(dest, dest_stride, destcol, wait_s, also32=False):
        for g4 in range(2):
            bank = 4 + g4
            tvl = None
            for slot in range(4):
                dt = g4 * 4 + slot
                tvl = P.op("tensor", lambda e, dt=dt, bank=bank, slot=slot: e.transpose(
                    pb[bank][:, slot * 128:slot * 128 + 128],
                    s_xn_b[:, dt * 128:(dt + 1) * 128], s_eye32[:]),
                    waits=[("s", wait_s), ("c0", v_eye32)] +
                          (B_[bank].take() if slot == 0 else []), inc="t")
            for slot in range(4):
                dt = g4 * 4 + slot
                sv = P.op("scalar",
                          lambda e, dt=dt, bank=bank, slot=slot: e.copy(
                              dest[:, dt * dest_stride + destcol:dt * dest_stride + destcol + 128],
                              pb[bank][:, slot * 128:slot * 128 + 128]),
                          waits=[("t", tvl)], inc="s")
                B_[bank].read_by("s", sv)
                if also32:
                    sv32 = P.op("scalar",
                                lambda e, dt=dt, bank=bank, slot=slot: e.copy(
                                    s_xnt32[:, dt * 128:(dt + 1) * 128],
                                    pb[bank][:, slot * 128:slot * 128 + 128]),
                                waits=(Bxnt32.take() if dt == 0 else []), inc="s")
                    B_[bank].read_by("s", sv32)
        Bxn.read_by("t", P.counts["t"])
        return P.counts["s"]

    def mm_group(bank, cols, mms, waits=(), extra_first=(), rows=(0, 128)):
        n = len(mms)
        w0 = list(waits) + B_[bank].take() + list(extra_first)
        last = None
        for i, (fl, fr) in enumerate(mms):
            last = P.op("tensor",
                        lambda e, fl=fl, fr=fr, i=i: e.matmul(
                            pb[bank][rows[0]:rows[1], cols[0]:cols[1]], fl(), fr(),
                            start=(i == 0), stop=(i == n - 1)),
                        waits=(w0 if i == 0 else ()),
                        inc=("t" if i == n - 1 else None))
        return last

    def emit_router_tile(j, v32):
        mms = [(lambda dt=dt: s_xnt32[:, dt * 128:(dt + 1) * 128],
                lambda dt=dt: s_rw[:, dt * 8:(dt + 1) * 8]) for dt in range(NDC)]
        mms.append((lambda: s_ones32[0:1, 0:128], lambda: s_rb[0:1, 0:8]))
        tv = mm_group(6, (0, 8), mms, waits=[("c0", v_hw0), ("s", v32),
                                             ("v", v_ones)])
        Bxnt32.read_by("t", tv)
        vm = P.op("vector", lambda e: e.tensor_reduce(
            s_rsm[:, 0:1], pb[6][:, 0:8], AX.X, AluOpType.max, negate=True),
            waits=[("t", tv)] + Brsm.take(), inc="v")
        se = P.op("scalar", lambda e: e.activation(
            s_rsm[:, 1:9], pb[6][:, 0:8], AF.Exp, bias=s_rsm[:, 0:1],
            accum_out=s_rsm[:, 9:10]), waits=[("v", vm)], inc="s")
        B_[6].read_by("s", se)
        vv = P.op("vector", lambda e: e.reciprocal(s_rsm[:, 10:11], s_rsm[:, 9:10]),
                  waits=[("s", se)], inc="v")
        vv = P.op("vector", lambda e, j=j: e.tensor_scalar(
            s_comb[:, j * 8:j * 8 + 8], s_rsm[:, 1:9], s_rsm[:, 10:11], None,
            AluOpType.mult), inc="v")
        vv = P.op("vector", lambda e, j=j: e.tensor_reduce(
            s_rsm[:, 0:1], s_comb[:, j * 8:j * 8 + 8], AX.X, AluOpType.max), inc="v")
        vv = P.op("vector", lambda e, j=j: e.tensor_scalar(
            s_rsm[:, 1:9], s_comb[:, j * 8:j * 8 + 8], s_rsm[:, 0:1], None,
            AluOpType.is_equal), inc="v")
        vv = P.op("vector", lambda e, j=j: e.scalar_tensor_tensor(
            s_rsm[:, 1:9], s_rsm[:, 1:9], -4.0, s_comb[:, j * 8:j * 8 + 8],
            AluOpType.mult, AluOpType.add), inc="v")
        vv = P.op("vector", lambda e: e.tensor_reduce(
            s_rsm[:, 9:10], s_rsm[:, 1:9], AX.X, AluOpType.max), inc="v")
        vv = P.op("vector", lambda e: e.tensor_tensor(
            s_rsm[:, 11:12], s_rsm[:, 0:1], s_rsm[:, 9:10], AluOpType.add), inc="v")
        vv = P.op("vector", lambda e: e.reciprocal(s_rsm[:, 10:11], s_rsm[:, 11:12]),
                  inc="v")
        vv = P.op("vector", lambda e, j=j: e.tensor_scalar(
            s_rsm[:, 1:9], s_comb[:, j * 8:j * 8 + 8], s_rsm[:, 9:10], None,
            AluOpType.is_ge), inc="v")
        vv = P.op("vector", lambda e, j=j: e.tensor_tensor(
            s_rsm[:, 1:9], s_rsm[:, 1:9], s_comb[:, j * 8:j * 8 + 8],
            AluOpType.mult), inc="v")
        vv = P.op("vector", lambda e, j=j: e.tensor_scalar(
            s_comb[:, j * 8:j * 8 + 8], s_rsm[:, 1:9], s_rsm[:, 10:11], None,
            AluOpType.mult), inc="v")
        Brsm.read_by("v", vv)

    emitted_xnt_own = [False]
    v_own_holder = [0]
    LEVELS = {"kv0": 0, "own": 1, "attn0": 2, "kv1": 3, "attn1": 4, "router": 5,
              "full": 9}
    LVL = LEVELS[upto]

    def finish_debug():
        P.phase_barrier()
        for j in range(NSLOT):
            sd = P.dma("sync", d_x1p[j * 128:j * 128 + 128, :], s_xtile[:],
                       sem="st")
        P.barrier_wait("sync", [("st", P.counts["st"])])
        les.close()
        return

    def emit_half(half):
        nonlocal s_js, s_xnt32
        ph = ExitStack()
        s_js = ph.enter_context(nc.sbuf_tensor(f"junk_s{half}", [128, T], BF16))
        s_xnt32 = ph.enter_context(nc.sbuf_tensor(f"xnt32_{half}", [128, NDC * 128], F32))
        s_KT = ph.enter_context(nc.sbuf_tensor(f"KT{half}", [128, 4 * T], BF16))
        s_V = ph.enter_context(nc.sbuf_tensor(f"V{half}", [128, NT * 512], BF16))
        BKT, BV = Buf(), Buf()

        kv = ExitStack()
        s_wk = kv.enter_context(nc.sbuf_tensor(f"wk{half}", [128, NDC * 512], BF16))
        s_wv = kv.enter_context(nc.sbuf_tensor(f"wv{half}", [128, NDC * 512], BF16))
        s_xnTc = kv.enter_context(nc.sbuf_tensor(f"xnTc{half}", [128, NDC * 512], BF16))
        BxnTc = Buf()
        P.phase_barrier()
        for dt in range(NDC):
            P.dma("gpsimd", s_wk[:, dt * 512:(dt + 1) * 512],
                  d_wT["wkT"][dt * 128:(dt + 1) * 128, half * 512:half * 512 + 512])
            P.dma("gpsimd", s_wv[:, dt * 512:(dt + 1) * 512],
                  d_wT["wvT"][dt * 128:(dt + 1) * 128, half * 512:half * 512 + 512])
        v_swk = P.counts["sw"]
        for tc in range(ntc):
            for ti in range(4):
                tok0 = tc * 512 + ti * 128
                if sub >= 2:
                    sx = ln_tile(d_x[tok0:tok0 + 128, :])
                if sub >= 3:
                    xpose_xn(s_xnTc, 512, ti * 128, sx)
            vxn = P.counts["s"]
            wxn = [("s", vxn), ("sw", v_swk)]
            for ec in range(4 if sub >= 4 else 0):
                bank = ec % 2
                mms = [(lambda dt=dt, ec=ec: s_wk[:, dt * 512 + ec * 128:dt * 512 + ec * 128 + 128],
                        lambda dt=dt: s_xnTc[:, dt * 512:(dt + 1) * 512])
                       for dt in range(NDC)]
                tv = mm_group(bank, (0, 512), mms, waits=wxn)
                sv = P.op("scalar", lambda e, ec=ec, tc=tc, bank=bank: e.copy(
                    s_KT[:, ec * T + tc * 512:ec * T + tc * 512 + 512],
                    pb[bank][:, 0:512]), waits=[("t", tv)], inc="s")
                B_[bank].read_by("s", sv)
            for ti in range(4 if sub >= 5 else 0):
                bank = 2 + ti % 2
                tt = tc * 4 + ti
                mms = [(lambda dt=dt, ti=ti: s_xnTc[:, dt * 512 + ti * 128:dt * 512 + ti * 128 + 128],
                        lambda dt=dt: s_wv[:, dt * 512:(dt + 1) * 512])
                       for dt in range(NDC)]
                tv = mm_group(bank, (0, 512), mms, waits=wxn)
                sv = P.op("scalar", lambda e, tt=tt, bank=bank: e.copy(
                    s_V[:, tt * 512:tt * 512 + 512], pb[bank][:, 0:512]),
                    waits=[("t", tv)], inc="s")
                B_[bank].read_by("s", sv)
            BxnTc.read_by("t", P.counts["t"])
        P.phase_barrier()
        kv.close()

        if LVL < 1 and half == 0:
            ph.close()
            return
        if not emitted_xnt_own[0]:
            emitted_xnt_own[0] = True
            for j in range(NSLOT):
                sx = ln_tile(d_xown[j * 128:j * 128 + 128, :])
                v32 = xpose_xn(s_xnT_own, TOWN, j * 128, sx, also32=True)
                emit_router_tile(j, v32)
            v_own_holder[0] = P.counts["s"]

        if LVL < 2 + 2 * half:
            ph.close()
            return
        at = ExitStack()
        s_wq = at.enter_context(nc.sbuf_tensor(f"wq{half}", [128, NDC * 512], BF16))
        s_scores = at.enter_context(nc.sbuf_tensor(f"sc{half}", [128, T], F32))
        s_attnT = at.enter_context(nc.sbuf_tensor(f"aT{half}", [128, T], BF16))
        s_QT = at.enter_context(nc.sbuf_tensor(f"QT{half}", [128, 4 * 128], BF16))
        s_outTh = at.enter_context(nc.sbuf_tensor(f"oT{half}", [128, 4 * TOWN], BF16))
        Bscores, BattnT, BQT, Btaub = Buf(), Buf(), Buf(), Buf()
        P.phase_barrier()
        for dt in range(NDC):
            P.dma("gpsimd", s_wq[:, dt * 512:(dt + 1) * 512],
                  d_wT["wqT"][dt * 128:(dt + 1) * 128, half * 512:half * 512 + 512])
        v_swq = P.counts["sw"]

        for j in range(NSLOT):
            E = 256 * (j + 1)
            nkc = E // 256
            qdeps = BQT.take()
            tvq = None
            for ec in range(4):
                mms = [(lambda dt=dt, ec=ec: s_wq[:, dt * 512 + ec * 128:dt * 512 + ec * 128 + 128],
                        lambda dt=dt, j=j: s_xnT_own[:, dt * TOWN + j * 128:dt * TOWN + j * 128 + 128])
                       for dt in range(NDC)]
                tvq = mm_group(6, (ec * 128, ec * 128 + 128), mms,
                               waits=[("sw", v_swq), ("s", v_own_holder[0])],
                               extra_first=(qdeps if ec == 0 else ()))
            for ec in range(4):
                sv = P.op("scalar", lambda e, ec=ec: e.activation(
                    s_QT[:, ec * 128:ec * 128 + 128],
                    pb[6][:, ec * 128:ec * 128 + 128], AF.Copy, scale=0.125),
                    waits=[("t", tvq)], inc="s")
                B_[6].read_by("s", sv)
            vqt = P.counts["s"]
            for hl in range(8):
                ec, po = hl // 2, 64 * (hl % 2)
                sdeps = Bscores.take()
                for kc in range(nkc):
                    bank = kc % 2
                    co = 0
                    tv = mm_group(bank, (co, co + 256),
                                  [(lambda ec=ec, po=po: s_QT[po:po + 64, ec * 128:ec * 128 + 128],
                                    lambda ec=ec, po=po, kc=kc: s_KT[po:po + 64, ec * T + kc * 256:ec * T + kc * 256 + 256])],
                                  waits=[("s", vqt)])
                    if kc >= nkc - 2:
                        mo = 256 * (kc - (nkc - 2))
                        ev = P.op("vector", lambda e, kc=kc, bank=bank, co=co, mo=mo, j=j:
                                  e.tensor_tensor(
                                      s_scores[:, kc * 256:kc * 256 + 256],
                                      pb[bank][:, co:co + 256],
                                      s_smask[:, (j % 2) * 512 + mo:(j % 2) * 512 + mo + 256],
                                      AluOpType.add),
                                  waits=[("t", tv), ("c0", v_hw0)] +
                                        (sdeps if kc == 0 else []), inc="v")
                        B_[bank].read_by("v", ev)
                    else:
                        ev = P.op("scalar", lambda e, kc=kc, bank=bank, co=co:
                                  e.copy(s_scores[:, kc * 256:kc * 256 + 256],
                                         pb[bank][:, co:co + 256]),
                                  waits=[("t", tv)] + (sdeps if kc == 0 else []),
                                  inc="s")
                        B_[bank].read_by("s", ev)
                wsc = [("s", P.counts["s"]), ("v", P.counts["v"])]
                vr = P.op("vector", lambda e, hl=hl, E=E: e.tensor_reduce(
                    s_gc[:, 24 + hl:25 + hl], s_scores[:, 0:E], AX.X,
                    AluOpType.max, negate=True),
                    waits=wsc + (Bgc.take() if hl == 0 else []), inc="v")
                vr = P.op("vector", lambda e, hl=hl: e.tensor_scalar(
                    s_gc[:, 24 + hl:25 + hl], s_gc[:, 24 + hl:25 + hl], 1.0,
                    None, AluOpType.add), inc="v")
                vr = P.op("vector", lambda e, hl=hl: e.tensor_scalar(
                    s_gc[:, 16 + hl:17 + hl], s_gc[:, 24 + hl:25 + hl], -1.0,
                    None, AluOpType.mult), inc="v")
                adeps = BattnT.take()
                for it in range(J_ITERS):
                    sg = P.op("scalar", lambda e, hl=hl, E=E: e.activation(
                        s_js[:, 0:E], s_scores[:, 0:E], AF.Relu,
                        bias=s_gc[:, 24 + hl:25 + hl],
                        accum_out=s_gc[:, hl:hl + 1]),
                        waits=[("v", vr)] + Bjs.take(), inc="s")
                    Bjs.read_by("s", sg)
                    vc = P.op("vector", lambda e, hl=hl, E=E: e.tensor_scalar(
                        s_attnT[:, 0:E], s_scores[:, 0:E],
                        s_gc[:, 16 + hl:17 + hl], None, AluOpType.is_gt,
                        AluOpType.add, accum_out=s_gc[:, 8 + hl:9 + hl]),
                        waits=(adeps if it == 0 else []), inc="v")
                    vu = P.op("vector", lambda e, hl=hl: e.reciprocal(
                        s_gc[:, 32 + hl:33 + hl], s_gc[:, 8 + hl:9 + hl]), inc="v")
                    vu = P.op("vector", lambda e, hl=hl: e.scalar_tensor_tensor(
                        s_gc[:, 32 + hl:33 + hl], s_gc[:, hl:hl + 1], -1.0,
                        s_gc[:, 32 + hl:33 + hl], AluOpType.add, AluOpType.mult),
                        waits=[("s", sg)], inc="v")
                    vr = P.op("vector", lambda e, hl=hl: e.tensor_tensor(
                        s_gc[:, 24 + hl:25 + hl], s_gc[:, 24 + hl:25 + hl],
                        s_gc[:, 32 + hl:33 + hl], AluOpType.subtract), inc="v")
                    vu = P.op("vector", lambda e, hl=hl: e.tensor_tensor(
                        s_gc[:, 16 + hl:17 + hl], s_gc[:, 16 + hl:17 + hl],
                        s_gc[:, 32 + hl:33 + hl], AluOpType.add), inc="v")
                Bscores.read_by("v", vr)
                Bscores.read_by("s", P.counts["s"])
                jdeps = Bjs.take()
                sgfl = []
                for g2 in range((E + 511) // 512):
                    w0 = g2 * 512
                    w1 = min(E, w0 + 512)
                    sgf = P.op("scalar", lambda e, hl=hl, w0=w0, w1=w1:
                               e.activation(s_js[:, w0:w1], s_scores[:, w0:w1],
                                            AF.Relu,
                                            bias=s_gc[:, 24 + hl:25 + hl]),
                               waits=(([("v", vr)] + jdeps) if g2 == 0 else ()),
                               inc="s")
                    sgfl.append(sgf)
                Bscores.read_by("s", sgfl[-1])
                ttl = None
                evl = None
                for grp in range(j + 1):
                    bank = 3 + (grp % 2) * 2
                    zdeps = B_[bank].take()
                    tvz = None
                    for kt in range(2):
                        ktg = grp * 2 + kt
                        tvz = P.op("tensor",
                                   lambda e, ktg=ktg, bank=bank, kt=kt:
                                   e.matmul(pb[bank][:, kt * 128:kt * 128 + 128],
                                            s_js[:, ktg * 128:ktg * 128 + 128],
                                            s_eyeb[:], start=True, stop=True),
                                   waits=(([("s", sgfl[grp // 2]),
                                            ("sw", v_eyeb)] + zdeps)
                                          if kt == 0 else ()), inc="t")
                    ttl = tvz
                    evl = P.op("scalar", lambda e, grp=grp, bank=bank: e.copy(
                        s_attnT[:, grp * 256:grp * 256 + 256],
                        pb[bank][:, 0:256]), waits=[("t", tvz)], inc="s")
                    B_[bank].read_by("s", evl)
                Bjs.read_by("t", ttl)
                mms = []
                for ktg in range(2 * (j + 1)):
                    mms.append((lambda ktg=ktg, hl=hl: s_V[:, ktg * 512 + hl * 64:ktg * 512 + hl * 64 + 64],
                                lambda ktg=ktg: s_attnT[:, ktg * 128:ktg * 128 + 128]))
                tv = mm_group(2, (0, 128), mms, waits=[("s", evl)], rows=(0, 64))
                BattnT.read_by("t", tv)
                BQT.read_by("t", tv)
                BKT.read_by("t", tv)
                BV.read_by("t", tv)
                dl, po_o = hl // 2, 64 * (hl % 2)
                sv = P.op("scalar", lambda e, dl=dl, po_o=po_o, j=j: e.copy(
                    s_outTh[po_o:po_o + 64, dl * TOWN + j * 128:dl * TOWN + j * 128 + 128],
                    pb[2][0:64, 0:128]), waits=[("t", tv)], inc="s")
                B_[2].read_by("s", sv)

        vsp = P.dma("sync", d_osp[half], s_outTh[:],
                    waits=[("s", P.counts["s"])], sem="osp")
        P.phase_barrier()
        at.close()
        ph.close()

    emit_half(0)
    if LVL < 3:
        return finish_debug()
    emit_half(1)
    if LVL < 5:
        return finish_debug()

    ff = ExitStack()
    s_outT = ff.enter_context(nc.sbuf_tensor("outT", [128, NDC * TOWN], BF16))
    s_wo = ff.enter_context(nc.sbuf_tensor("wo_b", [128, NDC * D], BF16))
    s_w1e = ff.enter_context(nc.sbuf_tensor("w1e", [128, 2 * NDC * 512], BF16))
    s_w2c = [ff.enter_context(nc.sbuf_tensor(f"w2c{i}", [128, D], BF16))
             for i in range(3)]
    s_h = ff.enter_context(nc.sbuf_tensor("h_sb", [128, 32 * 512], BF16))
    s_acc = ff.enter_context(nc.sbuf_tensor("acc", [128, 4 * D], F32))
    s_stile = ff.enter_context(nc.sbuf_tensor("stile", [128, 4 * D], F32))
    Bh, Bacc, Bstile = Buf(), Buf(), Buf()
    Bw1e = [Buf(), Buf()]
    Bw2 = [Buf(), Buf(), Buf()]
    Bb1, Bb2 = Buf(), Buf()
    P.phase_barrier()
    v_spills = P.counts["osp"] if "osp" in P.counts else 32
    for half in (0, 1):
        P.dma("sync", s_outT[:, half * 4 * TOWN:(half + 1) * 4 * TOWN], d_osp[half],
              waits=[("osp", v_spills)], sem="osp")
    v_hwo = P.counts["osp"]
    for dt in range(NDC):
        P.dma("gpsimd", s_wo[:, dt * D:(dt + 1) * D],
              d_wT["woT"][dt * 128:(dt + 1) * 128, :])
    v_swo = P.counts["sw"]

    for ch in range(4):
        sdeps = Bstile.take()
        for ti in range(4):
            j = ch * 4 + ti
            for ecc in range(2):
                bank = ecc
                mms = [(lambda dt=dt, j=j: s_outT[:, dt * TOWN + j * 128:dt * TOWN + j * 128 + 128],
                        lambda dt=dt, ecc=ecc: s_wo[:, dt * D + ecc * 512:dt * D + ecc * 512 + 512])
                       for dt in range(NDC)]
                tv = mm_group(bank, (0, 512), mms,
                              waits=[("sw", v_swo), ("osp", v_hwo)],
                              extra_first=(sdeps if ti == 0 and ecc == 0 else ()))
                sv = P.op("scalar", lambda e, ti=ti, ecc=ecc, bank=bank: e.copy(
                    s_stile[:, ti * D + ecc * 512:ti * D + ecc * 512 + 512],
                    pb[bank][:, 0:512]), waits=[("t", tv)], inc="s")
                B_[bank].read_by("s", sv)
        for e_i in range(NE):
            hdeps = Bh.take()
            vb1 = P.dma("gpsimd", s_b1[:], d_b1[e_i], waits=Bb1.take(), sem="b1s")
            vb2 = P.dma("gpsimd", s_b2[:], d_b2[e_i], waits=Bb2.take(), sem="b2s")
            for fc in range(8):
                lv = None
                par = fc % 2
                w1deps = Bw1e[par].take()
                wsem1 = "w1a" if par == 0 else "w1b"
                for dt in range(NDC):
                    lv = P.dma("gpsimd",
                               s_w1e[:, par * 4096 + dt * 512:par * 4096 + (dt + 1) * 512],
                               d_w1T[e_i][dt * 128:(dt + 1) * 128, fc * 512:(fc + 1) * 512],
                               waits=(w1deps if dt == 0 else []), sem=wsem1)
                wld = [(wsem1, lv)]
                for ft in range(4):
                    fglob = fc * 4 + ft
                    bank = 2 + fglob % 2
                    mms = [(lambda dt=dt, ft=ft, par=par: s_w1e[:, par * 4096 + dt * 512 + ft * 128:par * 4096 + dt * 512 + ft * 128 + 128],
                            lambda dt=dt, ch=ch: s_xnT_own[:, dt * TOWN + ch * 512:dt * TOWN + ch * 512 + 512])
                           for dt in range(NDC)]
                    mms.append((lambda fglob=fglob: s_b1[0:1, fglob * 128:fglob * 128 + 128],
                                lambda: s_ones[0:1, 0:512]))
                    tv = mm_group(bank, (0, 512), mms,
                                  waits=wld + [("b1s", vb1), ("s", v_own_holder[0])],
                                  extra_first=(hdeps if fglob == 0 else ()))
                    Bb1.read_by("t", tv)
                    sv = P.op("scalar", lambda e, fglob=fglob, bank=bank:
                              e.activation(s_h[:, fglob * 512:fglob * 512 + 512],
                                           pb[bank][:, 0:512], AF.Gelu_apprx_tanh),
                              waits=[("t", tv)], inc="s")
                    B_[bank].read_by("s", sv)
                    Bw1e[par].read_by("t", tv)
            vh = P.counts["s"]
            odeps = [B_[b].take() for b in range(8)]
            for ft in range(32):
                wbuf2 = s_w2c[ft % 3]
                wsem = ("w2a", "w2b", "w2c")[ft % 3]
                lv = P.dma("gpsimd", wbuf2[:],
                           d_w2T[e_i][ft * 128:(ft + 1) * 128, :],
                           waits=Bw2[ft % 3].take(), sem=wsem)
                for ti in range(4):
                    for dc in range(2):
                        bank = ti * 2 + dc
                        tvw = P.op("tensor",
                             lambda e, ft=ft, ti=ti, dc=dc, bank=bank, wbuf2=wbuf2:
                             e.matmul(pb[bank][:, 0:512],
                                      s_h[:, ft * 512 + ti * 128:ft * 512 + ti * 128 + 128],
                                      wbuf2[:, dc * 512:dc * 512 + 512],
                                      start=(ft == 0), stop=False),
                             waits=([(wsem, lv), ("s", vh)] +
                                    (odeps[bank] if ft == 0 else [])),
                             inc=("t" if (ti == 3 and dc == 1) else None))
                Bw2[ft % 3].read_by("t", tvw)
            vo = None
            for ti in range(4):
                for dc in range(2):
                    bank = ti * 2 + dc
                    vo = P.op("tensor", lambda e, dc=dc, bank=bank:
                              e.matmul(pb[bank][:, 0:512],
                                       s_onesrow[0:1, 0:128],
                                       s_b2[0:1, dc * 512:dc * 512 + 512],
                                       start=False, stop=True),
                              waits=[("v", v_ones), ("b2s", vb2)], inc="t")
                    Bb2.read_by("t", vo)
            Bh.read_by("t", vo)
            for ti in range(4):
                for dc in range(2):
                    bank = ti * 2 + dc
                    j = ch * 4 + ti
                    if e_i == 0:
                        sv = P.op("scalar", lambda e, ti=ti, dc=dc, bank=bank:
                                  e.copy(s_acc[:, ti * D + dc * 512:ti * D + dc * 512 + 512],
                                         pb[bank][:, 0:512]),
                                  waits=[("t", vo)] +
                                        (Bacc.take() if ti == 0 and dc == 0 else []),
                                  inc="s")
                        B_[bank].read_by("s", sv)
                    else:
                        vv = P.op("vector",
                                  lambda e, ti=ti, dc=dc, bank=bank, j=j, e_i=e_i:
                                  e.scalar_tensor_tensor(
                                      s_acc[:, ti * D + dc * 512:ti * D + dc * 512 + 512],
                                      pb[bank][:, 0:512],
                                      s_comb[:, j * 8 + (e_i - 1):j * 8 + e_i],
                                      s_acc[:, ti * D + dc * 512:ti * D + dc * 512 + 512],
                                      AluOpType.mult, AluOpType.add),
                                  waits=[("t", vo), ("s", P.counts["s"])], inc="v")
                        B_[bank].read_by("v", vv)
        for ti in range(4):
            j = ch * 4 + ti
            sg = P.op("scalar", lambda e, ti=ti: e.activation(
                s_gtile[:], s_acc[:, ti * D:(ti + 1) * D], AF.Silu),
                waits=[("v", P.counts["v"])] + Bgtile.take(), inc="s")
            Bacc.read_by("s", sg)
            vd = P.dma("sync", s_xtile[:], d_xown[j * 128:j * 128 + 128, :],
                       waits=Bxtile.take())
            vv = P.op("vector", lambda e, ti=ti: e.tensor_tensor(
                s_gtile[:], s_gtile[:], s_stile[:, ti * D:(ti + 1) * D],
                AluOpType.mult), waits=[("s", sg)], inc="v")
            vv = P.op("vector", lambda e: e.tensor_tensor(
                s_x1t[:], s_xtile[:], s_gtile[:], AluOpType.add),
                waits=[("hw", vd)] + Bx1t.take(), inc="v")
            Bxtile.read_by("v", vv)
            Bgtile.read_by("v", vv)
            Bstile.read_by("v", vv)
            sd = P.dma("sync", d_x1p[j * 128:j * 128 + 128, :], s_x1t[:],
                       waits=[("v", vv)], sem="st")
            Bx1t.read_by("st", sd)
            if scatter is not None:
                sc = scatter(j, s_x1t, [("v", vv)])
                Bx1t.read_by("sc", sc)

    P.phase_barrier()
    ff.close()
    les.close()
    return


def build_l0(upto="full", ntc=8, sub=9):
    nc = bass.Bass("TRN2", target_bir_lowering=False)
    es = ExitStack()
    P = Prog(nc, es)
    dd = declare_l0(nc)
    emit_l0(nc, es, P, dd, upto=upto, ntc=ntc, sub=sub)
    P.barrier_wait("sync", [("st", P.counts["st"])])
    P.finalize()
    es.close()
    return nc, es



def host_prep_l1(inputs, x1=None):
    I = inputs
    f32 = np.float32
    sh = {}
    for nm, key in (("wq", "L1_attn_wq"), ("wk", "L1_attn_wk"),
                    ("wv", "L1_attn_wv"), ("wout", "L1_attn_w_out")):
        sh[nm] = np.ascontiguousarray(np.asarray(I[key], f32))
    sh["wgq"] = np.ascontiguousarray(np.asarray(I["L1_attn_wgq"], f32))
    sh["wgk"] = np.ascontiguousarray(np.asarray(I["L1_attn_wgk"], f32))
    sh["wgv"] = np.ascontiguousarray(np.asarray(I["L1_attn_wgv"], f32))
    sh["rout"] = np.ascontiguousarray(np.asarray(I["L1_attn_r_out_w"], f32))
    gwbd = np.zeros((D, 32), f32)
    glw = np.asarray(I["L1_attn_gate_local_w"], f32)
    grw = np.asarray(I["L1_attn_gate_remote_w"], f32)
    for h in range(16):
        gwbd[64 * h:64 * h + 64, h] = glw[h]
        gwbd[64 * h:64 * h + 64, 16 + h] = grw[h]
    sh["gwbd"] = gwbd
    ehot = np.zeros((32, 8 * 128), f32)
    for ec in range(8):
        for m in range(128):
            ehot[16 + 2 * ec + m // 64, ec * 128 + m] = 1.0
    sh["ehot"] = ehot
    sh["mu2"] = np.ascontiguousarray(np.asarray(I["L1_ffn_mu"], f32).reshape(D, 256))
    sh["w1s"] = np.ascontiguousarray(np.asarray(I["L1_ffn_w1"], f32))
    sh["w2s"] = np.ascontiguousarray(np.asarray(I["L1_ffn_w2"], f32))
    sh["b1T"] = np.ascontiguousarray(np.asarray(I["L1_ffn_b1"], f32).T.reshape(2, 128, 16))
    sh["b2T"] = np.ascontiguousarray(np.asarray(I["L1_ffn_b2"], f32).T.reshape(8, 128, 16))
    sh["gammab"] = np.ascontiguousarray(
        np.broadcast_to(np.asarray(I["L1_ffn_ln_gamma"], f32), (128, D)))
    sh["betab"] = np.ascontiguousarray(
        np.broadcast_to(np.asarray(I["L1_ffn_ln_beta"], f32), (128, D)))
    sh["scaleb"] = np.ascontiguousarray(
        np.broadcast_to(np.asarray(I["L1_ffn_scale"], f32).reshape(1), (128, 1)))
    sh["eye"] = np.eye(128, dtype=f32)
    sh["ones1"] = np.ones((1, 512), f32)
    pmat = np.zeros((32, 128, 64), f32)
    for tt in range(32):
        for r in range(128):
            pmat[tt, r, 2 * tt + r // 64] = 1.0 / 64.0
    sh["pmat"] = pmat

    per_core = []
    for core in range(8):
        b, role = core // 2, core % 2
        snake = SNAKE_A if role == 0 else SNAKE_B
        wmask = np.zeros((16, 128, 512), f32)
        lmask = np.zeros((16, 128, 64), f32)
        for j, g in enumerate(snake):
            qg = 128 * g + np.arange(128)[:, None]
            kg = 128 * (2 * j - 2) + np.arange(512)[None, :]
            ok = (kg <= qg) & (kg > qg - 256) & (kg >= 0)
            wmask[j] = np.where(ok, 0.0, NEG)
            lat = np.arange(64)[None, :]
            lmask[j] = np.where(lat <= qg // 64, 0.0, NEG)
        m = {"wmask": wmask, "lmask": lmask, **sh}
        if x1 is not None:
            xb = np.ascontiguousarray(np.asarray(x1[b], f32))
            m["x1"] = xb
            m["x1_own"] = np.ascontiguousarray(
                np.stack([xb[128 * g:128 * g + 128] for g in snake]).reshape(TOWN, D))
        per_core.append(m)
    return per_core


def declare_l1(nc, shared=None, with_x1=True):
    dd = {}
    if with_x1:
        dd["x1"] = nc.dram_tensor("x1", [T, D], F32, kind="ExternalInput").ap()
        dd["x1_own"] = nc.dram_tensor("x1_own", [TOWN, D], F32, kind="ExternalInput").ap()
    for w in ("wq", "wk", "wv", "wout"):
        dd[w] = nc.dram_tensor(w, [D, D], F32, kind="ExternalInput").ap()
    for w in ("wgq", "wgk", "wgv"):
        dd[w] = nc.dram_tensor(w, [D, 512], F32, kind="ExternalInput").ap()
    dd["rout"] = nc.dram_tensor("rout", [512, D], F32, kind="ExternalInput").ap()
    dd["gwbd"] = nc.dram_tensor("gwbd", [D, 32], F32, kind="ExternalInput").ap()
    dd["ehot"] = nc.dram_tensor("ehot", [32, 8 * 128], F32, kind="ExternalInput").ap()
    dd["mu2"] = nc.dram_tensor("mu2", [D, 256], F32, kind="ExternalInput").ap()
    dd["w1s"] = nc.dram_tensor("w1s", [16, D, 256], F32, kind="ExternalInput").ap()
    dd["w2s"] = nc.dram_tensor("w2s", [16, 256, D], F32, kind="ExternalInput").ap()
    dd["b1T"] = nc.dram_tensor("b1T", [2, 128, 16], F32, kind="ExternalInput").ap()
    dd["b2T"] = nc.dram_tensor("b2T", [8, 128, 16], F32, kind="ExternalInput").ap()
    dd["gammab"] = nc.dram_tensor("gammab", [128, D], F32, kind="ExternalInput").ap()
    dd["betab"] = nc.dram_tensor("betab", [128, D], F32, kind="ExternalInput").ap()
    dd["scaleb"] = nc.dram_tensor("scaleb", [128, 1], F32, kind="ExternalInput").ap()
    if shared and "eye" in shared:
        dd["eye"] = shared["eye"]
        dd["ones1"] = shared["ones1"]
    else:
        dd["eye"] = nc.dram_tensor("eye", [128, 128], F32, kind="ExternalInput").ap()
        dd["ones1"] = nc.dram_tensor("ones1", [1, 512], F32, kind="ExternalInput").ap()
    dd["wmask"] = nc.dram_tensor("wmask", [16, 128, 512], F32, kind="ExternalInput").ap()
    dd["lmask"] = nc.dram_tensor("lmask", [16, 128, 64], F32, kind="ExternalInput").ap()
    dd["pmat"] = nc.dram_tensor("pmat", [32, 128, 64], F32, kind="ExternalInput").ap()
    dd["opre_sp"] = nc.dram_tensor("opre_sp", [NSLOT, 128, 16 * 128], BF16, kind="Internal").ap()
    dd["x2p"] = nc.dram_tensor("x2p", [TOWN, D], F32, kind="ExternalOutput").ap()
    return dd


def emit_l1(nc, es, P, dd, x1_row=None, x1own_row=None):
    if x1_row is None:
        x1_row = lambda tok0: dd["x1"][tok0:tok0 + 128, :]
    if x1own_row is None:
        x1own_row = lambda r0: dd["x1_own"][r0:r0 + 128, :]
    d_w = {w: dd[w] for w in ("wq", "wk", "wv", "wout")}
    d_wg = {w: dd[w] for w in ("wgq", "wgk", "wgv")}
    d_rout = dd["rout"]
    d_gwbd = dd["gwbd"]
    d_ehot = dd["ehot"]
    d_mu = dd["mu2"]
    d_w1s = dd["w1s"]
    d_w2s = dd["w2s"]
    d_b1T = dd["b1T"]
    d_b2T = dd["b2T"]
    d_gammab = dd["gammab"]
    d_betab = dd["betab"]
    d_scaleb = dd["scaleb"]
    d_eye = dd["eye"]
    d_ones = dd["ones1"]
    d_wmask = dd["wmask"]
    d_lmask = dd["lmask"]
    d_pmat = dd["pmat"]
    d_opsp = dd["opre_sp"]
    d_x2p = dd["x2p"]
    les = ExitStack()

    def sb(name, shape, dt):
        return les.enter_context(nc.sbuf_tensor(name, shape, dt))

    def ps(name, shape, dt):
        return les.enter_context(nc.psum_tensor(name, shape, dt))

    s_eye32 = sb("eye32L1", [128, 128], F32)
    s_ones = sb("ones_bL1", [1, 512], BF16)
    s_ones32 = sb("ones32rL1", [1, 128], F32)
    s_gammab = sb("gammab_s", [128, D], F32)
    s_betab = sb("betab_s", [128, D], F32)
    s_scaleb = sb("scaleb_s", [128, 1], F32)
    s_stat = sb("statL1", [128, 12], F32)
    s_xtile = sb("xtileL1", [128, D], F32)
    s_xn_b = sb("xn_bL1", [128, D], F32)
    s_xi = sb("xi", [128, D], F32)
    s_xu = sb("xu", [128, D], F32)
    s_js = sb("junk_sL1", [128, D], BF16)
    s_jv = sb("junk_vL1", [128, D], BF16)
    s_muu = sb("muu", [128, 8 * 256], BF16)
    s_pooled = sb("pooled", [64, D], F32)
    s_pooledT = sb("pooledT", [128, 8 * 64], BF16)
    s_GK = sb("GK", [64, 512], F32)
    s_GV = sb("GV", [64, 512], BF16)
    s_GKT = sb("GKT", [128, 4 * 64], BF16)
    s_gates = sb("gates", [128, NSLOT * 32], F32)
    s_so = sb("so", [128, 2 * D], BF16)
    s_eps = sb("epscolL1", [128, 1], F32)
    s_e12 = sb("e12col", [128, 1], F32)
    s_onescol = sb("onescolL1", [128, 1], F32)
    s_x2t = sb("x2t", [128, D], F32)

    pb = [ps(f"pbL1{i}", [128, 512], F32) for i in range(8)]
    B_ = [Buf() for _ in range(8)]
    Bxtile, Bxn, Bxi, Bxu, Bstat, Bjs, Bjv = (Buf() for _ in range(7))
    Bpooled, BpooledT, Bgates, Bx2t, Bmuu = (Buf() for _ in range(5))

    P.dma("sync", s_eye32[:], d_eye, sem="c0")
    P.dma("sync", s_gammab[:], d_gammab, sem="c0")
    P.dma("sync", s_betab[:], d_betab, sem="c0")
    P.dma("sync", s_scaleb[:], d_scaleb, sem="c0")
    v_hw0 = P.counts["c0"]
    P.dma("gpsimd", s_ones[:], d_ones)
    v_sw0 = P.counts["sw"]
    v_ones = P.op("vector", lambda e: e.memset(s_ones32[:], 1.0), inc="v")
    v_ones = P.op("vector", lambda e: e.memset(s_eps[:], 1e-5), inc="v")
    v_ones = P.op("vector", lambda e: e.memset(s_e12[:], 1e-12), inc="v")
    v_ones = P.op("vector", lambda e: e.memset(s_onescol[:], 1.0), inc="v")

    def ln_into(inbuf, outbuf, Bin, Bout, eps_ap, src_row=None, affine=False):
        wx = []
        if src_row is not None:
            vd = P.dma("sync", inbuf[:], src_row, waits=Bin.take())
            wx = [("hw", vd)]
        vm = P.op("vector", lambda e: e.tensor_scalar(
            s_jv[:, 0:D], inbuf[:], 1.0 / D, None, AluOpType.mult,
            AluOpType.add, accum_out=s_stat[:, 0:1]),
            waits=wx + Bjv.take() + Bstat.take(), inc="v")
        vs = P.op("scalar", lambda e: e.activation(
            s_js[:, 0:D], inbuf[:], AF.Square, accum_out=s_stat[:, 1:2]),
            waits=wx + Bjs.take(), inc="s")
        Bin.read_by("v", vm)
        Bin.read_by("s", vs)
        Bjv.read_by("v", vm)
        Bjs.read_by("s", vs)
        v1 = P.op("vector", lambda e: e.scalar_tensor_tensor(
            s_stat[:, 2:3], s_stat[:, 0:1], -1.0, s_stat[:, 0:1],
            AluOpType.mult, AluOpType.mult), waits=[("s", vs)], inc="v")
        v2 = P.op("vector", lambda e: e.scalar_tensor_tensor(
            s_stat[:, 3:4], s_stat[:, 1:2], 1.0 / D, s_stat[:, 2:3],
            AluOpType.mult, AluOpType.add), inc="v")
        s1 = P.op("scalar", lambda e: e.activation(
            s_stat[:, 4:5], s_stat[:, 3:4], AF.Sqrt, bias=eps_ap),
            waits=[("v", v2), ("v", v_ones)], inc="s")
        v3 = P.op("vector", lambda e: e.reciprocal(s_stat[:, 5:6], s_stat[:, 4:5]),
                  waits=[("s", s1)], inc="v")
        v4 = P.op("vector", lambda e: e.scalar_tensor_tensor(
            s_stat[:, 6:7], s_stat[:, 0:1], -1.0, s_stat[:, 5:6],
            AluOpType.mult, AluOpType.mult), inc="v")
        sx = P.op("scalar", lambda e: e.activation(
            outbuf[:], inbuf[:], AF.Identity, bias=s_stat[:, 6:7],
            scale=s_stat[:, 5:6]), waits=[("v", v4)] + Bout.take(), inc="s")
        Bin.read_by("s", sx)
        Bstat.read_by("s", sx)
        last = ("s", sx)
        if affine:
            vv = P.op("vector", lambda e: e.tensor_tensor(
                outbuf[:], outbuf[:], s_gammab[:], AluOpType.mult),
                waits=[("s", sx), ("c0", v_hw0)], inc="v")
            vv = P.op("vector", lambda e: e.tensor_tensor(
                outbuf[:], outbuf[:], s_betab[:], AluOpType.add), inc="v")
            last = ("v", vv)
        return last

    def l2unit(inbuf, outbuf, Bin, Bout, wait):
        ss = P.op("scalar", lambda e: e.activation(
            s_js[:, 0:D], inbuf[:], AF.Square, accum_out=s_stat[:, 7:8]),
            waits=[wait] + Bjs.take(), inc="s")
        Bjs.read_by("s", ss)
        Bin.read_by("s", ss)
        s2 = P.op("scalar", lambda e: e.activation(
            s_stat[:, 8:9], s_stat[:, 7:8], AF.Sqrt), waits=[("s", ss)], inc="s")
        vv = P.op("vector", lambda e: e.tensor_tensor(
            s_stat[:, 9:10], s_stat[:, 8:9], s_e12[:], AluOpType.max),
            waits=[("s", s2), ("v", v_ones)], inc="v")
        vv = P.op("vector", lambda e: e.reciprocal(s_stat[:, 10:11], s_stat[:, 9:10]),
                  inc="v")
        vv = P.op("vector", lambda e: e.tensor_tensor(
            s_stat[:, 10:11], s_stat[:, 10:11], s_scaleb[:], AluOpType.mult),
            waits=[("c0", v_hw0)], inc="v")
        sx = P.op("scalar", lambda e: e.activation(
            outbuf[:], inbuf[:], AF.Identity, scale=s_stat[:, 10:11]),
            waits=[("v", vv)] + Bout.take(), inc="s")
        Bin.read_by("s", sx)
        Bstat.read_by("s", sx)
        return sx

    def xpose(srcbuf, scol, destbuf, dstride, dcol, wait, nd=NDC, Bsrc=None):
        last = None
        for g4 in range((nd + 3) // 4):
            bank = 4 + g4 % 2
            tvs = None
            n_in_g = min(4, nd - g4 * 4)
            for slot in range(n_in_g):
                dt = g4 * 4 + slot
                tvs = P.op("tensor", lambda e, dt=dt, bank=bank, slot=slot: e.transpose(
                    pb[bank][:, slot * 128:slot * 128 + 128],
                    srcbuf[:, scol + dt * 128:scol + (dt + 1) * 128], s_eye32[:]),
                    waits=[wait, ("c0", v_hw0)] + (B_[bank].take() if slot == 0 else []),
                    inc="t")
            for slot in range(n_in_g):
                dt = g4 * 4 + slot
                sv = P.op("scalar", lambda e, dt=dt, bank=bank, slot=slot: e.copy(
                    destbuf[:, dstride * dt + dcol:dstride * dt + dcol + 128],
                    pb[bank][:, slot * 128:slot * 128 + 128]),
                    waits=[("t", tvs)], inc="s")
                B_[bank].read_by("s", sv)
                last = sv
        if Bsrc is not None:
            Bsrc.read_by("t", P.counts["t"])
        return last

    def mm_group(bank, cols, mms, waits=(), extra_first=(), rows=(0, 128)):
        n = len(mms)
        w0 = list(waits) + B_[bank].take() + list(extra_first)
        last = None
        for i, (fl, fr) in enumerate(mms):
            last = P.op("tensor",
                        lambda e, fl=fl, fr=fr, i=i: e.matmul(
                            pb[bank][rows[0]:rows[1], cols[0]:cols[1]], fl(), fr(),
                            start=(i == 0), stop=(i == n - 1)),
                        waits=(w0 if i == 0 else ()),
                        inc=("t" if i == n - 1 else None))
        return last

    pa = ExitStack()
    s_xni = pa.enter_context(nc.sbuf_tensor("xni", [128, NT * D], BF16))
    s_expT = pa.enter_context(nc.sbuf_tensor("expT", [128, 2 * T], F32))
    s_dsum = pa.enter_context(nc.sbuf_tensor("dsum", [128, 2 * 9], F32))
    pa2 = ExitStack()
    s_xuT = pa2.enter_context(nc.sbuf_tensor("xuT", [128, NDC * 512], BF16))
    s_mu = pa2.enter_context(nc.sbuf_tensor("mu_s", [128, 8 * 256], F32))
    s_msq = pa2.enter_context(nc.sbuf_tensor("msq", [128, 256], F32))
    s_pm = pa2.enter_context(nc.sbuf_tensor("pm", [128, 64], F32))
    s_nrow = pa2.enter_context(nc.sbuf_tensor("nrow", [1, 256], F32))
    s_recb = pa2.enter_context(nc.sbuf_tensor("recb", [128, 256], F32))
    Bxni, BexpT, BxuT, Bpm, Bmsq = (Buf() for _ in range(5))

    for dt in range(NDC):
        P.dma("sync", s_mu[:, dt * 256:(dt + 1) * 256], d_mu[dt * 128:(dt + 1) * 128, :],
              sem="c1")
    v_mu = P.counts["c1"]
    for dt in range(NDC):
        sq = P.op("scalar", lambda e, dt=dt: e.activation(
            s_msq[:], s_mu[:, dt * 256:(dt + 1) * 256], AF.Square),
            waits=[("c1", v_mu)] + Bmsq.take(), inc="s")
        tv = P.op("tensor", lambda e, dt=dt: e.matmul(
            pb[6][0:1, 0:256], s_onescol[:], s_msq[:],
            start=(dt == 0), stop=(dt == NDC - 1)),
            waits=[("s", sq), ("v", v_ones)] + (B_[6].take() if dt == 0 else []),
            inc="t")
        Bmsq.read_by("t", tv)
    sv = P.op("scalar", lambda e: e.activation(s_nrow[0:1, :], pb[6][0:1, 0:256],
                                               AF.Sqrt), waits=[("t", tv)], inc="s")
    B_[6].read_by("s", sv)
    vv = P.op("vector", lambda e: e.tensor_scalar(
        s_nrow[0:1, :], s_nrow[0:1, :], 1e-12, None, AluOpType.max),
        waits=[("s", sv)], inc="v")
    vv = P.op("vector", lambda e: e.reciprocal(s_nrow[0:1, :], s_nrow[0:1, :]), inc="v")
    tv = P.op("tensor", lambda e: e.matmul(
        pb[6][:, 256:512], s_ones32[0:1, 0:128], s_nrow[0:1, 0:256],
        start=True, stop=True), waits=[("v", vv)] + B_[6].take(), inc="t")
    sv = P.op("scalar", lambda e: e.copy(s_recb[:], pb[6][:, 256:512]),
              waits=[("t", tv)], inc="s")
    B_[6].read_by("s", sv)
    for dt in range(NDC):
        vv = P.op("vector", lambda e, dt=dt: e.tensor_tensor(
            s_muu[:, dt * 256:(dt + 1) * 256], s_mu[:, dt * 256:(dt + 1) * 256],
            s_recb[:], AluOpType.mult), waits=[("s", sv)] + (Bmuu.take() if dt == 0 else []),
            inc="v")
    v_muu = P.counts["v"]

    for tt in range(NT):
        vp = P.dma("sync", s_pm[:], d_pmat[tt], waits=Bpm.take(), sem="pm")
        wl = ln_into(s_xtile, s_xn_b, Bxtile, Bxn, s_eps[:],
                     src_row=x1_row(tt * 128))
        for dc in range(2):
            tvp = P.op("tensor", lambda e, tt=tt, dc=dc: e.matmul(
                pb[2 + dc][0:64, 0:512], s_pm[:],
                s_xn_b[:, dc * 512:(dc + 1) * 512],
                start=(tt == 0), stop=(tt == NT - 1)),
                waits=[("pm", vp), wl] + (B_[2 + dc].take() if tt == 0 else []),
                inc=("t" if tt == NT - 1 else None))
        Bpm.read_by("t", P.counts["t"] if tt == NT - 1 else P.counts["t"] + 1)
        Bxn.read_by("t", P.counts["t"])
        wi = ln_into(s_xn_b, s_xi, Bxn, Bxi, s_eps[:], affine=True)
        sc = P.op("scalar", lambda e, tt=tt: e.copy(
            s_xni[:, tt * D:(tt + 1) * D], s_xi[:]),
            waits=[wi] + (Bxni.take() if tt == 0 else []), inc="s")
        Bxi.read_by("s", sc)
        su = l2unit(s_xi, s_xu, Bxi, Bxu, ("s", sc))
        ti = tt % 4
        xpose(s_xu, 0, s_xuT, 512, ti * 128, ("s", su), Bsrc=Bxu)
        if ti == 3:
            nchunk = tt // 4
            vxt = P.counts["s"]
            for esc in range(2):
                bank = esc
                mms = [(lambda dt=dt, esc=esc: s_muu[:, dt * 256 + esc * 128:dt * 256 + esc * 128 + 128],
                        lambda dt=dt: s_xuT[:, dt * 512:(dt + 1) * 512])
                       for dt in range(NDC)]
                tvl = mm_group(bank, (0, 512), mms, waits=[("s", vxt), ("v", v_muu)])
                se = P.op("scalar", lambda e, esc=esc, nchunk=nchunk, bank=bank: e.activation(
                    s_expT[:, esc * T + nchunk * 512:esc * T + nchunk * 512 + 512],
                    pb[bank][:, 0:512], AF.Exp,
                    accum_out=s_dsum[:, esc * 9 + nchunk:esc * 9 + nchunk + 1]),
                    waits=[("t", tvl)] + (BexpT.take() if nchunk == 0 and esc == 0 else []),
                    inc="s")
                B_[bank].read_by("s", se)
            BxuT.read_by("t", P.counts["t"])
    v_exp = P.counts["s"]
    for esc in range(2):
        vv = P.op("vector", lambda e, esc=esc: e.tensor_reduce(
            s_dsum[:, esc * 9 + 8:esc * 9 + 9], s_dsum[:, esc * 9:esc * 9 + 8],
            AX.X, AluOpType.add), waits=[("s", v_exp)], inc="v")
        vv = P.op("vector", lambda e, esc=esc: e.reciprocal(
            s_dsum[:, esc * 9 + 8:esc * 9 + 9], s_dsum[:, esc * 9 + 8:esc * 9 + 9]),
            inc="v")
        vv = P.op("vector", lambda e, esc=esc: e.tensor_scalar(
            s_expT[:, esc * T:esc * T + T], s_expT[:, esc * T:esc * T + T],
            s_dsum[:, esc * 9 + 8:esc * 9 + 9], None, AluOpType.mult), inc="v")
    v_dispT = P.counts["v"]
    for dc in range(2):
        sv = P.op("scalar", lambda e, dc=dc: e.copy(
            s_pooled[0:64, dc * 512:(dc + 1) * 512], pb[2 + dc][0:64, 0:512]),
            waits=[("t", P.counts["t"])] + (Bpooled.take() if dc == 0 else []), inc="s")
        B_[2 + dc].read_by("s", sv)
    v_pool = P.counts["s"]

    P.phase_barrier()
    pa2.close()
    pbx = ExitStack()
    s_disp = pbx.enter_context(nc.sbuf_tensor("disp", [128, NT * 256], BF16))
    s_sin = pbx.enter_context(nc.sbuf_tensor("sin", [128, 2 * D], F32))
    s_sinT = pbx.enter_context(nc.sbuf_tensor("sinT", [128, NDC * 256], BF16))
    s_hT = pbx.enter_context(nc.sbuf_tensor("hT", [128, 2 * 16], BF16))
    s_soT = pbx.enter_context(nc.sbuf_tensor("soT", [128, NDC * 256], F32))
    s_w1l = pbx.enter_context(nc.sbuf_tensor("w1l", [128, NDC * 256], BF16))
    s_w2l = pbx.enter_context(nc.sbuf_tensor("w2l", [128, 2 * D], BF16))
    s_b1T = pbx.enter_context(nc.sbuf_tensor("b1T_s", [128, 2 * 16], F32))
    s_b2T = pbx.enter_context(nc.sbuf_tensor("b2T_s", [128, 8 * 16], F32))
    s_wgl = pbx.enter_context(nc.sbuf_tensor("wgl", [128, NDC * 512], BF16))
    Bdisp, Bsin, BsinT, BhT, BsoT, Bw1l, Bw2l = (Buf() for _ in range(7))
    P.phase_barrier()
    for g4 in range(2):
        bank = 4 + g4
        tvs = None
        for slot in range(4):
            dt = g4 * 4 + slot
            tvs = P.op("tensor", lambda e, dt=dt, bank=bank, slot=slot: e.transpose(
                pb[bank][:, slot * 128:slot * 128 + 64],
                s_pooled[0:64, dt * 128:(dt + 1) * 128], s_eye32[0:64, 0:64]),
                waits=[("s", v_pool), ("c0", v_hw0)] +
                      (B_[bank].take() if slot == 0 else []), inc="t")
        for slot in range(4):
            dt = g4 * 4 + slot
            sv = P.op("scalar", lambda e, dt=dt, bank=bank, slot=slot: e.copy(
                s_pooledT[:, dt * 64:(dt + 1) * 64],
                pb[bank][:, slot * 128:slot * 128 + 64]),
                waits=[("t", tvs)], inc="s")
            B_[bank].read_by("s", sv)
    Bpooled.read_by("t", P.counts["t"])
    v_pt = P.counts["s"]
    Bwgl = Buf()
    for wname, dest in (("wgk", s_GK), ("wgv", s_GV)):
        wgd = Bwgl.take()
        for dt in range(NDC):
            P.dma("gpsimd", s_wgl[:, dt * 512:(dt + 1) * 512],
                  d_wg[wname][dt * 128:(dt + 1) * 128, :],
                  waits=(wgd if dt == 0 else []))
        v_w = P.counts["sw"]
        mms = [(lambda dt=dt: s_pooledT[:, dt * 64:(dt + 1) * 64],
                lambda dt=dt: s_wgl[:, dt * 512:(dt + 1) * 512])
               for dt in range(NDC)]
        tv = mm_group(1, (0, 512), mms, waits=[("sw", v_w), ("s", v_pt)], rows=(0, 64))
        Bwgl.read_by("t", tv)
        sv = P.op("scalar", lambda e, dest=dest: e.copy(dest[:], pb[1][0:64, 0:512]),
                  waits=[("t", tv)], inc="s")
        B_[1].read_by("s", sv)
    v_gkv = P.counts["s"]
    tvs = None
    for lc in range(4):
        tvs = P.op("tensor", lambda e, lc=lc: e.transpose(
            pb[4][:, lc * 128:lc * 128 + 64],
            s_GK[0:64, lc * 128:lc * 128 + 128], s_eye32[0:64, 0:64]),
            waits=[("s", v_gkv)] + (B_[4].take() if lc == 0 else []), inc="t")
    for lc in range(4):
        sv = P.op("scalar", lambda e, lc=lc: e.copy(
            s_GKT[:, lc * 64:lc * 64 + 64], pb[4][:, lc * 128:lc * 128 + 64]),
            waits=[("t", tvs)], inc="s")
        B_[4].read_by("s", sv)
    v_gkt = P.counts["s"]

    for tt in range(NT):
        for esc in range(2):
            bank = 4 + esc
            tv = P.op("tensor", lambda e, tt=tt, esc=esc, bank=bank: e.transpose(
                pb[bank][:, 0:128], s_expT[:, esc * T + tt * 128:esc * T + tt * 128 + 128],
                s_eye32[:]), waits=[("v", v_dispT)] + B_[bank].take(), inc="t")
            sv = P.op("scalar", lambda e, tt=tt, esc=esc, bank=bank: e.copy(
                s_disp[:, tt * 256 + esc * 128:tt * 256 + esc * 128 + 128],
                pb[bank][:, 0:128]),
                waits=[("t", tv)] + (Bdisp.take() if tt == 0 and esc == 0 else []),
                inc="s")
            B_[bank].read_by("s", sv)
    BexpT.read_by("t", P.counts["t"])
    v_dispn = P.counts["s"]
    for esc in range(2):
        for dc in range(2):
            bank = esc * 2 + dc
            mms = [(lambda tt=tt, esc=esc: s_disp[:, tt * 256 + esc * 128:tt * 256 + esc * 128 + 128],
                    lambda tt=tt, dc=dc: s_xni[:, tt * D + dc * 512:tt * D + dc * 512 + 512])
                   for tt in range(NT)]
            tv = mm_group(bank, (0, 512), mms, waits=[("s", v_dispn)])
            sv = P.op("scalar", lambda e, esc=esc, dc=dc, bank=bank: e.copy(
                s_sin[:, esc * D + dc * 512:esc * D + dc * 512 + 512],
                pb[bank][:, 0:512]),
                waits=[("t", tv)] + (Bsin.take() if esc == 0 and dc == 0 else []), inc="s")
            B_[bank].read_by("s", sv)
    v_sin = P.counts["s"]
    for esc in range(2):
        xpose(s_sin, esc * D, s_sinT, 256, esc * 128, ("s", v_sin), Bsrc=Bsin)
    v_sinT = P.counts["s"]
    for hc in range(2):
        P.dma("sync", s_b1T[:, hc * 16:(hc + 1) * 16], d_b1T[hc], sem="c2")
    for dt in range(8):
        P.dma("sync", s_b2T[:, dt * 16:(dt + 1) * 16], d_b2T[dt], sem="c2")
    v_bT = P.counts["c2"]
    Bso = Buf()
    for ei in range(16):
        w1d = Bw1l.take()
        lv1 = None
        for dt in range(NDC):
            lv1 = P.dma("gpsimd", s_w1l[:, dt * 256:(dt + 1) * 256],
                        d_w1s[ei][dt * 128:(dt + 1) * 128, :],
                        waits=(w1d if dt == 0 else []), sem="w1")
        w2d = Bw2l.take()
        lv2 = None
        for hc in range(2):
            lv2 = P.dma("gpsimd", s_w2l[:, hc * D:(hc + 1) * D],
                        d_w2s[ei][hc * 128:(hc + 1) * 128, :],
                        waits=(w2d if hc == 0 else []), sem="w2a")
        hd = BhT.take()
        for hc in range(2):
            bank = hc
            mms = [(lambda dt=dt, hc=hc: s_w1l[:, dt * 256 + hc * 128:dt * 256 + hc * 128 + 128],
                    lambda dt=dt, ei=ei: s_sinT[:, dt * 256 + ei * 16:dt * 256 + ei * 16 + 16])
                   for dt in range(NDC)]
            tv = mm_group(bank, (0, 16), mms,
                          waits=[("w1", lv1), ("s", v_sinT), ("c2", v_bT)],
                          extra_first=(hd if hc == 0 else ()))
            Bw1l.read_by("t", tv)
            BsinT.read_by("t", tv)
            sv = P.op("scalar", lambda e, hc=hc, ei=ei, bank=bank: e.activation(
                s_hT[:, hc * 16:(hc + 1) * 16], pb[bank][:, 0:16], AF.Gelu,
                bias=s_b1T[:, hc * 16 + ei:hc * 16 + ei + 1]),
                waits=[("t", tv)], inc="s")
            B_[bank].read_by("s", sv)
        v_h = P.counts["s"]
        for dt in range(NDC):
            bank = 2 + dt % 2
            mms = [(lambda hc=hc, dt=dt: s_w2l[:, hc * D + dt * 128:hc * D + dt * 128 + 128],
                    lambda hc=hc: s_hT[:, hc * 16:(hc + 1) * 16])
                   for hc in range(2)]
            tv = mm_group(bank, (0, 16), mms, waits=[("w2a", lv2), ("s", v_h)])
            Bw2l.read_by("t", tv)
            BhT.read_by("t", tv)
            sv = P.op("scalar", lambda e, dt=dt, ei=ei, bank=bank: e.activation(
                s_soT[:, dt * 256 + ei * 16:dt * 256 + ei * 16 + 16],
                pb[bank][:, 0:16], AF.Identity,
                bias=s_b2T[:, dt * 16 + ei:dt * 16 + ei + 1]),
                waits=[("t", tv)] + (BsoT.take() if ei == 0 and dt == 0 else []),
                inc="s")
            B_[bank].read_by("s", sv)
    v_soT = P.counts["s"]
    for dt in range(NDC):
        bank = 4 + (dt % 2)
        tvs = None
        for esc in range(2):
            tvs = P.op("tensor", lambda e, dt=dt, esc=esc, bank=bank: e.transpose(
                pb[bank][:, esc * 128:esc * 128 + 128],
                s_soT[:, dt * 256 + esc * 128:dt * 256 + esc * 128 + 128],
                s_eye32[:]),
                waits=[("s", v_soT)] + (B_[bank].take() if esc == 0 else []), inc="t")
        for esc in range(2):
            sv = P.op("scalar", lambda e, dt=dt, esc=esc, bank=bank: e.copy(
                s_so[:, esc * D + dt * 128:esc * D + dt * 128 + 128],
                pb[bank][:, esc * 128:esc * 128 + 128]),
                waits=[("t", tvs)] + (Bso.take() if dt == 0 and esc == 0 else []),
                inc="s")
            B_[bank].read_by("s", sv)
    BsoT.read_by("t", P.counts["t"])
    P.phase_barrier()
    pbx.close()
    pa.close()

    PADC = 256
    KSTR = T + PADC
    cd = ExitStack()
    s_xnT_own = cd.enter_context(nc.sbuf_tensor("xnT_ownL1", [128, NDC * TOWN], BF16))
    s_K1 = cd.enter_context(nc.sbuf_tensor("K1", [128, 4 * KSTR], BF16))
    s_V1 = cd.enter_context(nc.sbuf_tensor("VL1", [128, 34 * 512], BF16))
    s_xnTc = cd.enter_context(nc.sbuf_tensor("xnTcL1", [128, NDC * 512], BF16))
    s_wkh = cd.enter_context(nc.sbuf_tensor("wkh", [128, NDC * 512], BF16))
    s_wvh = cd.enter_context(nc.sbuf_tensor("wvh", [128, NDC * 512], BF16))
    s_rout = cd.enter_context(nc.sbuf_tensor("rout_s", [128, 4 * D], BF16))
    s_wgq = cd.enter_context(nc.sbuf_tensor("wgq_s", [128, NDC * 512], BF16))
    s_gwb = cd.enter_context(nc.sbuf_tensor("gwb", [128, 4 * 32], BF16))
    s_QT = cd.enter_context(nc.sbuf_tensor("QTL1", [128, 4 * 128], BF16))
    s_GQT = cd.enter_context(nc.sbuf_tensor("GQT", [128, 4 * 128], BF16))
    s_sc = cd.enter_context(nc.sbuf_tensor("scL1", [128, 512], F32))
    s_pw = cd.enter_context(nc.sbuf_tensor("pw", [128, 512], F32))
    s_pT = cd.enter_context(nc.sbuf_tensor("pT", [128, 4 * 128], BF16))
    s_ls = cd.enter_context(nc.sbuf_tensor("ls", [128, 64], F32))
    s_lse = cd.enter_context(nc.sbuf_tensor("lse", [128, 64], F32))
    s_lpT = cd.enter_context(nc.sbuf_tensor("lpT", [64, 128], BF16))
    s_remT = cd.enter_context(nc.sbuf_tensor("remT", [128, 4 * 128], BF16))
    s_opst = cd.enter_context(nc.sbuf_tensor("opst", [128, 4 * 128], BF16))
    s_oprem = cd.enter_context(nc.sbuf_tensor("oprem", [128, 8 * 128], BF16))
    s_wmk = cd.enter_context(nc.sbuf_tensor("wmk", [128, 512], F32))
    s_lmk = cd.enter_context(nc.sbuf_tensor("lmk", [128, 64], F32))
    BK1, BV1, BxnTc1, BQT1, BGQT, Bsc, Bpw, BpT, Bwk = (Buf() for _ in range(9))
    Bls, Blse, BlpT, BremT, Bopst, Boprem, Bwmk, Blmk = (Buf() for _ in range(8))

    P.phase_barrier()
    for j in range(NSLOT):
        wl = ln_into(s_xtile, s_xn_b, Bxtile, Bxn, s_eps[:],
                     src_row=x1own_row(j * 128))
        xpose(s_xn_b, 0, s_xnT_own, TOWN, j * 128, wl, Bsrc=Bxn)
    v_own = P.counts["s"]

    for half in (0, 1):
        P.phase_barrier()
        wkd = Bwk.take()
        for dt in range(NDC):
            P.dma("gpsimd", s_wkh[:, dt * 512:(dt + 1) * 512],
                  d_w["wk"][dt * 128:(dt + 1) * 128, half * 512:half * 512 + 512],
                  waits=(wkd if dt == 0 else []))
            P.dma("gpsimd", s_wvh[:, dt * 512:(dt + 1) * 512],
                  d_w["wv"][dt * 128:(dt + 1) * 128, half * 512:half * 512 + 512])
        if half == 0:
            for dt in range(NDC):
                P.dma("gpsimd", s_wgq[:, dt * 512:(dt + 1) * 512],
                      d_wg["wgq"][dt * 128:(dt + 1) * 128, :])
            for lc in range(4):
                P.dma("gpsimd", s_rout[:, lc * D:(lc + 1) * D],
                      d_rout[lc * 128:(lc + 1) * 128, :])
        for ec in range(4):
            P.dma("gpsimd", s_gwb[:, ec * 32:(ec + 1) * 32],
                  d_gwbd[half * 512 + ec * 128:half * 512 + ec * 128 + 128, :])
        v_wh = P.counts["sw"]
        vg8 = P.op("vector", lambda e: e.tensor_scalar(
            s_gwb[:], s_gwb[:], 8.0, None, AluOpType.mult),
            waits=[("sw", v_wh)], inc="v")
        vz = P.op("vector", lambda e: e.memset(s_V1[:, 0:2 * 512], 0.0),
                  waits=BV1.take(), inc="v")
        for ec in range(4):
            vz = P.op("vector", lambda e, ec=ec: e.memset(
                s_K1[:, ec * KSTR:ec * KSTR + PADC], 0.0),
                waits=(BK1.take() if ec == 0 else []), inc="v")
        for tc in range(8):
            for ti in range(4):
                tok0 = tc * 512 + ti * 128
                wl = ln_into(s_xtile, s_xn_b, Bxtile, Bxn, s_eps[:],
                             src_row=x1_row(tok0))
                xpose(s_xn_b, 0, s_xnTc, 512, ti * 128, wl, Bsrc=Bxn)
            vxn = P.counts["s"]
            wxn = [("s", vxn), ("sw", v_wh), ("v", vz)]
            for ec in range(4):
                bank = ec % 2
                mms = [(lambda dt=dt, ec=ec: s_wkh[:, dt * 512 + ec * 128:dt * 512 + ec * 128 + 128],
                        lambda dt=dt: s_xnTc[:, dt * 512:(dt + 1) * 512])
                       for dt in range(NDC)]
                tv = mm_group(bank, (0, 512), mms, waits=wxn)
                Bwk.read_by("t", tv)
                sv = P.op("scalar", lambda e, ec=ec, tc=tc, bank=bank: e.copy(
                    s_K1[:, ec * KSTR + PADC + tc * 512:ec * KSTR + PADC + tc * 512 + 512],
                    pb[bank][:, 0:512]), waits=[("t", tv)], inc="s")
                B_[bank].read_by("s", sv)
            for ti in range(4):
                bank = 2 + ti % 2
                tt = tc * 4 + ti
                mms = [(lambda dt=dt, ti=ti: s_xnTc[:, dt * 512 + ti * 128:dt * 512 + ti * 128 + 128],
                        lambda dt=dt: s_wvh[:, dt * 512:(dt + 1) * 512])
                       for dt in range(NDC)]
                tv = mm_group(bank, (0, 512), mms, waits=wxn)
                sv = P.op("scalar", lambda e, tt=tt, bank=bank: e.copy(
                    s_V1[:, (2 + tt) * 512:(2 + tt) * 512 + 512], pb[bank][:, 0:512]),
                    waits=[("t", tv)], inc="s")
                B_[bank].read_by("s", sv)
            BxnTc1.read_by("t", P.counts["t"])
        v_kv = P.counts["s"]
        wkd = Bwk.take()
        for dt in range(NDC):
            P.dma("gpsimd", s_wkh[:, dt * 512:(dt + 1) * 512],
                  d_w["wq"][dt * 128:(dt + 1) * 128, half * 512:half * 512 + 512],
                  waits=(wkd if dt == 0 else []))
        v_wq = P.counts["sw"]

        for j in range(NSLOT):
            vmk = P.dma("sync", s_wmk[:], d_wmask[j], waits=Bwmk.take(), sem="wm")
            if half == 0:
                vlk = P.dma("sync", s_lmk[:], d_lmask[j], waits=Blmk.take(), sem="lm")
            qd = BQT1.take()
            tvq = None
            for ec in range(4):
                mms = [(lambda dt=dt, ec=ec: s_wkh[:, dt * 512 + ec * 128:dt * 512 + ec * 128 + 128],
                        lambda dt=dt, j=j: s_xnT_own[:, dt * TOWN + j * 128:dt * TOWN + j * 128 + 128])
                       for dt in range(NDC)]
                tvq = mm_group(6, (ec * 128, ec * 128 + 128), mms,
                               waits=[("sw", v_wq), ("s", v_own)],
                               extra_first=(qd if ec == 0 else ()))
                Bwk.read_by("t", tvq)
            for ec in range(4):
                sv = P.op("scalar", lambda e, ec=ec: e.activation(
                    s_QT[:, ec * 128:ec * 128 + 128],
                    pb[6][:, ec * 128:ec * 128 + 128], AF.Copy, scale=0.125),
                    waits=[("t", tvq)], inc="s")
                B_[6].read_by("s", sv)
            v_qt = P.counts["s"]
            mms = [(lambda ec=ec: s_QT[:, ec * 128:ec * 128 + 128],
                    lambda ec=ec: s_gwb[:, ec * 32:(ec + 1) * 32])
                   for ec in range(4)]
            tv = mm_group(6, (256, 288), mms, waits=[("s", v_qt), ("v", vg8)])
            BQT1.read_by("t", tv)
            for part in (0, 16):
                sv = P.op("scalar", lambda e, part=part, j=j, half=half: e.activation(
                    s_gates[:, j * 32 + part + half * 8:j * 32 + part + half * 8 + 8],
                    pb[6][:, 256 + part + half * 8:256 + part + half * 8 + 8],
                    AF.Sigmoid),
                    waits=[("t", tv)] + (Bgates.take() if part == 0 else []), inc="s")
                B_[6].read_by("s", sv)
            v_gate = P.counts["s"]
            od = Bopst.take()
            for hl in range(8):
                ec, po = hl // 2, 64 * (hl % 2)
                kcol0 = ec * KSTR + PADC + 128 * (2 * j - 2)
                bank = hl % 2
                tv = mm_group(bank, (0, 512),
                              [(lambda ec=ec, po=po: s_QT[po:po + 64, ec * 128:ec * 128 + 128],
                                lambda po=po, kcol0=kcol0: s_K1[po:po + 64, kcol0:kcol0 + 512])],
                              waits=[("s", v_kv), ("s", v_qt)])
                ev = P.op("vector", lambda e, bank=bank: e.tensor_tensor(
                    s_sc[:], pb[bank][:, 0:512], s_wmk[:], AluOpType.add),
                    waits=[("t", tv), ("wm", vmk)] + Bsc.take(), inc="v")
                B_[bank].read_by("v", ev)
                Bwmk.read_by("v", ev)
                vm = P.op("vector", lambda e: e.tensor_reduce(
                    s_stat[:, 11:12], s_sc[:], AX.X, AluOpType.max, negate=True),
                    waits=Bstat.take(), inc="v")
                se = P.op("scalar", lambda e: e.activation(
                    s_pw[:], s_sc[:], AF.Exp, bias=s_stat[:, 11:12],
                    accum_out=s_stat[:, 10:11]),
                    waits=[("v", vm)] + Bpw.take(), inc="s")
                Bsc.read_by("s", se)
                Bstat.read_by("s", se)
                vr = P.op("vector", lambda e: e.reciprocal(
                    s_stat[:, 9:10], s_stat[:, 10:11]), waits=[("s", se)], inc="v")
                vr = P.op("vector", lambda e, j=j, hl=hl, half=half: e.tensor_tensor(
                    s_stat[:, 9:10], s_stat[:, 9:10],
                    s_gates[:, j * 32 + half * 8 + hl:j * 32 + half * 8 + hl + 1],
                    AluOpType.mult), waits=[("s", v_gate)], inc="v")
                va = P.op("vector", lambda e: e.tensor_scalar(
                    s_pw[:], s_pw[:], s_stat[:, 9:10], None, AluOpType.mult),
                    inc="v")
                Bpw.read_by("v", va)
                Bstat.read_by("v", va)
                Bgates.read_by("v", va)
                vpt = xpose(s_pw, 0, s_pT, 128, 0, ("v", va), nd=4, Bsrc=Bpw)
                mms = []
                for kt in range(4):
                    vtile = 2 * j + kt
                    mms.append((lambda vtile=vtile, hl=hl: s_V1[:, vtile * 512 + hl * 64:vtile * 512 + hl * 64 + 64],
                                lambda kt=kt: s_pT[:, kt * 128:kt * 128 + 128]))
                tv = mm_group(2, (0, 128), mms,
                              waits=[("s", vpt), ("s", v_kv)], rows=(0, 64))
                BpT.read_by("t", tv)
                BV1.read_by("t", tv)
                BK1.read_by("t", tv)
                dl = hl // 2
                sv = P.op("scalar", lambda e, dl=dl, po=po: e.copy(
                    s_opst[po:po + 64, dl * 128:dl * 128 + 128], pb[2][0:64, 0:128]),
                    waits=[("t", tv)] + (od if hl == 0 else []), inc="s")
                B_[2].read_by("s", sv)
            v_loc = P.counts["s"]
            vsp = P.dma("sync", d_opsp[j][:, (12 * half) * 128:(12 * half) * 128 + 4 * 128],
                        s_opst[:], waits=[("s", v_loc)], sem="osp")
            if half == 1:
                Bopst.read_by("osp", P.counts["osp"])
            if half == 0:
                gqd = BGQT.take()
                tvg = None
                for lc in range(4):
                    mms = [(lambda dt=dt, lc=lc: s_wgq[:, dt * 512 + lc * 128:dt * 512 + lc * 128 + 128],
                            lambda dt=dt, j=j: s_xnT_own[:, dt * TOWN + j * 128:dt * TOWN + j * 128 + 128])
                           for dt in range(NDC)]
                    tvg = mm_group(5, (lc * 128, lc * 128 + 128), mms,
                                   waits=[("sw", v_wh), ("s", v_own)],
                                   extra_first=(gqd if lc == 0 else ()))
                for lc in range(4):
                    sv = P.op("scalar", lambda e, lc=lc: e.activation(
                        s_GQT[:, lc * 128:lc * 128 + 128],
                        pb[5][:, lc * 128:lc * 128 + 128], AF.Copy, scale=0.125),
                        waits=[("t", tvg)], inc="s")
                    B_[5].read_by("s", sv)
                v_gq = P.counts["s"]
                rd = BremT.take()
                for lh in range(8):
                    lc_l, po_l = lh // 2, 64 * (lh % 2)
                    tv = mm_group(3, (0, 64),
                                  [(lambda lc_l=lc_l, po_l=po_l: s_GQT[po_l:po_l + 64, lc_l * 128:lc_l * 128 + 128],
                                    lambda lc_l=lc_l, po_l=po_l: s_GKT[po_l:po_l + 64, lc_l * 64:lc_l * 64 + 64])],
                                  waits=[("s", v_gq), ("s", v_gkt)])
                    ev = P.op("vector", lambda e: e.tensor_tensor(
                        s_ls[:], pb[3][:, 0:64], s_lmk[:], AluOpType.add),
                        waits=[("t", tv), ("lm", vlk)] + Bls.take(), inc="v")
                    B_[3].read_by("v", ev)
                    Blmk.read_by("v", ev)
                    vm = P.op("vector", lambda e: e.tensor_reduce(
                        s_stat[:, 11:12], s_ls[:], AX.X, AluOpType.max, negate=True),
                        waits=Bstat.take(), inc="v")
                    se = P.op("scalar", lambda e: e.activation(
                        s_lse[:], s_ls[:], AF.Exp, bias=s_stat[:, 11:12],
                        accum_out=s_stat[:, 10:11]),
                        waits=[("v", vm)] + Blse.take(), inc="s")
                    Bls.read_by("s", se)
                    Bstat.read_by("s", se)
                    vr = P.op("vector", lambda e: e.reciprocal(
                        s_stat[:, 9:10], s_stat[:, 10:11]), waits=[("s", se)], inc="v")
                    va = P.op("vector", lambda e: e.tensor_scalar(
                        s_lse[:], s_lse[:], s_stat[:, 9:10], None, AluOpType.mult),
                        inc="v")
                    Bstat.read_by("v", va)
                    tv = P.op("tensor", lambda e: e.transpose(
                        pb[4][0:64, 0:128], s_lse[:], s_eye32[:]),
                        waits=[("v", va)] + B_[4].take(), inc="t")
                    Blse.read_by("t", tv)
                    sv = P.op("scalar", lambda e: e.copy(s_lpT[0:64, :],
                                                         pb[4][0:64, 0:128]),
                              waits=[("t", tv)] + BlpT.take(), inc="s")
                    B_[4].read_by("s", sv)
                    tv = mm_group(3, (128, 256),
                                  [(lambda lh=lh: s_GV[0:64, lh * 64:lh * 64 + 64],
                                    lambda: s_lpT[0:64, 0:128])],
                                  waits=[("s", P.counts["s"])], rows=(0, 64))
                    BlpT.read_by("t", tv)
                    sv = P.op("scalar", lambda e, lc_l=lc_l, po_l=po_l: e.copy(
                        s_remT[po_l:po_l + 64, lc_l * 128:lc_l * 128 + 128],
                        pb[3][0:64, 128:256]),
                        waits=[("t", tv)] + (rd if lh == 0 else []), inc="s")
                    B_[3].read_by("s", sv)
                v_rem = P.counts["s"]
                omd = Boprem.take()
                for ec in range(8):
                    bank = ec % 2
                    mms = [(lambda lc=lc, ec=ec: s_rout[:, lc * D + ec * 128:lc * D + ec * 128 + 128],
                            lambda lc=lc: s_remT[:, lc * 128:lc * 128 + 128])
                           for lc in range(4)]
                    tv = mm_group(bank, (0, 128), mms,
                                  waits=[("s", v_rem), ("sw", v_wh)],
                                  extra_first=(omd if ec == 0 else ()))
                    BremT.read_by("t", tv)
                    sv = P.op("scalar", lambda e, ec=ec, bank=bank: e.copy(
                        s_oprem[:, ec * 128:ec * 128 + 128], pb[bank][:, 0:128]),
                        waits=[("t", tv)], inc="s")
                    B_[bank].read_by("s", sv)
                v_orem = P.counts["s"]
                vsp = P.dma("sync", d_opsp[j][:, 4 * 128:12 * 128], s_oprem[:],
                            waits=[("s", v_orem)], sem="osp")
                Boprem.read_by("osp", P.counts["osp"])
                Bopst.read_by("osp", P.counts["osp"])
    v_gate_all = P.counts["s"]
    P.phase_barrier()
    cd.close()

    pe = ExitStack()
    s_wout = pe.enter_context(nc.sbuf_tensor("wout_s", [128, NDC * D], BF16))
    s_ehot = pe.enter_context(nc.sbuf_tensor("ehot_s", [32, 8 * 128], F32))
    s_xuTo = pe.enter_context(nc.sbuf_tensor("xuTo", [128, NDC * 128], BF16))
    s_ce = pe.enter_context(nc.sbuf_tensor("ce", [128, 256], F32))
    s_cT = pe.enter_context(nc.sbuf_tensor("cT", [128, 2 * 128], BF16))
    s_g1 = pe.enter_context(nc.sbuf_tensor("g1", [128, D], F32))
    s_opl = pe.enter_context(nc.sbuf_tensor("opl", [128, 16 * 128], BF16))
    s_gT = pe.enter_context(nc.sbuf_tensor("gT", [32, 128], F32))
    s_grb = pe.enter_context(nc.sbuf_tensor("grb", [128, 128], F32))
    s_otmp = pe.enter_context(nc.sbuf_tensor("otmp", [128, 128], F32))
    s_opreb = pe.enter_context(nc.sbuf_tensor("opreb", [128, 8 * 128], BF16))
    s_s1t = pe.enter_context(nc.sbuf_tensor("s1t", [128, D], F32))
    BxuTo, Bce, BcT, Bg1, Bopl, BgT, Bgrb, Botmp, Bopreb, Bs1t = (Buf() for _ in range(10))
    P.phase_barrier()
    for dt in range(NDC):
        P.dma("gpsimd", s_wout[:, dt * D:(dt + 1) * D],
              d_w["wout"][dt * 128:(dt + 1) * 128, :])
    v_wo = P.counts["sw"]
    v_eh = P.dma("sync", s_ehot[:], d_ehot, sem="lm")
    v_allsp = P.counts["osp"]

    for j in range(NSLOT):
        wl = ln_into(s_xtile, s_xn_b, Bxtile, Bxn, s_eps[:],
                     src_row=x1own_row(j * 128))
        wi = ln_into(s_xn_b, s_xi, Bxn, Bxi, s_eps[:], affine=True)
        su = l2unit(s_xi, s_xu, Bxi, Bxu, wi)
        vx = xpose(s_xu, 0, s_xuTo, 128, 0, ("s", su), Bsrc=Bxu)
        BxuTo.read_by("s", vx)
        mms = [(lambda dt=dt: s_xuTo[:, dt * 128:(dt + 1) * 128],
                lambda dt=dt: s_muu[:, dt * 256:(dt + 1) * 256])
               for dt in range(NDC)]
        tv = mm_group(6, (0, 256), mms, waits=[("s", vx), ("v", v_muu)])
        BxuTo.read_by("t", tv)
        vm = P.op("vector", lambda e: e.tensor_reduce(
            s_stat[:, 11:12], pb[6][:, 0:256], AX.X, AluOpType.max, negate=True),
            waits=[("t", tv)] + Bstat.take(), inc="v")
        se = P.op("scalar", lambda e: e.activation(
            s_ce[:], pb[6][:, 0:256], AF.Exp, bias=s_stat[:, 11:12],
            accum_out=s_stat[:, 10:11]), waits=[("v", vm)] + Bce.take(), inc="s")
        B_[6].read_by("s", se)
        Bstat.read_by("s", se)
        vr = P.op("vector", lambda e: e.reciprocal(s_stat[:, 9:10], s_stat[:, 10:11]),
                  waits=[("s", se)], inc="v")
        va = P.op("vector", lambda e: e.tensor_scalar(
            s_ce[:], s_ce[:], s_stat[:, 9:10], None, AluOpType.mult), inc="v")
        Bstat.read_by("v", va)
        for esc in range(2):
            bank = 4 + esc
            tv = P.op("tensor", lambda e, esc=esc, bank=bank: e.transpose(
                pb[bank][:, 0:128], s_ce[:, esc * 128:esc * 128 + 128], s_eye32[:]),
                waits=[("v", va)] + B_[bank].take(), inc="t")
            sv = P.op("scalar", lambda e, esc=esc, bank=bank: e.copy(
                s_cT[:, esc * 128:esc * 128 + 128], pb[bank][:, 0:128]),
                waits=[("t", tv)] + (BcT.take() if esc == 0 else []), inc="s")
            B_[bank].read_by("s", sv)
        Bce.read_by("t", P.counts["t"])
        v_ct = P.counts["s"]
        for dc in range(2):
            bank = dc
            mms = [(lambda esc=esc: s_cT[:, esc * 128:esc * 128 + 128],
                    lambda esc=esc, dc=dc: s_so[:, esc * D + dc * 512:esc * D + dc * 512 + 512])
                   for esc in range(2)]
            tv = mm_group(bank, (0, 512), mms, waits=[("s", v_ct)])
            BcT.read_by("t", tv)
            Bso.read_by("t", tv)
            sv = P.op("scalar", lambda e, dc=dc, bank=bank: e.activation(
                s_g1[:, dc * 512:(dc + 1) * 512], pb[bank][:, 0:512], AF.Silu),
                waits=[("t", tv)] + (Bg1.take() if dc == 0 else []), inc="s")
            B_[bank].read_by("s", sv)
        v_g1 = P.counts["s"]
        vop = P.dma("sync", s_opl[:], d_opsp[j], waits=[("osp", v_allsp)] + Bopl.take(),
                    sem="osp")
        tv = P.op("tensor", lambda e, j=j: e.transpose(
            pb[2][0:32, 0:128], s_gates[:, j * 32:(j + 1) * 32], s_eye32[:]),
            waits=B_[2].take() + [("s", v_gate_all)], inc="t")
        Bgates.read_by("t", tv)
        sv = P.op("scalar", lambda e: e.copy(s_gT[0:32, :], pb[2][0:32, 0:128]),
                  waits=[("t", tv)] + BgT.take(), inc="s")
        B_[2].read_by("s", sv)
        v_gT = P.counts["s"]
        for ec in range(NDC):
            bank = 2 + ec % 2
            tv = mm_group(bank, (0, 128),
                          [(lambda ec=ec: s_ehot[0:32, ec * 128:ec * 128 + 128],
                            lambda: s_gT[0:32, 0:128])],
                          waits=[("s", v_gT), ("lm", v_eh)])
            BgT.read_by("t", tv)
            vv = P.op("vector", lambda e, bank=bank: e.tensor_copy(
                s_grb[:], pb[bank][:, 0:128]),
                waits=[("t", tv)] + Bgrb.take(), inc="v")
            B_[bank].read_by("v", vv)
            vv = P.op("vector", lambda e, ec=ec: e.tensor_tensor(
                s_otmp[:], s_opl[:, (4 + ec) * 128:(4 + ec) * 128 + 128],
                s_grb[:], AluOpType.mult),
                waits=[("osp", vop)] + Botmp.take(), inc="v")
            Bgrb.read_by("v", vv)
            loccol = ec * 128 if ec < 4 else (8 + ec) * 128
            vv = P.op("vector", lambda e, ec=ec, loccol=loccol: e.tensor_tensor(
                s_opreb[:, ec * 128:ec * 128 + 128],
                s_opl[:, loccol:loccol + 128], s_otmp[:], AluOpType.add),
                waits=(Bopreb.take() if ec == 0 else []), inc="v")
            Botmp.read_by("v", vv)
            Bopl.read_by("v", vv)
        v_opb = P.counts["v"]
        for ecw in range(2):
            bank = ecw
            mms = [(lambda dt=dt: s_opreb[:, dt * 128:(dt + 1) * 128],
                    lambda dt=dt, ecw=ecw: s_wout[:, dt * D + ecw * 512:dt * D + ecw * 512 + 512])
                   for dt in range(NDC)]
            tv = mm_group(bank, (0, 512), mms, waits=[("v", v_opb), ("sw", v_wo)])
            Bopreb.read_by("t", tv)
            sv = P.op("scalar", lambda e, ecw=ecw, bank=bank: e.copy(
                s_s1t[:, ecw * 512:(ecw + 1) * 512], pb[bank][:, 0:512]),
                waits=[("t", tv)] + (Bs1t.take() if ecw == 0 else []), inc="s")
            B_[bank].read_by("s", sv)
        v_s1 = P.counts["s"]
        vv = P.op("vector", lambda e: e.tensor_tensor(
            s_g1[:], s_g1[:], s_s1t[:], AluOpType.mult),
            waits=[("s", v_s1), ("s", v_g1)], inc="v")
        Bs1t.read_by("v", vv)
        vv = P.op("vector", lambda e: e.tensor_tensor(
            s_x2t[:], s_xtile[:], s_g1[:], AluOpType.add),
            waits=Bx2t.take(), inc="v")
        Bg1.read_by("v", vv)
        Bxtile.read_by("v", vv)
        sd = P.dma("sync", d_x2p[j * 128:j * 128 + 128, :], s_x2t[:],
                   waits=[("v", vv)], sem="st")
        Bx2t.read_by("st", sd)

    P.phase_barrier()
    pe.close()
    les.close()
    return


def build_l1():
    nc = bass.Bass("TRN2", target_bir_lowering=False)
    es = ExitStack()
    P = Prog(nc, es)
    dd = declare_l1(nc)
    emit_l1(nc, es, P, dd)
    P.barrier_wait("sync", [("st", P.counts["st"])])
    P.finalize()
    es.close()
    return nc, es


U32 = mybir.dt.uint32

BLOCK_POS = {}
for _j, _g in enumerate(SNAKE_A):
    BLOCK_POS[_g] = (0, _j)
for _j, _g in enumerate(SNAKE_B):
    BLOCK_POS[_g] = (1, _j)

POLL_R = [4 * k for k in range(1, 151)]
while POLL_R[-1] < 4000000:
    POLL_R.append(min(int(POLL_R[-1] * 1.05) + 1, 4000000))
NREL = (POLL_R[-1] + 59999) // 60000


def build_fused():
    nc = bass.Bass("TRN2", target_bir_lowering=False)
    es = ExitStack()
    P = Prog(nc, es)
    dd0 = declare_l0(nc, out_kind="Internal")
    dd1 = declare_l1(nc, shared=dd0, with_x1=False)

    d_shx = nc.dram_tensor("x1shx", [T, D], F32, kind="Internal",
                           addr_space="Shared").ap()
    d_fl = nc.dram_tensor("exfl", [256, 4], U32, kind="Internal",
                          addr_space="Shared").ap()
    d_flf = d_fl.rearrange("a b -> (a b)")
    d_sidx = nc.dram_tensor("sidx", [128, NSLOT], U32, kind="ExternalInput").ap()
    d_fidx = nc.dram_tensor("fidx", [128, 1], U32, kind="ExternalInput").ap()
    d_run = nc.dram_tensor("runid", [128, 4], U32, kind="ExternalInput").ap()

    s_sidx = es.enter_context(nc.sbuf_tensor("s_sidx", [128, NSLOT], U32))
    s_fidx = es.enter_context(nc.sbuf_tensor("s_fidx", [128, 1], U32))
    s_run = es.enter_context(nc.sbuf_tensor("s_run", [128, 4], U32))
    s_flb = es.enter_context(nc.sbuf_tensor("s_flb", [1, 8], U32))
    s_tick = es.enter_context(nc.sbuf_tensor("s_tick", [128, 96], mybir.dt.bfloat16))

    P.dma("sync", s_sidx[:], d_sidx, sem="ex")
    P.dma("sync", s_fidx[:], d_fidx, sem="ex")
    P.dma("sync", s_run[:], d_run, sem="ex")

    def scatter(j, s_x1t, waits):
        return P.op("gpsimd", lambda e, j=j, s=s_x1t: e.indirect_dma_start(
            out=d_shx,
            out_offset=bass.IndirectOffsetOnAxis(ap=s_sidx[:, j:j + 1], axis=0),
            in_=s[:], in_offset=None),
            waits=waits + [("ex", 48)], inc="sc", inc_by=16)

    emit_l0(nc, es, P, dd0, scatter=scatter)

    P.op("gpsimd", lambda e: e.indirect_dma_start(
        out=d_fl,
        out_offset=bass.IndirectOffsetOnAxis(ap=s_fidx[:], axis=0),
        in_=s_run[:], in_offset=None),
        waits=[("sc", 16 * NSLOT)], inc="fl", inc_by=16)

    for rk in POLL_R:
        P.op("sync", lambda e: e.dma_start(out=s_flb[0:1, 0:4],
                                           in_=d_flf[0:4]),
             waits=[("pt", rk)], inc="pl", inc_by=16)
        P.op("sync", lambda e: e.dma_start(out=s_flb[0:1, 4:8],
                                           in_=d_flf[512:516]),
             waits=[("pt", rk)], inc="pl", inc_by=16)

    def spin(e):
        e.memset(s_flb[:], 0)
        with e.register("diff") as rdiff, e.register("rf") as rf, \
             e.register("rr") as rr, e.register("acc") as racc:
            e.reg_load(rr, s_run[0:1, 0:1])
            e.reg_mov(rdiff, 1)
            with e.While(rdiff):
                e.memset(s_tick[:], 0).then_inc(P.sems["pt"], 1)
                e.reg_mov(racc, 0)
                for cell in (0, 4):
                    e.reg_load(rf, s_flb[0:1, cell:cell + 1])
                    e.reg_alu(rf, rf, rr, AluOpType.bitwise_xor)
                    e.reg_alu(racc, racc, rf, AluOpType.bitwise_or)
                e.reg_alu(rdiff, racc, 0, AluOpType.bitwise_or)
        for _ in range(NREL):
            e.memset(s_tick[0:1, 0:4], 0).then_inc(P.sems["pt"], 60000)
        return e.memset(s_tick[0:1, 0:4], 0)

    P.sem("pt")
    P.counts["pt"] += NREL * 60000
    P.op("gpsimd", spin, waits=[("fl", 16)], inc="g")

    P.phase_barrier()

    def x1_row(tok0):
        return d_shx[tok0:tok0 + 128, :]

    def x1own_row(r0):
        return dd0["x1p"][r0:r0 + 128, :]

    emit_l1(nc, es, P, dd1, x1_row=x1_row, x1own_row=x1own_row)
    P.barrier_wait("sync", [("st", P.counts["st"])])
    P.finalize()
    es.close()
    return nc, es


def host_prep_fused(inputs):
    per0 = host_prep_l0(inputs)
    per1 = host_prep_l1(inputs, None)
    runid = np.uint32(np.random.randint(1, 2**31))
    out = []
    for c in range(8):
        role = c % 2
        snake = SNAKE_A if role == 0 else SNAKE_B
        sidx = np.empty((128, NSLOT), np.uint32)
        for j, g in enumerate(snake):
            sidx[:, j] = 128 * g + np.arange(128, dtype=np.uint32)
        ex = {
            "sidx": sidx,
            "fidx": (np.arange(128, dtype=np.uint32) + role * 128).reshape(128, 1),
            "runid": np.full((128, 4), runid, np.uint32),
        }
        out.append({**per0[c], **per1[c], **ex})
    return out


from concourse.bass_utils import run_bass_kernel_spmd

_NC_CACHE = {}


def kernel(**inputs):
    if "f" not in _NC_CACHE:
        _NC_CACHE["f"] = build_fused()
    nc, _es = _NC_CACHE["f"]
    per = host_prep_fused(inputs)
    res = run_bass_kernel_spmd(nc, per, core_ids=list(range(8)))
    out = np.zeros((4, T, D), np.float32)
    for c in range(8):
        part = np.asarray(res.results[c]["x2p"])
        snake = SNAKE_A if c % 2 == 0 else SNAKE_B
        for j, g in enumerate(snake):
            out[c // 2, 128 * g:128 * g + 128] = part[128 * j:128 * j + 128]
    return out


# version 18 (final)
# speedup vs baseline: 1.7213x; 1.0741x over previous
import numpy as np
from contextlib import ExitStack
import concourse.bass as bass
import concourse.mybir as mybir
from concourse.alu_op_type import AluOpType

F32 = mybir.dt.float32
BF16 = mybir.dt.bfloat16
AF = mybir.ActivationFunctionType
AX = mybir.AxisListType

D = 1024
T = 4096
TOWN = 2048
NT = 32
NSLOT = 16
NDC = 8
FF = 4096
NE = 9
NEG = -1.0e9
J_ITERS = 7

SNAKE_A = [0, 3, 4, 7, 8, 11, 12, 15, 16, 19, 20, 23, 24, 27, 28, 31]
SNAKE_B = [1, 2, 5, 6, 9, 10, 13, 14, 17, 18, 21, 22, 25, 26, 29, 30]

ENG_OF_SEM = {"t": "tensor", "s": "scalar", "v": "vector", "g": "gpsimd"}


class Buf:
    __slots__ = ("deps",)

    def __init__(self):
        self.deps = []

    def read_by(self, sem, val):
        if val is not None:
            self.deps.append((sem, val))

    def take(self):
        d = self.deps
        self.deps = []
        return d


class Prog:
    ENGS = ("tensor", "scalar", "vector", "gpsimd", "sync")

    def __init__(self, nc, es):
        self.nc = nc
        self.es = es
        self.streams = {e: [] for e in self.ENGS}
        self.seq = 0
        self.sems = {}
        self.counts = {"t": 0, "s": 0, "v": 0, "g": 0, "hw": 0, "sw": 0}
        self.watermark = {e: {} for e in self.ENGS}

    def sem(self, name):
        if name not in self.sems:
            self.sems[name] = self.es.enter_context(self.nc.semaphore(name))
            self.counts[name] = 0
        return self.sems[name]

    SELF_SEM = {"scalar": "s", "vector": "v", "gpsimd": "g"}

    def op(self, eng, fn, waits=(), inc=None, inc_by=1):
        wm = self.watermark[eng]
        own = self.SELF_SEM.get(eng)
        waits = list(waits)
        if own is not None:
            self.sem(own)
            waits.append((own, self.counts[own]))
            if inc is None:
                inc = own
        need = {}
        for w, v in waits:
            if v is None or v <= 0:
                continue
            if ENG_OF_SEM.get(w) == eng and w != own:
                continue
            if v > need.get(w, -1):
                need[w] = v
        wl = []
        for w, v in need.items():
            if wm.get(w, 0) >= v:
                continue
            wm[w] = v
            self.sem(w)
            wl.append((w, v))
        val = None
        if inc is not None:
            self.sem(inc)
            self.counts[inc] += inc_by
            val = self.counts[inc]
        semh = self.sems[inc] if inc is not None else None

        def run(e, fn=fn, wl=tuple(wl), semh=semh, by=inc_by):
            for wname, wval in wl:
                e.wait_ge(self.sems[wname], wval)
            ins = fn(e)
            if semh is not None:
                ins.then_inc(semh, by)
        self.seq += 1
        self.streams[eng].append((self.seq, run))
        return val

    def dma(self, queue, out, in_, waits=(), sem=None):
        if sem is None:
            sem = "hw" if queue in ("sync", "scalar") else "sw"
        return self.op(queue, lambda e, o=out, i=in_: e.dma_start(out=o, in_=i),
                       waits=waits, inc=sem, inc_by=16)

    def barrier_wait(self, eng, waits):
        wm = self.watermark[eng]
        for w, v in waits:
            if v is None or v <= 0 or ENG_OF_SEM.get(w) == eng:
                continue
            if wm.get(w, 0) >= v:
                continue
            wm[w] = v
            self.sem(w)
            sh = self.sems[w]
            self.seq += 1
            self.streams[eng].append(
                (self.seq, lambda e, sh=sh, v=v: e.wait_ge(sh, v)))

    def phase_barrier(self, exclude=("pl", "pt")):
        snap = [(k, v) for k, v in self.counts.items() if k not in exclude]
        for eng in self.ENGS:
            self.barrier_wait(eng, snap)

    def finalize(self, truncate=None):
        import os
        if truncate is None:
            truncate = int(os.environ.get("MK_TRUNC", "0")) or None
        with self.nc.Block() as blk:
            for eng in self.ENGS:
                ops = self.streams[eng]
                if truncate is not None:
                    ops = [x for x in ops if x[0] <= truncate]
                if not ops:
                    continue

                def body(e, ops=ops):
                    for _, f in ops:
                        f(e)
                getattr(blk, eng)(body)


def host_prep_l0(inputs):
    I = inputs
    f32 = np.float32
    shared = {}
    shared["wqT"] = np.ascontiguousarray(np.asarray(I["L0_attn_Wq"], f32).T)
    shared["wkT"] = np.ascontiguousarray(np.asarray(I["L0_attn_Wk"], f32).T)
    shared["wvT"] = np.ascontiguousarray(np.asarray(I["L0_attn_Wv"], f32).T)
    shared["woT"] = np.ascontiguousarray(np.asarray(I["L0_attn_Wo"], f32).T)
    shared["rwT"] = np.ascontiguousarray(np.asarray(I["L0_ffn_router_w"], f32).T)
    shared["rb"] = np.asarray(I["L0_ffn_router_b"], f32).reshape(1, 8).copy()
    w1T = np.empty((NE, D, FF), f32)
    w2T = np.empty((NE, FF, D), f32)
    b1 = np.empty((NE, 1, FF), f32)
    b2 = np.empty((NE, 1, D), f32)
    w1T[0] = np.asarray(I["L0_ffn_shared_w1"], f32).T
    w2T[0] = np.asarray(I["L0_ffn_shared_w2"], f32).T
    b1[0, 0] = I["L0_ffn_shared_b1"]
    b2[0, 0] = I["L0_ffn_shared_b2"]
    for e in range(8):
        w1T[e + 1] = np.asarray(I["L0_ffn_spec_w1"][e], f32).T
        w2T[e + 1] = np.asarray(I["L0_ffn_spec_w2"][e], f32).T
        b1[e + 1, 0] = I["L0_ffn_spec_b1"][e]
        b2[e + 1, 0] = I["L0_ffn_spec_b2"][e]
    shared["w1T"] = np.ascontiguousarray(w1T)
    shared["w2T"] = np.ascontiguousarray(w2T)
    shared["b1r"] = np.ascontiguousarray(b1)
    shared["b2r"] = np.ascontiguousarray(b2)
    shared["eye"] = np.eye(128, dtype=f32)
    shared["ones1"] = np.ones((1, 512), f32)

    qr = np.arange(128)[:, None]
    cc = np.arange(512)[None, :]
    smask_low = np.where(cc > 256 + qr, NEG, 0.0).astype(f32)
    smask_up = np.where(cc > 384 + qr, NEG, 0.0).astype(f32)
    kr = np.arange(128)[:, None]
    qq = np.arange(128)[None, :]
    tri = np.where(kr > qq, NEG, 0.0).astype(f32)
    full = np.full((128, 128), NEG, f32)
    zero = np.zeros((128, 128), f32)
    tmaskT_low = np.concatenate([tri, full], axis=1)
    tmaskT_up = np.concatenate([zero, tri], axis=1)
    smask_roleA = np.stack([smask_low, smask_up])
    smask_roleB = np.stack([smask_up, smask_low])
    tmaskT_roleA = np.stack([tmaskT_low, tmaskT_up])
    tmaskT_roleB = np.stack([tmaskT_up, tmaskT_low])

    x = np.asarray(I["L0_attn_x"], f32)
    per_core = []
    for core in range(8):
        b, role = core // 2, core % 2
        snake = SNAKE_A if role == 0 else SNAKE_B
        xb = x[b]
        x_own = np.ascontiguousarray(
            np.stack([xb[128 * g:128 * g + 128] for g in snake]).reshape(TOWN, D))
        per_core.append({
            "x": np.ascontiguousarray(xb),
            "x_own": x_own,
            "smask": smask_roleA if role == 0 else smask_roleB,
            "tmaskT": tmaskT_roleA if role == 0 else tmaskT_roleB,
            **shared,
        })
    return per_core


def unscramble(part, role):
    snake = SNAKE_A if role == 0 else SNAKE_B
    return [(g, part[128 * j:128 * j + 128]) for j, g in enumerate(snake)]


def declare_l0(nc, out_kind="ExternalOutput", shared=None):
    dd = {}
    dd["x"] = nc.dram_tensor("x", [T, D], F32, kind="ExternalInput").ap()
    dd["x_own"] = nc.dram_tensor("x_own", [TOWN, D], F32, kind="ExternalInput").ap()
    for w in ("wqT", "wkT", "wvT", "woT"):
        dd[w] = nc.dram_tensor(w, [D, D], F32, kind="ExternalInput").ap()
    dd["rwT"] = nc.dram_tensor("rwT", [D, 8], F32, kind="ExternalInput").ap()
    dd["rb"] = nc.dram_tensor("rb", [1, 8], F32, kind="ExternalInput").ap()
    dd["w1T"] = nc.dram_tensor("w1T", [NE, D, FF], F32, kind="ExternalInput").ap()
    dd["w2T"] = nc.dram_tensor("w2T", [NE, FF, D], F32, kind="ExternalInput").ap()
    dd["b1r"] = nc.dram_tensor("b1r", [NE, 1, FF], F32, kind="ExternalInput").ap()
    dd["b2r"] = nc.dram_tensor("b2r", [NE, 1, D], F32, kind="ExternalInput").ap()
    if shared and "eye" in shared:
        dd["eye"] = shared["eye"]
        dd["ones1"] = shared["ones1"]
    else:
        dd["eye"] = nc.dram_tensor("eye", [128, 128], F32, kind="ExternalInput").ap()
        dd["ones1"] = nc.dram_tensor("ones1", [1, 512], F32, kind="ExternalInput").ap()
    dd["smask"] = nc.dram_tensor("smask", [2, 128, 512], F32, kind="ExternalInput").ap()
    dd["tmaskT"] = nc.dram_tensor("tmaskT", [2, 128, 256], F32, kind="ExternalInput").ap()
    dd["outT_sp"] = nc.dram_tensor("outT_sp", [2, 128, 4 * TOWN], BF16, kind="Internal").ap()
    dd["x1p"] = nc.dram_tensor("x1p", [TOWN, D], F32, kind=out_kind).ap()
    return dd


def emit_l0(nc, es, P, dd, upto="full", ntc=8, sub=9, scatter=None):
    d_x = dd["x"]
    d_xown = dd["x_own"]
    d_wT = {w: dd[w] for w in ("wqT", "wkT", "wvT", "woT")}
    d_rwT = dd["rwT"]
    d_rb = dd["rb"]
    d_w1T = dd["w1T"]
    d_w2T = dd["w2T"]
    d_b1 = dd["b1r"]
    d_b2 = dd["b2r"]
    d_eye = dd["eye"]
    d_ones = dd["ones1"]
    d_smask = dd["smask"]
    d_tmaskT = dd["tmaskT"]
    d_osp = dd["outT_sp"]
    d_x1p = dd["x1p"]
    les = ExitStack()

    def sb(name, shape, dt):
        return les.enter_context(nc.sbuf_tensor(name, shape, dt))

    def ps(name, shape, dt):
        return les.enter_context(nc.psum_tensor(name, shape, dt))

    s_eye32 = sb("eye32", [128, 128], F32)
    s_eyeb = sb("eyeb", [128, 128], BF16)
    s_ones = sb("ones_b", [1, 512], BF16)
    s_onesrow = sb("onesrow", [1, 128], BF16)
    s_ones32 = sb("ones32r", [1, 128], F32)
    s_smask = sb("smask_s", [128, 2 * 512], F32)
    s_rw = sb("rw_b", [128, NDC * 8], F32)
    s_rb = sb("rb_b", [1, 8], F32)
    s_xnT_own = sb("xnT_own", [128, NDC * TOWN], BF16)
    s_js = None
    s_xnt32 = None
    s_jvs = sb("junk_v", [128, D], BF16)
    s_gc = sb("gc", [128, 40], F32)
    s_stat = sb("stat", [128, 8], F32)
    s_xtile = sb("xtile", [128, D], F32)
    s_xn_b = sb("xn_b", [128, D], F32)
    s_comb = sb("comb", [128, NSLOT * 8], F32)
    s_rsm = sb("rsm", [128, 12], F32)
    s_b1 = sb("b1_b", [1, FF], BF16)
    s_b2 = sb("b2_b", [1, D], BF16)
    s_gtile = sb("gtile", [128, D], F32)
    s_x1t = sb("x1t", [128, D], F32)

    pb = [ps(f"pb{i}", [128, 512], F32) for i in range(8)]
    B_ = [Buf() for _ in range(8)]
    Bxtile, Bxn, Bgc, Btau, Bstat, Brsm = (Buf() for _ in range(6))
    Bxnt32 = Buf()
    Bjs, Bjv, Bgtile, Bx1t = (Buf() for _ in range(4))

    P.dma("sync", s_eye32[:], d_eye, sem="c0")
    P.dma("gpsimd", s_eyeb[:], d_eye, sem="sw")
    v_eyeb = P.counts["sw"]
    for p in range(2):
        P.dma("sync", s_smask[:, p * 512:(p + 1) * 512], d_smask[p], sem="c0")
    for dt in range(NDC):
        P.dma("sync", s_rw[:, dt * 8:(dt + 1) * 8],
              d_rwT[dt * 128:(dt + 1) * 128, :], sem="c0")
    P.dma("sync", s_rb[:], d_rb, sem="c0")
    v_hw0 = P.counts["c0"]
    v_eye32 = v_hw0
    P.dma("gpsimd", s_ones[:], d_ones)
    v_sw0 = P.counts["sw"]
    v_ones = P.op("vector", lambda e: e.memset(s_onesrow[:], 1.0), inc="v")
    v_ones = P.op("vector", lambda e: e.memset(s_ones32[:], 1.0), inc="v")
    s_eps = sb("epscol", [128, 1], F32)
    v_eps = P.op("vector", lambda e: e.memset(s_eps[:], 1e-5), inc="v")

    def ln_tile(src_row):
        vd = P.dma("sync", s_xtile[:], src_row, waits=Bxtile.take())
        wx = [("hw", vd)]
        vm = P.op("vector", lambda e: e.tensor_scalar(
            s_jvs[:, 0:D], s_xtile[:], 1.0 / D, None, AluOpType.mult,
            AluOpType.add, accum_out=s_stat[:, 0:1]),
            waits=wx + Bjv.take() + Bstat.take(), inc="v")
        vs = P.op("scalar", lambda e: e.activation(
            s_js[:, 0:D], s_xtile[:], AF.Square, accum_out=s_stat[:, 1:2]),
            waits=wx + Bjs.take(), inc="s")
        Bxtile.read_by("v", vm)
        Bxtile.read_by("s", vs)
        Bjv.read_by("v", vm)
        Bjs.read_by("s", vs)
        v1 = P.op("vector", lambda e: e.scalar_tensor_tensor(
            s_stat[:, 2:3], s_stat[:, 0:1], -1.0, s_stat[:, 0:1],
            AluOpType.mult, AluOpType.mult), waits=[("s", vs)], inc="v")
        v2 = P.op("vector", lambda e: e.scalar_tensor_tensor(
            s_stat[:, 3:4], s_stat[:, 1:2], 1.0 / D, s_stat[:, 2:3],
            AluOpType.mult, AluOpType.add), inc="v")
        s1 = P.op("scalar", lambda e: e.activation(
            s_stat[:, 4:5], s_stat[:, 3:4], AF.Sqrt, bias=s_eps[:]),
            waits=[("v", v2), ("v", v_eps)], inc="s")
        v3 = P.op("vector", lambda e: e.reciprocal(s_stat[:, 5:6], s_stat[:, 4:5]),
                  waits=[("s", s1)], inc="v")
        v4 = P.op("vector", lambda e: e.scalar_tensor_tensor(
            s_stat[:, 6:7], s_stat[:, 0:1], -1.0, s_stat[:, 5:6],
            AluOpType.mult, AluOpType.mult), inc="v")
        sx = P.op("scalar", lambda e: e.activation(
            s_xn_b[:], s_xtile[:], AF.Identity, bias=s_stat[:, 6:7],
            scale=s_stat[:, 5:6]), waits=[("v", v4)] + Bxn.take(), inc="s")
        Bxtile.read_by("s", sx)
        Bstat.read_by("s", sx)
        return sx

    def xpose_xn(dest, dest_stride, destcol, wait_s, also32=False):
        for g4 in range(2):
            bank = 4 + g4
            tvl = None
            for slot in range(4):
                dt = g4 * 4 + slot
                tvl = P.op("tensor", lambda e, dt=dt, bank=bank, slot=slot: e.transpose(
                    pb[bank][:, slot * 128:slot * 128 + 128],
                    s_xn_b[:, dt * 128:(dt + 1) * 128], s_eye32[:]),
                    waits=[("s", wait_s), ("c0", v_eye32)] +
                          (B_[bank].take() if slot == 0 else []), inc="t")
            for slot in range(4):
                dt = g4 * 4 + slot
                sv = P.op("scalar",
                          lambda e, dt=dt, bank=bank, slot=slot: e.copy(
                              dest[:, dt * dest_stride + destcol:dt * dest_stride + destcol + 128],
                              pb[bank][:, slot * 128:slot * 128 + 128]),
                          waits=[("t", tvl)], inc="s")
                B_[bank].read_by("s", sv)
                if also32:
                    sv32 = P.op("scalar",
                                lambda e, dt=dt, bank=bank, slot=slot: e.copy(
                                    s_xnt32[:, dt * 128:(dt + 1) * 128],
                                    pb[bank][:, slot * 128:slot * 128 + 128]),
                                waits=(Bxnt32.take() if dt == 0 else []), inc="s")
                    B_[bank].read_by("s", sv32)
        Bxn.read_by("t", P.counts["t"])
        return P.counts["s"]

    def mm_group(bank, cols, mms, waits=(), extra_first=(), rows=(0, 128)):
        n = len(mms)
        w0 = list(waits) + B_[bank].take() + list(extra_first)
        last = None
        for i, (fl, fr) in enumerate(mms):
            last = P.op("tensor",
                        lambda e, fl=fl, fr=fr, i=i: e.matmul(
                            pb[bank][rows[0]:rows[1], cols[0]:cols[1]], fl(), fr(),
                            start=(i == 0), stop=(i == n - 1)),
                        waits=(w0 if i == 0 else ()),
                        inc=("t" if i == n - 1 else None))
        return last

    def emit_router_tile(j, v32):
        mms = [(lambda dt=dt: s_xnt32[:, dt * 128:(dt + 1) * 128],
                lambda dt=dt: s_rw[:, dt * 8:(dt + 1) * 8]) for dt in range(NDC)]
        mms.append((lambda: s_ones32[0:1, 0:128], lambda: s_rb[0:1, 0:8]))
        tv = mm_group(6, (0, 8), mms, waits=[("c0", v_hw0), ("s", v32),
                                             ("v", v_ones)])
        Bxnt32.read_by("t", tv)
        vm = P.op("vector", lambda e: e.tensor_reduce(
            s_rsm[:, 0:1], pb[6][:, 0:8], AX.X, AluOpType.max, negate=True),
            waits=[("t", tv)] + Brsm.take(), inc="v")
        se = P.op("scalar", lambda e: e.activation(
            s_rsm[:, 1:9], pb[6][:, 0:8], AF.Exp, bias=s_rsm[:, 0:1],
            accum_out=s_rsm[:, 9:10]), waits=[("v", vm)], inc="s")
        B_[6].read_by("s", se)
        vv = P.op("vector", lambda e: e.reciprocal(s_rsm[:, 10:11], s_rsm[:, 9:10]),
                  waits=[("s", se)], inc="v")
        vv = P.op("vector", lambda e, j=j: e.tensor_scalar(
            s_comb[:, j * 8:j * 8 + 8], s_rsm[:, 1:9], s_rsm[:, 10:11], None,
            AluOpType.mult), inc="v")
        vv = P.op("vector", lambda e, j=j: e.tensor_reduce(
            s_rsm[:, 0:1], s_comb[:, j * 8:j * 8 + 8], AX.X, AluOpType.max), inc="v")
        vv = P.op("vector", lambda e, j=j: e.tensor_scalar(
            s_rsm[:, 1:9], s_comb[:, j * 8:j * 8 + 8], s_rsm[:, 0:1], None,
            AluOpType.is_equal), inc="v")
        vv = P.op("vector", lambda e, j=j: e.scalar_tensor_tensor(
            s_rsm[:, 1:9], s_rsm[:, 1:9], -4.0, s_comb[:, j * 8:j * 8 + 8],
            AluOpType.mult, AluOpType.add), inc="v")
        vv = P.op("vector", lambda e: e.tensor_reduce(
            s_rsm[:, 9:10], s_rsm[:, 1:9], AX.X, AluOpType.max), inc="v")
        vv = P.op("vector", lambda e: e.tensor_tensor(
            s_rsm[:, 11:12], s_rsm[:, 0:1], s_rsm[:, 9:10], AluOpType.add), inc="v")
        vv = P.op("vector", lambda e: e.reciprocal(s_rsm[:, 10:11], s_rsm[:, 11:12]),
                  inc="v")
        vv = P.op("vector", lambda e, j=j: e.tensor_scalar(
            s_rsm[:, 1:9], s_comb[:, j * 8:j * 8 + 8], s_rsm[:, 9:10], None,
            AluOpType.is_ge), inc="v")
        vv = P.op("vector", lambda e, j=j: e.tensor_tensor(
            s_rsm[:, 1:9], s_rsm[:, 1:9], s_comb[:, j * 8:j * 8 + 8],
            AluOpType.mult), inc="v")
        vv = P.op("vector", lambda e, j=j: e.tensor_scalar(
            s_comb[:, j * 8:j * 8 + 8], s_rsm[:, 1:9], s_rsm[:, 10:11], None,
            AluOpType.mult), inc="v")
        Brsm.read_by("v", vv)

    emitted_xnt_own = [False]
    v_own_holder = [0]
    LEVELS = {"kv0": 0, "own": 1, "attn0": 2, "kv1": 3, "attn1": 4, "router": 5,
              "full": 9}
    LVL = LEVELS[upto]

    def finish_debug():
        P.phase_barrier()
        for j in range(NSLOT):
            sd = P.dma("sync", d_x1p[j * 128:j * 128 + 128, :], s_xtile[:],
                       sem="st")
        P.barrier_wait("sync", [("st", P.counts["st"])])
        les.close()
        return

    def emit_half(half):
        nonlocal s_js, s_xnt32
        ph = ExitStack()
        s_js = ph.enter_context(nc.sbuf_tensor(f"junk_s{half}", [128, T], BF16))
        s_xnt32 = ph.enter_context(nc.sbuf_tensor(f"xnt32_{half}", [128, NDC * 128], F32))
        s_KT = ph.enter_context(nc.sbuf_tensor(f"KT{half}", [128, 4 * T], BF16))
        s_V = ph.enter_context(nc.sbuf_tensor(f"V{half}", [128, NT * 512], BF16))
        BKT, BV = Buf(), Buf()

        kv = ExitStack()
        s_wk = kv.enter_context(nc.sbuf_tensor(f"wk{half}", [128, NDC * 512], BF16))
        s_wv = kv.enter_context(nc.sbuf_tensor(f"wv{half}", [128, NDC * 512], BF16))
        s_xnTc = kv.enter_context(nc.sbuf_tensor(f"xnTc{half}", [128, NDC * 512], BF16))
        BxnTc = Buf()
        P.phase_barrier()
        for dt in range(NDC):
            P.dma("gpsimd", s_wk[:, dt * 512:(dt + 1) * 512],
                  d_wT["wkT"][dt * 128:(dt + 1) * 128, half * 512:half * 512 + 512])
            P.dma("gpsimd", s_wv[:, dt * 512:(dt + 1) * 512],
                  d_wT["wvT"][dt * 128:(dt + 1) * 128, half * 512:half * 512 + 512])
        v_swk = P.counts["sw"]
        for tc in range(ntc):
            for ti in range(4):
                tok0 = tc * 512 + ti * 128
                if sub >= 2:
                    sx = ln_tile(d_x[tok0:tok0 + 128, :])
                if sub >= 3:
                    xpose_xn(s_xnTc, 512, ti * 128, sx)
            vxn = P.counts["s"]
            wxn = [("s", vxn), ("sw", v_swk)]
            for ec in range(4 if sub >= 4 else 0):
                bank = ec % 2
                mms = [(lambda dt=dt, ec=ec: s_wk[:, dt * 512 + ec * 128:dt * 512 + ec * 128 + 128],
                        lambda dt=dt: s_xnTc[:, dt * 512:(dt + 1) * 512])
                       for dt in range(NDC)]
                tv = mm_group(bank, (0, 512), mms, waits=wxn)
                sv = P.op("scalar", lambda e, ec=ec, tc=tc, bank=bank: e.copy(
                    s_KT[:, ec * T + tc * 512:ec * T + tc * 512 + 512],
                    pb[bank][:, 0:512]), waits=[("t", tv)], inc="s")
                B_[bank].read_by("s", sv)
            for ti in range(4 if sub >= 5 else 0):
                bank = 2 + ti % 2
                tt = tc * 4 + ti
                mms = [(lambda dt=dt, ti=ti: s_xnTc[:, dt * 512 + ti * 128:dt * 512 + ti * 128 + 128],
                        lambda dt=dt: s_wv[:, dt * 512:(dt + 1) * 512])
                       for dt in range(NDC)]
                tv = mm_group(bank, (0, 512), mms, waits=wxn)
                sv = P.op("scalar", lambda e, tt=tt, bank=bank: e.copy(
                    s_V[:, tt * 512:tt * 512 + 512], pb[bank][:, 0:512]),
                    waits=[("t", tv)], inc="s")
                B_[bank].read_by("s", sv)
            BxnTc.read_by("t", P.counts["t"])
        P.phase_barrier()
        kv.close()

        if LVL < 1 and half == 0:
            ph.close()
            return
        if not emitted_xnt_own[0]:
            emitted_xnt_own[0] = True
            for j in range(NSLOT):
                sx = ln_tile(d_xown[j * 128:j * 128 + 128, :])
                v32 = xpose_xn(s_xnT_own, TOWN, j * 128, sx, also32=True)
                emit_router_tile(j, v32)
            v_own_holder[0] = P.counts["s"]

        if LVL < 2 + 2 * half:
            ph.close()
            return
        at = ExitStack()
        s_wq = at.enter_context(nc.sbuf_tensor(f"wq{half}", [128, NDC * 512], BF16))
        s_scores = at.enter_context(nc.sbuf_tensor(f"sc{half}", [128, T], F32))
        s_attnT = at.enter_context(nc.sbuf_tensor(f"aT{half}", [128, T], BF16))
        s_QT = at.enter_context(nc.sbuf_tensor(f"QT{half}", [128, 4 * 128], BF16))
        s_outTh = at.enter_context(nc.sbuf_tensor(f"oT{half}", [128, 4 * TOWN], BF16))
        Bscores, BattnT, BQT, Btaub = Buf(), Buf(), Buf(), Buf()
        P.phase_barrier()
        for dt in range(NDC):
            P.dma("gpsimd", s_wq[:, dt * 512:(dt + 1) * 512],
                  d_wT["wqT"][dt * 128:(dt + 1) * 128, half * 512:half * 512 + 512])
        v_swq = P.counts["sw"]

        for j in range(NSLOT):
            E = 256 * (j + 1)
            nkc = E // 256
            qdeps = BQT.take()
            tvq = None
            for ec in range(4):
                mms = [(lambda dt=dt, ec=ec: s_wq[:, dt * 512 + ec * 128:dt * 512 + ec * 128 + 128],
                        lambda dt=dt, j=j: s_xnT_own[:, dt * TOWN + j * 128:dt * TOWN + j * 128 + 128])
                       for dt in range(NDC)]
                tvq = mm_group(6, (ec * 128, ec * 128 + 128), mms,
                               waits=[("sw", v_swq), ("s", v_own_holder[0])],
                               extra_first=(qdeps if ec == 0 else ()))
            for ec in range(4):
                sv = P.op("scalar", lambda e, ec=ec: e.activation(
                    s_QT[:, ec * 128:ec * 128 + 128],
                    pb[6][:, ec * 128:ec * 128 + 128], AF.Copy, scale=0.125),
                    waits=[("t", tvq)], inc="s")
                B_[6].read_by("s", sv)
            vqt = P.counts["s"]
            for hl in range(8):
                ec, po = hl // 2, 64 * (hl % 2)
                sdeps = Bscores.take()
                for kc in range(nkc):
                    bank = kc % 2
                    co = 0
                    tv = mm_group(bank, (co, co + 256),
                                  [(lambda ec=ec, po=po: s_QT[po:po + 64, ec * 128:ec * 128 + 128],
                                    lambda ec=ec, po=po, kc=kc: s_KT[po:po + 64, ec * T + kc * 256:ec * T + kc * 256 + 256])],
                                  waits=[("s", vqt)])
                    if kc >= nkc - 2:
                        mo = 256 * (kc - (nkc - 2))
                        ev = P.op("vector", lambda e, kc=kc, bank=bank, co=co, mo=mo, j=j:
                                  e.tensor_tensor(
                                      s_scores[:, kc * 256:kc * 256 + 256],
                                      pb[bank][:, co:co + 256],
                                      s_smask[:, (j % 2) * 512 + mo:(j % 2) * 512 + mo + 256],
                                      AluOpType.add),
                                  waits=[("t", tv), ("c0", v_hw0)] +
                                        (sdeps if kc == 0 else []), inc="v")
                        B_[bank].read_by("v", ev)
                    else:
                        ev = P.op("scalar", lambda e, kc=kc, bank=bank, co=co:
                                  e.copy(s_scores[:, kc * 256:kc * 256 + 256],
                                         pb[bank][:, co:co + 256]),
                                  waits=[("t", tv)] + (sdeps if kc == 0 else []),
                                  inc="s")
                        B_[bank].read_by("s", ev)
                wsc = [("s", P.counts["s"]), ("v", P.counts["v"])]
                vr = P.op("vector", lambda e, hl=hl, E=E: e.tensor_reduce(
                    s_gc[:, 24 + hl:25 + hl], s_scores[:, 0:E], AX.X,
                    AluOpType.max, negate=True),
                    waits=wsc + (Bgc.take() if hl == 0 else []), inc="v")
                vr = P.op("vector", lambda e, hl=hl: e.tensor_scalar(
                    s_gc[:, 24 + hl:25 + hl], s_gc[:, 24 + hl:25 + hl], 1.0,
                    None, AluOpType.add), inc="v")
                vr = P.op("vector", lambda e, hl=hl: e.tensor_scalar(
                    s_gc[:, 16 + hl:17 + hl], s_gc[:, 24 + hl:25 + hl], -1.0,
                    None, AluOpType.mult), inc="v")
                adeps = BattnT.take()
                for it in range(J_ITERS):
                    sg = P.op("scalar", lambda e, hl=hl, E=E: e.activation(
                        s_js[:, 0:E], s_scores[:, 0:E], AF.Relu,
                        bias=s_gc[:, 24 + hl:25 + hl],
                        accum_out=s_gc[:, hl:hl + 1]),
                        waits=[("v", vr)] + Bjs.take(), inc="s")
                    Bjs.read_by("s", sg)
                    vc = P.op("vector", lambda e, hl=hl, E=E: e.tensor_scalar(
                        s_attnT[:, 0:E], s_scores[:, 0:E],
                        s_gc[:, 16 + hl:17 + hl], None, AluOpType.is_gt,
                        AluOpType.add, accum_out=s_gc[:, 8 + hl:9 + hl]),
                        waits=(adeps if it == 0 else []), inc="v")
                    vu = P.op("vector", lambda e, hl=hl: e.reciprocal(
                        s_gc[:, 32 + hl:33 + hl], s_gc[:, 8 + hl:9 + hl]), inc="v")
                    vu = P.op("vector", lambda e, hl=hl: e.scalar_tensor_tensor(
                        s_gc[:, 32 + hl:33 + hl], s_gc[:, hl:hl + 1], -1.0,
                        s_gc[:, 32 + hl:33 + hl], AluOpType.add, AluOpType.mult),
                        waits=[("s", sg)], inc="v")
                    vr = P.op("vector", lambda e, hl=hl: e.tensor_tensor(
                        s_gc[:, 24 + hl:25 + hl], s_gc[:, 24 + hl:25 + hl],
                        s_gc[:, 32 + hl:33 + hl], AluOpType.subtract), inc="v")
                    vu = P.op("vector", lambda e, hl=hl: e.tensor_tensor(
                        s_gc[:, 16 + hl:17 + hl], s_gc[:, 16 + hl:17 + hl],
                        s_gc[:, 32 + hl:33 + hl], AluOpType.add), inc="v")
                Bscores.read_by("v", vr)
                Bscores.read_by("s", P.counts["s"])
                jdeps = Bjs.take()
                sgfl = []
                for g2 in range((E + 511) // 512):
                    w0 = g2 * 512
                    w1 = min(E, w0 + 512)
                    sgf = P.op("scalar", lambda e, hl=hl, w0=w0, w1=w1:
                               e.activation(s_js[:, w0:w1], s_scores[:, w0:w1],
                                            AF.Relu,
                                            bias=s_gc[:, 24 + hl:25 + hl]),
                               waits=(([("v", vr)] + jdeps) if g2 == 0 else ()),
                               inc="s")
                    sgfl.append(sgf)
                Bscores.read_by("s", sgfl[-1])
                ttl = None
                evls = []
                for grp in range(j + 1):
                    bank = 3 + (grp % 2) * 2
                    zdeps = B_[bank].take()
                    tvz = None
                    for kt in range(2):
                        ktg = grp * 2 + kt
                        tvz = P.op("tensor",
                                   lambda e, ktg=ktg, bank=bank, kt=kt:
                                   e.matmul(pb[bank][:, kt * 128:kt * 128 + 128],
                                            s_js[:, ktg * 128:ktg * 128 + 128],
                                            s_eyeb[:], start=True, stop=True),
                                   waits=(([("s", sgfl[grp // 2]),
                                            ("sw", v_eyeb)] + zdeps)
                                          if kt == 0 else ()), inc="t")
                    ttl = tvz
                    evl = P.op("scalar", lambda e, grp=grp, bank=bank: e.copy(
                        s_attnT[:, grp * 256:grp * 256 + 256],
                        pb[bank][:, 0:256]), waits=[("t", tvz)], inc="s")
                    evls.append(evl)
                    B_[bank].read_by("s", evl)
                Bjs.read_by("t", ttl)
                av2 = B_[2].take()
                nmm = 2 * (j + 1)
                tv = None
                for ktg in range(nmm):
                    tv = P.op("tensor",
                              lambda e, ktg=ktg, hl=hl, nmm=nmm: e.matmul(
                                  pb[2][0:64, 0:128],
                                  s_V[:, ktg * 512 + hl * 64:ktg * 512 + hl * 64 + 64],
                                  s_attnT[:, ktg * 128:ktg * 128 + 128],
                                  start=(ktg == 0), stop=(ktg == nmm - 1)),
                              waits=([("s", evls[ktg // 2])] +
                                     (av2 if ktg == 0 else [])),
                              inc=("t" if ktg == nmm - 1 else None))
                BattnT.read_by("t", tv)
                BQT.read_by("t", tv)
                BKT.read_by("t", tv)
                BV.read_by("t", tv)
                dl, po_o = hl // 2, 64 * (hl % 2)
                sv = P.op("scalar", lambda e, dl=dl, po_o=po_o, j=j: e.copy(
                    s_outTh[po_o:po_o + 64, dl * TOWN + j * 128:dl * TOWN + j * 128 + 128],
                    pb[2][0:64, 0:128]), waits=[("t", tv)], inc="s")
                B_[2].read_by("s", sv)

        vsp = P.dma("sync", d_osp[half], s_outTh[:],
                    waits=[("s", P.counts["s"])], sem="osp")
        P.phase_barrier()
        at.close()
        ph.close()

    emit_half(0)
    if LVL < 3:
        return finish_debug()
    emit_half(1)
    if LVL < 5:
        return finish_debug()

    ff = ExitStack()
    s_outT = ff.enter_context(nc.sbuf_tensor("outT", [128, NDC * TOWN], BF16))
    s_wo = ff.enter_context(nc.sbuf_tensor("wo_b", [128, NDC * D], BF16))
    s_w1e = ff.enter_context(nc.sbuf_tensor("w1e", [128, 2 * NDC * 512], BF16))
    s_w2c = [ff.enter_context(nc.sbuf_tensor(f"w2c{i}", [128, D], BF16))
             for i in range(3)]
    s_h = ff.enter_context(nc.sbuf_tensor("h_sb", [128, 32 * 512], BF16))
    s_acc = ff.enter_context(nc.sbuf_tensor("acc", [128, 4 * D], F32))
    s_stile = ff.enter_context(nc.sbuf_tensor("stile", [128, 4 * D], F32))
    Bh, Bacc, Bstile = Buf(), Buf(), Buf()
    Bw1e = [Buf(), Buf()]
    Bw2 = [Buf(), Buf(), Buf()]
    Bb1, Bb2 = Buf(), Buf()
    P.phase_barrier()
    v_spills = P.counts["osp"] if "osp" in P.counts else 32
    for half in (0, 1):
        P.dma("sync", s_outT[:, half * 4 * TOWN:(half + 1) * 4 * TOWN], d_osp[half],
              waits=[("osp", v_spills)], sem="osp")
    v_hwo = P.counts["osp"]
    for dt in range(NDC):
        P.dma("gpsimd", s_wo[:, dt * D:(dt + 1) * D],
              d_wT["woT"][dt * 128:(dt + 1) * 128, :])
    v_swo = P.counts["sw"]

    for ch in range(4):
        sdeps = Bstile.take()
        for ti in range(4):
            j = ch * 4 + ti
            for ecc in range(2):
                bank = ecc
                mms = [(lambda dt=dt, j=j: s_outT[:, dt * TOWN + j * 128:dt * TOWN + j * 128 + 128],
                        lambda dt=dt, ecc=ecc: s_wo[:, dt * D + ecc * 512:dt * D + ecc * 512 + 512])
                       for dt in range(NDC)]
                tv = mm_group(bank, (0, 512), mms,
                              waits=[("sw", v_swo), ("osp", v_hwo)],
                              extra_first=(sdeps if ti == 0 and ecc == 0 else ()))
                sv = P.op("scalar", lambda e, ti=ti, ecc=ecc, bank=bank: e.copy(
                    s_stile[:, ti * D + ecc * 512:ti * D + ecc * 512 + 512],
                    pb[bank][:, 0:512]), waits=[("t", tv)], inc="s")
                B_[bank].read_by("s", sv)
        for e_i in range(NE):
            hdeps = Bh.take()
            vb1 = P.dma("gpsimd", s_b1[:], d_b1[e_i], waits=Bb1.take(), sem="b1s")
            vb2 = P.dma("gpsimd", s_b2[:], d_b2[e_i], waits=Bb2.take(), sem="b2s")
            for fc in range(8):
                lv = None
                par = fc % 2
                w1deps = Bw1e[par].take()
                wsem1 = "w1a" if par == 0 else "w1b"
                for dt in range(NDC):
                    lv = P.dma("gpsimd",
                               s_w1e[:, par * 4096 + dt * 512:par * 4096 + (dt + 1) * 512],
                               d_w1T[e_i][dt * 128:(dt + 1) * 128, fc * 512:(fc + 1) * 512],
                               waits=(w1deps if dt == 0 else []), sem=wsem1)
                wld = [(wsem1, lv)]
                for ft in range(4):
                    fglob = fc * 4 + ft
                    bank = 2 + fglob % 2
                    mms = [(lambda dt=dt, ft=ft, par=par: s_w1e[:, par * 4096 + dt * 512 + ft * 128:par * 4096 + dt * 512 + ft * 128 + 128],
                            lambda dt=dt, ch=ch: s_xnT_own[:, dt * TOWN + ch * 512:dt * TOWN + ch * 512 + 512])
                           for dt in range(NDC)]
                    mms.append((lambda fglob=fglob: s_b1[0:1, fglob * 128:fglob * 128 + 128],
                                lambda: s_ones[0:1, 0:512]))
                    tv = mm_group(bank, (0, 512), mms,
                                  waits=wld + [("b1s", vb1), ("s", v_own_holder[0])],
                                  extra_first=(hdeps if fglob == 0 else ()))
                    Bb1.read_by("t", tv)
                    sv = P.op("scalar", lambda e, fglob=fglob, bank=bank:
                              e.activation(s_h[:, fglob * 512:fglob * 512 + 512],
                                           pb[bank][:, 0:512], AF.Gelu_apprx_tanh),
                              waits=[("t", tv)], inc="s")
                    B_[bank].read_by("s", sv)
                    Bw1e[par].read_by("t", tv)
            vh = P.counts["s"]
            odeps = [B_[b].take() for b in range(8)]
            for ft in range(32):
                wbuf2 = s_w2c[ft % 3]
                wsem = ("w2a", "w2b", "w2c")[ft % 3]
                lv = P.dma("gpsimd", wbuf2[:],
                           d_w2T[e_i][ft * 128:(ft + 1) * 128, :],
                           waits=Bw2[ft % 3].take(), sem=wsem)
                for ti in range(4):
                    for dc in range(2):
                        bank = ti * 2 + dc
                        tvw = P.op("tensor",
                             lambda e, ft=ft, ti=ti, dc=dc, bank=bank, wbuf2=wbuf2:
                             e.matmul(pb[bank][:, 0:512],
                                      s_h[:, ft * 512 + ti * 128:ft * 512 + ti * 128 + 128],
                                      wbuf2[:, dc * 512:dc * 512 + 512],
                                      start=(ft == 0), stop=False),
                             waits=([(wsem, lv), ("s", vh)] +
                                    (odeps[bank] if ft == 0 else [])),
                             inc=("t" if (ti == 3 and dc == 1) else None))
                Bw2[ft % 3].read_by("t", tvw)
            vo = None
            for ti in range(4):
                for dc in range(2):
                    bank = ti * 2 + dc
                    vo = P.op("tensor", lambda e, dc=dc, bank=bank:
                              e.matmul(pb[bank][:, 0:512],
                                       s_onesrow[0:1, 0:128],
                                       s_b2[0:1, dc * 512:dc * 512 + 512],
                                       start=False, stop=True),
                              waits=[("v", v_ones), ("b2s", vb2)], inc="t")
                    Bb2.read_by("t", vo)
            Bh.read_by("t", vo)
            for ti in range(4):
                for dc in range(2):
                    bank = ti * 2 + dc
                    j = ch * 4 + ti
                    if e_i == 0:
                        sv = P.op("scalar", lambda e, ti=ti, dc=dc, bank=bank:
                                  e.copy(s_acc[:, ti * D + dc * 512:ti * D + dc * 512 + 512],
                                         pb[bank][:, 0:512]),
                                  waits=[("t", vo)] +
                                        (Bacc.take() if ti == 0 and dc == 0 else []),
                                  inc="s")
                        B_[bank].read_by("s", sv)
                    else:
                        vv = P.op("vector",
                                  lambda e, ti=ti, dc=dc, bank=bank, j=j, e_i=e_i:
                                  e.scalar_tensor_tensor(
                                      s_acc[:, ti * D + dc * 512:ti * D + dc * 512 + 512],
                                      pb[bank][:, 0:512],
                                      s_comb[:, j * 8 + (e_i - 1):j * 8 + e_i],
                                      s_acc[:, ti * D + dc * 512:ti * D + dc * 512 + 512],
                                      AluOpType.mult, AluOpType.add),
                                  waits=[("t", vo), ("s", P.counts["s"])], inc="v")
                        B_[bank].read_by("v", vv)
        for ti in range(4):
            j = ch * 4 + ti
            sg = P.op("scalar", lambda e, ti=ti: e.activation(
                s_gtile[:], s_acc[:, ti * D:(ti + 1) * D], AF.Silu),
                waits=[("v", P.counts["v"])] + Bgtile.take(), inc="s")
            Bacc.read_by("s", sg)
            vd = P.dma("sync", s_xtile[:], d_xown[j * 128:j * 128 + 128, :],
                       waits=Bxtile.take())
            vv = P.op("vector", lambda e, ti=ti: e.tensor_tensor(
                s_gtile[:], s_gtile[:], s_stile[:, ti * D:(ti + 1) * D],
                AluOpType.mult), waits=[("s", sg)], inc="v")
            vv = P.op("vector", lambda e: e.tensor_tensor(
                s_x1t[:], s_xtile[:], s_gtile[:], AluOpType.add),
                waits=[("hw", vd)] + Bx1t.take(), inc="v")
            Bxtile.read_by("v", vv)
            Bgtile.read_by("v", vv)
            Bstile.read_by("v", vv)
            sd = P.dma("sync", d_x1p[j * 128:j * 128 + 128, :], s_x1t[:],
                       waits=[("v", vv)], sem="st")
            Bx1t.read_by("st", sd)
            if scatter is not None:
                sc = scatter(j, s_x1t, [("v", vv)])
                Bx1t.read_by("sc", sc)

    P.phase_barrier()
    ff.close()
    les.close()
    return


def build_l0(upto="full", ntc=8, sub=9):
    nc = bass.Bass("TRN2", target_bir_lowering=False)
    es = ExitStack()
    P = Prog(nc, es)
    dd = declare_l0(nc)
    emit_l0(nc, es, P, dd, upto=upto, ntc=ntc, sub=sub)
    P.barrier_wait("sync", [("st", P.counts["st"])])
    P.finalize()
    es.close()
    return nc, es



def host_prep_l1(inputs, x1=None):
    I = inputs
    f32 = np.float32
    sh = {}
    for nm, key in (("wq", "L1_attn_wq"), ("wk", "L1_attn_wk"),
                    ("wv", "L1_attn_wv"), ("wout", "L1_attn_w_out")):
        sh[nm] = np.ascontiguousarray(np.asarray(I[key], f32))
    sh["wgq"] = np.ascontiguousarray(np.asarray(I["L1_attn_wgq"], f32))
    sh["wgk"] = np.ascontiguousarray(np.asarray(I["L1_attn_wgk"], f32))
    sh["wgv"] = np.ascontiguousarray(np.asarray(I["L1_attn_wgv"], f32))
    sh["rout"] = np.ascontiguousarray(np.asarray(I["L1_attn_r_out_w"], f32))
    gwbd = np.zeros((D, 32), f32)
    glw = np.asarray(I["L1_attn_gate_local_w"], f32)
    grw = np.asarray(I["L1_attn_gate_remote_w"], f32)
    for h in range(16):
        gwbd[64 * h:64 * h + 64, h] = glw[h]
        gwbd[64 * h:64 * h + 64, 16 + h] = grw[h]
    sh["gwbd"] = gwbd
    ehot = np.zeros((32, 8 * 128), f32)
    for ec in range(8):
        for m in range(128):
            ehot[16 + 2 * ec + m // 64, ec * 128 + m] = 1.0
    sh["ehot"] = ehot
    sh["mu2"] = np.ascontiguousarray(np.asarray(I["L1_ffn_mu"], f32).reshape(D, 256))
    sh["w1s"] = np.ascontiguousarray(np.asarray(I["L1_ffn_w1"], f32))
    sh["w2s"] = np.ascontiguousarray(np.asarray(I["L1_ffn_w2"], f32))
    sh["b1T"] = np.ascontiguousarray(np.asarray(I["L1_ffn_b1"], f32).T.reshape(2, 128, 16))
    sh["b2T"] = np.ascontiguousarray(np.asarray(I["L1_ffn_b2"], f32).T.reshape(8, 128, 16))
    sh["gammab"] = np.ascontiguousarray(
        np.broadcast_to(np.asarray(I["L1_ffn_ln_gamma"], f32), (128, D)))
    sh["betab"] = np.ascontiguousarray(
        np.broadcast_to(np.asarray(I["L1_ffn_ln_beta"], f32), (128, D)))
    sh["scaleb"] = np.ascontiguousarray(
        np.broadcast_to(np.asarray(I["L1_ffn_scale"], f32).reshape(1), (128, 1)))
    sh["eye"] = np.eye(128, dtype=f32)
    sh["ones1"] = np.ones((1, 512), f32)
    pmat = np.zeros((32, 128, 64), f32)
    for tt in range(32):
        for r in range(128):
            pmat[tt, r, 2 * tt + r // 64] = 1.0 / 64.0
    sh["pmat"] = pmat

    per_core = []
    for core in range(8):
        b, role = core // 2, core % 2
        snake = SNAKE_A if role == 0 else SNAKE_B
        wmask = np.zeros((16, 128, 512), f32)
        lmask = np.zeros((16, 128, 64), f32)
        for j, g in enumerate(snake):
            qg = 128 * g + np.arange(128)[:, None]
            kg = 128 * (2 * j - 2) + np.arange(512)[None, :]
            ok = (kg <= qg) & (kg > qg - 256) & (kg >= 0)
            wmask[j] = np.where(ok, 0.0, NEG)
            lat = np.arange(64)[None, :]
            lmask[j] = np.where(lat <= qg // 64, 0.0, NEG)
        m = {"wmask": wmask, "lmask": lmask, **sh}
        if x1 is not None:
            xb = np.ascontiguousarray(np.asarray(x1[b], f32))
            m["x1"] = xb
            m["x1_own"] = np.ascontiguousarray(
                np.stack([xb[128 * g:128 * g + 128] for g in snake]).reshape(TOWN, D))
        per_core.append(m)
    return per_core


def declare_l1(nc, shared=None, with_x1=True):
    dd = {}
    if with_x1:
        dd["x1"] = nc.dram_tensor("x1", [T, D], F32, kind="ExternalInput").ap()
        dd["x1_own"] = nc.dram_tensor("x1_own", [TOWN, D], F32, kind="ExternalInput").ap()
    for w in ("wq", "wk", "wv", "wout"):
        dd[w] = nc.dram_tensor(w, [D, D], F32, kind="ExternalInput").ap()
    for w in ("wgq", "wgk", "wgv"):
        dd[w] = nc.dram_tensor(w, [D, 512], F32, kind="ExternalInput").ap()
    dd["rout"] = nc.dram_tensor("rout", [512, D], F32, kind="ExternalInput").ap()
    dd["gwbd"] = nc.dram_tensor("gwbd", [D, 32], F32, kind="ExternalInput").ap()
    dd["ehot"] = nc.dram_tensor("ehot", [32, 8 * 128], F32, kind="ExternalInput").ap()
    dd["mu2"] = nc.dram_tensor("mu2", [D, 256], F32, kind="ExternalInput").ap()
    dd["w1s"] = nc.dram_tensor("w1s", [16, D, 256], F32, kind="ExternalInput").ap()
    dd["w2s"] = nc.dram_tensor("w2s", [16, 256, D], F32, kind="ExternalInput").ap()
    dd["b1T"] = nc.dram_tensor("b1T", [2, 128, 16], F32, kind="ExternalInput").ap()
    dd["b2T"] = nc.dram_tensor("b2T", [8, 128, 16], F32, kind="ExternalInput").ap()
    dd["gammab"] = nc.dram_tensor("gammab", [128, D], F32, kind="ExternalInput").ap()
    dd["betab"] = nc.dram_tensor("betab", [128, D], F32, kind="ExternalInput").ap()
    dd["scaleb"] = nc.dram_tensor("scaleb", [128, 1], F32, kind="ExternalInput").ap()
    if shared and "eye" in shared:
        dd["eye"] = shared["eye"]
        dd["ones1"] = shared["ones1"]
    else:
        dd["eye"] = nc.dram_tensor("eye", [128, 128], F32, kind="ExternalInput").ap()
        dd["ones1"] = nc.dram_tensor("ones1", [1, 512], F32, kind="ExternalInput").ap()
    dd["wmask"] = nc.dram_tensor("wmask", [16, 128, 512], F32, kind="ExternalInput").ap()
    dd["lmask"] = nc.dram_tensor("lmask", [16, 128, 64], F32, kind="ExternalInput").ap()
    dd["pmat"] = nc.dram_tensor("pmat", [32, 128, 64], F32, kind="ExternalInput").ap()
    dd["opre_sp"] = nc.dram_tensor("opre_sp", [NSLOT, 128, 16 * 128], BF16, kind="Internal").ap()
    dd["x2p"] = nc.dram_tensor("x2p", [TOWN, D], F32, kind="ExternalOutput").ap()
    return dd


def emit_l1(nc, es, P, dd, x1_row=None, x1own_row=None):
    if x1_row is None:
        x1_row = lambda tok0: dd["x1"][tok0:tok0 + 128, :]
    if x1own_row is None:
        x1own_row = lambda r0: dd["x1_own"][r0:r0 + 128, :]
    d_w = {w: dd[w] for w in ("wq", "wk", "wv", "wout")}
    d_wg = {w: dd[w] for w in ("wgq", "wgk", "wgv")}
    d_rout = dd["rout"]
    d_gwbd = dd["gwbd"]
    d_ehot = dd["ehot"]
    d_mu = dd["mu2"]
    d_w1s = dd["w1s"]
    d_w2s = dd["w2s"]
    d_b1T = dd["b1T"]
    d_b2T = dd["b2T"]
    d_gammab = dd["gammab"]
    d_betab = dd["betab"]
    d_scaleb = dd["scaleb"]
    d_eye = dd["eye"]
    d_ones = dd["ones1"]
    d_wmask = dd["wmask"]
    d_lmask = dd["lmask"]
    d_pmat = dd["pmat"]
    d_opsp = dd["opre_sp"]
    d_x2p = dd["x2p"]
    les = ExitStack()

    def sb(name, shape, dt):
        return les.enter_context(nc.sbuf_tensor(name, shape, dt))

    def ps(name, shape, dt):
        return les.enter_context(nc.psum_tensor(name, shape, dt))

    s_eye32 = sb("eye32L1", [128, 128], F32)
    s_ones = sb("ones_bL1", [1, 512], BF16)
    s_ones32 = sb("ones32rL1", [1, 128], F32)
    s_gammab = sb("gammab_s", [128, D], F32)
    s_betab = sb("betab_s", [128, D], F32)
    s_scaleb = sb("scaleb_s", [128, 1], F32)
    s_stat = sb("statL1", [128, 12], F32)
    s_xtile = sb("xtileL1", [128, D], F32)
    s_xn_b = sb("xn_bL1", [128, D], F32)
    s_xi = sb("xi", [128, D], F32)
    s_xu = sb("xu", [128, D], F32)
    s_js = sb("junk_sL1", [128, D], BF16)
    s_jv = sb("junk_vL1", [128, D], BF16)
    s_muu = sb("muu", [128, 8 * 256], BF16)
    s_pooled = sb("pooled", [64, D], F32)
    s_pooledT = sb("pooledT", [128, 8 * 64], BF16)
    s_GK = sb("GK", [64, 512], F32)
    s_GV = sb("GV", [64, 512], BF16)
    s_GKT = sb("GKT", [128, 4 * 64], BF16)
    s_gates = sb("gates", [128, NSLOT * 32], F32)
    s_so = sb("so", [128, 2 * D], BF16)
    s_eps = sb("epscolL1", [128, 1], F32)
    s_e12 = sb("e12col", [128, 1], F32)
    s_onescol = sb("onescolL1", [128, 1], F32)
    s_x2t = sb("x2t", [128, D], F32)

    pb = [ps(f"pbL1{i}", [128, 512], F32) for i in range(8)]
    B_ = [Buf() for _ in range(8)]
    Bxtile, Bxn, Bxi, Bxu, Bstat, Bjs, Bjv = (Buf() for _ in range(7))
    Bpooled, BpooledT, Bgates, Bx2t, Bmuu = (Buf() for _ in range(5))

    P.dma("sync", s_eye32[:], d_eye, sem="c0")
    P.dma("sync", s_gammab[:], d_gammab, sem="c0")
    P.dma("sync", s_betab[:], d_betab, sem="c0")
    P.dma("sync", s_scaleb[:], d_scaleb, sem="c0")
    v_hw0 = P.counts["c0"]
    P.dma("gpsimd", s_ones[:], d_ones)
    v_sw0 = P.counts["sw"]
    v_ones = P.op("vector", lambda e: e.memset(s_ones32[:], 1.0), inc="v")
    v_ones = P.op("vector", lambda e: e.memset(s_eps[:], 1e-5), inc="v")
    v_ones = P.op("vector", lambda e: e.memset(s_e12[:], 1e-12), inc="v")
    v_ones = P.op("vector", lambda e: e.memset(s_onescol[:], 1.0), inc="v")

    def ln_into(inbuf, outbuf, Bin, Bout, eps_ap, src_row=None, affine=False):
        wx = []
        if src_row is not None:
            vd = P.dma("sync", inbuf[:], src_row, waits=Bin.take())
            wx = [("hw", vd)]
        vm = P.op("vector", lambda e: e.tensor_scalar(
            s_jv[:, 0:D], inbuf[:], 1.0 / D, None, AluOpType.mult,
            AluOpType.add, accum_out=s_stat[:, 0:1]),
            waits=wx + Bjv.take() + Bstat.take(), inc="v")
        vs = P.op("scalar", lambda e: e.activation(
            s_js[:, 0:D], inbuf[:], AF.Square, accum_out=s_stat[:, 1:2]),
            waits=wx + Bjs.take(), inc="s")
        Bin.read_by("v", vm)
        Bin.read_by("s", vs)
        Bjv.read_by("v", vm)
        Bjs.read_by("s", vs)
        v1 = P.op("vector", lambda e: e.scalar_tensor_tensor(
            s_stat[:, 2:3], s_stat[:, 0:1], -1.0, s_stat[:, 0:1],
            AluOpType.mult, AluOpType.mult), waits=[("s", vs)], inc="v")
        v2 = P.op("vector", lambda e: e.scalar_tensor_tensor(
            s_stat[:, 3:4], s_stat[:, 1:2], 1.0 / D, s_stat[:, 2:3],
            AluOpType.mult, AluOpType.add), inc="v")
        s1 = P.op("scalar", lambda e: e.activation(
            s_stat[:, 4:5], s_stat[:, 3:4], AF.Sqrt, bias=eps_ap),
            waits=[("v", v2), ("v", v_ones)], inc="s")
        v3 = P.op("vector", lambda e: e.reciprocal(s_stat[:, 5:6], s_stat[:, 4:5]),
                  waits=[("s", s1)], inc="v")
        v4 = P.op("vector", lambda e: e.scalar_tensor_tensor(
            s_stat[:, 6:7], s_stat[:, 0:1], -1.0, s_stat[:, 5:6],
            AluOpType.mult, AluOpType.mult), inc="v")
        sx = P.op("scalar", lambda e: e.activation(
            outbuf[:], inbuf[:], AF.Identity, bias=s_stat[:, 6:7],
            scale=s_stat[:, 5:6]), waits=[("v", v4)] + Bout.take(), inc="s")
        Bin.read_by("s", sx)
        Bstat.read_by("s", sx)
        last = ("s", sx)
        if affine:
            vv = P.op("vector", lambda e: e.tensor_tensor(
                outbuf[:], outbuf[:], s_gammab[:], AluOpType.mult),
                waits=[("s", sx), ("c0", v_hw0)], inc="v")
            vv = P.op("vector", lambda e: e.tensor_tensor(
                outbuf[:], outbuf[:], s_betab[:], AluOpType.add), inc="v")
            last = ("v", vv)
        return last

    def l2unit(inbuf, outbuf, Bin, Bout, wait):
        ss = P.op("scalar", lambda e: e.activation(
            s_js[:, 0:D], inbuf[:], AF.Square, accum_out=s_stat[:, 7:8]),
            waits=[wait] + Bjs.take(), inc="s")
        Bjs.read_by("s", ss)
        Bin.read_by("s", ss)
        s2 = P.op("scalar", lambda e: e.activation(
            s_stat[:, 8:9], s_stat[:, 7:8], AF.Sqrt), waits=[("s", ss)], inc="s")
        vv = P.op("vector", lambda e: e.tensor_tensor(
            s_stat[:, 9:10], s_stat[:, 8:9], s_e12[:], AluOpType.max),
            waits=[("s", s2), ("v", v_ones)], inc="v")
        vv = P.op("vector", lambda e: e.reciprocal(s_stat[:, 10:11], s_stat[:, 9:10]),
                  inc="v")
        vv = P.op("vector", lambda e: e.tensor_tensor(
            s_stat[:, 10:11], s_stat[:, 10:11], s_scaleb[:], AluOpType.mult),
            waits=[("c0", v_hw0)], inc="v")
        sx = P.op("scalar", lambda e: e.activation(
            outbuf[:], inbuf[:], AF.Identity, scale=s_stat[:, 10:11]),
            waits=[("v", vv)] + Bout.take(), inc="s")
        Bin.read_by("s", sx)
        Bstat.read_by("s", sx)
        return sx

    def xpose(srcbuf, scol, destbuf, dstride, dcol, wait, nd=NDC, Bsrc=None):
        last = None
        for g4 in range((nd + 3) // 4):
            bank = 4 + g4 % 2
            tvs = None
            n_in_g = min(4, nd - g4 * 4)
            for slot in range(n_in_g):
                dt = g4 * 4 + slot
                tvs = P.op("tensor", lambda e, dt=dt, bank=bank, slot=slot: e.transpose(
                    pb[bank][:, slot * 128:slot * 128 + 128],
                    srcbuf[:, scol + dt * 128:scol + (dt + 1) * 128], s_eye32[:]),
                    waits=[wait, ("c0", v_hw0)] + (B_[bank].take() if slot == 0 else []),
                    inc="t")
            for slot in range(n_in_g):
                dt = g4 * 4 + slot
                sv = P.op("scalar", lambda e, dt=dt, bank=bank, slot=slot: e.copy(
                    destbuf[:, dstride * dt + dcol:dstride * dt + dcol + 128],
                    pb[bank][:, slot * 128:slot * 128 + 128]),
                    waits=[("t", tvs)], inc="s")
                B_[bank].read_by("s", sv)
                last = sv
        if Bsrc is not None:
            Bsrc.read_by("t", P.counts["t"])
        return last

    def mm_group(bank, cols, mms, waits=(), extra_first=(), rows=(0, 128)):
        n = len(mms)
        w0 = list(waits) + B_[bank].take() + list(extra_first)
        last = None
        for i, (fl, fr) in enumerate(mms):
            last = P.op("tensor",
                        lambda e, fl=fl, fr=fr, i=i: e.matmul(
                            pb[bank][rows[0]:rows[1], cols[0]:cols[1]], fl(), fr(),
                            start=(i == 0), stop=(i == n - 1)),
                        waits=(w0 if i == 0 else ()),
                        inc=("t" if i == n - 1 else None))
        return last

    pa = ExitStack()
    s_xni = pa.enter_context(nc.sbuf_tensor("xni", [128, NT * D], BF16))
    s_expT = pa.enter_context(nc.sbuf_tensor("expT", [128, 2 * T], F32))
    s_dsum = pa.enter_context(nc.sbuf_tensor("dsum", [128, 2 * 9], F32))
    pa2 = ExitStack()
    s_xuT = pa2.enter_context(nc.sbuf_tensor("xuT", [128, NDC * 512], BF16))
    s_mu = pa2.enter_context(nc.sbuf_tensor("mu_s", [128, 8 * 256], F32))
    s_msq = pa2.enter_context(nc.sbuf_tensor("msq", [128, 256], F32))
    s_pm = pa2.enter_context(nc.sbuf_tensor("pm", [128, 64], F32))
    s_nrow = pa2.enter_context(nc.sbuf_tensor("nrow", [1, 256], F32))
    s_recb = pa2.enter_context(nc.sbuf_tensor("recb", [128, 256], F32))
    Bxni, BexpT, BxuT, Bpm, Bmsq = (Buf() for _ in range(5))

    for dt in range(NDC):
        P.dma("sync", s_mu[:, dt * 256:(dt + 1) * 256], d_mu[dt * 128:(dt + 1) * 128, :],
              sem="c1")
    v_mu = P.counts["c1"]
    for dt in range(NDC):
        sq = P.op("scalar", lambda e, dt=dt: e.activation(
            s_msq[:], s_mu[:, dt * 256:(dt + 1) * 256], AF.Square),
            waits=[("c1", v_mu)] + Bmsq.take(), inc="s")
        tv = P.op("tensor", lambda e, dt=dt: e.matmul(
            pb[6][0:1, 0:256], s_onescol[:], s_msq[:],
            start=(dt == 0), stop=(dt == NDC - 1)),
            waits=[("s", sq), ("v", v_ones)] + (B_[6].take() if dt == 0 else []),
            inc="t")
        Bmsq.read_by("t", tv)
    sv = P.op("scalar", lambda e: e.activation(s_nrow[0:1, :], pb[6][0:1, 0:256],
                                               AF.Sqrt), waits=[("t", tv)], inc="s")
    B_[6].read_by("s", sv)
    vv = P.op("vector", lambda e: e.tensor_scalar(
        s_nrow[0:1, :], s_nrow[0:1, :], 1e-12, None, AluOpType.max),
        waits=[("s", sv)], inc="v")
    vv = P.op("vector", lambda e: e.reciprocal(s_nrow[0:1, :], s_nrow[0:1, :]), inc="v")
    tv = P.op("tensor", lambda e: e.matmul(
        pb[6][:, 256:512], s_ones32[0:1, 0:128], s_nrow[0:1, 0:256],
        start=True, stop=True), waits=[("v", vv)] + B_[6].take(), inc="t")
    sv = P.op("scalar", lambda e: e.copy(s_recb[:], pb[6][:, 256:512]),
              waits=[("t", tv)], inc="s")
    B_[6].read_by("s", sv)
    for dt in range(NDC):
        vv = P.op("vector", lambda e, dt=dt: e.tensor_tensor(
            s_muu[:, dt * 256:(dt + 1) * 256], s_mu[:, dt * 256:(dt + 1) * 256],
            s_recb[:], AluOpType.mult), waits=[("s", sv)] + (Bmuu.take() if dt == 0 else []),
            inc="v")
    v_muu = P.counts["v"]

    for tt in range(NT):
        vp = P.dma("sync", s_pm[:], d_pmat[tt], waits=Bpm.take(), sem="pm")
        wl = ln_into(s_xtile, s_xn_b, Bxtile, Bxn, s_eps[:],
                     src_row=x1_row(tt * 128))
        for dc in range(2):
            tvp = P.op("tensor", lambda e, tt=tt, dc=dc: e.matmul(
                pb[2 + dc][0:64, 0:512], s_pm[:],
                s_xn_b[:, dc * 512:(dc + 1) * 512],
                start=(tt == 0), stop=(tt == NT - 1)),
                waits=[("pm", vp), wl] + (B_[2 + dc].take() if tt == 0 else []),
                inc=("t" if tt == NT - 1 else None))
        Bpm.read_by("t", P.counts["t"] if tt == NT - 1 else P.counts["t"] + 1)
        Bxn.read_by("t", P.counts["t"])
        wi = ln_into(s_xn_b, s_xi, Bxn, Bxi, s_eps[:], affine=True)
        sc = P.op("scalar", lambda e, tt=tt: e.copy(
            s_xni[:, tt * D:(tt + 1) * D], s_xi[:]),
            waits=[wi] + (Bxni.take() if tt == 0 else []), inc="s")
        Bxi.read_by("s", sc)
        su = l2unit(s_xi, s_xu, Bxi, Bxu, ("s", sc))
        ti = tt % 4
        xpose(s_xu, 0, s_xuT, 512, ti * 128, ("s", su), Bsrc=Bxu)
        if ti == 3:
            nchunk = tt // 4
            vxt = P.counts["s"]
            for esc in range(2):
                bank = esc
                mms = [(lambda dt=dt, esc=esc: s_muu[:, dt * 256 + esc * 128:dt * 256 + esc * 128 + 128],
                        lambda dt=dt: s_xuT[:, dt * 512:(dt + 1) * 512])
                       for dt in range(NDC)]
                tvl = mm_group(bank, (0, 512), mms, waits=[("s", vxt), ("v", v_muu)])
                se = P.op("scalar", lambda e, esc=esc, nchunk=nchunk, bank=bank: e.activation(
                    s_expT[:, esc * T + nchunk * 512:esc * T + nchunk * 512 + 512],
                    pb[bank][:, 0:512], AF.Exp,
                    accum_out=s_dsum[:, esc * 9 + nchunk:esc * 9 + nchunk + 1]),
                    waits=[("t", tvl)] + (BexpT.take() if nchunk == 0 and esc == 0 else []),
                    inc="s")
                B_[bank].read_by("s", se)
            BxuT.read_by("t", P.counts["t"])
    v_exp = P.counts["s"]
    for esc in range(2):
        vv = P.op("vector", lambda e, esc=esc: e.tensor_reduce(
            s_dsum[:, esc * 9 + 8:esc * 9 + 9], s_dsum[:, esc * 9:esc * 9 + 8],
            AX.X, AluOpType.add), waits=[("s", v_exp)], inc="v")
        vv = P.op("vector", lambda e, esc=esc: e.reciprocal(
            s_dsum[:, esc * 9 + 8:esc * 9 + 9], s_dsum[:, esc * 9 + 8:esc * 9 + 9]),
            inc="v")
        vv = P.op("vector", lambda e, esc=esc: e.tensor_scalar(
            s_expT[:, esc * T:esc * T + T], s_expT[:, esc * T:esc * T + T],
            s_dsum[:, esc * 9 + 8:esc * 9 + 9], None, AluOpType.mult), inc="v")
    v_dispT = P.counts["v"]
    for dc in range(2):
        sv = P.op("scalar", lambda e, dc=dc: e.copy(
            s_pooled[0:64, dc * 512:(dc + 1) * 512], pb[2 + dc][0:64, 0:512]),
            waits=[("t", P.counts["t"])] + (Bpooled.take() if dc == 0 else []), inc="s")
        B_[2 + dc].read_by("s", sv)
    v_pool = P.counts["s"]

    P.phase_barrier()
    pa2.close()
    pbx = ExitStack()
    s_disp = pbx.enter_context(nc.sbuf_tensor("disp", [128, NT * 256], BF16))
    s_sin = pbx.enter_context(nc.sbuf_tensor("sin", [128, 2 * D], F32))
    s_sinT = pbx.enter_context(nc.sbuf_tensor("sinT", [128, NDC * 256], BF16))
    s_hT = pbx.enter_context(nc.sbuf_tensor("hT", [128, 2 * 16], BF16))
    s_soT = pbx.enter_context(nc.sbuf_tensor("soT", [128, NDC * 256], F32))
    s_w1l = pbx.enter_context(nc.sbuf_tensor("w1l", [128, NDC * 256], BF16))
    s_w2l = pbx.enter_context(nc.sbuf_tensor("w2l", [128, 2 * D], BF16))
    s_b1T = pbx.enter_context(nc.sbuf_tensor("b1T_s", [128, 2 * 16], F32))
    s_b2T = pbx.enter_context(nc.sbuf_tensor("b2T_s", [128, 8 * 16], F32))
    s_wgl = pbx.enter_context(nc.sbuf_tensor("wgl", [128, NDC * 512], BF16))
    Bdisp, Bsin, BsinT, BhT, BsoT, Bw1l, Bw2l = (Buf() for _ in range(7))
    P.phase_barrier()
    for g4 in range(2):
        bank = 4 + g4
        tvs = None
        for slot in range(4):
            dt = g4 * 4 + slot
            tvs = P.op("tensor", lambda e, dt=dt, bank=bank, slot=slot: e.transpose(
                pb[bank][:, slot * 128:slot * 128 + 64],
                s_pooled[0:64, dt * 128:(dt + 1) * 128], s_eye32[0:64, 0:64]),
                waits=[("s", v_pool), ("c0", v_hw0)] +
                      (B_[bank].take() if slot == 0 else []), inc="t")
        for slot in range(4):
            dt = g4 * 4 + slot
            sv = P.op("scalar", lambda e, dt=dt, bank=bank, slot=slot: e.copy(
                s_pooledT[:, dt * 64:(dt + 1) * 64],
                pb[bank][:, slot * 128:slot * 128 + 64]),
                waits=[("t", tvs)], inc="s")
            B_[bank].read_by("s", sv)
    Bpooled.read_by("t", P.counts["t"])
    v_pt = P.counts["s"]
    Bwgl = Buf()
    for wname, dest in (("wgk", s_GK), ("wgv", s_GV)):
        wgd = Bwgl.take()
        for dt in range(NDC):
            P.dma("gpsimd", s_wgl[:, dt * 512:(dt + 1) * 512],
                  d_wg[wname][dt * 128:(dt + 1) * 128, :],
                  waits=(wgd if dt == 0 else []))
        v_w = P.counts["sw"]
        mms = [(lambda dt=dt: s_pooledT[:, dt * 64:(dt + 1) * 64],
                lambda dt=dt: s_wgl[:, dt * 512:(dt + 1) * 512])
               for dt in range(NDC)]
        tv = mm_group(1, (0, 512), mms, waits=[("sw", v_w), ("s", v_pt)], rows=(0, 64))
        Bwgl.read_by("t", tv)
        sv = P.op("scalar", lambda e, dest=dest: e.copy(dest[:], pb[1][0:64, 0:512]),
                  waits=[("t", tv)], inc="s")
        B_[1].read_by("s", sv)
    v_gkv = P.counts["s"]
    tvs = None
    for lc in range(4):
        tvs = P.op("tensor", lambda e, lc=lc: e.transpose(
            pb[4][:, lc * 128:lc * 128 + 64],
            s_GK[0:64, lc * 128:lc * 128 + 128], s_eye32[0:64, 0:64]),
            waits=[("s", v_gkv)] + (B_[4].take() if lc == 0 else []), inc="t")
    for lc in range(4):
        sv = P.op("scalar", lambda e, lc=lc: e.copy(
            s_GKT[:, lc * 64:lc * 64 + 64], pb[4][:, lc * 128:lc * 128 + 64]),
            waits=[("t", tvs)], inc="s")
        B_[4].read_by("s", sv)
    v_gkt = P.counts["s"]

    for tt in range(NT):
        for esc in range(2):
            bank = 4 + esc
            tv = P.op("tensor", lambda e, tt=tt, esc=esc, bank=bank: e.transpose(
                pb[bank][:, 0:128], s_expT[:, esc * T + tt * 128:esc * T + tt * 128 + 128],
                s_eye32[:]), waits=[("v", v_dispT)] + B_[bank].take(), inc="t")
            sv = P.op("scalar", lambda e, tt=tt, esc=esc, bank=bank: e.copy(
                s_disp[:, tt * 256 + esc * 128:tt * 256 + esc * 128 + 128],
                pb[bank][:, 0:128]),
                waits=[("t", tv)] + (Bdisp.take() if tt == 0 and esc == 0 else []),
                inc="s")
            B_[bank].read_by("s", sv)
    BexpT.read_by("t", P.counts["t"])
    v_dispn = P.counts["s"]
    for esc in range(2):
        for dc in range(2):
            bank = esc * 2 + dc
            mms = [(lambda tt=tt, esc=esc: s_disp[:, tt * 256 + esc * 128:tt * 256 + esc * 128 + 128],
                    lambda tt=tt, dc=dc: s_xni[:, tt * D + dc * 512:tt * D + dc * 512 + 512])
                   for tt in range(NT)]
            tv = mm_group(bank, (0, 512), mms, waits=[("s", v_dispn)])
            sv = P.op("scalar", lambda e, esc=esc, dc=dc, bank=bank: e.copy(
                s_sin[:, esc * D + dc * 512:esc * D + dc * 512 + 512],
                pb[bank][:, 0:512]),
                waits=[("t", tv)] + (Bsin.take() if esc == 0 and dc == 0 else []), inc="s")
            B_[bank].read_by("s", sv)
    v_sin = P.counts["s"]
    for esc in range(2):
        xpose(s_sin, esc * D, s_sinT, 256, esc * 128, ("s", v_sin), Bsrc=Bsin)
    v_sinT = P.counts["s"]
    for hc in range(2):
        P.dma("sync", s_b1T[:, hc * 16:(hc + 1) * 16], d_b1T[hc], sem="c2")
    for dt in range(8):
        P.dma("sync", s_b2T[:, dt * 16:(dt + 1) * 16], d_b2T[dt], sem="c2")
    v_bT = P.counts["c2"]
    Bso = Buf()
    for ei in range(16):
        w1d = Bw1l.take()
        lv1 = None
        for dt in range(NDC):
            lv1 = P.dma("gpsimd", s_w1l[:, dt * 256:(dt + 1) * 256],
                        d_w1s[ei][dt * 128:(dt + 1) * 128, :],
                        waits=(w1d if dt == 0 else []), sem="w1")
        w2d = Bw2l.take()
        lv2 = None
        for hc in range(2):
            lv2 = P.dma("gpsimd", s_w2l[:, hc * D:(hc + 1) * D],
                        d_w2s[ei][hc * 128:(hc + 1) * 128, :],
                        waits=(w2d if hc == 0 else []), sem="w2a")
        hd = BhT.take()
        for hc in range(2):
            bank = hc
            mms = [(lambda dt=dt, hc=hc: s_w1l[:, dt * 256 + hc * 128:dt * 256 + hc * 128 + 128],
                    lambda dt=dt, ei=ei: s_sinT[:, dt * 256 + ei * 16:dt * 256 + ei * 16 + 16])
                   for dt in range(NDC)]
            tv = mm_group(bank, (0, 16), mms,
                          waits=[("w1", lv1), ("s", v_sinT), ("c2", v_bT)],
                          extra_first=(hd if hc == 0 else ()))
            Bw1l.read_by("t", tv)
            BsinT.read_by("t", tv)
            sv = P.op("scalar", lambda e, hc=hc, ei=ei, bank=bank: e.activation(
                s_hT[:, hc * 16:(hc + 1) * 16], pb[bank][:, 0:16], AF.Gelu,
                bias=s_b1T[:, hc * 16 + ei:hc * 16 + ei + 1]),
                waits=[("t", tv)], inc="s")
            B_[bank].read_by("s", sv)
        v_h = P.counts["s"]
        for dt in range(NDC):
            bank = 2 + dt % 2
            mms = [(lambda hc=hc, dt=dt: s_w2l[:, hc * D + dt * 128:hc * D + dt * 128 + 128],
                    lambda hc=hc: s_hT[:, hc * 16:(hc + 1) * 16])
                   for hc in range(2)]
            tv = mm_group(bank, (0, 16), mms, waits=[("w2a", lv2), ("s", v_h)])
            Bw2l.read_by("t", tv)
            BhT.read_by("t", tv)
            sv = P.op("scalar", lambda e, dt=dt, ei=ei, bank=bank: e.activation(
                s_soT[:, dt * 256 + ei * 16:dt * 256 + ei * 16 + 16],
                pb[bank][:, 0:16], AF.Identity,
                bias=s_b2T[:, dt * 16 + ei:dt * 16 + ei + 1]),
                waits=[("t", tv)] + (BsoT.take() if ei == 0 and dt == 0 else []),
                inc="s")
            B_[bank].read_by("s", sv)
    v_soT = P.counts["s"]
    for dt in range(NDC):
        bank = 4 + (dt % 2)
        tvs = None
        for esc in range(2):
            tvs = P.op("tensor", lambda e, dt=dt, esc=esc, bank=bank: e.transpose(
                pb[bank][:, esc * 128:esc * 128 + 128],
                s_soT[:, dt * 256 + esc * 128:dt * 256 + esc * 128 + 128],
                s_eye32[:]),
                waits=[("s", v_soT)] + (B_[bank].take() if esc == 0 else []), inc="t")
        for esc in range(2):
            sv = P.op("scalar", lambda e, dt=dt, esc=esc, bank=bank: e.copy(
                s_so[:, esc * D + dt * 128:esc * D + dt * 128 + 128],
                pb[bank][:, esc * 128:esc * 128 + 128]),
                waits=[("t", tvs)] + (Bso.take() if dt == 0 and esc == 0 else []),
                inc="s")
            B_[bank].read_by("s", sv)
    BsoT.read_by("t", P.counts["t"])
    P.phase_barrier()
    pbx.close()
    pa.close()

    PADC = 256
    KSTR = T + PADC
    cd = ExitStack()
    s_xnT_own = cd.enter_context(nc.sbuf_tensor("xnT_ownL1", [128, NDC * TOWN], BF16))
    s_K1 = cd.enter_context(nc.sbuf_tensor("K1", [128, 4 * KSTR], BF16))
    s_V1 = cd.enter_context(nc.sbuf_tensor("VL1", [128, 34 * 512], BF16))
    s_xnTc = cd.enter_context(nc.sbuf_tensor("xnTcL1", [128, NDC * 512], BF16))
    s_wkh = cd.enter_context(nc.sbuf_tensor("wkh", [128, NDC * 512], BF16))
    s_wvh = cd.enter_context(nc.sbuf_tensor("wvh", [128, NDC * 512], BF16))
    s_rout = cd.enter_context(nc.sbuf_tensor("rout_s", [128, 4 * D], BF16))
    s_wgq = cd.enter_context(nc.sbuf_tensor("wgq_s", [128, NDC * 512], BF16))
    s_gwb = cd.enter_context(nc.sbuf_tensor("gwb", [128, 4 * 32], BF16))
    s_QT = cd.enter_context(nc.sbuf_tensor("QTL1", [128, 4 * 128], BF16))
    s_GQT = cd.enter_context(nc.sbuf_tensor("GQT", [128, 4 * 128], BF16))
    s_sc = cd.enter_context(nc.sbuf_tensor("scL1", [128, 512], F32))
    s_pw = cd.enter_context(nc.sbuf_tensor("pw", [128, 512], F32))
    s_pT = cd.enter_context(nc.sbuf_tensor("pT", [128, 4 * 128], BF16))
    s_ls = cd.enter_context(nc.sbuf_tensor("ls", [128, 64], F32))
    s_lse = cd.enter_context(nc.sbuf_tensor("lse", [128, 64], F32))
    s_lpT = cd.enter_context(nc.sbuf_tensor("lpT", [64, 128], BF16))
    s_remT = cd.enter_context(nc.sbuf_tensor("remT", [128, 4 * 128], BF16))
    s_opst = cd.enter_context(nc.sbuf_tensor("opst", [128, 4 * 128], BF16))
    s_oprem = cd.enter_context(nc.sbuf_tensor("oprem", [128, 8 * 128], BF16))
    s_wmk = cd.enter_context(nc.sbuf_tensor("wmk", [128, 512], F32))
    s_lmk = cd.enter_context(nc.sbuf_tensor("lmk", [128, 64], F32))
    BK1, BV1, BxnTc1, BQT1, BGQT, Bsc, Bpw, BpT, Bwk = (Buf() for _ in range(9))
    Bls, Blse, BlpT, BremT, Bopst, Boprem, Bwmk, Blmk = (Buf() for _ in range(8))

    P.phase_barrier()
    for j in range(NSLOT):
        wl = ln_into(s_xtile, s_xn_b, Bxtile, Bxn, s_eps[:],
                     src_row=x1own_row(j * 128))
        xpose(s_xn_b, 0, s_xnT_own, TOWN, j * 128, wl, Bsrc=Bxn)
    v_own = P.counts["s"]

    for half in (0, 1):
        P.phase_barrier()
        wkd = Bwk.take()
        for dt in range(NDC):
            P.dma("gpsimd", s_wkh[:, dt * 512:(dt + 1) * 512],
                  d_w["wk"][dt * 128:(dt + 1) * 128, half * 512:half * 512 + 512],
                  waits=(wkd if dt == 0 else []))
            P.dma("gpsimd", s_wvh[:, dt * 512:(dt + 1) * 512],
                  d_w["wv"][dt * 128:(dt + 1) * 128, half * 512:half * 512 + 512])
        if half == 0:
            for dt in range(NDC):
                P.dma("gpsimd", s_wgq[:, dt * 512:(dt + 1) * 512],
                      d_wg["wgq"][dt * 128:(dt + 1) * 128, :])
            for lc in range(4):
                P.dma("gpsimd", s_rout[:, lc * D:(lc + 1) * D],
                      d_rout[lc * 128:(lc + 1) * 128, :])
        for ec in range(4):
            P.dma("gpsimd", s_gwb[:, ec * 32:(ec + 1) * 32],
                  d_gwbd[half * 512 + ec * 128:half * 512 + ec * 128 + 128, :])
        v_wh = P.counts["sw"]
        vg8 = P.op("vector", lambda e: e.tensor_scalar(
            s_gwb[:], s_gwb[:], 8.0, None, AluOpType.mult),
            waits=[("sw", v_wh)], inc="v")
        vz = P.op("vector", lambda e: e.memset(s_V1[:, 0:2 * 512], 0.0),
                  waits=BV1.take(), inc="v")
        for ec in range(4):
            vz = P.op("vector", lambda e, ec=ec: e.memset(
                s_K1[:, ec * KSTR:ec * KSTR + PADC], 0.0),
                waits=(BK1.take() if ec == 0 else []), inc="v")
        for tc in range(8):
            for ti in range(4):
                tok0 = tc * 512 + ti * 128
                wl = ln_into(s_xtile, s_xn_b, Bxtile, Bxn, s_eps[:],
                             src_row=x1_row(tok0))
                xpose(s_xn_b, 0, s_xnTc, 512, ti * 128, wl, Bsrc=Bxn)
            vxn = P.counts["s"]
            wxn = [("s", vxn), ("sw", v_wh), ("v", vz)]
            for ec in range(4):
                bank = ec % 2
                mms = [(lambda dt=dt, ec=ec: s_wkh[:, dt * 512 + ec * 128:dt * 512 + ec * 128 + 128],
                        lambda dt=dt: s_xnTc[:, dt * 512:(dt + 1) * 512])
                       for dt in range(NDC)]
                tv = mm_group(bank, (0, 512), mms, waits=wxn)
                Bwk.read_by("t", tv)
                sv = P.op("scalar", lambda e, ec=ec, tc=tc, bank=bank: e.copy(
                    s_K1[:, ec * KSTR + PADC + tc * 512:ec * KSTR + PADC + tc * 512 + 512],
                    pb[bank][:, 0:512]), waits=[("t", tv)], inc="s")
                B_[bank].read_by("s", sv)
            for ti in range(4):
                bank = 2 + ti % 2
                tt = tc * 4 + ti
                mms = [(lambda dt=dt, ti=ti: s_xnTc[:, dt * 512 + ti * 128:dt * 512 + ti * 128 + 128],
                        lambda dt=dt: s_wvh[:, dt * 512:(dt + 1) * 512])
                       for dt in range(NDC)]
                tv = mm_group(bank, (0, 512), mms, waits=wxn)
                sv = P.op("scalar", lambda e, tt=tt, bank=bank: e.copy(
                    s_V1[:, (2 + tt) * 512:(2 + tt) * 512 + 512], pb[bank][:, 0:512]),
                    waits=[("t", tv)], inc="s")
                B_[bank].read_by("s", sv)
            BxnTc1.read_by("t", P.counts["t"])
        v_kv = P.counts["s"]
        wkd = Bwk.take()
        for dt in range(NDC):
            P.dma("gpsimd", s_wkh[:, dt * 512:(dt + 1) * 512],
                  d_w["wq"][dt * 128:(dt + 1) * 128, half * 512:half * 512 + 512],
                  waits=(wkd if dt == 0 else []))
        v_wq = P.counts["sw"]

        for j in range(NSLOT):
            vmk = P.dma("sync", s_wmk[:], d_wmask[j], waits=Bwmk.take(), sem="wm")
            if half == 0:
                vlk = P.dma("sync", s_lmk[:], d_lmask[j], waits=Blmk.take(), sem="lm")
            qd = BQT1.take()
            tvq = None
            for ec in range(4):
                mms = [(lambda dt=dt, ec=ec: s_wkh[:, dt * 512 + ec * 128:dt * 512 + ec * 128 + 128],
                        lambda dt=dt, j=j: s_xnT_own[:, dt * TOWN + j * 128:dt * TOWN + j * 128 + 128])
                       for dt in range(NDC)]
                tvq = mm_group(6, (ec * 128, ec * 128 + 128), mms,
                               waits=[("sw", v_wq), ("s", v_own)],
                               extra_first=(qd if ec == 0 else ()))
                Bwk.read_by("t", tvq)
            for ec in range(4):
                sv = P.op("scalar", lambda e, ec=ec: e.activation(
                    s_QT[:, ec * 128:ec * 128 + 128],
                    pb[6][:, ec * 128:ec * 128 + 128], AF.Copy, scale=0.125),
                    waits=[("t", tvq)], inc="s")
                B_[6].read_by("s", sv)
            v_qt = P.counts["s"]
            mms = [(lambda ec=ec: s_QT[:, ec * 128:ec * 128 + 128],
                    lambda ec=ec: s_gwb[:, ec * 32:(ec + 1) * 32])
                   for ec in range(4)]
            tv = mm_group(6, (256, 288), mms, waits=[("s", v_qt), ("v", vg8)])
            BQT1.read_by("t", tv)
            for part in (0, 16):
                sv = P.op("scalar", lambda e, part=part, j=j, half=half: e.activation(
                    s_gates[:, j * 32 + part + half * 8:j * 32 + part + half * 8 + 8],
                    pb[6][:, 256 + part + half * 8:256 + part + half * 8 + 8],
                    AF.Sigmoid),
                    waits=[("t", tv)] + (Bgates.take() if part == 0 else []), inc="s")
                B_[6].read_by("s", sv)
            v_gate = P.counts["s"]
            od = Bopst.take()
            for hl in range(8):
                ec, po = hl // 2, 64 * (hl % 2)
                kcol0 = ec * KSTR + PADC + 128 * (2 * j - 2)
                bank = hl % 2
                tv = mm_group(bank, (0, 512),
                              [(lambda ec=ec, po=po: s_QT[po:po + 64, ec * 128:ec * 128 + 128],
                                lambda po=po, kcol0=kcol0: s_K1[po:po + 64, kcol0:kcol0 + 512])],
                              waits=[("s", v_kv), ("s", v_qt)])
                ev = P.op("vector", lambda e, bank=bank: e.tensor_tensor(
                    s_sc[:], pb[bank][:, 0:512], s_wmk[:], AluOpType.add),
                    waits=[("t", tv), ("wm", vmk)] + Bsc.take(), inc="v")
                B_[bank].read_by("v", ev)
                Bwmk.read_by("v", ev)
                vm = P.op("vector", lambda e: e.tensor_reduce(
                    s_stat[:, 11:12], s_sc[:], AX.X, AluOpType.max, negate=True),
                    waits=Bstat.take(), inc="v")
                se = P.op("scalar", lambda e: e.activation(
                    s_pw[:], s_sc[:], AF.Exp, bias=s_stat[:, 11:12],
                    accum_out=s_stat[:, 10:11]),
                    waits=[("v", vm)] + Bpw.take(), inc="s")
                Bsc.read_by("s", se)
                Bstat.read_by("s", se)
                vr = P.op("vector", lambda e: e.reciprocal(
                    s_stat[:, 9:10], s_stat[:, 10:11]), waits=[("s", se)], inc="v")
                vr = P.op("vector", lambda e, j=j, hl=hl, half=half: e.tensor_tensor(
                    s_stat[:, 9:10], s_stat[:, 9:10],
                    s_gates[:, j * 32 + half * 8 + hl:j * 32 + half * 8 + hl + 1],
                    AluOpType.mult), waits=[("s", v_gate)], inc="v")
                va = P.op("vector", lambda e: e.tensor_scalar(
                    s_pw[:], s_pw[:], s_stat[:, 9:10], None, AluOpType.mult),
                    inc="v")
                Bpw.read_by("v", va)
                Bstat.read_by("v", va)
                Bgates.read_by("v", va)
                vpt = xpose(s_pw, 0, s_pT, 128, 0, ("v", va), nd=4, Bsrc=Bpw)
                mms = []
                for kt in range(4):
                    vtile = 2 * j + kt
                    mms.append((lambda vtile=vtile, hl=hl: s_V1[:, vtile * 512 + hl * 64:vtile * 512 + hl * 64 + 64],
                                lambda kt=kt: s_pT[:, kt * 128:kt * 128 + 128]))
                tv = mm_group(2, (0, 128), mms,
                              waits=[("s", vpt), ("s", v_kv)], rows=(0, 64))
                BpT.read_by("t", tv)
                BV1.read_by("t", tv)
                BK1.read_by("t", tv)
                dl = hl // 2
                sv = P.op("scalar", lambda e, dl=dl, po=po: e.copy(
                    s_opst[po:po + 64, dl * 128:dl * 128 + 128], pb[2][0:64, 0:128]),
                    waits=[("t", tv)] + (od if hl == 0 else []), inc="s")
                B_[2].read_by("s", sv)
            v_loc = P.counts["s"]
            vsp = P.dma("sync", d_opsp[j][:, (12 * half) * 128:(12 * half) * 128 + 4 * 128],
                        s_opst[:], waits=[("s", v_loc)], sem="osp")
            if half == 1:
                Bopst.read_by("osp", P.counts["osp"])
            if half == 0:
                gqd = BGQT.take()
                tvg = None
                for lc in range(4):
                    mms = [(lambda dt=dt, lc=lc: s_wgq[:, dt * 512 + lc * 128:dt * 512 + lc * 128 + 128],
                            lambda dt=dt, j=j: s_xnT_own[:, dt * TOWN + j * 128:dt * TOWN + j * 128 + 128])
                           for dt in range(NDC)]
                    tvg = mm_group(5, (lc * 128, lc * 128 + 128), mms,
                                   waits=[("sw", v_wh), ("s", v_own)],
                                   extra_first=(gqd if lc == 0 else ()))
                for lc in range(4):
                    sv = P.op("scalar", lambda e, lc=lc: e.activation(
                        s_GQT[:, lc * 128:lc * 128 + 128],
                        pb[5][:, lc * 128:lc * 128 + 128], AF.Copy, scale=0.125),
                        waits=[("t", tvg)], inc="s")
                    B_[5].read_by("s", sv)
                v_gq = P.counts["s"]
                rd = BremT.take()
                for lh in range(8):
                    lc_l, po_l = lh // 2, 64 * (lh % 2)
                    tv = mm_group(3, (0, 64),
                                  [(lambda lc_l=lc_l, po_l=po_l: s_GQT[po_l:po_l + 64, lc_l * 128:lc_l * 128 + 128],
                                    lambda lc_l=lc_l, po_l=po_l: s_GKT[po_l:po_l + 64, lc_l * 64:lc_l * 64 + 64])],
                                  waits=[("s", v_gq), ("s", v_gkt)])
                    ev = P.op("vector", lambda e: e.tensor_tensor(
                        s_ls[:], pb[3][:, 0:64], s_lmk[:], AluOpType.add),
                        waits=[("t", tv), ("lm", vlk)] + Bls.take(), inc="v")
                    B_[3].read_by("v", ev)
                    Blmk.read_by("v", ev)
                    vm = P.op("vector", lambda e: e.tensor_reduce(
                        s_stat[:, 11:12], s_ls[:], AX.X, AluOpType.max, negate=True),
                        waits=Bstat.take(), inc="v")
                    se = P.op("scalar", lambda e: e.activation(
                        s_lse[:], s_ls[:], AF.Exp, bias=s_stat[:, 11:12],
                        accum_out=s_stat[:, 10:11]),
                        waits=[("v", vm)] + Blse.take(), inc="s")
                    Bls.read_by("s", se)
                    Bstat.read_by("s", se)
                    vr = P.op("vector", lambda e: e.reciprocal(
                        s_stat[:, 9:10], s_stat[:, 10:11]), waits=[("s", se)], inc="v")
                    va = P.op("vector", lambda e: e.tensor_scalar(
                        s_lse[:], s_lse[:], s_stat[:, 9:10], None, AluOpType.mult),
                        inc="v")
                    Bstat.read_by("v", va)
                    tv = P.op("tensor", lambda e: e.transpose(
                        pb[4][0:64, 0:128], s_lse[:], s_eye32[:]),
                        waits=[("v", va)] + B_[4].take(), inc="t")
                    Blse.read_by("t", tv)
                    sv = P.op("scalar", lambda e: e.copy(s_lpT[0:64, :],
                                                         pb[4][0:64, 0:128]),
                              waits=[("t", tv)] + BlpT.take(), inc="s")
                    B_[4].read_by("s", sv)
                    tv = mm_group(3, (128, 256),
                                  [(lambda lh=lh: s_GV[0:64, lh * 64:lh * 64 + 64],
                                    lambda: s_lpT[0:64, 0:128])],
                                  waits=[("s", P.counts["s"])], rows=(0, 64))
                    BlpT.read_by("t", tv)
                    sv = P.op("scalar", lambda e, lc_l=lc_l, po_l=po_l: e.copy(
                        s_remT[po_l:po_l + 64, lc_l * 128:lc_l * 128 + 128],
                        pb[3][0:64, 128:256]),
                        waits=[("t", tv)] + (rd if lh == 0 else []), inc="s")
                    B_[3].read_by("s", sv)
                v_rem = P.counts["s"]
                omd = Boprem.take()
                for ec in range(8):
                    bank = ec % 2
                    mms = [(lambda lc=lc, ec=ec: s_rout[:, lc * D + ec * 128:lc * D + ec * 128 + 128],
                            lambda lc=lc: s_remT[:, lc * 128:lc * 128 + 128])
                           for lc in range(4)]
                    tv = mm_group(bank, (0, 128), mms,
                                  waits=[("s", v_rem), ("sw", v_wh)],
                                  extra_first=(omd if ec == 0 else ()))
                    BremT.read_by("t", tv)
                    sv = P.op("scalar", lambda e, ec=ec, bank=bank: e.copy(
                        s_oprem[:, ec * 128:ec * 128 + 128], pb[bank][:, 0:128]),
                        waits=[("t", tv)], inc="s")
                    B_[bank].read_by("s", sv)
                v_orem = P.counts["s"]
                vsp = P.dma("sync", d_opsp[j][:, 4 * 128:12 * 128], s_oprem[:],
                            waits=[("s", v_orem)], sem="osp")
                Boprem.read_by("osp", P.counts["osp"])
                Bopst.read_by("osp", P.counts["osp"])
    v_gate_all = P.counts["s"]
    P.phase_barrier()
    cd.close()

    pe = ExitStack()
    s_wout = pe.enter_context(nc.sbuf_tensor("wout_s", [128, NDC * D], BF16))
    s_ehot = pe.enter_context(nc.sbuf_tensor("ehot_s", [32, 8 * 128], F32))
    s_xuTo = pe.enter_context(nc.sbuf_tensor("xuTo", [128, NDC * 128], BF16))
    s_ce = pe.enter_context(nc.sbuf_tensor("ce", [128, 256], F32))
    s_cT = pe.enter_context(nc.sbuf_tensor("cT", [128, 2 * 128], BF16))
    s_g1 = pe.enter_context(nc.sbuf_tensor("g1", [128, D], F32))
    s_opl = pe.enter_context(nc.sbuf_tensor("opl", [128, 16 * 128], BF16))
    s_gT = pe.enter_context(nc.sbuf_tensor("gT", [32, 128], F32))
    s_grb = pe.enter_context(nc.sbuf_tensor("grb", [128, 128], F32))
    s_otmp = pe.enter_context(nc.sbuf_tensor("otmp", [128, 128], F32))
    s_opreb = pe.enter_context(nc.sbuf_tensor("opreb", [128, 8 * 128], BF16))
    s_s1t = pe.enter_context(nc.sbuf_tensor("s1t", [128, D], F32))
    BxuTo, Bce, BcT, Bg1, Bopl, BgT, Bgrb, Botmp, Bopreb, Bs1t = (Buf() for _ in range(10))
    P.phase_barrier()
    for dt in range(NDC):
        P.dma("gpsimd", s_wout[:, dt * D:(dt + 1) * D],
              d_w["wout"][dt * 128:(dt + 1) * 128, :])
    v_wo = P.counts["sw"]
    v_eh = P.dma("sync", s_ehot[:], d_ehot, sem="lm")
    v_allsp = P.counts["osp"]

    for j in range(NSLOT):
        wl = ln_into(s_xtile, s_xn_b, Bxtile, Bxn, s_eps[:],
                     src_row=x1own_row(j * 128))
        wi = ln_into(s_xn_b, s_xi, Bxn, Bxi, s_eps[:], affine=True)
        su = l2unit(s_xi, s_xu, Bxi, Bxu, wi)
        vx = xpose(s_xu, 0, s_xuTo, 128, 0, ("s", su), Bsrc=Bxu)
        BxuTo.read_by("s", vx)
        mms = [(lambda dt=dt: s_xuTo[:, dt * 128:(dt + 1) * 128],
                lambda dt=dt: s_muu[:, dt * 256:(dt + 1) * 256])
               for dt in range(NDC)]
        tv = mm_group(6, (0, 256), mms, waits=[("s", vx), ("v", v_muu)])
        BxuTo.read_by("t", tv)
        vm = P.op("vector", lambda e: e.tensor_reduce(
            s_stat[:, 11:12], pb[6][:, 0:256], AX.X, AluOpType.max, negate=True),
            waits=[("t", tv)] + Bstat.take(), inc="v")
        se = P.op("scalar", lambda e: e.activation(
            s_ce[:], pb[6][:, 0:256], AF.Exp, bias=s_stat[:, 11:12],
            accum_out=s_stat[:, 10:11]), waits=[("v", vm)] + Bce.take(), inc="s")
        B_[6].read_by("s", se)
        Bstat.read_by("s", se)
        vr = P.op("vector", lambda e: e.reciprocal(s_stat[:, 9:10], s_stat[:, 10:11]),
                  waits=[("s", se)], inc="v")
        va = P.op("vector", lambda e: e.tensor_scalar(
            s_ce[:], s_ce[:], s_stat[:, 9:10], None, AluOpType.mult), inc="v")
        Bstat.read_by("v", va)
        for esc in range(2):
            bank = 4 + esc
            tv = P.op("tensor", lambda e, esc=esc, bank=bank: e.transpose(
                pb[bank][:, 0:128], s_ce[:, esc * 128:esc * 128 + 128], s_eye32[:]),
                waits=[("v", va)] + B_[bank].take(), inc="t")
            sv = P.op("scalar", lambda e, esc=esc, bank=bank: e.copy(
                s_cT[:, esc * 128:esc * 128 + 128], pb[bank][:, 0:128]),
                waits=[("t", tv)] + (BcT.take() if esc == 0 else []), inc="s")
            B_[bank].read_by("s", sv)
        Bce.read_by("t", P.counts["t"])
        v_ct = P.counts["s"]
        for dc in range(2):
            bank = dc
            mms = [(lambda esc=esc: s_cT[:, esc * 128:esc * 128 + 128],
                    lambda esc=esc, dc=dc: s_so[:, esc * D + dc * 512:esc * D + dc * 512 + 512])
                   for esc in range(2)]
            tv = mm_group(bank, (0, 512), mms, waits=[("s", v_ct)])
            BcT.read_by("t", tv)
            Bso.read_by("t", tv)
            sv = P.op("scalar", lambda e, dc=dc, bank=bank: e.activation(
                s_g1[:, dc * 512:(dc + 1) * 512], pb[bank][:, 0:512], AF.Silu),
                waits=[("t", tv)] + (Bg1.take() if dc == 0 else []), inc="s")
            B_[bank].read_by("s", sv)
        v_g1 = P.counts["s"]
        vop = P.dma("sync", s_opl[:], d_opsp[j], waits=[("osp", v_allsp)] + Bopl.take(),
                    sem="osp")
        tv = P.op("tensor", lambda e, j=j: e.transpose(
            pb[2][0:32, 0:128], s_gates[:, j * 32:(j + 1) * 32], s_eye32[:]),
            waits=B_[2].take() + [("s", v_gate_all)], inc="t")
        Bgates.read_by("t", tv)
        sv = P.op("scalar", lambda e: e.copy(s_gT[0:32, :], pb[2][0:32, 0:128]),
                  waits=[("t", tv)] + BgT.take(), inc="s")
        B_[2].read_by("s", sv)
        v_gT = P.counts["s"]
        for ec in range(NDC):
            bank = 2 + ec % 2
            tv = mm_group(bank, (0, 128),
                          [(lambda ec=ec: s_ehot[0:32, ec * 128:ec * 128 + 128],
                            lambda: s_gT[0:32, 0:128])],
                          waits=[("s", v_gT), ("lm", v_eh)])
            BgT.read_by("t", tv)
            vv = P.op("vector", lambda e, bank=bank: e.tensor_copy(
                s_grb[:], pb[bank][:, 0:128]),
                waits=[("t", tv)] + Bgrb.take(), inc="v")
            B_[bank].read_by("v", vv)
            vv = P.op("vector", lambda e, ec=ec: e.tensor_tensor(
                s_otmp[:], s_opl[:, (4 + ec) * 128:(4 + ec) * 128 + 128],
                s_grb[:], AluOpType.mult),
                waits=[("osp", vop)] + Botmp.take(), inc="v")
            Bgrb.read_by("v", vv)
            loccol = ec * 128 if ec < 4 else (8 + ec) * 128
            vv = P.op("vector", lambda e, ec=ec, loccol=loccol: e.tensor_tensor(
                s_opreb[:, ec * 128:ec * 128 + 128],
                s_opl[:, loccol:loccol + 128], s_otmp[:], AluOpType.add),
                waits=(Bopreb.take() if ec == 0 else []), inc="v")
            Botmp.read_by("v", vv)
            Bopl.read_by("v", vv)
        v_opb = P.counts["v"]
        for ecw in range(2):
            bank = ecw
            mms = [(lambda dt=dt: s_opreb[:, dt * 128:(dt + 1) * 128],
                    lambda dt=dt, ecw=ecw: s_wout[:, dt * D + ecw * 512:dt * D + ecw * 512 + 512])
                   for dt in range(NDC)]
            tv = mm_group(bank, (0, 512), mms, waits=[("v", v_opb), ("sw", v_wo)])
            Bopreb.read_by("t", tv)
            sv = P.op("scalar", lambda e, ecw=ecw, bank=bank: e.copy(
                s_s1t[:, ecw * 512:(ecw + 1) * 512], pb[bank][:, 0:512]),
                waits=[("t", tv)] + (Bs1t.take() if ecw == 0 else []), inc="s")
            B_[bank].read_by("s", sv)
        v_s1 = P.counts["s"]
        vv = P.op("vector", lambda e: e.tensor_tensor(
            s_g1[:], s_g1[:], s_s1t[:], AluOpType.mult),
            waits=[("s", v_s1), ("s", v_g1)], inc="v")
        Bs1t.read_by("v", vv)
        vv = P.op("vector", lambda e: e.tensor_tensor(
            s_x2t[:], s_xtile[:], s_g1[:], AluOpType.add),
            waits=Bx2t.take(), inc="v")
        Bg1.read_by("v", vv)
        Bxtile.read_by("v", vv)
        sd = P.dma("sync", d_x2p[j * 128:j * 128 + 128, :], s_x2t[:],
                   waits=[("v", vv)], sem="st")
        Bx2t.read_by("st", sd)

    P.phase_barrier()
    pe.close()
    les.close()
    return


def build_l1():
    nc = bass.Bass("TRN2", target_bir_lowering=False)
    es = ExitStack()
    P = Prog(nc, es)
    dd = declare_l1(nc)
    emit_l1(nc, es, P, dd)
    P.barrier_wait("sync", [("st", P.counts["st"])])
    P.finalize()
    es.close()
    return nc, es


U32 = mybir.dt.uint32

BLOCK_POS = {}
for _j, _g in enumerate(SNAKE_A):
    BLOCK_POS[_g] = (0, _j)
for _j, _g in enumerate(SNAKE_B):
    BLOCK_POS[_g] = (1, _j)

POLL_R = [4 * k for k in range(1, 151)]
while POLL_R[-1] < 4000000:
    POLL_R.append(min(int(POLL_R[-1] * 1.05) + 1, 4000000))
NREL = (POLL_R[-1] + 59999) // 60000


def build_fused():
    nc = bass.Bass("TRN2", target_bir_lowering=False)
    es = ExitStack()
    P = Prog(nc, es)
    dd0 = declare_l0(nc, out_kind="Internal")
    dd1 = declare_l1(nc, shared=dd0, with_x1=False)

    d_shx = nc.dram_tensor("x1shx", [T, D], F32, kind="Internal",
                           addr_space="Shared").ap()
    d_fl = nc.dram_tensor("exfl", [256, 4], U32, kind="Internal",
                          addr_space="Shared").ap()
    d_flf = d_fl.rearrange("a b -> (a b)")
    d_sidx = nc.dram_tensor("sidx", [128, NSLOT], U32, kind="ExternalInput").ap()
    d_fidx = nc.dram_tensor("fidx", [128, 1], U32, kind="ExternalInput").ap()
    d_run = nc.dram_tensor("runid", [128, 4], U32, kind="ExternalInput").ap()

    s_sidx = es.enter_context(nc.sbuf_tensor("s_sidx", [128, NSLOT], U32))
    s_fidx = es.enter_context(nc.sbuf_tensor("s_fidx", [128, 1], U32))
    s_run = es.enter_context(nc.sbuf_tensor("s_run", [128, 4], U32))
    s_flb = es.enter_context(nc.sbuf_tensor("s_flb", [1, 8], U32))
    s_tick = es.enter_context(nc.sbuf_tensor("s_tick", [128, 96], mybir.dt.bfloat16))

    P.dma("sync", s_sidx[:], d_sidx, sem="ex")
    P.dma("sync", s_fidx[:], d_fidx, sem="ex")
    P.dma("sync", s_run[:], d_run, sem="ex")

    def scatter(j, s_x1t, waits):
        return P.op("gpsimd", lambda e, j=j, s=s_x1t: e.indirect_dma_start(
            out=d_shx,
            out_offset=bass.IndirectOffsetOnAxis(ap=s_sidx[:, j:j + 1], axis=0),
            in_=s[:], in_offset=None),
            waits=waits + [("ex", 48)], inc="sc", inc_by=16)

    emit_l0(nc, es, P, dd0, scatter=scatter)

    P.op("gpsimd", lambda e: e.indirect_dma_start(
        out=d_fl,
        out_offset=bass.IndirectOffsetOnAxis(ap=s_fidx[:], axis=0),
        in_=s_run[:], in_offset=None),
        waits=[("sc", 16 * NSLOT)], inc="fl", inc_by=16)

    for rk in POLL_R:
        P.op("sync", lambda e: e.dma_start(out=s_flb[0:1, 0:4],
                                           in_=d_flf[0:4]),
             waits=[("pt", rk)], inc="pl", inc_by=16)
        P.op("sync", lambda e: e.dma_start(out=s_flb[0:1, 4:8],
                                           in_=d_flf[512:516]),
             waits=[("pt", rk)], inc="pl", inc_by=16)

    def spin(e):
        e.memset(s_flb[:], 0)
        with e.register("diff") as rdiff, e.register("rf") as rf, \
             e.register("rr") as rr, e.register("acc") as racc:
            e.reg_load(rr, s_run[0:1, 0:1])
            e.reg_mov(rdiff, 1)
            with e.While(rdiff):
                e.memset(s_tick[:], 0).then_inc(P.sems["pt"], 1)
                e.reg_mov(racc, 0)
                for cell in (0, 4):
                    e.reg_load(rf, s_flb[0:1, cell:cell + 1])
                    e.reg_alu(rf, rf, rr, AluOpType.bitwise_xor)
                    e.reg_alu(racc, racc, rf, AluOpType.bitwise_or)
                e.reg_alu(rdiff, racc, 0, AluOpType.bitwise_or)
        for _ in range(NREL):
            e.memset(s_tick[0:1, 0:4], 0).then_inc(P.sems["pt"], 60000)
        return e.memset(s_tick[0:1, 0:4], 0)

    P.sem("pt")
    P.counts["pt"] += NREL * 60000
    P.op("gpsimd", spin, waits=[("fl", 16)], inc="g")

    P.phase_barrier()

    def x1_row(tok0):
        return d_shx[tok0:tok0 + 128, :]

    def x1own_row(r0):
        return dd0["x1p"][r0:r0 + 128, :]

    emit_l1(nc, es, P, dd1, x1_row=x1_row, x1own_row=x1own_row)
    P.barrier_wait("sync", [("st", P.counts["st"])])
    P.finalize()
    es.close()
    return nc, es


def host_prep_fused(inputs):
    per0 = host_prep_l0(inputs)
    per1 = host_prep_l1(inputs, None)
    runid = np.uint32(np.random.randint(1, 2**31))
    out = []
    for c in range(8):
        role = c % 2
        snake = SNAKE_A if role == 0 else SNAKE_B
        sidx = np.empty((128, NSLOT), np.uint32)
        for j, g in enumerate(snake):
            sidx[:, j] = 128 * g + np.arange(128, dtype=np.uint32)
        ex = {
            "sidx": sidx,
            "fidx": (np.arange(128, dtype=np.uint32) + role * 128).reshape(128, 1),
            "runid": np.full((128, 4), runid, np.uint32),
        }
        out.append({**per0[c], **per1[c], **ex})
    return out


from concourse.bass_utils import run_bass_kernel_spmd

_NC_CACHE = {}


def kernel(**inputs):
    if "f" not in _NC_CACHE:
        _NC_CACHE["f"] = build_fused()
    nc, _es = _NC_CACHE["f"]
    per = host_prep_fused(inputs)
    res = run_bass_kernel_spmd(nc, per, core_ids=list(range(8)))
    out = np.zeros((4, T, D), np.float32)
    for c in range(8):
        part = np.asarray(res.results[c]["x2p"])
        snake = SNAKE_A if c % 2 == 0 else SNAKE_B
        for j, g in enumerate(snake):
            out[c // 2, 128 * g:128 * g + 128] = part[128 * j:128 * j + 128]
    return out


# version 19
# speedup vs baseline: 1.7979x; 1.0445x over previous
import numpy as np
from contextlib import ExitStack
import concourse.bass as bass
import concourse.mybir as mybir
from concourse.alu_op_type import AluOpType

F32 = mybir.dt.float32
BF16 = mybir.dt.bfloat16
AF = mybir.ActivationFunctionType
AX = mybir.AxisListType

D = 1024
T = 4096
TOWN = 2048
NT = 32
NSLOT = 16
NDC = 8
FF = 4096
NE = 9
NEG = -1.0e9
J_ITERS = 6

SNAKE_A = [0, 3, 4, 7, 8, 11, 12, 15, 16, 19, 20, 23, 24, 27, 28, 31]
SNAKE_B = [1, 2, 5, 6, 9, 10, 13, 14, 17, 18, 21, 22, 25, 26, 29, 30]

ENG_OF_SEM = {"t": "tensor", "s": "scalar", "v": "vector", "g": "gpsimd"}


class Buf:
    __slots__ = ("deps",)

    def __init__(self):
        self.deps = []

    def read_by(self, sem, val):
        if val is not None:
            self.deps.append((sem, val))

    def take(self):
        d = self.deps
        self.deps = []
        return d


class Prog:
    ENGS = ("tensor", "scalar", "vector", "gpsimd", "sync")

    def __init__(self, nc, es):
        self.nc = nc
        self.es = es
        self.streams = {e: [] for e in self.ENGS}
        self.seq = 0
        self.sems = {}
        self.counts = {"t": 0, "s": 0, "v": 0, "g": 0, "hw": 0, "sw": 0}
        self.watermark = {e: {} for e in self.ENGS}

    def sem(self, name):
        if name not in self.sems:
            self.sems[name] = self.es.enter_context(self.nc.semaphore(name))
            self.counts[name] = 0
        return self.sems[name]

    SELF_SEM = {"scalar": "s", "vector": "v", "gpsimd": "g"}

    def op(self, eng, fn, waits=(), inc=None, inc_by=1):
        wm = self.watermark[eng]
        own = self.SELF_SEM.get(eng)
        waits = list(waits)
        if own is not None:
            self.sem(own)
            waits.append((own, self.counts[own]))
            if inc is None:
                inc = own
        need = {}
        for w, v in waits:
            if v is None or v <= 0:
                continue
            if ENG_OF_SEM.get(w) == eng and w != own:
                continue
            if v > need.get(w, -1):
                need[w] = v
        wl = []
        for w, v in need.items():
            if wm.get(w, 0) >= v:
                continue
            wm[w] = v
            self.sem(w)
            wl.append((w, v))
        val = None
        if inc is not None:
            self.sem(inc)
            self.counts[inc] += inc_by
            val = self.counts[inc]
        semh = self.sems[inc] if inc is not None else None

        def run(e, fn=fn, wl=tuple(wl), semh=semh, by=inc_by):
            for wname, wval in wl:
                e.wait_ge(self.sems[wname], wval)
            ins = fn(e)
            if semh is not None:
                ins.then_inc(semh, by)
        self.seq += 1
        self.streams[eng].append((self.seq, run))
        return val

    def dma(self, queue, out, in_, waits=(), sem=None):
        if sem is None:
            sem = "hw" if queue in ("sync", "scalar") else "sw"
        return self.op(queue, lambda e, o=out, i=in_: e.dma_start(out=o, in_=i),
                       waits=waits, inc=sem, inc_by=16)

    def barrier_wait(self, eng, waits):
        wm = self.watermark[eng]
        for w, v in waits:
            if v is None or v <= 0 or ENG_OF_SEM.get(w) == eng:
                continue
            if wm.get(w, 0) >= v:
                continue
            wm[w] = v
            self.sem(w)
            sh = self.sems[w]
            self.seq += 1
            self.streams[eng].append(
                (self.seq, lambda e, sh=sh, v=v: e.wait_ge(sh, v)))

    def phase_barrier(self, exclude=("pl", "pt")):
        snap = [(k, v) for k, v in self.counts.items() if k not in exclude]
        for eng in self.ENGS:
            self.barrier_wait(eng, snap)

    def finalize(self, truncate=None):
        import os
        if truncate is None:
            truncate = int(os.environ.get("MK_TRUNC", "0")) or None
        with self.nc.Block() as blk:
            for eng in self.ENGS:
                ops = self.streams[eng]
                if truncate is not None:
                    ops = [x for x in ops if x[0] <= truncate]
                if not ops:
                    continue

                def body(e, ops=ops):
                    for _, f in ops:
                        f(e)
                getattr(blk, eng)(body)


def host_prep_l0(inputs):
    I = inputs
    f32 = np.float32
    shared = {}
    shared["wqT"] = np.ascontiguousarray(np.asarray(I["L0_attn_Wq"], f32).T)
    shared["wkT"] = np.ascontiguousarray(np.asarray(I["L0_attn_Wk"], f32).T)
    shared["wvT"] = np.ascontiguousarray(np.asarray(I["L0_attn_Wv"], f32).T)
    shared["woT"] = np.ascontiguousarray(np.asarray(I["L0_attn_Wo"], f32).T)
    shared["rwT"] = np.ascontiguousarray(np.asarray(I["L0_ffn_router_w"], f32).T)
    shared["rb"] = np.asarray(I["L0_ffn_router_b"], f32).reshape(1, 8).copy()
    w1T = np.empty((NE, D, FF), f32)
    w2T = np.empty((NE, FF, D), f32)
    b1 = np.empty((NE, 1, FF), f32)
    b2 = np.empty((NE, 1, D), f32)
    w1T[0] = np.asarray(I["L0_ffn_shared_w1"], f32).T
    w2T[0] = np.asarray(I["L0_ffn_shared_w2"], f32).T
    b1[0, 0] = I["L0_ffn_shared_b1"]
    b2[0, 0] = I["L0_ffn_shared_b2"]
    for e in range(8):
        w1T[e + 1] = np.asarray(I["L0_ffn_spec_w1"][e], f32).T
        w2T[e + 1] = np.asarray(I["L0_ffn_spec_w2"][e], f32).T
        b1[e + 1, 0] = I["L0_ffn_spec_b1"][e]
        b2[e + 1, 0] = I["L0_ffn_spec_b2"][e]
    shared["w1T"] = np.ascontiguousarray(w1T)
    shared["w2T"] = np.ascontiguousarray(w2T)
    shared["b1r"] = np.ascontiguousarray(b1)
    shared["b2r"] = np.ascontiguousarray(b2)
    shared["eye"] = np.eye(128, dtype=f32)
    shared["ones1"] = np.ones((1, 512), f32)

    qr = np.arange(128)[:, None]
    cc = np.arange(512)[None, :]
    smask_low = np.where(cc > 256 + qr, NEG, 0.0).astype(f32)
    smask_up = np.where(cc > 384 + qr, NEG, 0.0).astype(f32)
    kr = np.arange(128)[:, None]
    qq = np.arange(128)[None, :]
    tri = np.where(kr > qq, NEG, 0.0).astype(f32)
    full = np.full((128, 128), NEG, f32)
    zero = np.zeros((128, 128), f32)
    tmaskT_low = np.concatenate([tri, full], axis=1)
    tmaskT_up = np.concatenate([zero, tri], axis=1)
    smask_roleA = np.stack([smask_low, smask_up])
    smask_roleB = np.stack([smask_up, smask_low])
    tmaskT_roleA = np.stack([tmaskT_low, tmaskT_up])
    tmaskT_roleB = np.stack([tmaskT_up, tmaskT_low])

    x = np.asarray(I["L0_attn_x"], f32)
    per_core = []
    for core in range(8):
        b, role = core // 2, core % 2
        snake = SNAKE_A if role == 0 else SNAKE_B
        xb = x[b]
        x_own = np.ascontiguousarray(
            np.stack([xb[128 * g:128 * g + 128] for g in snake]).reshape(TOWN, D))
        per_core.append({
            "x": np.ascontiguousarray(xb),
            "x_own": x_own,
            "smask": smask_roleA if role == 0 else smask_roleB,
            "tmaskT": tmaskT_roleA if role == 0 else tmaskT_roleB,
            **shared,
        })
    return per_core


def unscramble(part, role):
    snake = SNAKE_A if role == 0 else SNAKE_B
    return [(g, part[128 * j:128 * j + 128]) for j, g in enumerate(snake)]


def declare_l0(nc, out_kind="ExternalOutput", shared=None):
    dd = {}
    dd["x"] = nc.dram_tensor("x", [T, D], F32, kind="ExternalInput").ap()
    dd["x_own"] = nc.dram_tensor("x_own", [TOWN, D], F32, kind="ExternalInput").ap()
    for w in ("wqT", "wkT", "wvT", "woT"):
        dd[w] = nc.dram_tensor(w, [D, D], F32, kind="ExternalInput").ap()
    dd["rwT"] = nc.dram_tensor("rwT", [D, 8], F32, kind="ExternalInput").ap()
    dd["rb"] = nc.dram_tensor("rb", [1, 8], F32, kind="ExternalInput").ap()
    dd["w1T"] = nc.dram_tensor("w1T", [NE, D, FF], F32, kind="ExternalInput").ap()
    dd["w2T"] = nc.dram_tensor("w2T", [NE, FF, D], F32, kind="ExternalInput").ap()
    dd["b1r"] = nc.dram_tensor("b1r", [NE, 1, FF], F32, kind="ExternalInput").ap()
    dd["b2r"] = nc.dram_tensor("b2r", [NE, 1, D], F32, kind="ExternalInput").ap()
    if shared and "eye" in shared:
        dd["eye"] = shared["eye"]
        dd["ones1"] = shared["ones1"]
    else:
        dd["eye"] = nc.dram_tensor("eye", [128, 128], F32, kind="ExternalInput").ap()
        dd["ones1"] = nc.dram_tensor("ones1", [1, 512], F32, kind="ExternalInput").ap()
    dd["smask"] = nc.dram_tensor("smask", [2, 128, 512], F32, kind="ExternalInput").ap()
    dd["tmaskT"] = nc.dram_tensor("tmaskT", [2, 128, 256], F32, kind="ExternalInput").ap()
    dd["outT_sp"] = nc.dram_tensor("outT_sp", [2, 128, 4 * TOWN], BF16, kind="Internal").ap()
    dd["x1p"] = nc.dram_tensor("x1p", [TOWN, D], F32, kind=out_kind).ap()
    return dd


def emit_l0(nc, es, P, dd, upto="full", ntc=8, sub=9, scatter=None):
    d_x = dd["x"]
    d_xown = dd["x_own"]
    d_wT = {w: dd[w] for w in ("wqT", "wkT", "wvT", "woT")}
    d_rwT = dd["rwT"]
    d_rb = dd["rb"]
    d_w1T = dd["w1T"]
    d_w2T = dd["w2T"]
    d_b1 = dd["b1r"]
    d_b2 = dd["b2r"]
    d_eye = dd["eye"]
    d_ones = dd["ones1"]
    d_smask = dd["smask"]
    d_tmaskT = dd["tmaskT"]
    d_osp = dd["outT_sp"]
    d_x1p = dd["x1p"]
    les = ExitStack()

    def sb(name, shape, dt):
        return les.enter_context(nc.sbuf_tensor(name, shape, dt))

    def ps(name, shape, dt):
        return les.enter_context(nc.psum_tensor(name, shape, dt))

    s_eye32 = sb("eye32", [128, 128], F32)
    s_eyeb = sb("eyeb", [128, 128], BF16)
    s_ones = sb("ones_b", [1, 512], BF16)
    s_onesrow = sb("onesrow", [1, 128], BF16)
    s_ones32 = sb("ones32r", [1, 128], F32)
    s_smask = sb("smask_s", [128, 2 * 512], F32)
    s_rw = sb("rw_b", [128, NDC * 8], F32)
    s_rb = sb("rb_b", [1, 8], F32)
    s_xnT_own = sb("xnT_own", [128, NDC * TOWN], BF16)
    s_js = None
    s_xnt32 = None
    s_jvs = sb("junk_v", [128, D], BF16)
    s_gc = sb("gc", [128, 40], F32)
    s_stat = sb("stat", [128, 8], F32)
    s_xtile = sb("xtile", [128, D], F32)
    s_xn_b = sb("xn_b", [128, D], F32)
    s_comb = sb("comb", [128, NSLOT * 8], F32)
    s_rsm = sb("rsm", [128, 12], F32)
    s_b1 = sb("b1_b", [1, FF], BF16)
    s_b2 = sb("b2_b", [1, D], BF16)
    s_gtile = sb("gtile", [128, D], F32)
    s_x1t = sb("x1t", [128, D], F32)

    pb = [ps(f"pb{i}", [128, 512], F32) for i in range(8)]
    B_ = [Buf() for _ in range(8)]
    Bxtile, Bxn, Bgc, Btau, Bstat, Brsm = (Buf() for _ in range(6))
    Bxnt32 = Buf()
    Bjs, Bjv, Bgtile, Bx1t = (Buf() for _ in range(4))

    P.dma("sync", s_eye32[:], d_eye, sem="c0")
    P.dma("gpsimd", s_eyeb[:], d_eye, sem="sw")
    v_eyeb = P.counts["sw"]
    for p in range(2):
        P.dma("sync", s_smask[:, p * 512:(p + 1) * 512], d_smask[p], sem="c0")
    for dt in range(NDC):
        P.dma("sync", s_rw[:, dt * 8:(dt + 1) * 8],
              d_rwT[dt * 128:(dt + 1) * 128, :], sem="c0")
    P.dma("sync", s_rb[:], d_rb, sem="c0")
    v_hw0 = P.counts["c0"]
    v_eye32 = v_hw0
    P.dma("gpsimd", s_ones[:], d_ones)
    v_sw0 = P.counts["sw"]
    v_ones = P.op("vector", lambda e: e.memset(s_onesrow[:], 1.0), inc="v")
    v_ones = P.op("vector", lambda e: e.memset(s_ones32[:], 1.0), inc="v")
    s_eps = sb("epscol", [128, 1], F32)
    v_eps = P.op("vector", lambda e: e.memset(s_eps[:], 1e-5), inc="v")

    def ln_tile(src_row):
        vd = P.dma("sync", s_xtile[:], src_row, waits=Bxtile.take())
        wx = [("hw", vd)]
        vm = P.op("vector", lambda e: e.tensor_scalar(
            s_jvs[:, 0:D], s_xtile[:], 1.0 / D, None, AluOpType.mult,
            AluOpType.add, accum_out=s_stat[:, 0:1]),
            waits=wx + Bjv.take() + Bstat.take(), inc="v")
        vs = P.op("scalar", lambda e: e.activation(
            s_js[:, 0:D], s_xtile[:], AF.Square, accum_out=s_stat[:, 1:2]),
            waits=wx + Bjs.take(), inc="s")
        Bxtile.read_by("v", vm)
        Bxtile.read_by("s", vs)
        Bjv.read_by("v", vm)
        Bjs.read_by("s", vs)
        v1 = P.op("vector", lambda e: e.scalar_tensor_tensor(
            s_stat[:, 2:3], s_stat[:, 0:1], -1.0, s_stat[:, 0:1],
            AluOpType.mult, AluOpType.mult), waits=[("s", vs)], inc="v")
        v2 = P.op("vector", lambda e: e.scalar_tensor_tensor(
            s_stat[:, 3:4], s_stat[:, 1:2], 1.0 / D, s_stat[:, 2:3],
            AluOpType.mult, AluOpType.add), inc="v")
        s1 = P.op("scalar", lambda e: e.activation(
            s_stat[:, 4:5], s_stat[:, 3:4], AF.Sqrt, bias=s_eps[:]),
            waits=[("v", v2), ("v", v_eps)], inc="s")
        v3 = P.op("vector", lambda e: e.reciprocal(s_stat[:, 5:6], s_stat[:, 4:5]),
                  waits=[("s", s1)], inc="v")
        v4 = P.op("vector", lambda e: e.scalar_tensor_tensor(
            s_stat[:, 6:7], s_stat[:, 0:1], -1.0, s_stat[:, 5:6],
            AluOpType.mult, AluOpType.mult), inc="v")
        sx = P.op("scalar", lambda e: e.activation(
            s_xn_b[:], s_xtile[:], AF.Identity, bias=s_stat[:, 6:7],
            scale=s_stat[:, 5:6]), waits=[("v", v4)] + Bxn.take(), inc="s")
        Bxtile.read_by("s", sx)
        Bstat.read_by("s", sx)
        return sx

    def xpose_xn(dest, dest_stride, destcol, wait_s, also32=False):
        for g4 in range(2):
            bank = 4 + g4
            tvl = None
            for slot in range(4):
                dt = g4 * 4 + slot
                tvl = P.op("tensor", lambda e, dt=dt, bank=bank, slot=slot: e.transpose(
                    pb[bank][:, slot * 128:slot * 128 + 128],
                    s_xn_b[:, dt * 128:(dt + 1) * 128], s_eye32[:]),
                    waits=[("s", wait_s), ("c0", v_eye32)] +
                          (B_[bank].take() if slot == 0 else []), inc="t")
            for slot in range(4):
                dt = g4 * 4 + slot
                sv = P.op("scalar",
                          lambda e, dt=dt, bank=bank, slot=slot: e.copy(
                              dest[:, dt * dest_stride + destcol:dt * dest_stride + destcol + 128],
                              pb[bank][:, slot * 128:slot * 128 + 128]),
                          waits=[("t", tvl)], inc="s")
                B_[bank].read_by("s", sv)
                if also32:
                    sv32 = P.op("scalar",
                                lambda e, dt=dt, bank=bank, slot=slot: e.copy(
                                    s_xnt32[:, dt * 128:(dt + 1) * 128],
                                    pb[bank][:, slot * 128:slot * 128 + 128]),
                                waits=(Bxnt32.take() if dt == 0 else []), inc="s")
                    B_[bank].read_by("s", sv32)
        Bxn.read_by("t", P.counts["t"])
        return P.counts["s"]

    def mm_group(bank, cols, mms, waits=(), extra_first=(), rows=(0, 128)):
        n = len(mms)
        w0 = list(waits) + B_[bank].take() + list(extra_first)
        last = None
        for i, (fl, fr) in enumerate(mms):
            last = P.op("tensor",
                        lambda e, fl=fl, fr=fr, i=i: e.matmul(
                            pb[bank][rows[0]:rows[1], cols[0]:cols[1]], fl(), fr(),
                            start=(i == 0), stop=(i == n - 1)),
                        waits=(w0 if i == 0 else ()),
                        inc=("t" if i == n - 1 else None))
        return last

    def emit_router_tile(j, v32):
        mms = [(lambda dt=dt: s_xnt32[:, dt * 128:(dt + 1) * 128],
                lambda dt=dt: s_rw[:, dt * 8:(dt + 1) * 8]) for dt in range(NDC)]
        mms.append((lambda: s_ones32[0:1, 0:128], lambda: s_rb[0:1, 0:8]))
        tv = mm_group(6, (0, 8), mms, waits=[("c0", v_hw0), ("s", v32),
                                             ("v", v_ones)])
        Bxnt32.read_by("t", tv)
        vm = P.op("vector", lambda e: e.tensor_reduce(
            s_rsm[:, 0:1], pb[6][:, 0:8], AX.X, AluOpType.max, negate=True),
            waits=[("t", tv)] + Brsm.take(), inc="v")
        se = P.op("scalar", lambda e: e.activation(
            s_rsm[:, 1:9], pb[6][:, 0:8], AF.Exp, bias=s_rsm[:, 0:1],
            accum_out=s_rsm[:, 9:10]), waits=[("v", vm)], inc="s")
        B_[6].read_by("s", se)
        vv = P.op("vector", lambda e: e.reciprocal(s_rsm[:, 10:11], s_rsm[:, 9:10]),
                  waits=[("s", se)], inc="v")
        vv = P.op("vector", lambda e, j=j: e.tensor_scalar(
            s_comb[:, j * 8:j * 8 + 8], s_rsm[:, 1:9], s_rsm[:, 10:11], None,
            AluOpType.mult), inc="v")
        vv = P.op("vector", lambda e, j=j: e.tensor_reduce(
            s_rsm[:, 0:1], s_comb[:, j * 8:j * 8 + 8], AX.X, AluOpType.max), inc="v")
        vv = P.op("vector", lambda e, j=j: e.tensor_scalar(
            s_rsm[:, 1:9], s_comb[:, j * 8:j * 8 + 8], s_rsm[:, 0:1], None,
            AluOpType.is_equal), inc="v")
        vv = P.op("vector", lambda e, j=j: e.scalar_tensor_tensor(
            s_rsm[:, 1:9], s_rsm[:, 1:9], -4.0, s_comb[:, j * 8:j * 8 + 8],
            AluOpType.mult, AluOpType.add), inc="v")
        vv = P.op("vector", lambda e: e.tensor_reduce(
            s_rsm[:, 9:10], s_rsm[:, 1:9], AX.X, AluOpType.max), inc="v")
        vv = P.op("vector", lambda e: e.tensor_tensor(
            s_rsm[:, 11:12], s_rsm[:, 0:1], s_rsm[:, 9:10], AluOpType.add), inc="v")
        vv = P.op("vector", lambda e: e.reciprocal(s_rsm[:, 10:11], s_rsm[:, 11:12]),
                  inc="v")
        vv = P.op("vector", lambda e, j=j: e.tensor_scalar(
            s_rsm[:, 1:9], s_comb[:, j * 8:j * 8 + 8], s_rsm[:, 9:10], None,
            AluOpType.is_ge), inc="v")
        vv = P.op("vector", lambda e, j=j: e.tensor_tensor(
            s_rsm[:, 1:9], s_rsm[:, 1:9], s_comb[:, j * 8:j * 8 + 8],
            AluOpType.mult), inc="v")
        vv = P.op("vector", lambda e, j=j: e.tensor_scalar(
            s_comb[:, j * 8:j * 8 + 8], s_rsm[:, 1:9], s_rsm[:, 10:11], None,
            AluOpType.mult), inc="v")
        Brsm.read_by("v", vv)

    emitted_xnt_own = [False]
    v_own_holder = [0]
    LEVELS = {"kv0": 0, "own": 1, "attn0": 2, "kv1": 3, "attn1": 4, "router": 5,
              "full": 9}
    LVL = LEVELS[upto]

    def finish_debug():
        P.phase_barrier()
        for j in range(NSLOT):
            sd = P.dma("sync", d_x1p[j * 128:j * 128 + 128, :], s_xtile[:],
                       sem="st")
        P.barrier_wait("sync", [("st", P.counts["st"])])
        les.close()
        return

    def emit_half(half):
        nonlocal s_js, s_xnt32
        ph = ExitStack()
        s_js = ph.enter_context(nc.sbuf_tensor(f"junk_s{half}", [128, T], BF16))
        s_xnt32 = ph.enter_context(nc.sbuf_tensor(f"xnt32_{half}", [128, NDC * 128], F32))
        s_KT = ph.enter_context(nc.sbuf_tensor(f"KT{half}", [128, 4 * T], BF16))
        s_V = ph.enter_context(nc.sbuf_tensor(f"V{half}", [128, NT * 512], BF16))
        BKT, BV = Buf(), Buf()

        kv = ExitStack()
        s_wk = kv.enter_context(nc.sbuf_tensor(f"wk{half}", [128, NDC * 512], BF16))
        s_wv = kv.enter_context(nc.sbuf_tensor(f"wv{half}", [128, NDC * 512], BF16))
        s_xnTc = kv.enter_context(nc.sbuf_tensor(f"xnTc{half}", [128, NDC * 512], BF16))
        BxnTc = Buf()
        P.phase_barrier()
        for dt in range(NDC):
            P.dma("gpsimd", s_wk[:, dt * 512:(dt + 1) * 512],
                  d_wT["wkT"][dt * 128:(dt + 1) * 128, half * 512:half * 512 + 512])
            P.dma("gpsimd", s_wv[:, dt * 512:(dt + 1) * 512],
                  d_wT["wvT"][dt * 128:(dt + 1) * 128, half * 512:half * 512 + 512])
        v_swk = P.counts["sw"]
        for tc in range(ntc):
            for ti in range(4):
                tok0 = tc * 512 + ti * 128
                if sub >= 2:
                    sx = ln_tile(d_x[tok0:tok0 + 128, :])
                if sub >= 3:
                    xpose_xn(s_xnTc, 512, ti * 128, sx)
            vxn = P.counts["s"]
            wxn = [("s", vxn), ("sw", v_swk)]
            for ec in range(4 if sub >= 4 else 0):
                bank = ec % 2
                mms = [(lambda dt=dt, ec=ec: s_wk[:, dt * 512 + ec * 128:dt * 512 + ec * 128 + 128],
                        lambda dt=dt: s_xnTc[:, dt * 512:(dt + 1) * 512])
                       for dt in range(NDC)]
                tv = mm_group(bank, (0, 512), mms, waits=wxn)
                sv = P.op("scalar", lambda e, ec=ec, tc=tc, bank=bank: e.copy(
                    s_KT[:, ec * T + tc * 512:ec * T + tc * 512 + 512],
                    pb[bank][:, 0:512]), waits=[("t", tv)], inc="s")
                B_[bank].read_by("s", sv)
            for ti in range(4 if sub >= 5 else 0):
                bank = 2 + ti % 2
                tt = tc * 4 + ti
                mms = [(lambda dt=dt, ti=ti: s_xnTc[:, dt * 512 + ti * 128:dt * 512 + ti * 128 + 128],
                        lambda dt=dt: s_wv[:, dt * 512:(dt + 1) * 512])
                       for dt in range(NDC)]
                tv = mm_group(bank, (0, 512), mms, waits=wxn)
                sv = P.op("scalar", lambda e, tt=tt, bank=bank: e.copy(
                    s_V[:, tt * 512:tt * 512 + 512], pb[bank][:, 0:512]),
                    waits=[("t", tv)], inc="s")
                B_[bank].read_by("s", sv)
            BxnTc.read_by("t", P.counts["t"])
        P.phase_barrier()
        kv.close()

        if LVL < 1 and half == 0:
            ph.close()
            return
        if not emitted_xnt_own[0]:
            emitted_xnt_own[0] = True
            for j in range(NSLOT):
                sx = ln_tile(d_xown[j * 128:j * 128 + 128, :])
                v32 = xpose_xn(s_xnT_own, TOWN, j * 128, sx, also32=True)
                emit_router_tile(j, v32)
            v_own_holder[0] = P.counts["s"]

        if LVL < 2 + 2 * half:
            ph.close()
            return
        at = ExitStack()
        s_wq = at.enter_context(nc.sbuf_tensor(f"wq{half}", [128, NDC * 512], BF16))
        s_scores = at.enter_context(nc.sbuf_tensor(f"sc{half}", [128, T], F32))
        s_attnT = at.enter_context(nc.sbuf_tensor(f"aT{half}", [128, T], BF16))
        s_QT = at.enter_context(nc.sbuf_tensor(f"QT{half}", [128, 4 * 128], BF16))
        s_outTh = at.enter_context(nc.sbuf_tensor(f"oT{half}", [128, 4 * TOWN], BF16))
        Bscores, BattnT, BQT, Btaub = Buf(), Buf(), Buf(), Buf()
        P.phase_barrier()
        for dt in range(NDC):
            P.dma("gpsimd", s_wq[:, dt * 512:(dt + 1) * 512],
                  d_wT["wqT"][dt * 128:(dt + 1) * 128, half * 512:half * 512 + 512])
        v_swq = P.counts["sw"]

        for j in range(NSLOT):
            E = 256 * (j + 1)
            nkc = E // 256
            qdeps = BQT.take()
            tvq = None
            for ec in range(4):
                mms = [(lambda dt=dt, ec=ec: s_wq[:, dt * 512 + ec * 128:dt * 512 + ec * 128 + 128],
                        lambda dt=dt, j=j: s_xnT_own[:, dt * TOWN + j * 128:dt * TOWN + j * 128 + 128])
                       for dt in range(NDC)]
                tvq = mm_group(6, (ec * 128, ec * 128 + 128), mms,
                               waits=[("sw", v_swq), ("s", v_own_holder[0])],
                               extra_first=(qdeps if ec == 0 else ()))
            for ec in range(4):
                sv = P.op("scalar", lambda e, ec=ec: e.activation(
                    s_QT[:, ec * 128:ec * 128 + 128],
                    pb[6][:, ec * 128:ec * 128 + 128], AF.Copy, scale=0.125),
                    waits=[("t", tvq)], inc="s")
                B_[6].read_by("s", sv)
            vqt = P.counts["s"]
            for hl in range(8):
                ec, po = hl // 2, 64 * (hl % 2)
                sdeps = Bscores.take()
                for kc in range(nkc):
                    bank = kc % 2
                    co = 0
                    tv = mm_group(bank, (co, co + 256),
                                  [(lambda ec=ec, po=po: s_QT[po:po + 64, ec * 128:ec * 128 + 128],
                                    lambda ec=ec, po=po, kc=kc: s_KT[po:po + 64, ec * T + kc * 256:ec * T + kc * 256 + 256])],
                                  waits=[("s", vqt)])
                    if kc >= nkc - 2:
                        mo = 256 * (kc - (nkc - 2))
                        ev = P.op("vector", lambda e, kc=kc, bank=bank, co=co, mo=mo, j=j:
                                  e.tensor_tensor(
                                      s_scores[:, kc * 256:kc * 256 + 256],
                                      pb[bank][:, co:co + 256],
                                      s_smask[:, (j % 2) * 512 + mo:(j % 2) * 512 + mo + 256],
                                      AluOpType.add),
                                  waits=[("t", tv), ("c0", v_hw0)] +
                                        (sdeps if kc == 0 else []), inc="v")
                        B_[bank].read_by("v", ev)
                    else:
                        ev = P.op("scalar", lambda e, kc=kc, bank=bank, co=co:
                                  e.copy(s_scores[:, kc * 256:kc * 256 + 256],
                                         pb[bank][:, co:co + 256]),
                                  waits=[("t", tv)] + (sdeps if kc == 0 else []),
                                  inc="s")
                        B_[bank].read_by("s", ev)
                wsc = [("s", P.counts["s"]), ("v", P.counts["v"])]
                vr = P.op("vector", lambda e, hl=hl, E=E: e.tensor_reduce(
                    s_gc[:, 24 + hl:25 + hl], s_scores[:, 0:E], AX.X,
                    AluOpType.max, negate=True),
                    waits=wsc + (Bgc.take() if hl == 0 else []), inc="v")
                vr = P.op("vector", lambda e, hl=hl: e.tensor_scalar(
                    s_gc[:, 24 + hl:25 + hl], s_gc[:, 24 + hl:25 + hl], 1.0,
                    None, AluOpType.add), inc="v")
                vr = P.op("vector", lambda e, hl=hl: e.tensor_scalar(
                    s_gc[:, 16 + hl:17 + hl], s_gc[:, 24 + hl:25 + hl], -1.0,
                    None, AluOpType.mult), inc="v")
                adeps = BattnT.take()
                for it in range(J_ITERS):
                    sg = P.op("scalar", lambda e, hl=hl, E=E: e.activation(
                        s_js[:, 0:E], s_scores[:, 0:E], AF.Relu,
                        bias=s_gc[:, 24 + hl:25 + hl],
                        accum_out=s_gc[:, hl:hl + 1]),
                        waits=[("v", vr)] + Bjs.take(), inc="s")
                    Bjs.read_by("s", sg)
                    vc = P.op("vector", lambda e, hl=hl, E=E: e.tensor_scalar(
                        s_attnT[:, 0:E], s_scores[:, 0:E],
                        s_gc[:, 16 + hl:17 + hl], None, AluOpType.is_gt,
                        AluOpType.add, accum_out=s_gc[:, 8 + hl:9 + hl]),
                        waits=(adeps if it == 0 else []), inc="v")
                    vu = P.op("vector", lambda e, hl=hl: e.reciprocal(
                        s_gc[:, 32 + hl:33 + hl], s_gc[:, 8 + hl:9 + hl]), inc="v")
                    vu = P.op("vector", lambda e, hl=hl: e.scalar_tensor_tensor(
                        s_gc[:, 32 + hl:33 + hl], s_gc[:, hl:hl + 1], -1.0,
                        s_gc[:, 32 + hl:33 + hl], AluOpType.add, AluOpType.mult),
                        waits=[("s", sg)], inc="v")
                    vr = P.op("vector", lambda e, hl=hl: e.tensor_tensor(
                        s_gc[:, 24 + hl:25 + hl], s_gc[:, 24 + hl:25 + hl],
                        s_gc[:, 32 + hl:33 + hl], AluOpType.subtract), inc="v")
                    vu = P.op("vector", lambda e, hl=hl: e.tensor_tensor(
                        s_gc[:, 16 + hl:17 + hl], s_gc[:, 16 + hl:17 + hl],
                        s_gc[:, 32 + hl:33 + hl], AluOpType.add), inc="v")
                Bscores.read_by("v", vr)
                Bscores.read_by("s", P.counts["s"])
                jdeps = Bjs.take()
                sgfl = []
                for g2 in range((E + 511) // 512):
                    w0 = g2 * 512
                    w1 = min(E, w0 + 512)
                    sgf = P.op("scalar", lambda e, hl=hl, w0=w0, w1=w1:
                               e.activation(s_js[:, w0:w1], s_scores[:, w0:w1],
                                            AF.Relu,
                                            bias=s_gc[:, 24 + hl:25 + hl]),
                               waits=(([("v", vr)] + jdeps) if g2 == 0 else ()),
                               inc="s")
                    sgfl.append(sgf)
                Bscores.read_by("s", sgfl[-1])
                ttl = None
                evls = []
                for grp in range(j + 1):
                    bank = 3 + (grp % 2) * 2
                    zdeps = B_[bank].take()
                    tvz = None
                    for kt in range(2):
                        ktg = grp * 2 + kt
                        tvz = P.op("tensor",
                                   lambda e, ktg=ktg, bank=bank, kt=kt:
                                   e.matmul(pb[bank][:, kt * 128:kt * 128 + 128],
                                            s_js[:, ktg * 128:ktg * 128 + 128],
                                            s_eyeb[:], start=True, stop=True),
                                   waits=(([("s", sgfl[grp // 2]),
                                            ("sw", v_eyeb)] + zdeps)
                                          if kt == 0 else ()), inc="t")
                    ttl = tvz
                    evl = P.op("scalar", lambda e, grp=grp, bank=bank: e.copy(
                        s_attnT[:, grp * 256:grp * 256 + 256],
                        pb[bank][:, 0:256]), waits=[("t", tvz)], inc="s")
                    evls.append(evl)
                    B_[bank].read_by("s", evl)
                Bjs.read_by("t", ttl)
                av2 = B_[2].take()
                nmm = 2 * (j + 1)
                tv = None
                for ktg in range(nmm):
                    tv = P.op("tensor",
                              lambda e, ktg=ktg, hl=hl, nmm=nmm: e.matmul(
                                  pb[2][0:64, 0:128],
                                  s_V[:, ktg * 512 + hl * 64:ktg * 512 + hl * 64 + 64],
                                  s_attnT[:, ktg * 128:ktg * 128 + 128],
                                  start=(ktg == 0), stop=(ktg == nmm - 1)),
                              waits=([("s", evls[ktg // 2])] +
                                     (av2 if ktg == 0 else [])),
                              inc=("t" if ktg == nmm - 1 else None))
                BattnT.read_by("t", tv)
                BQT.read_by("t", tv)
                BKT.read_by("t", tv)
                BV.read_by("t", tv)
                dl, po_o = hl // 2, 64 * (hl % 2)
                sv = P.op("scalar", lambda e, dl=dl, po_o=po_o, j=j: e.copy(
                    s_outTh[po_o:po_o + 64, dl * TOWN + j * 128:dl * TOWN + j * 128 + 128],
                    pb[2][0:64, 0:128]), waits=[("t", tv)], inc="s")
                B_[2].read_by("s", sv)

        vsp = P.dma("sync", d_osp[half], s_outTh[:],
                    waits=[("s", P.counts["s"])], sem="osp")
        P.phase_barrier()
        at.close()
        ph.close()

    emit_half(0)
    if LVL < 3:
        return finish_debug()
    emit_half(1)
    if LVL < 5:
        return finish_debug()

    ff = ExitStack()
    s_outT = ff.enter_context(nc.sbuf_tensor("outT", [128, NDC * TOWN], BF16))
    s_wo = ff.enter_context(nc.sbuf_tensor("wo_b", [128, NDC * D], BF16))
    s_w1e = ff.enter_context(nc.sbuf_tensor("w1e", [128, 2 * NDC * 512], BF16))
    s_w2c = [ff.enter_context(nc.sbuf_tensor(f"w2c{i}", [128, D], BF16))
             for i in range(3)]
    s_h = ff.enter_context(nc.sbuf_tensor("h_sb", [128, 32 * 512], BF16))
    s_acc = ff.enter_context(nc.sbuf_tensor("acc", [128, 4 * D], F32))
    s_stile = ff.enter_context(nc.sbuf_tensor("stile", [128, 4 * D], F32))
    Bh, Bacc, Bstile = Buf(), Buf(), Buf()
    Bw1e = [Buf(), Buf()]
    Bw2 = [Buf(), Buf(), Buf()]
    Bb1, Bb2 = Buf(), Buf()
    P.phase_barrier()
    v_spills = P.counts["osp"] if "osp" in P.counts else 32
    for half in (0, 1):
        P.dma("sync", s_outT[:, half * 4 * TOWN:(half + 1) * 4 * TOWN], d_osp[half],
              waits=[("osp", v_spills)], sem="osp")
    v_hwo = P.counts["osp"]
    for dt in range(NDC):
        P.dma("gpsimd", s_wo[:, dt * D:(dt + 1) * D],
              d_wT["woT"][dt * 128:(dt + 1) * 128, :])
    v_swo = P.counts["sw"]

    for ch in range(4):
        sdeps = Bstile.take()
        for ti in range(4):
            j = ch * 4 + ti
            for ecc in range(2):
                bank = ecc
                mms = [(lambda dt=dt, j=j: s_outT[:, dt * TOWN + j * 128:dt * TOWN + j * 128 + 128],
                        lambda dt=dt, ecc=ecc: s_wo[:, dt * D + ecc * 512:dt * D + ecc * 512 + 512])
                       for dt in range(NDC)]
                tv = mm_group(bank, (0, 512), mms,
                              waits=[("sw", v_swo), ("osp", v_hwo)],
                              extra_first=(sdeps if ti == 0 and ecc == 0 else ()))
                sv = P.op("scalar", lambda e, ti=ti, ecc=ecc, bank=bank: e.copy(
                    s_stile[:, ti * D + ecc * 512:ti * D + ecc * 512 + 512],
                    pb[bank][:, 0:512]), waits=[("t", tv)], inc="s")
                B_[bank].read_by("s", sv)
        for e_i in range(NE):
            hdeps = Bh.take()
            vb1 = P.dma("gpsimd", s_b1[:], d_b1[e_i], waits=Bb1.take(), sem="b1s")
            vb2 = P.dma("gpsimd", s_b2[:], d_b2[e_i], waits=Bb2.take(), sem="b2s")
            for fc in range(8):
                lv = None
                par = fc % 2
                w1deps = Bw1e[par].take()
                wsem1 = "w1a" if par == 0 else "w1b"
                for dt in range(NDC):
                    lv = P.dma("gpsimd",
                               s_w1e[:, par * 4096 + dt * 512:par * 4096 + (dt + 1) * 512],
                               d_w1T[e_i][dt * 128:(dt + 1) * 128, fc * 512:(fc + 1) * 512],
                               waits=(w1deps if dt == 0 else []), sem=wsem1)
                wld = [(wsem1, lv)]
                for ft in range(4):
                    fglob = fc * 4 + ft
                    bank = 2 + fglob % 2
                    mms = [(lambda dt=dt, ft=ft, par=par: s_w1e[:, par * 4096 + dt * 512 + ft * 128:par * 4096 + dt * 512 + ft * 128 + 128],
                            lambda dt=dt, ch=ch: s_xnT_own[:, dt * TOWN + ch * 512:dt * TOWN + ch * 512 + 512])
                           for dt in range(NDC)]
                    mms.append((lambda fglob=fglob: s_b1[0:1, fglob * 128:fglob * 128 + 128],
                                lambda: s_ones[0:1, 0:512]))
                    tv = mm_group(bank, (0, 512), mms,
                                  waits=wld + [("b1s", vb1), ("s", v_own_holder[0])],
                                  extra_first=(hdeps if fglob == 0 else ()))
                    Bb1.read_by("t", tv)
                    sv = P.op("scalar", lambda e, fglob=fglob, bank=bank:
                              e.activation(s_h[:, fglob * 512:fglob * 512 + 512],
                                           pb[bank][:, 0:512], AF.Gelu_apprx_tanh),
                              waits=[("t", tv)], inc="s")
                    B_[bank].read_by("s", sv)
                    Bw1e[par].read_by("t", tv)
            vh = P.counts["s"]
            odeps = [B_[b].take() for b in range(8)]
            for ft in range(32):
                wbuf2 = s_w2c[ft % 3]
                wsem = ("w2a", "w2b", "w2c")[ft % 3]
                lv = P.dma("gpsimd", wbuf2[:],
                           d_w2T[e_i][ft * 128:(ft + 1) * 128, :],
                           waits=Bw2[ft % 3].take(), sem=wsem)
                for ti in range(4):
                    for dc in range(2):
                        bank = ti * 2 + dc
                        tvw = P.op("tensor",
                             lambda e, ft=ft, ti=ti, dc=dc, bank=bank, wbuf2=wbuf2:
                             e.matmul(pb[bank][:, 0:512],
                                      s_h[:, ft * 512 + ti * 128:ft * 512 + ti * 128 + 128],
                                      wbuf2[:, dc * 512:dc * 512 + 512],
                                      start=(ft == 0), stop=False),
                             waits=([(wsem, lv), ("s", vh)] +
                                    (odeps[bank] if ft == 0 else [])),
                             inc=("t" if (ti == 3 and dc == 1) else None))
                Bw2[ft % 3].read_by("t", tvw)
            vo = None
            for ti in range(4):
                for dc in range(2):
                    bank = ti * 2 + dc
                    vo = P.op("tensor", lambda e, dc=dc, bank=bank:
                              e.matmul(pb[bank][:, 0:512],
                                       s_onesrow[0:1, 0:128],
                                       s_b2[0:1, dc * 512:dc * 512 + 512],
                                       start=False, stop=True),
                              waits=[("v", v_ones), ("b2s", vb2)], inc="t")
                    Bb2.read_by("t", vo)
            Bh.read_by("t", vo)
            for ti in range(4):
                for dc in range(2):
                    bank = ti * 2 + dc
                    j = ch * 4 + ti
                    if e_i == 0:
                        sv = P.op("scalar", lambda e, ti=ti, dc=dc, bank=bank:
                                  e.copy(s_acc[:, ti * D + dc * 512:ti * D + dc * 512 + 512],
                                         pb[bank][:, 0:512]),
                                  waits=[("t", vo)] +
                                        (Bacc.take() if ti == 0 and dc == 0 else []),
                                  inc="s")
                        B_[bank].read_by("s", sv)
                    else:
                        vv = P.op("vector",
                                  lambda e, ti=ti, dc=dc, bank=bank, j=j, e_i=e_i:
                                  e.scalar_tensor_tensor(
                                      s_acc[:, ti * D + dc * 512:ti * D + dc * 512 + 512],
                                      pb[bank][:, 0:512],
                                      s_comb[:, j * 8 + (e_i - 1):j * 8 + e_i],
                                      s_acc[:, ti * D + dc * 512:ti * D + dc * 512 + 512],
                                      AluOpType.mult, AluOpType.add),
                                  waits=[("t", vo), ("s", P.counts["s"])], inc="v")
                        B_[bank].read_by("v", vv)
        for ti in range(4):
            j = ch * 4 + ti
            sg = P.op("scalar", lambda e, ti=ti: e.activation(
                s_gtile[:], s_acc[:, ti * D:(ti + 1) * D], AF.Silu),
                waits=[("v", P.counts["v"])] + Bgtile.take(), inc="s")
            Bacc.read_by("s", sg)
            vd = P.dma("sync", s_xtile[:], d_xown[j * 128:j * 128 + 128, :],
                       waits=Bxtile.take())
            vv = P.op("vector", lambda e, ti=ti: e.tensor_tensor(
                s_gtile[:], s_gtile[:], s_stile[:, ti * D:(ti + 1) * D],
                AluOpType.mult), waits=[("s", sg)], inc="v")
            vv = P.op("vector", lambda e: e.tensor_tensor(
                s_x1t[:], s_xtile[:], s_gtile[:], AluOpType.add),
                waits=[("hw", vd)] + Bx1t.take(), inc="v")
            Bxtile.read_by("v", vv)
            Bgtile.read_by("v", vv)
            Bstile.read_by("v", vv)
            sd = P.dma("sync", d_x1p[j * 128:j * 128 + 128, :], s_x1t[:],
                       waits=[("v", vv)], sem="st")
            Bx1t.read_by("st", sd)
            if scatter is not None:
                sc = scatter(j, s_x1t, [("v", vv)])
                Bx1t.read_by("sc", sc)

    P.phase_barrier()
    ff.close()
    les.close()
    return


def build_l0(upto="full", ntc=8, sub=9):
    nc = bass.Bass("TRN2", target_bir_lowering=False)
    es = ExitStack()
    P = Prog(nc, es)
    dd = declare_l0(nc)
    emit_l0(nc, es, P, dd, upto=upto, ntc=ntc, sub=sub)
    P.barrier_wait("sync", [("st", P.counts["st"])])
    P.finalize()
    es.close()
    return nc, es



def host_prep_l1(inputs, x1=None):
    I = inputs
    f32 = np.float32
    sh = {}
    for nm, key in (("wq", "L1_attn_wq"), ("wk", "L1_attn_wk"),
                    ("wv", "L1_attn_wv"), ("wout", "L1_attn_w_out")):
        sh[nm] = np.ascontiguousarray(np.asarray(I[key], f32))
    sh["wgq"] = np.ascontiguousarray(np.asarray(I["L1_attn_wgq"], f32))
    sh["wgk"] = np.ascontiguousarray(np.asarray(I["L1_attn_wgk"], f32))
    sh["wgv"] = np.ascontiguousarray(np.asarray(I["L1_attn_wgv"], f32))
    sh["rout"] = np.ascontiguousarray(np.asarray(I["L1_attn_r_out_w"], f32))
    gwbd = np.zeros((D, 32), f32)
    glw = np.asarray(I["L1_attn_gate_local_w"], f32)
    grw = np.asarray(I["L1_attn_gate_remote_w"], f32)
    for h in range(16):
        gwbd[64 * h:64 * h + 64, h] = glw[h]
        gwbd[64 * h:64 * h + 64, 16 + h] = grw[h]
    sh["gwbd"] = gwbd
    ehot = np.zeros((32, 8 * 128), f32)
    for ec in range(8):
        for m in range(128):
            ehot[16 + 2 * ec + m // 64, ec * 128 + m] = 1.0
    sh["ehot"] = ehot
    sh["mu2"] = np.ascontiguousarray(np.asarray(I["L1_ffn_mu"], f32).reshape(D, 256))
    sh["w1s"] = np.ascontiguousarray(np.asarray(I["L1_ffn_w1"], f32))
    sh["w2s"] = np.ascontiguousarray(np.asarray(I["L1_ffn_w2"], f32))
    sh["b1T"] = np.ascontiguousarray(np.asarray(I["L1_ffn_b1"], f32).T.reshape(2, 128, 16))
    sh["b2T"] = np.ascontiguousarray(np.asarray(I["L1_ffn_b2"], f32).T.reshape(8, 128, 16))
    sh["gammab"] = np.ascontiguousarray(
        np.broadcast_to(np.asarray(I["L1_ffn_ln_gamma"], f32), (128, D)))
    sh["betab"] = np.ascontiguousarray(
        np.broadcast_to(np.asarray(I["L1_ffn_ln_beta"], f32), (128, D)))
    sh["scaleb"] = np.ascontiguousarray(
        np.broadcast_to(np.asarray(I["L1_ffn_scale"], f32).reshape(1), (128, 1)))
    sh["eye"] = np.eye(128, dtype=f32)
    sh["ones1"] = np.ones((1, 512), f32)
    pmat = np.zeros((32, 128, 64), f32)
    for tt in range(32):
        for r in range(128):
            pmat[tt, r, 2 * tt + r // 64] = 1.0 / 64.0
    sh["pmat"] = pmat

    per_core = []
    for core in range(8):
        b, role = core // 2, core % 2
        snake = SNAKE_A if role == 0 else SNAKE_B
        wmask = np.zeros((16, 128, 512), f32)
        lmask = np.zeros((16, 128, 64), f32)
        for j, g in enumerate(snake):
            qg = 128 * g + np.arange(128)[:, None]
            kg = 128 * (2 * j - 2) + np.arange(512)[None, :]
            ok = (kg <= qg) & (kg > qg - 256) & (kg >= 0)
            wmask[j] = np.where(ok, 0.0, NEG)
            lat = np.arange(64)[None, :]
            lmask[j] = np.where(lat <= qg // 64, 0.0, NEG)
        m = {"wmask": wmask, "lmask": lmask, **sh}
        if x1 is not None:
            xb = np.ascontiguousarray(np.asarray(x1[b], f32))
            m["x1"] = xb
            m["x1_own"] = np.ascontiguousarray(
                np.stack([xb[128 * g:128 * g + 128] for g in snake]).reshape(TOWN, D))
        per_core.append(m)
    return per_core


def declare_l1(nc, shared=None, with_x1=True):
    dd = {}
    if with_x1:
        dd["x1"] = nc.dram_tensor("x1", [T, D], F32, kind="ExternalInput").ap()
        dd["x1_own"] = nc.dram_tensor("x1_own", [TOWN, D], F32, kind="ExternalInput").ap()
    for w in ("wq", "wk", "wv", "wout"):
        dd[w] = nc.dram_tensor(w, [D, D], F32, kind="ExternalInput").ap()
    for w in ("wgq", "wgk", "wgv"):
        dd[w] = nc.dram_tensor(w, [D, 512], F32, kind="ExternalInput").ap()
    dd["rout"] = nc.dram_tensor("rout", [512, D], F32, kind="ExternalInput").ap()
    dd["gwbd"] = nc.dram_tensor("gwbd", [D, 32], F32, kind="ExternalInput").ap()
    dd["ehot"] = nc.dram_tensor("ehot", [32, 8 * 128], F32, kind="ExternalInput").ap()
    dd["mu2"] = nc.dram_tensor("mu2", [D, 256], F32, kind="ExternalInput").ap()
    dd["w1s"] = nc.dram_tensor("w1s", [16, D, 256], F32, kind="ExternalInput").ap()
    dd["w2s"] = nc.dram_tensor("w2s", [16, 256, D], F32, kind="ExternalInput").ap()
    dd["b1T"] = nc.dram_tensor("b1T", [2, 128, 16], F32, kind="ExternalInput").ap()
    dd["b2T"] = nc.dram_tensor("b2T", [8, 128, 16], F32, kind="ExternalInput").ap()
    dd["gammab"] = nc.dram_tensor("gammab", [128, D], F32, kind="ExternalInput").ap()
    dd["betab"] = nc.dram_tensor("betab", [128, D], F32, kind="ExternalInput").ap()
    dd["scaleb"] = nc.dram_tensor("scaleb", [128, 1], F32, kind="ExternalInput").ap()
    if shared and "eye" in shared:
        dd["eye"] = shared["eye"]
        dd["ones1"] = shared["ones1"]
    else:
        dd["eye"] = nc.dram_tensor("eye", [128, 128], F32, kind="ExternalInput").ap()
        dd["ones1"] = nc.dram_tensor("ones1", [1, 512], F32, kind="ExternalInput").ap()
    dd["wmask"] = nc.dram_tensor("wmask", [16, 128, 512], F32, kind="ExternalInput").ap()
    dd["lmask"] = nc.dram_tensor("lmask", [16, 128, 64], F32, kind="ExternalInput").ap()
    dd["pmat"] = nc.dram_tensor("pmat", [32, 128, 64], F32, kind="ExternalInput").ap()
    dd["opre_sp"] = nc.dram_tensor("opre_sp", [NSLOT, 128, 16 * 128], BF16, kind="Internal").ap()
    dd["x2p"] = nc.dram_tensor("x2p", [TOWN, D], F32, kind="ExternalOutput").ap()
    return dd


def emit_l1(nc, es, P, dd, x1_row=None, x1own_row=None):
    if x1_row is None:
        x1_row = lambda tok0: dd["x1"][tok0:tok0 + 128, :]
    if x1own_row is None:
        x1own_row = lambda r0: dd["x1_own"][r0:r0 + 128, :]
    d_w = {w: dd[w] for w in ("wq", "wk", "wv", "wout")}
    d_wg = {w: dd[w] for w in ("wgq", "wgk", "wgv")}
    d_rout = dd["rout"]
    d_gwbd = dd["gwbd"]
    d_ehot = dd["ehot"]
    d_mu = dd["mu2"]
    d_w1s = dd["w1s"]
    d_w2s = dd["w2s"]
    d_b1T = dd["b1T"]
    d_b2T = dd["b2T"]
    d_gammab = dd["gammab"]
    d_betab = dd["betab"]
    d_scaleb = dd["scaleb"]
    d_eye = dd["eye"]
    d_ones = dd["ones1"]
    d_wmask = dd["wmask"]
    d_lmask = dd["lmask"]
    d_pmat = dd["pmat"]
    d_opsp = dd["opre_sp"]
    d_x2p = dd["x2p"]
    les = ExitStack()

    def sb(name, shape, dt):
        return les.enter_context(nc.sbuf_tensor(name, shape, dt))

    def ps(name, shape, dt):
        return les.enter_context(nc.psum_tensor(name, shape, dt))

    s_eye32 = sb("eye32L1", [128, 128], F32)
    s_ones = sb("ones_bL1", [1, 512], BF16)
    s_ones32 = sb("ones32rL1", [1, 128], F32)
    s_gammab = sb("gammab_s", [128, D], F32)
    s_betab = sb("betab_s", [128, D], F32)
    s_scaleb = sb("scaleb_s", [128, 1], F32)
    s_stat = sb("statL1", [128, 12], F32)
    s_xtile = sb("xtileL1", [128, D], F32)
    s_xn_b = sb("xn_bL1", [128, D], F32)
    s_xi = sb("xi", [128, D], F32)
    s_xu = sb("xu", [128, D], F32)
    s_js = sb("junk_sL1", [128, D], BF16)
    s_jv = sb("junk_vL1", [128, D], BF16)
    s_muu = sb("muu", [128, 8 * 256], BF16)
    s_pooled = sb("pooled", [64, D], F32)
    s_pooledT = sb("pooledT", [128, 8 * 64], BF16)
    s_GK = sb("GK", [64, 512], F32)
    s_GV = sb("GV", [64, 512], BF16)
    s_GKT = sb("GKT", [128, 4 * 64], BF16)
    s_gates = sb("gates", [128, NSLOT * 32], F32)
    s_so = sb("so", [128, 2 * D], BF16)
    s_eps = sb("epscolL1", [128, 1], F32)
    s_e12 = sb("e12col", [128, 1], F32)
    s_onescol = sb("onescolL1", [128, 1], F32)
    s_x2t = sb("x2t", [128, D], F32)

    pb = [ps(f"pbL1{i}", [128, 512], F32) for i in range(8)]
    B_ = [Buf() for _ in range(8)]
    Bxtile, Bxn, Bxi, Bxu, Bstat, Bjs, Bjv = (Buf() for _ in range(7))
    Bpooled, BpooledT, Bgates, Bx2t, Bmuu = (Buf() for _ in range(5))

    P.dma("sync", s_eye32[:], d_eye, sem="c0")
    P.dma("sync", s_gammab[:], d_gammab, sem="c0")
    P.dma("sync", s_betab[:], d_betab, sem="c0")
    P.dma("sync", s_scaleb[:], d_scaleb, sem="c0")
    v_hw0 = P.counts["c0"]
    P.dma("gpsimd", s_ones[:], d_ones)
    v_sw0 = P.counts["sw"]
    v_ones = P.op("vector", lambda e: e.memset(s_ones32[:], 1.0), inc="v")
    v_ones = P.op("vector", lambda e: e.memset(s_eps[:], 1e-5), inc="v")
    v_ones = P.op("vector", lambda e: e.memset(s_e12[:], 1e-12), inc="v")
    v_ones = P.op("vector", lambda e: e.memset(s_onescol[:], 1.0), inc="v")

    def ln_into(inbuf, outbuf, Bin, Bout, eps_ap, src_row=None, affine=False):
        wx = []
        if src_row is not None:
            vd = P.dma("sync", inbuf[:], src_row, waits=Bin.take())
            wx = [("hw", vd)]
        vm = P.op("vector", lambda e: e.tensor_scalar(
            s_jv[:, 0:D], inbuf[:], 1.0 / D, None, AluOpType.mult,
            AluOpType.add, accum_out=s_stat[:, 0:1]),
            waits=wx + Bjv.take() + Bstat.take(), inc="v")
        vs = P.op("scalar", lambda e: e.activation(
            s_js[:, 0:D], inbuf[:], AF.Square, accum_out=s_stat[:, 1:2]),
            waits=wx + Bjs.take(), inc="s")
        Bin.read_by("v", vm)
        Bin.read_by("s", vs)
        Bjv.read_by("v", vm)
        Bjs.read_by("s", vs)
        v1 = P.op("vector", lambda e: e.scalar_tensor_tensor(
            s_stat[:, 2:3], s_stat[:, 0:1], -1.0, s_stat[:, 0:1],
            AluOpType.mult, AluOpType.mult), waits=[("s", vs)], inc="v")
        v2 = P.op("vector", lambda e: e.scalar_tensor_tensor(
            s_stat[:, 3:4], s_stat[:, 1:2], 1.0 / D, s_stat[:, 2:3],
            AluOpType.mult, AluOpType.add), inc="v")
        s1 = P.op("scalar", lambda e: e.activation(
            s_stat[:, 4:5], s_stat[:, 3:4], AF.Sqrt, bias=eps_ap),
            waits=[("v", v2), ("v", v_ones)], inc="s")
        v3 = P.op("vector", lambda e: e.reciprocal(s_stat[:, 5:6], s_stat[:, 4:5]),
                  waits=[("s", s1)], inc="v")
        v4 = P.op("vector", lambda e: e.scalar_tensor_tensor(
            s_stat[:, 6:7], s_stat[:, 0:1], -1.0, s_stat[:, 5:6],
            AluOpType.mult, AluOpType.mult), inc="v")
        sx = P.op("scalar", lambda e: e.activation(
            outbuf[:], inbuf[:], AF.Identity, bias=s_stat[:, 6:7],
            scale=s_stat[:, 5:6]), waits=[("v", v4)] + Bout.take(), inc="s")
        Bin.read_by("s", sx)
        Bstat.read_by("s", sx)
        last = ("s", sx)
        if affine:
            vv = P.op("vector", lambda e: e.tensor_tensor(
                outbuf[:], outbuf[:], s_gammab[:], AluOpType.mult),
                waits=[("s", sx), ("c0", v_hw0)], inc="v")
            vv = P.op("vector", lambda e: e.tensor_tensor(
                outbuf[:], outbuf[:], s_betab[:], AluOpType.add), inc="v")
            last = ("v", vv)
        return last

    def l2unit(inbuf, outbuf, Bin, Bout, wait):
        ss = P.op("scalar", lambda e: e.activation(
            s_js[:, 0:D], inbuf[:], AF.Square, accum_out=s_stat[:, 7:8]),
            waits=[wait] + Bjs.take(), inc="s")
        Bjs.read_by("s", ss)
        Bin.read_by("s", ss)
        s2 = P.op("scalar", lambda e: e.activation(
            s_stat[:, 8:9], s_stat[:, 7:8], AF.Sqrt), waits=[("s", ss)], inc="s")
        vv = P.op("vector", lambda e: e.tensor_tensor(
            s_stat[:, 9:10], s_stat[:, 8:9], s_e12[:], AluOpType.max),
            waits=[("s", s2), ("v", v_ones)], inc="v")
        vv = P.op("vector", lambda e: e.reciprocal(s_stat[:, 10:11], s_stat[:, 9:10]),
                  inc="v")
        vv = P.op("vector", lambda e: e.tensor_tensor(
            s_stat[:, 10:11], s_stat[:, 10:11], s_scaleb[:], AluOpType.mult),
            waits=[("c0", v_hw0)], inc="v")
        sx = P.op("scalar", lambda e: e.activation(
            outbuf[:], inbuf[:], AF.Identity, scale=s_stat[:, 10:11]),
            waits=[("v", vv)] + Bout.take(), inc="s")
        Bin.read_by("s", sx)
        Bstat.read_by("s", sx)
        return sx

    def xpose(srcbuf, scol, destbuf, dstride, dcol, wait, nd=NDC, Bsrc=None):
        last = None
        for g4 in range((nd + 3) // 4):
            bank = 4 + g4 % 2
            tvs = None
            n_in_g = min(4, nd - g4 * 4)
            for slot in range(n_in_g):
                dt = g4 * 4 + slot
                tvs = P.op("tensor", lambda e, dt=dt, bank=bank, slot=slot: e.transpose(
                    pb[bank][:, slot * 128:slot * 128 + 128],
                    srcbuf[:, scol + dt * 128:scol + (dt + 1) * 128], s_eye32[:]),
                    waits=[wait, ("c0", v_hw0)] + (B_[bank].take() if slot == 0 else []),
                    inc="t")
            for slot in range(n_in_g):
                dt = g4 * 4 + slot
                sv = P.op("scalar", lambda e, dt=dt, bank=bank, slot=slot: e.copy(
                    destbuf[:, dstride * dt + dcol:dstride * dt + dcol + 128],
                    pb[bank][:, slot * 128:slot * 128 + 128]),
                    waits=[("t", tvs)], inc="s")
                B_[bank].read_by("s", sv)
                last = sv
        if Bsrc is not None:
            Bsrc.read_by("t", P.counts["t"])
        return last

    def mm_group(bank, cols, mms, waits=(), extra_first=(), rows=(0, 128)):
        n = len(mms)
        w0 = list(waits) + B_[bank].take() + list(extra_first)
        last = None
        for i, (fl, fr) in enumerate(mms):
            last = P.op("tensor",
                        lambda e, fl=fl, fr=fr, i=i: e.matmul(
                            pb[bank][rows[0]:rows[1], cols[0]:cols[1]], fl(), fr(),
                            start=(i == 0), stop=(i == n - 1)),
                        waits=(w0 if i == 0 else ()),
                        inc=("t" if i == n - 1 else None))
        return last

    pa = ExitStack()
    s_xni = pa.enter_context(nc.sbuf_tensor("xni", [128, NT * D], BF16))
    s_expT = pa.enter_context(nc.sbuf_tensor("expT", [128, 2 * T], F32))
    s_dsum = pa.enter_context(nc.sbuf_tensor("dsum", [128, 2 * 9], F32))
    pa2 = ExitStack()
    s_xuT = pa2.enter_context(nc.sbuf_tensor("xuT", [128, NDC * 512], BF16))
    s_mu = pa2.enter_context(nc.sbuf_tensor("mu_s", [128, 8 * 256], F32))
    s_msq = pa2.enter_context(nc.sbuf_tensor("msq", [128, 256], F32))
    s_pm = pa2.enter_context(nc.sbuf_tensor("pm", [128, 64], F32))
    s_nrow = pa2.enter_context(nc.sbuf_tensor("nrow", [1, 256], F32))
    s_recb = pa2.enter_context(nc.sbuf_tensor("recb", [128, 256], F32))
    Bxni, BexpT, BxuT, Bpm, Bmsq = (Buf() for _ in range(5))

    for dt in range(NDC):
        P.dma("sync", s_mu[:, dt * 256:(dt + 1) * 256], d_mu[dt * 128:(dt + 1) * 128, :],
              sem="c1")
    v_mu = P.counts["c1"]
    for dt in range(NDC):
        sq = P.op("scalar", lambda e, dt=dt: e.activation(
            s_msq[:], s_mu[:, dt * 256:(dt + 1) * 256], AF.Square),
            waits=[("c1", v_mu)] + Bmsq.take(), inc="s")
        tv = P.op("tensor", lambda e, dt=dt: e.matmul(
            pb[6][0:1, 0:256], s_onescol[:], s_msq[:],
            start=(dt == 0), stop=(dt == NDC - 1)),
            waits=[("s", sq), ("v", v_ones)] + (B_[6].take() if dt == 0 else []),
            inc="t")
        Bmsq.read_by("t", tv)
    sv = P.op("scalar", lambda e: e.activation(s_nrow[0:1, :], pb[6][0:1, 0:256],
                                               AF.Sqrt), waits=[("t", tv)], inc="s")
    B_[6].read_by("s", sv)
    vv = P.op("vector", lambda e: e.tensor_scalar(
        s_nrow[0:1, :], s_nrow[0:1, :], 1e-12, None, AluOpType.max),
        waits=[("s", sv)], inc="v")
    vv = P.op("vector", lambda e: e.reciprocal(s_nrow[0:1, :], s_nrow[0:1, :]), inc="v")
    tv = P.op("tensor", lambda e: e.matmul(
        pb[6][:, 256:512], s_ones32[0:1, 0:128], s_nrow[0:1, 0:256],
        start=True, stop=True), waits=[("v", vv)] + B_[6].take(), inc="t")
    sv = P.op("scalar", lambda e: e.copy(s_recb[:], pb[6][:, 256:512]),
              waits=[("t", tv)], inc="s")
    B_[6].read_by("s", sv)
    for dt in range(NDC):
        vv = P.op("vector", lambda e, dt=dt: e.tensor_tensor(
            s_muu[:, dt * 256:(dt + 1) * 256], s_mu[:, dt * 256:(dt + 1) * 256],
            s_recb[:], AluOpType.mult), waits=[("s", sv)] + (Bmuu.take() if dt == 0 else []),
            inc="v")
    v_muu = P.counts["v"]

    for tt in range(NT):
        vp = P.dma("sync", s_pm[:], d_pmat[tt], waits=Bpm.take(), sem="pm")
        wl = ln_into(s_xtile, s_xn_b, Bxtile, Bxn, s_eps[:],
                     src_row=x1_row(tt * 128))
        for dc in range(2):
            tvp = P.op("tensor", lambda e, tt=tt, dc=dc: e.matmul(
                pb[2 + dc][0:64, 0:512], s_pm[:],
                s_xn_b[:, dc * 512:(dc + 1) * 512],
                start=(tt == 0), stop=(tt == NT - 1)),
                waits=[("pm", vp), wl] + (B_[2 + dc].take() if tt == 0 else []),
                inc=("t" if tt == NT - 1 else None))
        Bpm.read_by("t", P.counts["t"] if tt == NT - 1 else P.counts["t"] + 1)
        Bxn.read_by("t", P.counts["t"])
        wi = ln_into(s_xn_b, s_xi, Bxn, Bxi, s_eps[:], affine=True)
        sc = P.op("scalar", lambda e, tt=tt: e.copy(
            s_xni[:, tt * D:(tt + 1) * D], s_xi[:]),
            waits=[wi] + (Bxni.take() if tt == 0 else []), inc="s")
        Bxi.read_by("s", sc)
        su = l2unit(s_xi, s_xu, Bxi, Bxu, ("s", sc))
        ti = tt % 4
        xpose(s_xu, 0, s_xuT, 512, ti * 128, ("s", su), Bsrc=Bxu)
        if ti == 3:
            nchunk = tt // 4
            vxt = P.counts["s"]
            for esc in range(2):
                bank = esc
                mms = [(lambda dt=dt, esc=esc: s_muu[:, dt * 256 + esc * 128:dt * 256 + esc * 128 + 128],
                        lambda dt=dt: s_xuT[:, dt * 512:(dt + 1) * 512])
                       for dt in range(NDC)]
                tvl = mm_group(bank, (0, 512), mms, waits=[("s", vxt), ("v", v_muu)])
                se = P.op("scalar", lambda e, esc=esc, nchunk=nchunk, bank=bank: e.activation(
                    s_expT[:, esc * T + nchunk * 512:esc * T + nchunk * 512 + 512],
                    pb[bank][:, 0:512], AF.Exp,
                    accum_out=s_dsum[:, esc * 9 + nchunk:esc * 9 + nchunk + 1]),
                    waits=[("t", tvl)] + (BexpT.take() if nchunk == 0 and esc == 0 else []),
                    inc="s")
                B_[bank].read_by("s", se)
            BxuT.read_by("t", P.counts["t"])
    v_exp = P.counts["s"]
    for esc in range(2):
        vv = P.op("vector", lambda e, esc=esc: e.tensor_reduce(
            s_dsum[:, esc * 9 + 8:esc * 9 + 9], s_dsum[:, esc * 9:esc * 9 + 8],
            AX.X, AluOpType.add), waits=[("s", v_exp)], inc="v")
        vv = P.op("vector", lambda e, esc=esc: e.reciprocal(
            s_dsum[:, esc * 9 + 8:esc * 9 + 9], s_dsum[:, esc * 9 + 8:esc * 9 + 9]),
            inc="v")
        vv = P.op("vector", lambda e, esc=esc: e.tensor_scalar(
            s_expT[:, esc * T:esc * T + T], s_expT[:, esc * T:esc * T + T],
            s_dsum[:, esc * 9 + 8:esc * 9 + 9], None, AluOpType.mult), inc="v")
    v_dispT = P.counts["v"]
    for dc in range(2):
        sv = P.op("scalar", lambda e, dc=dc: e.copy(
            s_pooled[0:64, dc * 512:(dc + 1) * 512], pb[2 + dc][0:64, 0:512]),
            waits=[("t", P.counts["t"])] + (Bpooled.take() if dc == 0 else []), inc="s")
        B_[2 + dc].read_by("s", sv)
    v_pool = P.counts["s"]

    P.phase_barrier()
    pa2.close()
    pbx = ExitStack()
    s_disp = pbx.enter_context(nc.sbuf_tensor("disp", [128, NT * 256], BF16))
    s_sin = pbx.enter_context(nc.sbuf_tensor("sin", [128, 2 * D], F32))
    s_sinT = pbx.enter_context(nc.sbuf_tensor("sinT", [128, NDC * 256], BF16))
    s_hT = pbx.enter_context(nc.sbuf_tensor("hT", [128, 2 * 16], BF16))
    s_soT = pbx.enter_context(nc.sbuf_tensor("soT", [128, NDC * 256], F32))
    s_w1l = pbx.enter_context(nc.sbuf_tensor("w1l", [128, NDC * 256], BF16))
    s_w2l = pbx.enter_context(nc.sbuf_tensor("w2l", [128, 2 * D], BF16))
    s_b1T = pbx.enter_context(nc.sbuf_tensor("b1T_s", [128, 2 * 16], F32))
    s_b2T = pbx.enter_context(nc.sbuf_tensor("b2T_s", [128, 8 * 16], F32))
    s_wgl = pbx.enter_context(nc.sbuf_tensor("wgl", [128, NDC * 512], BF16))
    Bdisp, Bsin, BsinT, BhT, BsoT, Bw1l, Bw2l = (Buf() for _ in range(7))
    P.phase_barrier()
    for g4 in range(2):
        bank = 4 + g4
        tvs = None
        for slot in range(4):
            dt = g4 * 4 + slot
            tvs = P.op("tensor", lambda e, dt=dt, bank=bank, slot=slot: e.transpose(
                pb[bank][:, slot * 128:slot * 128 + 64],
                s_pooled[0:64, dt * 128:(dt + 1) * 128], s_eye32[0:64, 0:64]),
                waits=[("s", v_pool), ("c0", v_hw0)] +
                      (B_[bank].take() if slot == 0 else []), inc="t")
        for slot in range(4):
            dt = g4 * 4 + slot
            sv = P.op("scalar", lambda e, dt=dt, bank=bank, slot=slot: e.copy(
                s_pooledT[:, dt * 64:(dt + 1) * 64],
                pb[bank][:, slot * 128:slot * 128 + 64]),
                waits=[("t", tvs)], inc="s")
            B_[bank].read_by("s", sv)
    Bpooled.read_by("t", P.counts["t"])
    v_pt = P.counts["s"]
    Bwgl = Buf()
    for wname, dest in (("wgk", s_GK), ("wgv", s_GV)):
        wgd = Bwgl.take()
        for dt in range(NDC):
            P.dma("gpsimd", s_wgl[:, dt * 512:(dt + 1) * 512],
                  d_wg[wname][dt * 128:(dt + 1) * 128, :],
                  waits=(wgd if dt == 0 else []))
        v_w = P.counts["sw"]
        mms = [(lambda dt=dt: s_pooledT[:, dt * 64:(dt + 1) * 64],
                lambda dt=dt: s_wgl[:, dt * 512:(dt + 1) * 512])
               for dt in range(NDC)]
        tv = mm_group(1, (0, 512), mms, waits=[("sw", v_w), ("s", v_pt)], rows=(0, 64))
        Bwgl.read_by("t", tv)
        sv = P.op("scalar", lambda e, dest=dest: e.copy(dest[:], pb[1][0:64, 0:512]),
                  waits=[("t", tv)], inc="s")
        B_[1].read_by("s", sv)
    v_gkv = P.counts["s"]
    tvs = None
    for lc in range(4):
        tvs = P.op("tensor", lambda e, lc=lc: e.transpose(
            pb[4][:, lc * 128:lc * 128 + 64],
            s_GK[0:64, lc * 128:lc * 128 + 128], s_eye32[0:64, 0:64]),
            waits=[("s", v_gkv)] + (B_[4].take() if lc == 0 else []), inc="t")
    for lc in range(4):
        sv = P.op("scalar", lambda e, lc=lc: e.copy(
            s_GKT[:, lc * 64:lc * 64 + 64], pb[4][:, lc * 128:lc * 128 + 64]),
            waits=[("t", tvs)], inc="s")
        B_[4].read_by("s", sv)
    v_gkt = P.counts["s"]

    for tt in range(NT):
        for esc in range(2):
            bank = 4 + esc
            tv = P.op("tensor", lambda e, tt=tt, esc=esc, bank=bank: e.transpose(
                pb[bank][:, 0:128], s_expT[:, esc * T + tt * 128:esc * T + tt * 128 + 128],
                s_eye32[:]), waits=[("v", v_dispT)] + B_[bank].take(), inc="t")
            sv = P.op("scalar", lambda e, tt=tt, esc=esc, bank=bank: e.copy(
                s_disp[:, tt * 256 + esc * 128:tt * 256 + esc * 128 + 128],
                pb[bank][:, 0:128]),
                waits=[("t", tv)] + (Bdisp.take() if tt == 0 and esc == 0 else []),
                inc="s")
            B_[bank].read_by("s", sv)
    BexpT.read_by("t", P.counts["t"])
    v_dispn = P.counts["s"]
    for esc in range(2):
        for dc in range(2):
            bank = esc * 2 + dc
            mms = [(lambda tt=tt, esc=esc: s_disp[:, tt * 256 + esc * 128:tt * 256 + esc * 128 + 128],
                    lambda tt=tt, dc=dc: s_xni[:, tt * D + dc * 512:tt * D + dc * 512 + 512])
                   for tt in range(NT)]
            tv = mm_group(bank, (0, 512), mms, waits=[("s", v_dispn)])
            sv = P.op("scalar", lambda e, esc=esc, dc=dc, bank=bank: e.copy(
                s_sin[:, esc * D + dc * 512:esc * D + dc * 512 + 512],
                pb[bank][:, 0:512]),
                waits=[("t", tv)] + (Bsin.take() if esc == 0 and dc == 0 else []), inc="s")
            B_[bank].read_by("s", sv)
    v_sin = P.counts["s"]
    for esc in range(2):
        xpose(s_sin, esc * D, s_sinT, 256, esc * 128, ("s", v_sin), Bsrc=Bsin)
    v_sinT = P.counts["s"]
    for hc in range(2):
        P.dma("sync", s_b1T[:, hc * 16:(hc + 1) * 16], d_b1T[hc], sem="c2")
    for dt in range(8):
        P.dma("sync", s_b2T[:, dt * 16:(dt + 1) * 16], d_b2T[dt], sem="c2")
    v_bT = P.counts["c2"]
    Bso = Buf()
    for ei in range(16):
        w1d = Bw1l.take()
        lv1 = None
        for dt in range(NDC):
            lv1 = P.dma("gpsimd", s_w1l[:, dt * 256:(dt + 1) * 256],
                        d_w1s[ei][dt * 128:(dt + 1) * 128, :],
                        waits=(w1d if dt == 0 else []), sem="w1")
        w2d = Bw2l.take()
        lv2 = None
        for hc in range(2):
            lv2 = P.dma("gpsimd", s_w2l[:, hc * D:(hc + 1) * D],
                        d_w2s[ei][hc * 128:(hc + 1) * 128, :],
                        waits=(w2d if hc == 0 else []), sem="w2a")
        hd = BhT.take()
        for hc in range(2):
            bank = hc
            mms = [(lambda dt=dt, hc=hc: s_w1l[:, dt * 256 + hc * 128:dt * 256 + hc * 128 + 128],
                    lambda dt=dt, ei=ei: s_sinT[:, dt * 256 + ei * 16:dt * 256 + ei * 16 + 16])
                   for dt in range(NDC)]
            tv = mm_group(bank, (0, 16), mms,
                          waits=[("w1", lv1), ("s", v_sinT), ("c2", v_bT)],
                          extra_first=(hd if hc == 0 else ()))
            Bw1l.read_by("t", tv)
            BsinT.read_by("t", tv)
            sv = P.op("scalar", lambda e, hc=hc, ei=ei, bank=bank: e.activation(
                s_hT[:, hc * 16:(hc + 1) * 16], pb[bank][:, 0:16], AF.Gelu,
                bias=s_b1T[:, hc * 16 + ei:hc * 16 + ei + 1]),
                waits=[("t", tv)], inc="s")
            B_[bank].read_by("s", sv)
        v_h = P.counts["s"]
        for dt in range(NDC):
            bank = 2 + dt % 2
            mms = [(lambda hc=hc, dt=dt: s_w2l[:, hc * D + dt * 128:hc * D + dt * 128 + 128],
                    lambda hc=hc: s_hT[:, hc * 16:(hc + 1) * 16])
                   for hc in range(2)]
            tv = mm_group(bank, (0, 16), mms, waits=[("w2a", lv2), ("s", v_h)])
            Bw2l.read_by("t", tv)
            BhT.read_by("t", tv)
            sv = P.op("scalar", lambda e, dt=dt, ei=ei, bank=bank: e.activation(
                s_soT[:, dt * 256 + ei * 16:dt * 256 + ei * 16 + 16],
                pb[bank][:, 0:16], AF.Identity,
                bias=s_b2T[:, dt * 16 + ei:dt * 16 + ei + 1]),
                waits=[("t", tv)] + (BsoT.take() if ei == 0 and dt == 0 else []),
                inc="s")
            B_[bank].read_by("s", sv)
    v_soT = P.counts["s"]
    for dt in range(NDC):
        bank = 4 + (dt % 2)
        tvs = None
        for esc in range(2):
            tvs = P.op("tensor", lambda e, dt=dt, esc=esc, bank=bank: e.transpose(
                pb[bank][:, esc * 128:esc * 128 + 128],
                s_soT[:, dt * 256 + esc * 128:dt * 256 + esc * 128 + 128],
                s_eye32[:]),
                waits=[("s", v_soT)] + (B_[bank].take() if esc == 0 else []), inc="t")
        for esc in range(2):
            sv = P.op("scalar", lambda e, dt=dt, esc=esc, bank=bank: e.copy(
                s_so[:, esc * D + dt * 128:esc * D + dt * 128 + 128],
                pb[bank][:, esc * 128:esc * 128 + 128]),
                waits=[("t", tvs)] + (Bso.take() if dt == 0 and esc == 0 else []),
                inc="s")
            B_[bank].read_by("s", sv)
    BsoT.read_by("t", P.counts["t"])
    P.phase_barrier()
    pbx.close()
    pa.close()

    PADC = 256
    KSTR = T + PADC
    cd = ExitStack()
    s_xnT_own = cd.enter_context(nc.sbuf_tensor("xnT_ownL1", [128, NDC * TOWN], BF16))
    s_K1 = cd.enter_context(nc.sbuf_tensor("K1", [128, 4 * KSTR], BF16))
    s_V1 = cd.enter_context(nc.sbuf_tensor("VL1", [128, 34 * 512], BF16))
    s_xnTc = cd.enter_context(nc.sbuf_tensor("xnTcL1", [128, NDC * 512], BF16))
    s_wkh = cd.enter_context(nc.sbuf_tensor("wkh", [128, NDC * 512], BF16))
    s_wvh = cd.enter_context(nc.sbuf_tensor("wvh", [128, NDC * 512], BF16))
    s_rout = cd.enter_context(nc.sbuf_tensor("rout_s", [128, 4 * D], BF16))
    s_wgq = cd.enter_context(nc.sbuf_tensor("wgq_s", [128, NDC * 512], BF16))
    s_gwb = cd.enter_context(nc.sbuf_tensor("gwb", [128, 4 * 32], BF16))
    s_QT = cd.enter_context(nc.sbuf_tensor("QTL1", [128, 4 * 128], BF16))
    s_GQT = cd.enter_context(nc.sbuf_tensor("GQT", [128, 4 * 128], BF16))
    s_sc = cd.enter_context(nc.sbuf_tensor("scL1", [128, 512], F32))
    s_pw = cd.enter_context(nc.sbuf_tensor("pw", [128, 512], F32))
    s_pT = cd.enter_context(nc.sbuf_tensor("pT", [128, 4 * 128], BF16))
    s_ls = cd.enter_context(nc.sbuf_tensor("ls", [128, 64], F32))
    s_lse = cd.enter_context(nc.sbuf_tensor("lse", [128, 64], F32))
    s_lpT = cd.enter_context(nc.sbuf_tensor("lpT", [64, 128], BF16))
    s_remT = cd.enter_context(nc.sbuf_tensor("remT", [128, 4 * 128], BF16))
    s_opst = cd.enter_context(nc.sbuf_tensor("opst", [128, 4 * 128], BF16))
    s_oprem = cd.enter_context(nc.sbuf_tensor("oprem", [128, 8 * 128], BF16))
    s_wmk = cd.enter_context(nc.sbuf_tensor("wmk", [128, 512], F32))
    s_lmk = cd.enter_context(nc.sbuf_tensor("lmk", [128, 64], F32))
    BK1, BV1, BxnTc1, BQT1, BGQT, Bsc, Bpw, BpT, Bwk = (Buf() for _ in range(9))
    Bls, Blse, BlpT, BremT, Bopst, Boprem, Bwmk, Blmk = (Buf() for _ in range(8))

    P.phase_barrier()
    for j in range(NSLOT):
        wl = ln_into(s_xtile, s_xn_b, Bxtile, Bxn, s_eps[:],
                     src_row=x1own_row(j * 128))
        xpose(s_xn_b, 0, s_xnT_own, TOWN, j * 128, wl, Bsrc=Bxn)
    v_own = P.counts["s"]

    for half in (0, 1):
        P.phase_barrier()
        wkd = Bwk.take()
        for dt in range(NDC):
            P.dma("gpsimd", s_wkh[:, dt * 512:(dt + 1) * 512],
                  d_w["wk"][dt * 128:(dt + 1) * 128, half * 512:half * 512 + 512],
                  waits=(wkd if dt == 0 else []))
            P.dma("gpsimd", s_wvh[:, dt * 512:(dt + 1) * 512],
                  d_w["wv"][dt * 128:(dt + 1) * 128, half * 512:half * 512 + 512])
        if half == 0:
            for dt in range(NDC):
                P.dma("gpsimd", s_wgq[:, dt * 512:(dt + 1) * 512],
                      d_wg["wgq"][dt * 128:(dt + 1) * 128, :])
            for lc in range(4):
                P.dma("gpsimd", s_rout[:, lc * D:(lc + 1) * D],
                      d_rout[lc * 128:(lc + 1) * 128, :])
        for ec in range(4):
            P.dma("gpsimd", s_gwb[:, ec * 32:(ec + 1) * 32],
                  d_gwbd[half * 512 + ec * 128:half * 512 + ec * 128 + 128, :])
        v_wh = P.counts["sw"]
        vg8 = P.op("vector", lambda e: e.tensor_scalar(
            s_gwb[:], s_gwb[:], 8.0, None, AluOpType.mult),
            waits=[("sw", v_wh)], inc="v")
        vz = P.op("vector", lambda e: e.memset(s_V1[:, 0:2 * 512], 0.0),
                  waits=BV1.take(), inc="v")
        for ec in range(4):
            vz = P.op("vector", lambda e, ec=ec: e.memset(
                s_K1[:, ec * KSTR:ec * KSTR + PADC], 0.0),
                waits=(BK1.take() if ec == 0 else []), inc="v")
        for tc in range(8):
            for ti in range(4):
                tok0 = tc * 512 + ti * 128
                wl = ln_into(s_xtile, s_xn_b, Bxtile, Bxn, s_eps[:],
                             src_row=x1_row(tok0))
                xpose(s_xn_b, 0, s_xnTc, 512, ti * 128, wl, Bsrc=Bxn)
            vxn = P.counts["s"]
            wxn = [("s", vxn), ("sw", v_wh), ("v", vz)]
            for ec in range(4):
                bank = ec % 2
                mms = [(lambda dt=dt, ec=ec: s_wkh[:, dt * 512 + ec * 128:dt * 512 + ec * 128 + 128],
                        lambda dt=dt: s_xnTc[:, dt * 512:(dt + 1) * 512])
                       for dt in range(NDC)]
                tv = mm_group(bank, (0, 512), mms, waits=wxn)
                Bwk.read_by("t", tv)
                sv = P.op("scalar", lambda e, ec=ec, tc=tc, bank=bank: e.copy(
                    s_K1[:, ec * KSTR + PADC + tc * 512:ec * KSTR + PADC + tc * 512 + 512],
                    pb[bank][:, 0:512]), waits=[("t", tv)], inc="s")
                B_[bank].read_by("s", sv)
            for ti in range(4):
                bank = 2 + ti % 2
                tt = tc * 4 + ti
                mms = [(lambda dt=dt, ti=ti: s_xnTc[:, dt * 512 + ti * 128:dt * 512 + ti * 128 + 128],
                        lambda dt=dt: s_wvh[:, dt * 512:(dt + 1) * 512])
                       for dt in range(NDC)]
                tv = mm_group(bank, (0, 512), mms, waits=wxn)
                sv = P.op("scalar", lambda e, tt=tt, bank=bank: e.copy(
                    s_V1[:, (2 + tt) * 512:(2 + tt) * 512 + 512], pb[bank][:, 0:512]),
                    waits=[("t", tv)], inc="s")
                B_[bank].read_by("s", sv)
            BxnTc1.read_by("t", P.counts["t"])
        v_kv = P.counts["s"]
        wkd = Bwk.take()
        for dt in range(NDC):
            P.dma("gpsimd", s_wkh[:, dt * 512:(dt + 1) * 512],
                  d_w["wq"][dt * 128:(dt + 1) * 128, half * 512:half * 512 + 512],
                  waits=(wkd if dt == 0 else []))
        v_wq = P.counts["sw"]

        for j in range(NSLOT):
            vmk = P.dma("sync", s_wmk[:], d_wmask[j], waits=Bwmk.take(), sem="wm")
            if half == 0:
                vlk = P.dma("sync", s_lmk[:], d_lmask[j], waits=Blmk.take(), sem="lm")
            qd = BQT1.take()
            tvq = None
            for ec in range(4):
                mms = [(lambda dt=dt, ec=ec: s_wkh[:, dt * 512 + ec * 128:dt * 512 + ec * 128 + 128],
                        lambda dt=dt, j=j: s_xnT_own[:, dt * TOWN + j * 128:dt * TOWN + j * 128 + 128])
                       for dt in range(NDC)]
                tvq = mm_group(6, (ec * 128, ec * 128 + 128), mms,
                               waits=[("sw", v_wq), ("s", v_own)],
                               extra_first=(qd if ec == 0 else ()))
                Bwk.read_by("t", tvq)
            for ec in range(4):
                sv = P.op("scalar", lambda e, ec=ec: e.activation(
                    s_QT[:, ec * 128:ec * 128 + 128],
                    pb[6][:, ec * 128:ec * 128 + 128], AF.Copy, scale=0.125),
                    waits=[("t", tvq)], inc="s")
                B_[6].read_by("s", sv)
            v_qt = P.counts["s"]
            mms = [(lambda ec=ec: s_QT[:, ec * 128:ec * 128 + 128],
                    lambda ec=ec: s_gwb[:, ec * 32:(ec + 1) * 32])
                   for ec in range(4)]
            tv = mm_group(6, (256, 288), mms, waits=[("s", v_qt), ("v", vg8)])
            BQT1.read_by("t", tv)
            for part in (0, 16):
                sv = P.op("scalar", lambda e, part=part, j=j, half=half: e.activation(
                    s_gates[:, j * 32 + part + half * 8:j * 32 + part + half * 8 + 8],
                    pb[6][:, 256 + part + half * 8:256 + part + half * 8 + 8],
                    AF.Sigmoid),
                    waits=[("t", tv)] + (Bgates.take() if part == 0 else []), inc="s")
                B_[6].read_by("s", sv)
            v_gate = P.counts["s"]
            od = Bopst.take()
            for hl in range(8):
                ec, po = hl // 2, 64 * (hl % 2)
                kcol0 = ec * KSTR + PADC + 128 * (2 * j - 2)
                bank = hl % 2
                tv = mm_group(bank, (0, 512),
                              [(lambda ec=ec, po=po: s_QT[po:po + 64, ec * 128:ec * 128 + 128],
                                lambda po=po, kcol0=kcol0: s_K1[po:po + 64, kcol0:kcol0 + 512])],
                              waits=[("s", v_kv), ("s", v_qt)])
                ev = P.op("vector", lambda e, bank=bank: e.tensor_tensor(
                    s_sc[:], pb[bank][:, 0:512], s_wmk[:], AluOpType.add),
                    waits=[("t", tv), ("wm", vmk)] + Bsc.take(), inc="v")
                B_[bank].read_by("v", ev)
                Bwmk.read_by("v", ev)
                vm = P.op("vector", lambda e: e.tensor_reduce(
                    s_stat[:, 11:12], s_sc[:], AX.X, AluOpType.max, negate=True),
                    waits=Bstat.take(), inc="v")
                se = P.op("scalar", lambda e: e.activation(
                    s_pw[:], s_sc[:], AF.Exp, bias=s_stat[:, 11:12],
                    accum_out=s_stat[:, 10:11]),
                    waits=[("v", vm)] + Bpw.take(), inc="s")
                Bsc.read_by("s", se)
                Bstat.read_by("s", se)
                vr = P.op("vector", lambda e: e.reciprocal(
                    s_stat[:, 9:10], s_stat[:, 10:11]), waits=[("s", se)], inc="v")
                vr = P.op("vector", lambda e, j=j, hl=hl, half=half: e.tensor_tensor(
                    s_stat[:, 9:10], s_stat[:, 9:10],
                    s_gates[:, j * 32 + half * 8 + hl:j * 32 + half * 8 + hl + 1],
                    AluOpType.mult), waits=[("s", v_gate)], inc="v")
                va = P.op("vector", lambda e: e.tensor_scalar(
                    s_pw[:], s_pw[:], s_stat[:, 9:10], None, AluOpType.mult),
                    inc="v")
                Bpw.read_by("v", va)
                Bstat.read_by("v", va)
                Bgates.read_by("v", va)
                vpt = xpose(s_pw, 0, s_pT, 128, 0, ("v", va), nd=4, Bsrc=Bpw)
                mms = []
                for kt in range(4):
                    vtile = 2 * j + kt
                    mms.append((lambda vtile=vtile, hl=hl: s_V1[:, vtile * 512 + hl * 64:vtile * 512 + hl * 64 + 64],
                                lambda kt=kt: s_pT[:, kt * 128:kt * 128 + 128]))
                tv = mm_group(2, (0, 128), mms,
                              waits=[("s", vpt), ("s", v_kv)], rows=(0, 64))
                BpT.read_by("t", tv)
                BV1.read_by("t", tv)
                BK1.read_by("t", tv)
                dl = hl // 2
                sv = P.op("scalar", lambda e, dl=dl, po=po: e.copy(
                    s_opst[po:po + 64, dl * 128:dl * 128 + 128], pb[2][0:64, 0:128]),
                    waits=[("t", tv)] + (od if hl == 0 else []), inc="s")
                B_[2].read_by("s", sv)
            v_loc = P.counts["s"]
            vsp = P.dma("sync", d_opsp[j][:, (12 * half) * 128:(12 * half) * 128 + 4 * 128],
                        s_opst[:], waits=[("s", v_loc)], sem="osp")
            if half == 1:
                Bopst.read_by("osp", P.counts["osp"])
            if half == 0:
                gqd = BGQT.take()
                tvg = None
                for lc in range(4):
                    mms = [(lambda dt=dt, lc=lc: s_wgq[:, dt * 512 + lc * 128:dt * 512 + lc * 128 + 128],
                            lambda dt=dt, j=j: s_xnT_own[:, dt * TOWN + j * 128:dt * TOWN + j * 128 + 128])
                           for dt in range(NDC)]
                    tvg = mm_group(5, (lc * 128, lc * 128 + 128), mms,
                                   waits=[("sw", v_wh), ("s", v_own)],
                                   extra_first=(gqd if lc == 0 else ()))
                for lc in range(4):
                    sv = P.op("scalar", lambda e, lc=lc: e.activation(
                        s_GQT[:, lc * 128:lc * 128 + 128],
                        pb[5][:, lc * 128:lc * 128 + 128], AF.Copy, scale=0.125),
                        waits=[("t", tvg)], inc="s")
                    B_[5].read_by("s", sv)
                v_gq = P.counts["s"]
                rd = BremT.take()
                for lh in range(8):
                    lc_l, po_l = lh // 2, 64 * (lh % 2)
                    tv = mm_group(3, (0, 64),
                                  [(lambda lc_l=lc_l, po_l=po_l: s_GQT[po_l:po_l + 64, lc_l * 128:lc_l * 128 + 128],
                                    lambda lc_l=lc_l, po_l=po_l: s_GKT[po_l:po_l + 64, lc_l * 64:lc_l * 64 + 64])],
                                  waits=[("s", v_gq), ("s", v_gkt)])
                    ev = P.op("vector", lambda e: e.tensor_tensor(
                        s_ls[:], pb[3][:, 0:64], s_lmk[:], AluOpType.add),
                        waits=[("t", tv), ("lm", vlk)] + Bls.take(), inc="v")
                    B_[3].read_by("v", ev)
                    Blmk.read_by("v", ev)
                    vm = P.op("vector", lambda e: e.tensor_reduce(
                        s_stat[:, 11:12], s_ls[:], AX.X, AluOpType.max, negate=True),
                        waits=Bstat.take(), inc="v")
                    se = P.op("scalar", lambda e: e.activation(
                        s_lse[:], s_ls[:], AF.Exp, bias=s_stat[:, 11:12],
                        accum_out=s_stat[:, 10:11]),
                        waits=[("v", vm)] + Blse.take(), inc="s")
                    Bls.read_by("s", se)
                    Bstat.read_by("s", se)
                    vr = P.op("vector", lambda e: e.reciprocal(
                        s_stat[:, 9:10], s_stat[:, 10:11]), waits=[("s", se)], inc="v")
                    va = P.op("vector", lambda e: e.tensor_scalar(
                        s_lse[:], s_lse[:], s_stat[:, 9:10], None, AluOpType.mult),
                        inc="v")
                    Bstat.read_by("v", va)
                    tv = P.op("tensor", lambda e: e.transpose(
                        pb[4][0:64, 0:128], s_lse[:], s_eye32[:]),
                        waits=[("v", va)] + B_[4].take(), inc="t")
                    Blse.read_by("t", tv)
                    sv = P.op("scalar", lambda e: e.copy(s_lpT[0:64, :],
                                                         pb[4][0:64, 0:128]),
                              waits=[("t", tv)] + BlpT.take(), inc="s")
                    B_[4].read_by("s", sv)
                    tv = mm_group(3, (128, 256),
                                  [(lambda lh=lh: s_GV[0:64, lh * 64:lh * 64 + 64],
                                    lambda: s_lpT[0:64, 0:128])],
                                  waits=[("s", P.counts["s"])], rows=(0, 64))
                    BlpT.read_by("t", tv)
                    sv = P.op("scalar", lambda e, lc_l=lc_l, po_l=po_l: e.copy(
                        s_remT[po_l:po_l + 64, lc_l * 128:lc_l * 128 + 128],
                        pb[3][0:64, 128:256]),
                        waits=[("t", tv)] + (rd if lh == 0 else []), inc="s")
                    B_[3].read_by("s", sv)
                v_rem = P.counts["s"]
                omd = Boprem.take()
                for ec in range(8):
                    bank = ec % 2
                    mms = [(lambda lc=lc, ec=ec: s_rout[:, lc * D + ec * 128:lc * D + ec * 128 + 128],
                            lambda lc=lc: s_remT[:, lc * 128:lc * 128 + 128])
                           for lc in range(4)]
                    tv = mm_group(bank, (0, 128), mms,
                                  waits=[("s", v_rem), ("sw", v_wh)],
                                  extra_first=(omd if ec == 0 else ()))
                    BremT.read_by("t", tv)
                    sv = P.op("scalar", lambda e, ec=ec, bank=bank: e.copy(
                        s_oprem[:, ec * 128:ec * 128 + 128], pb[bank][:, 0:128]),
                        waits=[("t", tv)], inc="s")
                    B_[bank].read_by("s", sv)
                v_orem = P.counts["s"]
                vsp = P.dma("sync", d_opsp[j][:, 4 * 128:12 * 128], s_oprem[:],
                            waits=[("s", v_orem)], sem="osp")
                Boprem.read_by("osp", P.counts["osp"])
                Bopst.read_by("osp", P.counts["osp"])
    v_gate_all = P.counts["s"]
    P.phase_barrier()
    cd.close()

    pe = ExitStack()
    s_wout = pe.enter_context(nc.sbuf_tensor("wout_s", [128, NDC * D], BF16))
    s_ehot = pe.enter_context(nc.sbuf_tensor("ehot_s", [32, 8 * 128], F32))
    s_xuTo = pe.enter_context(nc.sbuf_tensor("xuTo", [128, NDC * 128], BF16))
    s_ce = pe.enter_context(nc.sbuf_tensor("ce", [128, 256], F32))
    s_cT = pe.enter_context(nc.sbuf_tensor("cT", [128, 2 * 128], BF16))
    s_g1 = pe.enter_context(nc.sbuf_tensor("g1", [128, D], F32))
    s_opl = pe.enter_context(nc.sbuf_tensor("opl", [128, 16 * 128], BF16))
    s_gT = pe.enter_context(nc.sbuf_tensor("gT", [32, 128], F32))
    s_grb = pe.enter_context(nc.sbuf_tensor("grb", [128, 128], F32))
    s_otmp = pe.enter_context(nc.sbuf_tensor("otmp", [128, 128], F32))
    s_opreb = pe.enter_context(nc.sbuf_tensor("opreb", [128, 8 * 128], BF16))
    s_s1t = pe.enter_context(nc.sbuf_tensor("s1t", [128, D], F32))
    BxuTo, Bce, BcT, Bg1, Bopl, BgT, Bgrb, Botmp, Bopreb, Bs1t = (Buf() for _ in range(10))
    P.phase_barrier()
    for dt in range(NDC):
        P.dma("gpsimd", s_wout[:, dt * D:(dt + 1) * D],
              d_w["wout"][dt * 128:(dt + 1) * 128, :])
    v_wo = P.counts["sw"]
    v_eh = P.dma("sync", s_ehot[:], d_ehot, sem="lm")
    v_allsp = P.counts["osp"]

    for j in range(NSLOT):
        wl = ln_into(s_xtile, s_xn_b, Bxtile, Bxn, s_eps[:],
                     src_row=x1own_row(j * 128))
        wi = ln_into(s_xn_b, s_xi, Bxn, Bxi, s_eps[:], affine=True)
        su = l2unit(s_xi, s_xu, Bxi, Bxu, wi)
        vx = xpose(s_xu, 0, s_xuTo, 128, 0, ("s", su), Bsrc=Bxu)
        BxuTo.read_by("s", vx)
        mms = [(lambda dt=dt: s_xuTo[:, dt * 128:(dt + 1) * 128],
                lambda dt=dt: s_muu[:, dt * 256:(dt + 1) * 256])
               for dt in range(NDC)]
        tv = mm_group(6, (0, 256), mms, waits=[("s", vx), ("v", v_muu)])
        BxuTo.read_by("t", tv)
        vm = P.op("vector", lambda e: e.tensor_reduce(
            s_stat[:, 11:12], pb[6][:, 0:256], AX.X, AluOpType.max, negate=True),
            waits=[("t", tv)] + Bstat.take(), inc="v")
        se = P.op("scalar", lambda e: e.activation(
            s_ce[:], pb[6][:, 0:256], AF.Exp, bias=s_stat[:, 11:12],
            accum_out=s_stat[:, 10:11]), waits=[("v", vm)] + Bce.take(), inc="s")
        B_[6].read_by("s", se)
        Bstat.read_by("s", se)
        vr = P.op("vector", lambda e: e.reciprocal(s_stat[:, 9:10], s_stat[:, 10:11]),
                  waits=[("s", se)], inc="v")
        va = P.op("vector", lambda e: e.tensor_scalar(
            s_ce[:], s_ce[:], s_stat[:, 9:10], None, AluOpType.mult), inc="v")
        Bstat.read_by("v", va)
        for esc in range(2):
            bank = 4 + esc
            tv = P.op("tensor", lambda e, esc=esc, bank=bank: e.transpose(
                pb[bank][:, 0:128], s_ce[:, esc * 128:esc * 128 + 128], s_eye32[:]),
                waits=[("v", va)] + B_[bank].take(), inc="t")
            sv = P.op("scalar", lambda e, esc=esc, bank=bank: e.copy(
                s_cT[:, esc * 128:esc * 128 + 128], pb[bank][:, 0:128]),
                waits=[("t", tv)] + (BcT.take() if esc == 0 else []), inc="s")
            B_[bank].read_by("s", sv)
        Bce.read_by("t", P.counts["t"])
        v_ct = P.counts["s"]
        for dc in range(2):
            bank = dc
            mms = [(lambda esc=esc: s_cT[:, esc * 128:esc * 128 + 128],
                    lambda esc=esc, dc=dc: s_so[:, esc * D + dc * 512:esc * D + dc * 512 + 512])
                   for esc in range(2)]
            tv = mm_group(bank, (0, 512), mms, waits=[("s", v_ct)])
            BcT.read_by("t", tv)
            Bso.read_by("t", tv)
            sv = P.op("scalar", lambda e, dc=dc, bank=bank: e.activation(
                s_g1[:, dc * 512:(dc + 1) * 512], pb[bank][:, 0:512], AF.Silu),
                waits=[("t", tv)] + (Bg1.take() if dc == 0 else []), inc="s")
            B_[bank].read_by("s", sv)
        v_g1 = P.counts["s"]
        vop = P.dma("sync", s_opl[:], d_opsp[j], waits=[("osp", v_allsp)] + Bopl.take(),
                    sem="osp")
        tv = P.op("tensor", lambda e, j=j: e.transpose(
            pb[2][0:32, 0:128], s_gates[:, j * 32:(j + 1) * 32], s_eye32[:]),
            waits=B_[2].take() + [("s", v_gate_all)], inc="t")
        Bgates.read_by("t", tv)
        sv = P.op("scalar", lambda e: e.copy(s_gT[0:32, :], pb[2][0:32, 0:128]),
                  waits=[("t", tv)] + BgT.take(), inc="s")
        B_[2].read_by("s", sv)
        v_gT = P.counts["s"]
        for ec in range(NDC):
            bank = 2 + ec % 2
            tv = mm_group(bank, (0, 128),
                          [(lambda ec=ec: s_ehot[0:32, ec * 128:ec * 128 + 128],
                            lambda: s_gT[0:32, 0:128])],
                          waits=[("s", v_gT), ("lm", v_eh)])
            BgT.read_by("t", tv)
            vv = P.op("vector", lambda e, bank=bank: e.tensor_copy(
                s_grb[:], pb[bank][:, 0:128]),
                waits=[("t", tv)] + Bgrb.take(), inc="v")
            B_[bank].read_by("v", vv)
            vv = P.op("vector", lambda e, ec=ec: e.tensor_tensor(
                s_otmp[:], s_opl[:, (4 + ec) * 128:(4 + ec) * 128 + 128],
                s_grb[:], AluOpType.mult),
                waits=[("osp", vop)] + Botmp.take(), inc="v")
            Bgrb.read_by("v", vv)
            loccol = ec * 128 if ec < 4 else (8 + ec) * 128
            vv = P.op("vector", lambda e, ec=ec, loccol=loccol: e.tensor_tensor(
                s_opreb[:, ec * 128:ec * 128 + 128],
                s_opl[:, loccol:loccol + 128], s_otmp[:], AluOpType.add),
                waits=(Bopreb.take() if ec == 0 else []), inc="v")
            Botmp.read_by("v", vv)
            Bopl.read_by("v", vv)
        v_opb = P.counts["v"]
        for ecw in range(2):
            bank = ecw
            mms = [(lambda dt=dt: s_opreb[:, dt * 128:(dt + 1) * 128],
                    lambda dt=dt, ecw=ecw: s_wout[:, dt * D + ecw * 512:dt * D + ecw * 512 + 512])
                   for dt in range(NDC)]
            tv = mm_group(bank, (0, 512), mms, waits=[("v", v_opb), ("sw", v_wo)])
            Bopreb.read_by("t", tv)
            sv = P.op("scalar", lambda e, ecw=ecw, bank=bank: e.copy(
                s_s1t[:, ecw * 512:(ecw + 1) * 512], pb[bank][:, 0:512]),
                waits=[("t", tv)] + (Bs1t.take() if ecw == 0 else []), inc="s")
            B_[bank].read_by("s", sv)
        v_s1 = P.counts["s"]
        vv = P.op("vector", lambda e: e.tensor_tensor(
            s_g1[:], s_g1[:], s_s1t[:], AluOpType.mult),
            waits=[("s", v_s1), ("s", v_g1)], inc="v")
        Bs1t.read_by("v", vv)
        vv = P.op("vector", lambda e: e.tensor_tensor(
            s_x2t[:], s_xtile[:], s_g1[:], AluOpType.add),
            waits=Bx2t.take(), inc="v")
        Bg1.read_by("v", vv)
        Bxtile.read_by("v", vv)
        sd = P.dma("sync", d_x2p[j * 128:j * 128 + 128, :], s_x2t[:],
                   waits=[("v", vv)], sem="st")
        Bx2t.read_by("st", sd)

    P.phase_barrier()
    pe.close()
    les.close()
    return


def build_l1():
    nc = bass.Bass("TRN2", target_bir_lowering=False)
    es = ExitStack()
    P = Prog(nc, es)
    dd = declare_l1(nc)
    emit_l1(nc, es, P, dd)
    P.barrier_wait("sync", [("st", P.counts["st"])])
    P.finalize()
    es.close()
    return nc, es


U32 = mybir.dt.uint32

BLOCK_POS = {}
for _j, _g in enumerate(SNAKE_A):
    BLOCK_POS[_g] = (0, _j)
for _j, _g in enumerate(SNAKE_B):
    BLOCK_POS[_g] = (1, _j)

POLL_R = [4 * k for k in range(1, 151)]
while POLL_R[-1] < 4000000:
    POLL_R.append(min(int(POLL_R[-1] * 1.05) + 1, 4000000))
NREL = (POLL_R[-1] + 59999) // 60000


def build_fused():
    nc = bass.Bass("TRN2", target_bir_lowering=False)
    es = ExitStack()
    P = Prog(nc, es)
    dd0 = declare_l0(nc, out_kind="Internal")
    dd1 = declare_l1(nc, shared=dd0, with_x1=False)

    d_shx = nc.dram_tensor("x1shx", [T, D], F32, kind="Internal",
                           addr_space="Shared").ap()
    d_fl = nc.dram_tensor("exfl", [256, 4], U32, kind="Internal",
                          addr_space="Shared").ap()
    d_flf = d_fl.rearrange("a b -> (a b)")
    d_sidx = nc.dram_tensor("sidx", [128, NSLOT], U32, kind="ExternalInput").ap()
    d_fidx = nc.dram_tensor("fidx", [128, 1], U32, kind="ExternalInput").ap()
    d_run = nc.dram_tensor("runid", [128, 4], U32, kind="ExternalInput").ap()

    s_sidx = es.enter_context(nc.sbuf_tensor("s_sidx", [128, NSLOT], U32))
    s_fidx = es.enter_context(nc.sbuf_tensor("s_fidx", [128, 1], U32))
    s_run = es.enter_context(nc.sbuf_tensor("s_run", [128, 4], U32))
    s_flb = es.enter_context(nc.sbuf_tensor("s_flb", [1, 8], U32))
    s_tick = es.enter_context(nc.sbuf_tensor("s_tick", [128, 96], mybir.dt.bfloat16))

    P.dma("sync", s_sidx[:], d_sidx, sem="ex")
    P.dma("sync", s_fidx[:], d_fidx, sem="ex")
    P.dma("sync", s_run[:], d_run, sem="ex")

    def scatter(j, s_x1t, waits):
        return P.op("gpsimd", lambda e, j=j, s=s_x1t: e.indirect_dma_start(
            out=d_shx,
            out_offset=bass.IndirectOffsetOnAxis(ap=s_sidx[:, j:j + 1], axis=0),
            in_=s[:], in_offset=None),
            waits=waits + [("ex", 48)], inc="sc", inc_by=16)

    emit_l0(nc, es, P, dd0, scatter=scatter)

    P.op("gpsimd", lambda e: e.indirect_dma_start(
        out=d_fl,
        out_offset=bass.IndirectOffsetOnAxis(ap=s_fidx[:], axis=0),
        in_=s_run[:], in_offset=None),
        waits=[("sc", 16 * NSLOT)], inc="fl", inc_by=16)

    for rk in POLL_R:
        P.op("sync", lambda e: e.dma_start(out=s_flb[0:1, 0:4],
                                           in_=d_flf[0:4]),
             waits=[("pt", rk)], inc="pl", inc_by=16)
        P.op("sync", lambda e: e.dma_start(out=s_flb[0:1, 4:8],
                                           in_=d_flf[512:516]),
             waits=[("pt", rk)], inc="pl", inc_by=16)

    def spin(e):
        e.memset(s_flb[:], 0)
        with e.register("diff") as rdiff, e.register("rf") as rf, \
             e.register("rr") as rr, e.register("acc") as racc:
            e.reg_load(rr, s_run[0:1, 0:1])
            e.reg_mov(rdiff, 1)
            with e.While(rdiff):
                e.memset(s_tick[:], 0).then_inc(P.sems["pt"], 1)
                e.reg_mov(racc, 0)
                for cell in (0, 4):
                    e.reg_load(rf, s_flb[0:1, cell:cell + 1])
                    e.reg_alu(rf, rf, rr, AluOpType.bitwise_xor)
                    e.reg_alu(racc, racc, rf, AluOpType.bitwise_or)
                e.reg_alu(rdiff, racc, 0, AluOpType.bitwise_or)
        for _ in range(NREL):
            e.memset(s_tick[0:1, 0:4], 0).then_inc(P.sems["pt"], 60000)
        return e.memset(s_tick[0:1, 0:4], 0)

    P.sem("pt")
    P.counts["pt"] += NREL * 60000
    P.op("gpsimd", spin, waits=[("fl", 16)], inc="g")

    P.phase_barrier()

    def x1_row(tok0):
        return d_shx[tok0:tok0 + 128, :]

    def x1own_row(r0):
        return dd0["x1p"][r0:r0 + 128, :]

    emit_l1(nc, es, P, dd1, x1_row=x1_row, x1own_row=x1own_row)
    P.barrier_wait("sync", [("st", P.counts["st"])])
    P.finalize()
    es.close()
    return nc, es


def host_prep_fused(inputs):
    per0 = host_prep_l0(inputs)
    per1 = host_prep_l1(inputs, None)
    runid = np.uint32(np.random.randint(1, 2**31))
    out = []
    for c in range(8):
        role = c % 2
        snake = SNAKE_A if role == 0 else SNAKE_B
        sidx = np.empty((128, NSLOT), np.uint32)
        for j, g in enumerate(snake):
            sidx[:, j] = 128 * g + np.arange(128, dtype=np.uint32)
        ex = {
            "sidx": sidx,
            "fidx": (np.arange(128, dtype=np.uint32) + role * 128).reshape(128, 1),
            "runid": np.full((128, 4), runid, np.uint32),
        }
        out.append({**per0[c], **per1[c], **ex})
    return out


from concourse.bass_utils import run_bass_kernel_spmd

_NC_CACHE = {}


def kernel(**inputs):
    if "f" not in _NC_CACHE:
        _NC_CACHE["f"] = build_fused()
    nc, _es = _NC_CACHE["f"]
    per = host_prep_fused(inputs)
    res = run_bass_kernel_spmd(nc, per, core_ids=list(range(8)))
    out = np.zeros((4, T, D), np.float32)
    for c in range(8):
        part = np.asarray(res.results[c]["x2p"])
        snake = SNAKE_A if c % 2 == 0 else SNAKE_B
        for j, g in enumerate(snake):
            out[c // 2, 128 * g:128 * g + 128] = part[128 * j:128 * j + 128]
    return out
